# version 03
# baseline (speedup 1.0000x reference)
import numpy as np
from contextlib import ExitStack

import concourse.bass as bass
import concourse.mybir as mybir
from concourse.bass_utils import run_bass_kernel_spmd

F32 = mybir.dt.float32
F32R = mybir.dt.float32r
AF = mybir.ActivationFunctionType
ALU = mybir.AluOpType

NCORES = 8
T = 16384
S = T // NCORES
D = 1024
DI = 2048
DTR = 64
NST = 16
DSL = DI // NCORES
EPS = 1e-5
HALO = 3
NE = 8
HS = 2048
CM = 64
EH = 1024

SH = S + HALO
CH = [512, 512, 512, 512, 4]
CO = [0, 512, 1024, 1536, 2047]
NG = D // 128
NM = 2 * DI // 128
NDT = DI // 128


class Prog:

    def __init__(self, nc):
        self.nc = nc
        self.ops = {k: [] for k in
                    ("sync", "vector", "scalar", "tensor", "gpsimd")}
        self.counts = {}
        self.base_waits = []

    def op(self, engine, fn, waits=(), inc=None, inc_by=1):
        ev = None
        if inc is not None:
            key = inc.name
            self.counts[key] = self.counts.get(key, 0) + inc_by
            ev = (inc, self.counts[key])
        allw = list(self.base_waits) + [w for w in waits if w is not None]
        self.ops[engine].append((fn, allw,
                                 (inc, inc_by) if inc is not None else None))
        return ev

    def dma(self, engine, out, in_, sem, waits=()):
        return self.op(engine, lambda eng, o=out, i=in_: eng.dma_start(out=o, in_=i),
                       waits=waits, inc=sem, inc_by=16)

    def emit(self, block):
        def mk(name):
            def body(eng):
                hi = {}
                for fn, waits, inc in self.ops[name]:
                    for wt in waits:
                        sem, val = wt[0], wt[1]
                        force = len(wt) > 2 and wt[2]
                        if not force and hi.get(sem.name, -1) >= val:
                            continue
                        eng.wait_ge(sem, val)
                        hi[sem.name] = max(hi.get(sem.name, -1), val)
                    ins = fn(eng)
                    if inc is not None:
                        ins.then_inc(inc[0], inc[1])
            return body
        block.sync(mk("sync"))
        block.vector(mk("vector"))
        block.scalar(mk("scalar"))
        block.tensor(mk("tensor"))
        block.gpsimd(mk("gpsimd"))


def slot(c):
    return 515 * (c % 2)


def build_L1():
    nc = bass.Bass("TRN2", target_bir_lowering=False, debug=False,
                   detect_race_conditions=False)

    xh = nc.dram_tensor("xh", [128, NG, SH], F32, kind="ExternalInput").ap()
    inw = nc.dram_tensor("inw", [NM, 128, NG, 128], F32, kind="ExternalInput").ap()
    normw = nc.dram_tensor("normw", [128, NG], F32, kind="ExternalInput").ap()
    xpw = nc.dram_tensor("xpw", [128, NDT, 128], F32, kind="ExternalInput").ap()
    dtw = nc.dram_tensor("dtw", [128, DI], F32, kind="ExternalInput").ap()
    convw = nc.dram_tensor("convw", [128, NDT, 4], F32, kind="ExternalInput").ap()
    convb = nc.dram_tensor("convb", [128, NDT], F32, kind="ExternalInput").ap()
    dtb = nc.dram_tensor("dtb", [128, NDT], F32, kind="ExternalInput").ap()
    ones128 = nc.dram_tensor("ones128", [128, 2], F32, kind="ExternalInput").ap()
    ones1 = nc.dram_tensor("ones1", [1, 128], F32, kind="ExternalInput").ap()
    cst = nc.dram_tensor("cst", [128, 8], F32, kind="ExternalInput").ap()

    u_o = nc.dram_tensor("u_o", [DI, S], F32, kind="ExternalOutput").ap()
    z_o = nc.dram_tensor("z_o", [DI, SH], F32, kind="ExternalOutput").ap()
    dl_o = nc.dram_tensor("dl_o", [DI, S], F32, kind="ExternalOutput").ap()
    bc_o = nc.dram_tensor("bc_o", [64, S], F32, kind="ExternalOutput").ap()

    with ExitStack() as stk:
        e = stk.enter_context
        s_w = e(nc.sbuf_tensor("s_w", [128, 3, NG, 128], F32R))
        s_x = e(nc.sbuf_tensor("s_x", [128, 2, NG, 512], F32R))
        s_normw = e(nc.sbuf_tensor("s_normw", [128, NG], F32))
        s_xpw = e(nc.sbuf_tensor("s_xpw", [128, NDT, 128], F32R))
        s_dtw = e(nc.sbuf_tensor("s_dtw", [128, DI], F32R))
        s_convw = e(nc.sbuf_tensor("s_convw", [128, NDT, 4], F32))
        s_convb = e(nc.sbuf_tensor("s_convb", [128, NDT], F32))
        s_dtb = e(nc.sbuf_tensor("s_dtb", [128, NDT], F32))
        s_o128 = e(nc.sbuf_tensor("s_o128", [128, 2], F32R))
        s_o1 = e(nc.sbuf_tensor("s_o1", [1, 128], F32R))
        s_cst = e(nc.sbuf_tensor("s_cst", [128, 8], F32))
        s_sq = e(nc.sbuf_tensor("s_sq", [128, 2, 512], F32R))
        s_rstd = e(nc.sbuf_tensor("s_rstd", [1, 512], F32R))
        s_rstdb = e(nc.sbuf_tensor("s_rstdb", [128, 2, 512], F32))
        s_xp = e(nc.sbuf_tensor("s_xp", [128, NDT, 1030], F32))
        s_z = e(nc.sbuf_tensor("s_z", [128, 2, 512], F32))
        s_xc = e(nc.sbuf_tensor("s_xc", [128, 2, 512], F32))
        s_u = e(nc.sbuf_tensor("s_u", [128, 3, 512], F32R))
        s_xdbl = e(nc.sbuf_tensor("s_xdbl", [128, 512], F32R))
        s_sp = e(nc.sbuf_tensor("s_sp", [128, 2, 512], F32))
        s_dl = e(nc.sbuf_tensor("s_dl", [128, 2, 512], F32))
        p_mm = e(nc.psum_tensor("p_mm", [128, 2, 512], F32))
        p_ss = e(nc.psum_tensor("p_ss", [2, 512], F32))
        p_rb = e(nc.psum_tensor("p_rb", [128, 512], F32))
        p_xd = e(nc.psum_tensor("p_xd", [128, 512], F32))
        p_dt = e(nc.psum_tensor("p_dt", [128, 2, 512], F32))
        ld = e(nc.semaphore("ld"))
        lx = [e(nc.semaphore(f"lx{i}")) for i in range(2)]
        lw = [e(nc.semaphore(f"lw{i}")) for i in range(3)]
        oz = [e(nc.semaphore(f"oz{i}")) for i in range(2)]
        ou = [e(nc.semaphore(f"ou{i}")) for i in range(3)]
        od = [e(nc.semaphore(f"od{i}")) for i in range(2)]
        ob = e(nc.semaphore("ob"))
        tq = e(nc.semaphore("tq"))
        vq = e(nc.semaphore("vq"))
        aq = e(nc.semaphore("aq"))
        blk = e(nc.Block())

        P = Prog(nc)

        ev_c = None
        for dst, src in [(s_normw[:], normw), (s_xpw[:], xpw.bitcast(F32R)),
                         (s_dtw[:], dtw.bitcast(F32R)), (s_convw[:], convw),
                         (s_convb[:], convb), (s_dtb[:], dtb),
                         (s_o128[:], ones128.bitcast(F32R)),
                         (s_o1[:], ones1.bitcast(F32R)), (s_cst[:], cst)]:
            ev_c = P.dma("sync", dst, src, ld)
        P.base_waits = [ev_c]

        E = {}

        def prev_ring(kind, i, depth):
            return E.get((kind, i - depth))

        def chunk_section(c):
            w_ = CH[c]
            E[("xload", c)] = P.dma(
                "sync", s_x[:, c % 2, :, 0:w_],
                xh[:, :, CO[c]:CO[c] + w_].bitcast(F32R), lx[c % 2],
                waits=[E.get(("mm", (c - 2) * NM + NM - 1, NG - 1)),
                       E.get(("xscale", c - 2, NG - 1))])

            for g in range(NG):
                gg = c * NG + g
                E[("sq", c, g)] = P.op(
                    "scalar",
                    lambda eng, g=g, w_=w_, c=c: eng.activation(
                        s_sq[:, (c * NG + g) % 2, 0:w_],
                        s_x[:, c % 2, g, 0:w_].bitcast(F32), AF.Square),
                    waits=[E[("xload", c)], ev_c, prev_ring("stats", gg, 2)],
                    inc=aq)
                E[("stats", gg)] = P.op(
                    "tensor",
                    lambda eng, g=g, w_=w_, c=c: eng.matmul(
                        p_ss[:, 0:w_], s_o128[:],
                        s_sq[:, (c * NG + g) % 2, 0:w_],
                        start=(g == 0), stop=(g == NG - 1)),
                    waits=[E[("sq", c, g)],
                           E.get(("rstdL", c - 1)) if g == 0 else None],
                    inc=tq)
                E[("xscale", c, g)] = P.op(
                    "vector",
                    lambda eng, g=g, w_=w_, c=c: eng.tensor_scalar_mul(
                        s_x[:, c % 2, g, 0:w_],
                        s_x[:, c % 2, g, 0:w_].bitcast(F32),
                        s_normw[:, g:g + 1]),
                    waits=[E[("sq", c, g)]], inc=vq)

            E[("rstdL", c)] = P.op(
                "scalar",
                lambda eng, w_=w_: eng.activation(
                    s_rstd[0:1, 0:w_], p_ss[0:1, 0:w_],
                    AF.Ln, bias=s_cst[0:1, 0:1], scale=s_cst[0:1, 1:2]),
                waits=[E[("stats", c * NG + NG - 1)]], inc=aq)
            E[("rstd", c)] = P.op(
                "scalar",
                lambda eng, w_=w_: eng.activation(
                    s_rstd[0:1, 0:w_],
                    s_rstd[0:1, 0:w_].bitcast(F32), AF.Exp,
                    scale=s_cst[0:1, 4:5]),
                inc=aq)
            E[("rb", c)] = P.op(
                "tensor",
                lambda eng, w_=w_: eng.matmul(
                    p_rb[:, 0:w_], s_o1[:], s_rstd[0:1, 0:w_],
                    start=True, stop=True),
                waits=[E[("rstd", c)], E.get(("rstdb", c - 1))], inc=tq)
            E[("rstdb", c)] = P.op(
                "scalar",
                lambda eng, w_=w_, c=c: eng.activation(
                    s_rstdb[:, c % 2, 0:w_], p_rb[:, 0:w_], AF.Copy),
                waits=[E[("rb", c)], E.get(("evac", c - 2, NM - 1))], inc=aq)

            for m in range(NM):
                k = c * NM + m
                E[("wload", k)] = P.dma(
                    "gpsimd", s_w[:, k % 3, :, :], inw[m, :, :, :].bitcast(F32R),
                    lw[k % 3],
                    waits=[E.get(("mm", k - 3, NG - 1))])
                for g in range(NG):
                    bank_war = None
                    if g == 0:
                        bank_war = E.get(("evac", (k - 2) // NM, (k - 2) % NM)) \
                            if k - 2 >= 0 else None
                    E[("mm", k, g)] = P.op(
                        "tensor",
                        lambda eng, m=m, g=g, w_=w_, c=c, k=k: eng.matmul(
                            p_mm[:, m % 2, 0:w_],
                            s_w[:, k % 3, g, :],
                            s_x[:, c % 2, g, 0:w_],
                            start=(g == 0), stop=(g == NG - 1)),
                        waits=[E[("wload", k)], E[("xscale", c, g)], bank_war],
                        inc=tq)
                emm = E[("mm", k, NG - 1)]
                if m < NDT:
                    last = None
                    if c < 4:
                        last = P.op(
                            "vector",
                            lambda eng, m=m, w_=w_, c=c: eng.tensor_mul(
                                s_xp[:, m, slot(c):slot(c) + w_],
                                p_mm[:, m % 2, 0:w_], s_rstdb[:, c % 2, 0:w_]),
                            waits=[emm, E[("rstdb", c)],
                                   E.get(("conv", c - 2, m))],
                            inc=vq)
                    if c >= 1:
                        tl = 1 if c == 4 else 0
                        last = P.op(
                            "vector",
                            lambda eng, m=m, c=c, tl=tl: eng.tensor_mul(
                                s_xp[:, m, slot(c - 1) + 512:slot(c - 1) + 515],
                                p_mm[:, m % 2, tl:tl + 3],
                                s_rstdb[:, c % 2, tl:tl + 3]),
                            waits=[emm, E[("rstdb", c)],
                                   E.get(("conv", c - 3, m))],
                            inc=vq)
                        E[("tail", c, m)] = last
                    E[("evac", c, m)] = last
                else:
                    mz = m - NDT
                    E[("evac", c, m)] = P.op(
                        "vector",
                        lambda eng, m=m, mz=mz, w_=w_, c=c: eng.tensor_mul(
                            s_z[:, mz % 2, 0:w_],
                            p_mm[:, m % 2, 0:w_], s_rstdb[:, c % 2, 0:w_]),
                        waits=[emm, E[("rstdb", c)],
                               prev_ring("zdma", c * NDT + mz, 2)],
                        inc=vq)
                    zlo = 1 if c == 4 else 0
                    E[("zdma", c * NDT + mz)] = P.dma(
                        "sync", z_o[mz * 128:(mz + 1) * 128,
                                    CO[c] + zlo:CO[c] + w_],
                        s_z[:, mz % 2, zlo:w_], oz[mz % 2],
                        waits=[E[("evac", c, m)]])

        def out_section(j):
            for g in range(NDT):
                ug = j * NDT + g
                first = P.op(
                    "vector",
                    lambda eng, g=g, j=j: eng.tensor_scalar_mul(
                        s_xc[:, g % 2, :],
                        s_xp[:, g, slot(j) + 0:slot(j) + 512],
                        s_convw[:, g:g + 1, 0]),
                    waits=[E.get(("evac", j, g)), E.get(("tail", j + 1, g)),
                           prev_ring("silu", ug, 2)],
                    inc=vq)
                for jj in (1, 2, 3):
                    last = P.op(
                        "vector",
                        lambda eng, g=g, j=j, jj=jj: eng.scalar_tensor_tensor(
                            s_xc[:, g % 2, :],
                            s_xp[:, g, slot(j) + jj:slot(j) + jj + 512],
                            s_convw[:, g:g + 1, jj],
                            s_xc[:, g % 2, :],
                            ALU.mult, ALU.add),
                        inc=vq)
                E[("conv", j, g)] = last
                E[("silu", ug)] = P.op(
                    "scalar",
                    lambda eng, g=g, ug=ug: eng.activation(
                        s_u[:, ug % 3, :],
                        s_xc[:, g % 2, :], AF.Silu,
                        bias=s_convb[:, g:g + 1]),
                    waits=[E[("conv", j, g)],
                           prev_ring("xproj", ug, 3), prev_ring("udma", ug, 3)],
                    inc=aq)
                E[("udma", ug)] = P.dma(
                    "sync", u_o[g * 128:(g + 1) * 128, j * 512:(j + 1) * 512],
                    s_u[:, ug % 3, :].bitcast(F32), ou[ug % 3],
                    waits=[E[("silu", ug)]])
                E[("xproj", ug)] = P.op(
                    "tensor",
                    lambda eng, g=g, ug=ug: eng.matmul(
                        p_xd[:, :], s_xpw[:, g, :], s_u[:, ug % 3, :],
                        start=(g == 0), stop=(g == NDT - 1)),
                    waits=[E[("silu", ug)],
                           E.get(("xdev", j - 1)) if g == 0 else None],
                    inc=tq)
            E[("xdev", j)] = P.op(
                "scalar",
                lambda eng: eng.activation(s_xdbl[:, :],
                                           p_xd[:, :], AF.Copy),
                waits=[E[("xproj", j * NDT + NDT - 1)], E.get(("bcdma", j - 1)),
                       E.get(("dt", j - 1, NDT - 1))],
                inc=aq)
            E[("bcdma", j)] = P.dma(
                "sync", bc_o[0:48, j * 512:(j + 1) * 512],
                s_xdbl[0:48, :].bitcast(F32), ob, waits=[E[("xdev", j)]])
            for mt in range(NDT):
                mtg = j * NDT + mt
                E[("dt", j, mt)] = P.op(
                    "tensor",
                    lambda eng, mt=mt: eng.matmul(
                        p_dt[:, mt % 2, :],
                        s_dtw[64:128, mt * 128:(mt + 1) * 128],
                        s_xdbl[64:128, :], start=True, stop=True),
                    waits=[E[("xdev", j)], prev_ring("relu", mtg, 2)],
                    inc=tq)
                P.op("scalar",
                     lambda eng, mt=mt: eng.activation(
                         s_sp[:, mt % 2, :], p_dt[:, mt % 2, :], AF.Abs,
                         bias=s_dtb[:, mt:mt + 1]),
                     waits=[E[("dt", j, mt)], prev_ring("dladd", mtg, 2)],
                     inc=aq)
                P.op("scalar",
                     lambda eng, mt=mt: eng.activation(
                         s_sp[:, mt % 2, :], s_sp[:, mt % 2, :], AF.Exp,
                         scale=s_cst[:, 3:4]),
                     inc=aq)
                P.op("scalar",
                     lambda eng, mt=mt: eng.activation(
                         s_sp[:, mt % 2, :], s_sp[:, mt % 2, :], AF.Ln,
                         bias=s_cst[:, 2:3]),
                     inc=aq)
                E[("relu", mtg)] = P.op(
                    "scalar",
                    lambda eng, mt=mt: eng.activation(
                        s_dl[:, mt % 2, :], p_dt[:, mt % 2, :], AF.Relu,
                        bias=s_dtb[:, mt:mt + 1]),
                    waits=[prev_ring("dldma", mtg, 2)],
                    inc=aq)
                E[("dladd", mtg)] = P.op(
                    "vector",
                    lambda eng, mt=mt: eng.tensor_add(
                        s_dl[:, mt % 2, :], s_dl[:, mt % 2, :],
                        s_sp[:, mt % 2, :]),
                    waits=[E[("relu", mtg)]], inc=vq)
                E[("dldma", mtg)] = P.dma(
                    "sync", dl_o[mt * 128:(mt + 1) * 128, j * 512:(j + 1) * 512],
                    s_dl[:, mt % 2, :], od[mt % 2],
                    waits=[E[("dladd", mtg)]])

        for c in range(5):
            chunk_section(c)
            if c >= 1:
                out_section(c - 1)

        allsems = [ld] + lx + lw + oz + ou + od + [ob]
        P.op("gpsimd", lambda eng: eng.memset(s_o128[:].bitcast(F32), 1.0),
             waits=[(s, P.counts[s.name]) for s in allsems
                    if s.name in P.counts] +
                   [(vq, P.counts["vq"]), (aq, P.counts["aq"]),
                    (tq, P.counts["tq"])],
             inc=ld, inc_by=1)

        P.emit(blk)
    return nc


def _l1_host_inputs(inputs):
    x = np.ascontiguousarray(inputs["L0_attn_x"][0].T).astype(np.float32, copy=False)
    in_proj_w = inputs["L0_attn_in_proj_w"]
    x_proj_w = inputs["L0_attn_x_proj_w"]
    dt_proj_w = inputs["L0_attn_dt_proj_w"]
    conv_w = inputs["L0_attn_conv_w"]
    conv_b = inputs["L0_attn_conv_b"]
    dt_proj_b = inputs["L0_attn_dt_proj_b"]
    norm_w = inputs["L0_attn_norm_w"]

    inw = np.ascontiguousarray(
        np.asarray(in_proj_w).reshape(NM, 128, NG, 128).transpose(0, 3, 2, 1))
    normw = np.ascontiguousarray(np.asarray(norm_w).reshape(NG, 128).T)
    xpw = np.zeros((128, NDT, 128), dtype=np.float32)
    xpt = np.asarray(x_proj_w)
    for g in range(NDT):
        blkw = xpt[:, g * 128:(g + 1) * 128]
        xpw[:, g, 0:16] = blkw[64:80].T
        xpw[:, g, 32:48] = blkw[80:96].T
        xpw[:, g, 64:128] = blkw[0:64].T
    dtw = np.zeros((128, DI), dtype=np.float32)
    dtw[64:128, :] = np.asarray(dt_proj_w).T
    convw = np.ascontiguousarray(
        np.asarray(conv_w)[:, 0, :].reshape(NDT, 128, 4).transpose(1, 0, 2))
    convb = np.ascontiguousarray(np.asarray(conv_b).reshape(NDT, 128).T)
    dtb = np.ascontiguousarray(np.asarray(dt_proj_b).reshape(NDT, 128).T)
    ones128 = np.ones((128, 2), dtype=np.float32)
    ones1 = np.ones((1, 128), dtype=np.float32)
    cstv = np.zeros((128, 8), dtype=np.float32)
    cstv[:, 0] = EPS
    cstv[:, 1] = 1.0 / D
    cstv[:, 2] = 1.0
    cstv[:, 3] = -1.0
    cstv[:, 4] = -0.5

    maps = []
    for k in range(NCORES):
        lo = k * S - HALO
        if lo < 0:
            xh2 = np.concatenate(
                [np.zeros((D, HALO), np.float32), x[:, 0:(k + 1) * S]], axis=1)
        else:
            xh2 = x[:, lo:(k + 1) * S]
        xh2 = np.ascontiguousarray(xh2.reshape(NG, 128, SH).transpose(1, 0, 2))
        maps.append({
            "xh": xh2, "inw": inw, "normw": normw, "xpw": xpw, "dtw": dtw,
            "convw": convw, "convb": convb, "dtb": dtb,
            "ones128": ones128, "ones1": ones1, "cst": cstv,
        })
    return maps




TC2 = 1024
NC2 = T // TC2
NSUB = TC2 // 512


def build_L2():
    nc = bass.Bass("TRN2", target_bir_lowering=False, debug=False,
                   detect_race_conditions=False)

    u_i = nc.dram_tensor("u_i", [DSL, T], F32, kind="ExternalInput").ap()
    dl_i = nc.dram_tensor("dl_i", [DSL, T], F32, kind="ExternalInput").ap()
    B_i = nc.dram_tensor("B_i", [16, T], F32, kind="ExternalInput").ap()
    C_i = nc.dram_tensor("C_i", [16, T], F32, kind="ExternalInput").ap()
    alog = nc.dram_tensor("alog", [128, 2, NST], F32, kind="ExternalInput").ap()
    dvec = nc.dram_tensor("dvec", [128, 2], F32, kind="ExternalInput").ap()
    sel = nc.dram_tensor("sel", [16, 16 * 128], F32, kind="ExternalInput").ap()
    ident = nc.dram_tensor("ident", [128, 128], F32, kind="ExternalInput").ap()
    cst = nc.dram_tensor("cst", [128, 8], F32, kind="ExternalInput").ap()

    y_o = nc.dram_tensor("y_o", [DSL, T], F32, kind="ExternalOutput").ap()

    with ExitStack() as stk:
        e = stk.enter_context
        s_u = e(nc.sbuf_tensor("s_u", [128, 2, 2, TC2], F32))
        s_dl = e(nc.sbuf_tensor("s_dl", [128, 2, 2, TC2], F32))
        s_du = e(nc.sbuf_tensor("s_du", [128, 2, 2, TC2], F32))
        s_B = e(nc.sbuf_tensor("s_B", [16, 2, TC2], F32R))
        s_C = e(nc.sbuf_tensor("s_C", [16, 2, TC2], F32R))
        s_sel = e(nc.sbuf_tensor("s_sel", [16, 16 * 128], F32R))
        s_ident = e(nc.sbuf_tensor("s_ident", [128, 128], F32R))
        s_alog = e(nc.sbuf_tensor("s_alog", [128, 2, NST], F32))
        s_A = e(nc.sbuf_tensor("s_A", [128, 2, NST], F32))
        s_D = e(nc.sbuf_tensor("s_D", [128, 2], F32))
        s_cst = e(nc.sbuf_tensor("s_cst", [128, 8], F32))
        s_dA = e(nc.sbuf_tensor("s_dA", [128, 2, 2, TC2], F32))
        s_w = e(nc.sbuf_tensor("s_w", [128, 2, TC2], F32))
        s_h = e(nc.sbuf_tensor("s_h", [128, 2, TC2], F32))
        s_p = e(nc.sbuf_tensor("s_p", [128, 2, TC2], F32R))
        s_Ct = e(nc.sbuf_tensor("s_Ct", [128, 2, TC2], F32))
        s_carry = e(nc.sbuf_tensor("s_carry", [128, 2, NST], F32))
        s_y = e(nc.sbuf_tensor("s_y", [128, 2, TC2], F32))
        p_B = e(nc.psum_tensor("p_B", [128, 2, 512], F32))
        p_C = e(nc.psum_tensor("p_C", [128, 2, 512], F32))
        p_y = e(nc.psum_tensor("p_y", [128, 2, NSUB, 512], F32))
        ld = e(nc.semaphore("ld"))
        ldl = [e(nc.semaphore(f"ldl{i}")) for i in range(2)]
        lu = [e(nc.semaphore(f"lu{i}")) for i in range(2)]
        lbc = [e(nc.semaphore(f"lbc{i}")) for i in range(2)]
        oy = [e(nc.semaphore(f"oy{i}")) for i in range(2)]
        tq = e(nc.semaphore("tq"))
        vq = e(nc.semaphore("vq"))
        aq = e(nc.semaphore("aq"))
        pq = e(nc.semaphore("pq"))
        blk = e(nc.Block())

        P = Prog(nc)
        E = {}

        ev_c = None
        for dst, src_ in [(s_sel[:], sel.bitcast(F32R)),
                          (s_ident[:], ident.bitcast(F32R)),
                          (s_alog[:], alog), (s_D[:], dvec), (s_cst[:], cst)]:
            ev_c = P.dma("sync", dst, src_, ld)
        P.base_waits = [ev_c]

        eva0 = P.op("scalar",
                    lambda eng: eng.activation(s_A[:], s_alog[:], AF.Exp),
                    waits=[ev_c], inc=aq)
        evA = P.op("vector",
                   lambda eng: eng.tensor_scalar_mul(s_A[:], s_A[:],
                                                     s_cst[:, 3:4]),
                   waits=[eva0], inc=vq)

        for c in range(NC2):
            r = c % 2
            for dt2 in range(2):
                E[("lu", c, dt2)] = P.dma(
                    "sync", s_u[:, dt2, r, :],
                    u_i[dt2 * 128:(dt2 + 1) * 128, c * TC2:(c + 1) * TC2],
                    lu[r], waits=[E.get(("yev", c - 2, dt2, NSUB - 1))])
                E[("ldl", c, dt2)] = P.dma(
                    "sync", s_dl[:, dt2, r, :],
                    dl_i[dt2 * 128:(dt2 + 1) * 128, c * TC2:(c + 1) * TC2],
                    ldl[r], waits=[E.get(("dA", c - 2, dt2, NST - 1))])
            E[("lB", c)] = P.dma(
                "sync", s_B[:, r, :], B_i[:, c * TC2:(c + 1) * TC2].bitcast(F32R),
                lbc[r], waits=[E.get(("bb", c - 2, NST - 1, NSUB - 1))])
            E[("lC", c)] = P.dma(
                "sync", s_C[:, r, :], C_i[:, c * TC2:(c + 1) * TC2].bitcast(F32R),
                lbc[r], waits=[E.get(("cb", c - 2, NST - 1, NSUB - 1))])

            for dt2 in range(2):
                E[("du", c, dt2)] = P.op(
                    "vector",
                    lambda eng, dt2=dt2, r=r: eng.tensor_mul(
                        s_du[:, dt2, r, :], s_dl[:, dt2, r, :],
                        s_u[:, dt2, r, :]),
                    waits=[E[("lu", c, 1)], E[("ldl", c, 1)],
                           E.get(("w", c - 2, dt2, NST - 1))],
                    inc=vq)

            for n in range(NST):
                nb = (n * NSUB) % 2
                for s in range(NSUB):
                    gi = (c * NST + n) * NSUB + s
                    E[("bb", c, n, s)] = P.op(
                        "tensor",
                        lambda eng, n=n, s=s, r=r, gi=gi: eng.matmul(
                            p_B[:, gi % 2, :],
                            s_sel[:, n * 128:(n + 1) * 128],
                            s_B[:, r, s * 512:(s + 1) * 512],
                            start=True, stop=True),
                        waits=[E[("lC", c)], E.get(("w2", gi - 2))],
                        inc=tq)
                for dt2 in range(2):
                    E[("dA", c, dt2, n)] = P.op(
                        "scalar",
                        lambda eng, dt2=dt2, n=n, r=r: eng.activation(
                            s_dA[:, dt2, (n + NST * c) % 2, :],
                            s_dl[:, dt2, r, :], AF.Exp,
                            scale=s_A[:, dt2, n:n + 1]),
                        waits=[E[("ldl", c, 1)], evA,
                               E.get(("scan", c, dt2, n - 2)) or
                               E.get(("scan", c - 1, dt2, NST - 2 + (n % 2)))],
                        inc=aq)
                    for s in range(NSUB):
                        gi = (c * NST + n) * NSUB + s
                        E[("w2", gi)] = P.op(
                            "vector",
                            lambda eng, dt2=dt2, n=n, s=s, r=r, gi=gi:
                            eng.tensor_mul(
                                s_w[:, (2 * n + dt2) % 2,
                                    s * 512:(s + 1) * 512],
                                s_du[:, dt2, r, s * 512:(s + 1) * 512],
                                p_B[:, gi % 2, :]),
                            waits=[E[("bb", c, n, s)], E[("du", c, dt2)]],
                            inc=vq)
                    E[("w", c, dt2, n)] = E[("w2", (c * NST + n) * NSUB + NSUB - 1)]
                    hwar = (E.get(("pp", c, dt2, n - 1)) if n >= 1
                            else E.get(("pp", c - 1, dt2, NST - 1)))
                    E[("scan", c, dt2, n)] = P.op(
                        "vector",
                        lambda eng, dt2=dt2, n=n, c=c: eng.tensor_tensor_scan(
                            s_h[:, (2 * n + dt2) % 2, :],
                            s_dA[:, dt2, (n + NST * c) % 2, :],
                            s_w[:, (2 * n + dt2) % 2, :],
                            (0.0 if c == 0 else s_carry[:, dt2, n:n + 1]),
                            ALU.mult, ALU.add),
                        waits=[E[("dA", c, dt2, n)],
                               (E[("carry", c - 1, dt2, n)][0],
                                E[("carry", c - 1, dt2, n)][1], True)
                               if c >= 1 else None, hwar],
                        inc=vq)
                    sc_ev = E[("scan", c, dt2, n)]
                    E[("carry", c, dt2, n)] = P.op(
                        "vector",
                        lambda eng, dt2=dt2, n=n: eng.tensor_copy(
                            s_carry[:, dt2, n:n + 1],
                            s_h[:, (2 * n + dt2) % 2, TC2 - 1:TC2]),
                        waits=[(sc_ev[0], sc_ev[1], True)],
                        inc=vq)
                for s in range(NSUB):
                    gi = (c * NST + n) * NSUB + s
                    E[("cb", c, n, s)] = P.op(
                        "tensor",
                        lambda eng, n=n, s=s, r=r, gi=gi: eng.matmul(
                            p_C[:, gi % 2, :],
                            s_sel[:, n * 128:(n + 1) * 128],
                            s_C[:, r, s * 512:(s + 1) * 512],
                            start=True, stop=True),
                        waits=[E[("lC", c)], E.get(("ctev", gi - 2))],
                        inc=tq)
                    E[("ctev", gi)] = P.op(
                        "scalar",
                        lambda eng, n=n, s=s, gi=gi: eng.activation(
                            s_Ct[:, n % 2, s * 512:(s + 1) * 512],
                            p_C[:, gi % 2, :], AF.Copy),
                        waits=[E[("cb", c, n, s)],
                               E.get(("pp", c, 1, n - 2)) or
                               (E.get(("pp", c - 1, 1, NST - 2 + (n % 2)))
                                if c >= 1 else None)],
                        inc=aq)
                for dt2 in range(2):
                    E[("pp", c, dt2, n)] = P.op(
                        "gpsimd",
                        lambda eng, dt2=dt2, n=n: eng.tensor_mul(
                            s_p[:, (2 * n + dt2) % 2, :],
                            s_h[:, (2 * n + dt2) % 2, :],
                            s_Ct[:, n % 2, :]),
                        waits=[E[("scan", c, dt2, n)],
                               E[("ctev", (c * NST + n) * NSUB + NSUB - 1)],
                               (E.get(("ya", c, dt2, n - 1, NSUB - 1))
                                if n >= 1
                                else E.get(("ya", c - 1, dt2, NST - 1,
                                            NSUB - 1)))],
                        inc=pq)
                    for s in range(NSUB):
                        E[("ya", c, dt2, n, s)] = P.op(
                            "tensor",
                            lambda eng, dt2=dt2, n=n, s=s: eng.matmul(
                                p_y[:, dt2, s, :], s_ident[:],
                                s_p[:, (2 * n + dt2) % 2,
                                    s * 512:(s + 1) * 512],
                                start=(n == 0), stop=(n == NST - 1)),
                            waits=[E[("pp", c, dt2, n)],
                                   (E.get(("yev", c - 1, dt2, s))
                                    if n == 0 else None)],
                            inc=tq)

            for dt2 in range(2):
                for s in range(NSUB):
                    E[("yev", c, dt2, s)] = P.op(
                        "vector",
                        lambda eng, dt2=dt2, s=s, r=r: eng.scalar_tensor_tensor(
                            s_y[:, dt2 % 2, s * 512:(s + 1) * 512],
                            s_u[:, dt2, r, s * 512:(s + 1) * 512],
                            s_D[:, dt2:dt2 + 1],
                            p_y[:, dt2, s, :],
                            ALU.mult, ALU.add),
                        waits=[E[("ya", c, dt2, NST - 1, s)],
                               E.get(("ydma", c - 1, dt2))],
                        inc=vq)
                E[("ydma", c, dt2)] = P.dma(
                    "sync", y_o[dt2 * 128:(dt2 + 1) * 128, c * TC2:(c + 1) * TC2],
                    s_y[:, dt2 % 2, :], oy[dt2],
                    waits=[E[("yev", c, dt2, NSUB - 1)]])

        allsems = [ld] + ldl + lu + lbc + oy
        P.op("gpsimd", lambda eng: eng.memset(s_cst[:, 7:8], 0.0),
             waits=[(sm, P.counts[sm.name]) for sm in allsems
                    if sm.name in P.counts] +
                   [(vq, P.counts["vq"]), (aq, P.counts["aq"]),
                    (tq, P.counts["tq"]), (pq, P.counts["pq"])],
             inc=pq, inc_by=1)

        P.emit(blk)
    return nc


def _l2_host_inputs(inputs, res1):
    A_log = np.asarray(inputs["L0_attn_A_log"])
    Dv = np.asarray(inputs["L0_attn_D"])
    u_full = [res1.results[k]["u_o"] for k in range(NCORES)]
    dl_full = [res1.results[k]["dl_o"] for k in range(NCORES)]
    bc_full = np.concatenate([res1.results[k]["bc_o"] for k in range(NCORES)],
                             axis=1)
    B_full = np.ascontiguousarray(bc_full[0:16])
    C_full = np.ascontiguousarray(bc_full[32:48])
    sel = np.zeros((16, 16 * 128), dtype=np.float32)
    for n in range(16):
        sel[n, n * 128:(n + 1) * 128] = 1.0
    ident = np.eye(128, dtype=np.float32)
    cstv = np.zeros((128, 8), dtype=np.float32)
    cstv[:, 0] = EPS
    cstv[:, 1] = 1.0 / D
    cstv[:, 2] = 1.0
    cstv[:, 3] = -1.0
    cstv[:, 4] = -0.5

    maps = []
    for k in range(NCORES):
        lo = k * DSL
        u_sl = np.ascontiguousarray(
            np.concatenate([u_full[j][lo:lo + DSL] for j in range(NCORES)],
                           axis=1))
        dl_sl = np.ascontiguousarray(
            np.concatenate([dl_full[j][lo:lo + DSL] for j in range(NCORES)],
                           axis=1))
        al = np.ascontiguousarray(
            A_log[lo:lo + DSL].reshape(2, 128, NST).transpose(1, 0, 2))
        dv = np.ascontiguousarray(Dv[lo:lo + DSL].reshape(2, 128).T)
        maps.append({
            "u_i": u_sl, "dl_i": dl_sl, "B_i": B_full, "C_i": C_full,
            "alog": al, "dvec": dv, "sel": sel, "ident": ident, "cst": cstv,
        })
    return maps


TCA = 512
NCA = S // TCA
TCB = 256
NCB = S // TCB
NMT = D // 128
NKT = DI // 128


def build_L3(debug=False):
    nc = bass.Bass("TRN2", target_bir_lowering=False, debug=False,
                   detect_race_conditions=False)
    BF16 = mybir.dt.bfloat16

    y_i = nc.dram_tensor("y_i", [DI, S], F32, kind="ExternalInput").ap()
    z_i = nc.dram_tensor("z_i", [DI, S], F32, kind="ExternalInput").ap()
    xr = nc.dram_tensor("xr", [128, NMT, S], F32, kind="ExternalInput").ap()
    outw = nc.dram_tensor("outw", [128, NKT, D], F32, kind="ExternalInput").ap()
    wenc = nc.dram_tensor("wenc", [128, NMT, HS], F32, kind="ExternalInput").ap()
    wdec = nc.dram_tensor("wdec", [128, NKT, D], F32, kind="ExternalInput").ap()
    ws2c = nc.dram_tensor("ws2c", [128, NKT, CM], F32, kind="ExternalInput").ap()
    wr_ = nc.dram_tensor("wr_", [64, 8], F32, kind="ExternalInput").ap()
    w1 = nc.dram_tensor("w1", [128, NE // 2, NE, 128], F32, kind="ExternalInput").ap()
    w2 = nc.dram_tensor("w2", [128, 8, NE, CM], F32, kind="ExternalInput").ap()
    wc2s = nc.dram_tensor("wc2s", [64, HS], F32, kind="ExternalInput").ap()
    benc = nc.dram_tensor("benc", [128, NKT], F32, kind="ExternalInput").ap()
    bs2c = nc.dram_tensor("bs2c", [64, 1], F32, kind="ExternalInput").ap()
    br_ = nc.dram_tensor("br_", [8, 1], F32, kind="ExternalInput").ap()
    b1 = nc.dram_tensor("b1", [128, NE, 8], F32, kind="ExternalInput").ap()
    b2 = nc.dram_tensor("b2", [64, NE], F32, kind="ExternalInput").ap()
    bc2s = nc.dram_tensor("bc2s", [128, NKT], F32, kind="ExternalInput").ap()
    bdec = nc.dram_tensor("bdec", [128, NMT], F32, kind="ExternalInput").ap()
    lng = nc.dram_tensor("lng", [128, NMT], F32, kind="ExternalInput").ap()
    lnb = nc.dram_tensor("lnb", [128, NMT], F32, kind="ExternalInput").ap()
    sel8 = nc.dram_tensor("sel8", [8, NE, CM], F32, kind="ExternalInput").ap()
    ident = nc.dram_tensor("ident", [128, 128], F32, kind="ExternalInput").ap()
    ones128 = nc.dram_tensor("ones128", [128, 2], F32, kind="ExternalInput").ap()
    ones1 = nc.dram_tensor("ones1", [1, 128], F32, kind="ExternalInput").ap()
    cst = nc.dram_tensor("cst", [128, 8], F32, kind="ExternalInput").ap()
    xn2_d = nc.dram_tensor("xn2_d", [D, S], F32,
                           kind=("ExternalOutput" if debug else "Internal"))
    if debug:
        dbg_gate = nc.dram_tensor("dbg_gate", [8, S], F32, kind="ExternalOutput").ap()
        dbg_eo = nc.dram_tensor("dbg_eo", [64, S], F32, kind="ExternalOutput").ap()
        dbg_ffn = nc.dram_tensor("dbg_ffn", [D, S], F32, kind="ExternalOutput").ap()
        dbg_x2 = nc.dram_tensor("dbg_x2", [D, S], F32, kind="ExternalOutput").ap()
        dbg_mub = nc.dram_tensor("dbg_mub", [128, S], F32, kind="ExternalOutput").ap()
        dbg_rsb = nc.dram_tensor("dbg_rsb", [128, S], F32, kind="ExternalOutput").ap()

    out_o = nc.dram_tensor("out_o", [D, S], F32, kind="ExternalOutput").ap()

    with ExitStack() as stk:
        e = stk.enter_context
        s_ws2c = e(nc.sbuf_tensor("s_ws2c", [128, NKT, CM], F32R))
        s_wr = e(nc.sbuf_tensor("s_wr", [64, 8], F32R))
        s_w1 = e(nc.sbuf_tensor("s_w1", [128, NE // 2, NE, 128], F32R))
        s_w2 = e(nc.sbuf_tensor("s_w2", [128, 8, NE, CM], F32R))
        s_wc2s = e(nc.sbuf_tensor("s_wc2s", [64, HS], F32R))
        s_benc = e(nc.sbuf_tensor("s_benc", [128, NKT], F32))
        s_bs2c = e(nc.sbuf_tensor("s_bs2c", [64, 1], F32))
        s_br = e(nc.sbuf_tensor("s_br", [8, 1], F32))
        s_b1 = e(nc.sbuf_tensor("s_b1", [128, NE, 8], F32))
        s_b2 = e(nc.sbuf_tensor("s_b2", [64, NE], F32))
        s_bc2s = e(nc.sbuf_tensor("s_bc2s", [128, NKT], F32))
        s_bdec = e(nc.sbuf_tensor("s_bdec", [128, NMT], F32))
        s_lng = e(nc.sbuf_tensor("s_lng", [128, NMT], F32))
        s_lnb = e(nc.sbuf_tensor("s_lnb", [128, NMT], F32))
        s_sel8 = e(nc.sbuf_tensor("s_sel8", [8, NE, CM], F32R))
        s_ident = e(nc.sbuf_tensor("s_ident", [128, 128], F32))
        s_o128 = e(nc.sbuf_tensor("s_o128", [128, 2], F32R))
        s_o1 = e(nc.sbuf_tensor("s_o1", [1, 128], F32R))
        s_cst = e(nc.sbuf_tensor("s_cst", [128, 8], F32))
        s_sq = e(nc.sbuf_tensor("s_sq3", [128, 2, TCA], F32R))
        s_mu = e(nc.sbuf_tensor("s_mu", [1, TCA], F32R))
        s_mu2 = e(nc.sbuf_tensor("s_mu2", [1, TCA], F32))
        s_var = e(nc.sbuf_tensor("s_var", [1, TCA], F32R))
        s_mub = e(nc.sbuf_tensor("s_mub", [128, TCA], F32))
        s_fnc = e(nc.sbuf_tensor("s_fnc", [1, 2], F32))
        s_rsb = e(nc.sbuf_tensor("s_rsb", [128, TCA], F32))
        p_mm = e(nc.psum_tensor("p_mm3", [128, 2, TCA], F32))
        p_cont = e(nc.psum_tensor("p_cont", [128, TCB], F32))
        p_ge = e(nc.psum_tensor("p_ge", [64, TCB], F32))
        p_ye = e(nc.psum_tensor("p_ye", [64, TCB], F32))
        p_s = e(nc.psum_tensor("p_s", [2, 2, TCA], F32))
        p_msc = e(nc.psum_tensor("p_msc", [128, 512], F32))
        ld = e(nc.semaphore("ld"))
        lwA = e(nc.semaphore("lwA"))
        lwB = e(nc.semaphore("lwB"))
        lyz = [e(nc.semaphore(f"lyz{i}")) for i in range(2)]
        lxt = [e(nc.semaphore(f"lxt{i}")) for i in range(2)]
        lxn = [e(nc.semaphore(f"lxn{i}")) for i in range(2)]
        oxn = [e(nc.semaphore(f"oxn{i}")) for i in range(2)]
        oo = [e(nc.semaphore(f"oo{i}")) for i in range(2)]
        tq = e(nc.semaphore("tq"))
        vq = e(nc.semaphore("vq"))
        aq = e(nc.semaphore("aq"))
        blk = e(nc.Block())

        P = Prog(nc)
        E = {}

        ev_c = None
        for dst, src_ in [(s_ws2c[:], ws2c.bitcast(F32R)), (s_wr[:], wr_.bitcast(F32R)),
                          (s_w1[:], w1.bitcast(F32R)), (s_w2[:], w2.bitcast(F32R)),
                          (s_wc2s[:], wc2s.bitcast(F32R)), (s_benc[:], benc),
                          (s_bs2c[:], bs2c), (s_br[:], br_), (s_b1[:], b1),
                          (s_b2[:], b2), (s_bc2s[:], bc2s), (s_bdec[:], bdec),
                          (s_lng[:], lng), (s_lnb[:], lnb),
                          (s_sel8[:], sel8.bitcast(F32R)), (s_ident[:], ident),
                          (s_o128[:], ones128.bitcast(F32R)),
                          (s_o1[:], ones1.bitcast(F32R)), (s_cst[:], cst)]:
            ev_c = P.dma("sync", dst, src_, ld)
        P.base_waits = [ev_c]

        lnstate = {"prev": None}

        def vfence():
            return P.op("vector",
                        lambda eng: eng.tensor_copy(s_fnc[0:1, 0:1],
                                                    s_fnc[0:1, 1:2]),
                        inc=vq)

        def layer_norm(tag, src_fn, src_ev, ntiles, w_, affine, dst_fn,
                       dst_war_fn, post_fn=None):
            prev = lnstate["prev"]
            for i in range(ntiles):
                E[(tag, "sq", i)] = P.op(
                    "scalar",
                    lambda eng, i=i: eng.activation(
                        s_sq[:, i % 2, 0:w_], src_fn(i).bitcast(F32),
                        AF.Square),
                    waits=[src_ev(i), E.get((tag, "st2", i - 2)),
                           (E.get((prev, "app2", i)) if prev else None)],
                    inc=aq)
                E[(tag, "st1", i)] = P.op(
                    "tensor",
                    lambda eng, i=i: eng.matmul(
                        p_s[:, 0, 0:w_], s_o128[:], src_fn(i),
                        start=(i == 0), stop=(i == ntiles - 1)),
                    waits=[src_ev(i),
                           (E.get((prev, "mu")) if prev and i == 0 else None)],
                    inc=tq)
                E[(tag, "st2", i)] = P.op(
                    "tensor",
                    lambda eng, i=i: eng.matmul(
                        p_s[:, 1, 0:w_], s_o128[:], s_sq[:, i % 2, 0:w_],
                        start=(i == 0), stop=(i == ntiles - 1)),
                    waits=[E[(tag, "sq", i)],
                           (E.get((prev, "var")) if prev and i == 0 else None)],
                    inc=tq)
            E[(tag, "mu")] = P.op(
                "vector",
                lambda eng: eng.tensor_scalar(
                    s_mu[0:1, 0:w_], p_s[0:1, 0, 0:w_], s_cst[0:1, 1:2],
                    None, ALU.mult),
                waits=[E[(tag, "st1", ntiles - 1)], ev_c,
                       (E.get((prev, "bc1")) if prev else None)],
                inc=vq)
            E[(tag, "mu")] = vfence()
            E[(tag, "mu2")] = P.op(
                "scalar",
                lambda eng: eng.activation(s_mu2[0:1, 0:w_],
                                           s_mu[0:1, 0:w_].bitcast(F32),
                                           AF.Square),
                waits=[E[(tag, "mu")],
                       (E.get((prev, "var")) if prev else None)],
                inc=aq)
            E[(tag, "var")] = P.op(
                "vector",
                lambda eng: eng.scalar_tensor_tensor(
                    s_var[0:1, 0:w_], p_s[0:1, 1, 0:w_],
                    s_cst[0:1, 1:2], s_mu2[0:1, 0:w_],
                    ALU.mult, ALU.subtract),
                waits=[E[(tag, "st2", ntiles - 1)], E[(tag, "mu2")],
                       (E.get((prev, "bc2")) if prev else None)],
                inc=vq)
            E[(tag, "var")] = vfence()
            E[(tag, "rstd")] = P.op(
                "scalar",
                lambda eng: eng.activation(s_var[0:1, 0:w_],
                                           s_var[0:1, 0:w_].bitcast(F32),
                                           AF.Ln, bias=s_cst[0:1, 0:1]),
                waits=[E[(tag, "var")]], inc=aq)
            E[(tag, "rstd2")] = P.op(
                "scalar",
                lambda eng: eng.activation(s_var[0:1, 0:w_],
                                           s_var[0:1, 0:w_].bitcast(F32),
                                           AF.Exp, scale=s_cst[0:1, 4:5]),
                inc=aq)
            E[(tag, "bc1")] = P.op(
                "tensor",
                lambda eng: eng.matmul(p_msc[:, 0:w_], s_o1[:],
                                       s_mu[0:1, 0:w_], start=True, stop=True),
                waits=[E[(tag, "mu")],
                       (E.get((prev, "bcev1")) if prev else None),
                       (E.get(("msc_free", tag[1])) if tag[0] == "lnF"
                        else None)],
                inc=tq)
            E[(tag, "bcev1")] = P.op(
                "scalar",
                lambda eng: eng.activation(s_mub[:, 0:w_], p_msc[:, 0:w_],
                                           AF.Copy),
                waits=[E[(tag, "bc1")],
                       (E.get((prev, "app1_last")) if prev else None)],
                inc=aq)
            E[(tag, "bc2")] = P.op(
                "tensor",
                lambda eng: eng.matmul(p_msc[:, 0:w_], s_o1[:],
                                       s_var[0:1, 0:w_], start=True, stop=True),
                waits=[E[(tag, "rstd2")], E[(tag, "bcev1")]], inc=tq)
            E[(tag, "bcev2")] = P.op(
                "scalar",
                lambda eng: eng.activation(s_rsb[:, 0:w_], p_msc[:, 0:w_],
                                           AF.Copy),
                waits=[E[(tag, "bc2")],
                       (E.get((prev, "app2_last")) if prev else None)],
                inc=aq)
            for i in range(ntiles):
                E[(tag, "app1", i)] = P.op(
                    "vector",
                    lambda eng, i=i: eng.tensor_sub(
                        s_sq[:, i % 2, 0:w_],
                        src_fn(i).bitcast(F32), s_mub[:, 0:w_]),
                    waits=[E[(tag, "bcev1")], E[(tag, "bcev2")],
                           E[(tag, "st2", i)]], inc=vq)
                ev = P.op(
                    "vector",
                    lambda eng, i=i: eng.tensor_mul(
                        dst_fn(i), s_sq[:, i % 2, 0:w_].bitcast(F32),
                        s_rsb[:, 0:w_]),
                    waits=[dst_war_fn(i)], inc=vq)
                if affine is not None:
                    gcol, bcol = affine
                    ev = P.op(
                        "vector",
                        lambda eng, i=i: eng.tensor_scalar(
                            dst_fn(i), dst_fn(i).bitcast(F32),
                            gcol(i), bcol(i), ALU.mult, ALU.add),
                        waits=[ev], inc=vq)
                E[(tag, "app2", i)] = ev
                if post_fn is not None:
                    post_fn(i, ev)
            E[(tag, "app1_last")] = E[(tag, "app1", ntiles - 1)]
            E[(tag, "app2_last")] = E[(tag, "app2", ntiles - 1)]
            lnstate["prev"] = tag
            return tag

        with ExitStack() as stkA:
            eA = stkA.enter_context
            s_outw = eA(nc.sbuf_tensor("s_outw", [128, NKT, D], BF16))
            s_yz = eA(nc.sbuf_tensor("s_yz", [128, 2, 2, TCA], F32))
            s_y2 = eA(nc.sbuf_tensor("s_y2", [128, NKT, TCA], BF16))
            s_xt = eA(nc.sbuf_tensor("s_xt", [128, 2, TCA], F32))
            s_x2 = eA(nc.sbuf_tensor("s_x2", [128, NMT, TCA], F32R))
            s_xn2c = eA(nc.sbuf_tensor("s_xn2c", [128, 2, TCA], F32))

            ev_wA = P.dma("gpsimd", s_outw[:], outw, lwA)

            for c in range(NCA):
                for kt in range(NKT):
                    gyk = c * NKT + kt
                    r = gyk % 2
                    war_yz = (E.get(("y2", (gyk - 2) // NKT, (gyk - 2) % NKT))
                              if gyk >= 2 else None)
                    E[("ly", c, kt)] = P.dma(
                        "sync", s_yz[:, 0, r, :],
                        y_i[kt * 128:(kt + 1) * 128, c * TCA:(c + 1) * TCA],
                        lyz[r], waits=[war_yz])
                    E[("lz", c, kt)] = P.dma(
                        "sync", s_yz[:, 1, r, :],
                        z_i[kt * 128:(kt + 1) * 128, c * TCA:(c + 1) * TCA],
                        lyz[r], waits=[])
                    E[("sz", c, kt)] = P.op(
                        "scalar",
                        lambda eng, r=r: eng.activation(
                            s_yz[:, 1, r, :], s_yz[:, 1, r, :], AF.Silu),
                        waits=[E[("lz", c, kt)]], inc=aq)
                    E[("y2", c, kt)] = P.op(
                        "vector",
                        lambda eng, r=r, kt=kt: eng.tensor_mul(
                            s_y2[:, kt, :], s_yz[:, 0, r, :],
                            s_yz[:, 1, r, :]),
                        waits=[E[("sz", c, kt)], E[("lz", c, kt)],
                               E.get(("mmA", c - 1, NMT - 1, kt))],
                        inc=vq)
                for mt in range(NMT):
                    gx = c * NMT + mt
                    E[("lxt", c, mt)] = P.dma(
                        "sync", s_xt[:, gx % 2, :],
                        xr[:, mt, c * TCA:(c + 1) * TCA], lxt[gx % 2],
                        waits=[E.get(("x2ev", (gx - 2) // NMT, (gx - 2) % NMT))
                               if gx >= 2 else None])
                    for kt in range(NKT):
                        E[("mmA", c, mt, kt)] = P.op(
                            "tensor",
                            lambda eng, mt=mt, kt=kt: eng.matmul(
                                p_mm[:, mt % 2, :],
                                s_outw[:, kt, mt * 128:(mt + 1) * 128],
                                s_y2[:, kt, :],
                                start=(kt == 0), stop=(kt == NKT - 1)),
                            waits=[E[("y2", c, kt)],
                                   ev_wA if c == 0 and mt == 0 else None,
                                   ((E.get(("x2ev", c, mt - 2)) if mt >= 2
                                     else E.get(("x2ev", c - 1,
                                                 NMT - 2 + mt)))
                                    if kt == 0 else None)],
                            inc=tq)
                    P.op("vector",
                         lambda eng, mt=mt, gx=gx: eng.tensor_add(
                             s_x2[:, mt, :], p_mm[:, mt % 2, :],
                             s_xt[:, gx % 2, :]),
                         waits=[E[("mmA", c, mt, NKT - 1)], E[("lxt", c, mt)],
                                E.get((("lnA", c - 1), "app2", mt))],
                         inc=vq)
                    E[("x2ev", c, mt)] = vfence()
                if debug == "full":
                    for mt in range(NMT):
                        P.dma("sync", dbg_x2[mt * 128:(mt + 1) * 128,
                                             c * TCA:(c + 1) * TCA],
                              s_x2[:, mt, :].bitcast(F32), oo[1],
                              waits=[E[("x2ev", c, mt)]])
                tg = ("lnA", c)

                def xn_post(i, ev, c=c):
                    gxi = c * NMT + i
                    E[("xndma", gxi)] = P.dma(
                        "sync", xn2_d.ap()[i * 128:(i + 1) * 128,
                                           c * TCA:(c + 1) * TCA],
                        s_xn2c[:, gxi % 2, :], oxn[gxi % 2],
                        waits=[ev])

                layer_norm(
                    tg, lambda i: s_x2[:, i, :],
                    lambda i, c=c: E[("x2ev", c, i)],
                    NMT, TCA, None,
                    lambda i, c=c: s_xn2c[:, (c * NMT + i) % 2, :],
                    lambda i, c=c: E.get(("xndma", c * NMT + i - 2)),
                    post_fn=xn_post)
                if debug == "full":
                    P.dma("sync", dbg_mub[:, c * TCA:(c + 1) * TCA],
                          s_mub[:, :], oo[1], waits=[E[(tg, "bcev1")],
                                                     E[(tg, "app1_last")]])
                    P.dma("sync", dbg_rsb[:, c * TCA:(c + 1) * TCA],
                          s_rsb[:, :], oo[1], waits=[E[(tg, "bcev2")],
                                                     E[(tg, "app2_last")]])
            ev_a_done = E[("xndma", NCA * NMT - 1)]

        with ExitStack() as stkB:
            eB = stkB.enter_context
            s_wenc = eB(nc.sbuf_tensor("s_wenc", [128, NMT, HS], BF16))
            s_wdec = eB(nc.sbuf_tensor("s_wdec", [128, NKT, D], BF16))
            s_xnb = eB(nc.sbuf_tensor("s_xnb", [128, NMT, TCB], BF16))
            s_xnf = eB(nc.sbuf_tensor("s_xnf", [128, NMT, TCB], F32))
            s_spk = eB(nc.sbuf_tensor("s_spk", [128, 2, TCB], F32R))
            s_cont = eB(nc.sbuf_tensor("s_cont", [128, TCB], F32R))
            s_lgt = eB(nc.sbuf_tensor("s_lgt", [8, TCB], F32))
            s_gt = eB(nc.sbuf_tensor("s_gt", [128, 2, 8], F32))
            s_gtmp = eB(nc.sbuf_tensor("s_gtmp", [128, 2, 8], F32))
            s_gm = eB(nc.sbuf_tensor("s_gm", [128, 2, 4], F32))
            s_grow = eB(nc.sbuf_tensor("s_grow", [8, TCB], F32R))
            s_he = eB(nc.sbuf_tensor("s_he", [128, 8, TCB], F32R))
            s_ye = eB(nc.sbuf_tensor("s_ye", [64, TCB], F32))
            s_eo = eB(nc.sbuf_tensor("s_eo", [64, 2, TCB], F32R))
            s_sp2 = eB(nc.sbuf_tensor("s_sp2", [128, NKT, TCB], BF16))
            s_dec = eB(nc.sbuf_tensor("s_dec", [128, NMT, TCB], F32R))
            s_out = eB(nc.sbuf_tensor("s_out", [128, NMT, TCB], F32))

            P.dma("gpsimd", s_wenc[:], wenc, lwB, waits=[ev_a_done, ev_wA])
            ev_wB = P.dma("gpsimd", s_wdec[:], wdec, lwB)

            for c in range(NCB):
                ca = (c * TCB) // TCA
                E[("lxnb", c)] = P.dma(
                    "gpsimd", s_xnb[:, :, :],
                    xn2_d.ap().rearrange("(m p) t -> p m t", p=128)[
                        :, :, c * TCB:(c + 1) * TCB],
                    lxn[0],
                    waits=[ev_a_done,
                           E.get(("mmB", c - 1, NKT - 1, NMT - 1))])
                E[("lxnf", c)] = P.dma(
                    "sync", s_xnf[:, :, :],
                    xn2_d.ap().rearrange("(m p) t -> p m t", p=128)[
                        :, :, c * TCB:(c + 1) * TCB],
                    lxn[1],
                    waits=[ev_a_done, E.get(("x4", c - 1, NMT - 1))])
                for kt in range(NKT):
                    for mt in range(NMT):
                        E[("mmB", c, kt, mt)] = P.op(
                            "tensor",
                            lambda eng, kt=kt, mt=mt: eng.matmul(
                                p_mm[:, kt % 2, 0:TCB],
                                s_wenc[:, mt, kt * 128:(kt + 1) * 128],
                                s_xnb[:, mt, :],
                                start=(mt == 0), stop=(mt == NMT - 1)),
                            waits=[E[("lxnb", c)],
                                   ev_wB if c == 0 and kt == 0 else None,
                                   ((E.get(("spk", c, kt - 2)) if kt >= 2
                                     else E.get(("decev", c - 1,
                                                 NMT - 2 + kt)))
                                    if mt == 0 else None)],
                            inc=tq)
                    E[("spk", c, kt)] = P.op(
                        "scalar",
                        lambda eng, kt=kt: eng.activation(
                            s_spk[:, kt % 2, :], p_mm[:, kt % 2, 0:TCB],
                            AF.Sigmoid, bias=s_benc[:, kt:kt + 1]),
                        waits=[E[("mmB", c, kt, NMT - 1)],
                               E.get(("s2c", c, kt - 2)),
                               E.get(("s2c", c - 1, NKT - 2 + (kt % 2)))
                               if kt < 2 else None],
                        inc=aq)
                    E[("s2c", c, kt)] = P.op(
                        "tensor",
                        lambda eng, kt=kt: eng.matmul(
                            p_cont[0:64, :], s_ws2c[:, kt, :],
                            s_spk[:, kt % 2, :],
                            start=(kt == 0), stop=(kt == NKT - 1)),
                        waits=[E[("spk", c, kt)],
                               E.get(("contev", c - 1)) if kt == 0 else None],
                        inc=tq)
                ev1 = P.op(
                    "vector",
                    lambda eng: eng.tensor_scalar(
                        s_cont[0:64, :], p_cont[0:64, :], s_bs2c[:, 0:1],
                        None, ALU.add),
                    waits=[E[("s2c", c, NKT - 1)],
                           E.get(("w1m", c - 1, NE - 1, NE - 1))],
                    inc=vq)
                P.op("vector",
                     lambda eng: eng.tensor_scalar(
                         s_cont[64:128, :], p_cont[0:64, :], s_bs2c[:, 0:1],
                         None, ALU.add),
                     inc=vq)
                E[("contev", c)] = vfence()
                E[("wr", c)] = P.op(
                    "tensor",
                    lambda eng: eng.matmul(p_msc[0:8, 0:TCB], s_wr[:],
                                           s_cont[0:64, :],
                                           start=True, stop=True),
                    waits=[E[("contev", c)], E.get(("lgtev", c - 1)),
                           E.get((("lnO", c - 1), "bcev2"))], inc=tq)
                E[("lgtev", c)] = P.op(
                    "vector",
                    lambda eng: eng.tensor_scalar(
                        s_lgt[:, :], p_msc[0:8, 0:TCB], s_br[:, 0:1],
                        None, ALU.add),
                    waits=[E[("wr", c)], E.get(("tp", c - 1, 1))], inc=vq)
                E[("lgtev", c)] = vfence()
                for i in range(2):
                    E[("tp", c, i)] = P.op(
                        "tensor",
                        lambda eng, i=i: eng.transpose(
                            p_msc[:, 256 + i * 8:256 + (i + 1) * 8],
                            s_lgt[:, i * 128:(i + 1) * 128],
                            s_ident[0:8, 0:8]),
                        waits=[E[("lgtev", c)], E.get(("gtev", c - 1))], inc=tq)
                E[("gtev", c)] = P.op(
                    "vector",
                    lambda eng: eng.tensor_copy(
                        s_gt[:, :, :].rearrange("p a b -> p (a b)"),
                        p_msc[:, 256:272]),
                    waits=[E[("tp", c, 0)], E[("tp", c, 1)],
                           E.get(("gdone", c - 1))], inc=vq)

                def gop(fn, waits=(), engname="vector"):
                    return P.op(engname, fn, waits=waits,
                                inc=(vq if engname == "vector" else aq))

                ev0 = E[("gtev", c)]
                evm = gop(lambda eng: eng.tensor_reduce(
                    s_gm[:, :, 0:1], s_gt[:, :, :], mybir.AxisListType.X,
                    ALU.max, negate=True), waits=[ev0])
                evm = vfence()
                eve1 = gop(lambda eng: eng.activation(
                    s_gt[:, 0, :], s_gt[:, 0, :], AF.Exp,
                    bias=s_gm[:, 0, 0:1]), engname="scalar",
                    waits=[(evm[0], evm[1], True)])
                eve2 = gop(lambda eng: eng.activation(
                    s_gt[:, 1, :], s_gt[:, 1, :], AF.Exp,
                    bias=s_gm[:, 1, 0:1]), engname="scalar")
                evm1 = gop(lambda eng: eng.tensor_reduce(
                    s_gm[:, :, 0:1], s_gt[:, :, :], mybir.AxisListType.X,
                    ALU.max), waits=[eve1, eve2])
                ev = (evm1[0], evm1[1], True)
                for i in range(2):
                    ev = gop(lambda eng, i=i: eng.tensor_scalar(
                        s_gtmp[:, i, :], s_gt[:, i, :], s_gm[:, i, 0:1], None,
                        ALU.is_equal), waits=[ev])
                    ev = gop(lambda eng, i=i: eng.scalar_tensor_tensor(
                        s_gtmp[:, i, :], s_gtmp[:, i, :], s_cst[:, 5:6],
                        s_gt[:, i, :], ALU.mult, ALU.add))
                evm2 = gop(lambda eng: eng.tensor_reduce(
                    s_gm[:, :, 3:4], s_gtmp[:, :, :], mybir.AxisListType.X,
                    ALU.max), waits=[ev])
                ev = (evm2[0], evm2[1], True)
                for i in range(2):
                    ev = gop(lambda eng, i=i: eng.tensor_scalar(
                        s_gtmp[:, i, :], s_gt[:, i, :], s_gm[:, i, 0:1], None,
                        ALU.is_equal), waits=[ev])
                    ev = gop(lambda eng, i=i: eng.tensor_scalar(
                        s_gtmp[:, i, :], s_gtmp[:, i, :], s_gm[:, i, 0:1],
                        None, ALU.mult))
                    ev = gop(lambda eng, i=i: eng.tensor_scalar(
                        s_gt[:, i, :], s_gt[:, i, :], s_gm[:, i, 3:4], None,
                        ALU.is_equal))
                    ev = gop(lambda eng, i=i: eng.scalar_tensor_tensor(
                        s_gt[:, i, :], s_gt[:, i, :], s_gm[:, i, 3:4],
                        s_gtmp[:, i, :], ALU.mult, ALU.add))
                evdn = gop(lambda eng: eng.tensor_add(
                    s_gm[:, :, 0:1], s_gm[:, :, 0:1], s_gm[:, :, 3:4]),
                    waits=[ev])
                evrg = gop(lambda eng: eng.reciprocal(
                    s_gm[:, :, 2:3], s_gm[:, :, 0:1]),
                    waits=[(evdn[0], evdn[1], True)])
                ev = (evrg[0], evrg[1], True)
                for i in range(2):
                    ev = gop(lambda eng, i=i: eng.tensor_scalar(
                        s_gt[:, i, :], s_gt[:, i, :], s_gm[:, i, 2:3], None,
                        ALU.mult), waits=[ev])
                ev_gate = vfence()
                E[("gdone", c)] = ev_gate
                for i in range(2):
                    E[("tpb", c, i)] = P.op(
                        "tensor",
                        lambda eng, i=i: eng.transpose(
                            p_msc[0:8, 256 + i * 128:256 + (i + 1) * 128],
                            s_gt[:, i, :], s_ident[:]),
                        waits=[ev_gate, E.get(("growev", c - 1))], inc=tq)
                E[("growev", c)] = P.op(
                    "scalar",
                    lambda eng: eng.activation(s_grow[:, :],
                                               p_msc[0:8, 256:512], AF.Copy),
                    waits=[E[("tpb", c, 0)], E[("tpb", c, 1)]], inc=aq)
                if debug == "full":
                    P.dma("sync", dbg_gate[:, c * TCB:(c + 1) * TCB],
                          s_grow[:, :].bitcast(F32), oo[0],
                          waits=[E[("growev", c)]])
                for ex in range(NE):
                    E[("ge", c, ex)] = P.op(
                        "tensor",
                        lambda eng, ex=ex: eng.matmul(
                            p_ge[:, :], s_sel8[:, ex, :], s_grow[:, :],
                            start=True, stop=True),
                        waits=[E[("growev", c)],
                               (E.get(("eotmp", c, ex - 1)) if ex >= 1
                                else E.get(("eotmp", c - 1, NE - 1)))],
                        inc=tq)
                    base = 64 * (ex % 2)
                    for mt in range(NE):
                        E[("w1m", c, ex, mt)] = P.op(
                            "tensor",
                            lambda eng, ex=ex, mt=mt, base=base: eng.matmul(
                                p_mm[:, mt % 2, 0:TCB],
                                s_w1[base:base + 64, ex // 2, mt, :],
                                s_cont[base:base + 64, :],
                                start=True, stop=True),
                            waits=[E[("contev", c)],
                                   (E.get(("he", c, ex, mt - 2)) if mt >= 2
                                    else (E.get(("he", c, ex - 1, NE - 2 + mt))
                                          if ex >= 1
                                          else E.get(("he", c - 1, NE - 1,
                                                      NE - 2 + mt))))],
                            inc=tq)
                        E[("he", c, ex, mt)] = P.op(
                            "scalar",
                            lambda eng, ex=ex, mt=mt: eng.activation(
                                s_he[:, mt, :], p_mm[:, mt % 2, 0:TCB],
                                AF.Relu, bias=s_b1[:, ex, mt:mt + 1]),
                            waits=[E[("w1m", c, ex, mt)],
                                   (E.get(("w2m", c, ex - 1, NE - 1))
                                    if mt < 2 and ex >= 1 else None),
                                   (E.get(("w2m", c - 1, NE - 1, NE - 1))
                                    if mt < 2 and ex == 0 else None)],
                            inc=aq)
                    for kt in range(NE):
                        E[("w2m", c, ex, kt)] = P.op(
                            "tensor",
                            lambda eng, ex=ex, kt=kt: eng.matmul(
                                p_ye[:, :], s_w2[:, kt, ex, :],
                                s_he[:, kt, :],
                                start=(kt == 0), stop=(kt == NE - 1)),
                            waits=[E[("he", c, ex, kt)],
                                   ((E.get(("yeev", c, ex - 1)) if ex >= 1
                                     else E.get(("yeev", c - 1, NE - 1)))
                                    if kt == 0 else None)],
                            inc=tq)
                    E[("yeev", c, ex)] = P.op(
                        "vector",
                        lambda eng, ex=ex: eng.tensor_scalar(
                            s_ye[:, :], p_ye[:, :], s_b2[:, ex:ex + 1],
                            None, ALU.add),
                        waits=[E[("w2m", c, ex, NE - 1)],
                               (E.get(("eotmp", c, ex - 1)) if ex >= 1
                                else E.get(("eotmp", c - 1, NE - 1)))],
                        inc=vq)
                    E[("eotmp", c, ex)] = P.op(
                        "vector",
                        lambda eng, ex=ex: eng.tensor_mul(
                            s_eo[:, 1, :], s_ye[:, :], p_ge[:, :]),
                        waits=[E[("yeev", c, ex)], E[("ge", c, ex)],
                               E.get(("eoev", c, ex - 1))], inc=vq)
                    if ex == 0:
                        E[("eoev", c, ex)] = P.op(
                            "vector",
                            lambda eng: eng.tensor_copy(
                                s_eo[:, 0, :], s_eo[:, 1, :].bitcast(F32)),
                            waits=[E.get(("c2s", c - 1, NKT - 1))], inc=vq)
                    else:
                        E[("eoev", c, ex)] = P.op(
                            "vector",
                            lambda eng: eng.tensor_add(
                                s_eo[:, 0, :], s_eo[:, 0, :].bitcast(F32),
                                s_eo[:, 1, :].bitcast(F32)),
                            inc=vq)
                if debug == "full":
                    P.dma("sync", dbg_eo[:, c * TCB:(c + 1) * TCB],
                          s_eo[:, 0, :].bitcast(F32), oo[0],
                          waits=[E[("eoev", c, NE - 1)]])
                E[("eoev", c, NE - 1)] = vfence()
                for mt in range(NKT):
                    E[("c2s", c, mt)] = P.op(
                        "tensor",
                        lambda eng, mt=mt: eng.matmul(
                            p_mm[:, mt % 2, 0:TCB],
                            s_wc2s[:, mt * 128:(mt + 1) * 128],
                            s_eo[:, 0, :],
                            start=True, stop=True),
                        waits=[E[("eoev", c, NE - 1)],
                               (E.get(("sp2", c, mt - 2)) if mt >= 2
                                else E.get(("decev", c - 1, NMT - 2 + mt)))],
                        inc=tq)
                    E[("sp2", c, mt)] = P.op(
                        "scalar",
                        lambda eng, mt=mt: eng.activation(
                            s_sp2[:, mt, :], p_mm[:, mt % 2, 0:TCB],
                            AF.Sigmoid, bias=s_bc2s[:, mt:mt + 1]),
                        waits=[E[("c2s", c, mt)]], inc=aq)
                for mt in range(NMT):
                    for kt in range(NKT):
                        E[("mmD", c, mt, kt)] = P.op(
                            "tensor",
                            lambda eng, mt=mt, kt=kt: eng.matmul(
                                p_mm[:, mt % 2, 0:TCB],
                                s_wdec[:, kt, mt * 128:(mt + 1) * 128],
                                s_sp2[:, kt, :],
                                start=(kt == 0), stop=(kt == NKT - 1)),
                            waits=[E[("sp2", c, kt)],
                                   ((E.get(("decev", c, mt - 2)) if mt >= 2
                                     else E.get(("sp2", c, NKT - 2 + mt)))
                                    if kt == 0 else None)],
                            inc=tq)
                    E[("decev", c, mt)] = P.op(
                        "vector",
                        lambda eng, mt=mt: eng.tensor_scalar(
                            s_dec[:, mt, :], p_mm[:, mt % 2, 0:TCB],
                            s_bdec[:, mt:mt + 1], None, ALU.add),
                        waits=[E[("mmD", c, mt, NKT - 1)],
                               E.get((("lnO", c - 1), "app1", mt))],
                        inc=vq)
                E[("msc_free", c)] = E[("growev", c)]
                tg2 = ("lnF", c)
                layer_norm(
                    tg2, lambda i: s_dec[:, i, :],
                    lambda i, c=c: E[("decev", c, i)],
                    NMT, TCB,
                    (lambda i: s_lng[:, i:i + 1], lambda i: s_lnb[:, i:i + 1]),
                    lambda i: s_dec[:, i, :],
                    lambda i: None)
                if debug == "full":
                    for mt in range(NMT):
                        P.dma("sync", dbg_ffn[mt * 128:(mt + 1) * 128,
                                              c * TCB:(c + 1) * TCB],
                              s_dec[:, mt, :].bitcast(F32), oo[0],
                              waits=[E[(tg2, "app2", mt)]])
                for mt in range(NMT):
                    E[("x4", c, mt)] = P.op(
                        "vector",
                        lambda eng, mt=mt: eng.tensor_add(
                            s_dec[:, mt, :], s_dec[:, mt, :].bitcast(F32),
                            s_xnf[:, mt, :]),
                        waits=[E[(tg2, "app2", mt)], E[("lxnf", c)],
                               ((oo[0], P.counts.get("oo0", 0))
                                if debug == "full" else None)],
                        inc=vq)
                E[("x4f", c)] = vfence()
                E[("B", c, "dece_war")] = E[("x4f", c)]
                tg3 = ("lnO", c)
                layer_norm(
                    tg3, lambda i: s_dec[:, i, :],
                    lambda i, c=c: E[("x4f", c)],
                    NMT, TCB, None,
                    lambda i: s_out[:, i, :],
                    lambda i, c=c: E.get(("odma", c - 1, NMT - 1)))
                for mt in range(NMT):
                    E[("odma", c, mt)] = P.dma(
                        "sync", out_o[mt * 128:(mt + 1) * 128,
                                      c * TCB:(c + 1) * TCB],
                        s_out[:, mt, :], oo[c % 2],
                        waits=[E[(tg3, "app2", mt)]])

            allsems = [ld, lwA, lwB] + lyz + lxt + lxn + oxn + oo
            P.op("gpsimd", lambda eng: eng.memset(s_cst[:, 7:8], 0.0),
                 waits=[(sm, P.counts[sm.name]) for sm in allsems
                        if sm.name in P.counts] +
                       [(vq, P.counts["vq"]), (aq, P.counts["aq"]),
                        (tq, P.counts["tq"])],
                 inc=ld, inc_by=1)

            P.emit(blk)
    return nc


def _l3_host_inputs(inputs, res1, res2):
    out_proj_w = np.asarray(inputs["L0_attn_out_proj_w"])
    W_enc = np.asarray(inputs["L0_ffn_W_enc"])
    W_s2c = np.asarray(inputs["L0_ffn_W_s2c"])
    Wr = np.asarray(inputs["L0_ffn_Wr"])
    W1 = np.asarray(inputs["L0_ffn_W1"])
    W2 = np.asarray(inputs["L0_ffn_W2"])
    W_c2s = np.asarray(inputs["L0_ffn_W_c2s"])
    W_dec = np.asarray(inputs["L0_ffn_W_dec"])

    outw = np.ascontiguousarray(
        out_proj_w.T.reshape(NKT, 128, D).transpose(1, 0, 2))
    wenc = np.ascontiguousarray(W_enc.reshape(NMT, 128, HS).transpose(1, 0, 2))
    wdec = np.ascontiguousarray(W_dec.reshape(NKT, 128, D).transpose(1, 0, 2))
    ws2c = np.ascontiguousarray(W_s2c.reshape(NKT, 128, CM).transpose(1, 0, 2))
    w1 = np.zeros((128, NE // 2, NE, 128), dtype=np.float32)
    for ex in range(NE):
        base = 64 * (ex % 2)
        w1[base:base + 64, ex // 2] = W1[ex].reshape(CM, NE, 128).transpose(
            0, 1, 2)[:, :, :]
    w2 = np.ascontiguousarray(
        W2.reshape(NE, 8, 128, CM).transpose(2, 1, 0, 3))
    wc2s = np.ascontiguousarray(W_c2s)
    benc = np.ascontiguousarray(
        np.asarray(inputs["L0_ffn_b_enc"]).reshape(NKT, 128).T)
    bs2c = np.asarray(inputs["L0_ffn_b_s2c"]).reshape(CM, 1).astype(np.float32)
    br = np.asarray(inputs["L0_ffn_br"]).reshape(8, 1).astype(np.float32)
    b1 = np.ascontiguousarray(
        np.asarray(inputs["L0_ffn_b1"]).reshape(NE, NE, 128).transpose(2, 0, 1))
    b2 = np.ascontiguousarray(np.asarray(inputs["L0_ffn_b2"]).T)
    bc2s = np.ascontiguousarray(
        np.asarray(inputs["L0_ffn_b_c2s"]).reshape(NKT, 128).T)
    bdec = np.ascontiguousarray(
        np.asarray(inputs["L0_ffn_b_dec"]).reshape(NMT, 128).T)
    lng = np.ascontiguousarray(
        np.asarray(inputs["L0_ffn_ln_g"]).reshape(NMT, 128).T)
    lnb = np.ascontiguousarray(
        np.asarray(inputs["L0_ffn_ln_b"]).reshape(NMT, 128).T)
    sel8 = np.zeros((8, NE, CM), dtype=np.float32)
    for ex in range(NE):
        sel8[ex, ex, :] = 1.0
    ident = np.eye(128, dtype=np.float32)
    ones128 = np.ones((128, 2), dtype=np.float32)
    ones1 = np.ones((1, 128), dtype=np.float32)
    cstv = np.zeros((128, 8), dtype=np.float32)
    cstv[:, 0] = EPS
    cstv[:, 1] = 1.0 / D
    cstv[:, 2] = 1.0
    cstv[:, 3] = -1.0
    cstv[:, 4] = -0.5
    cstv[:, 5] = -1e30

    x = np.ascontiguousarray(
        np.asarray(inputs["L0_attn_x"])[0].T).astype(np.float32, copy=False)
    y_sl = [res2.results[k]["y_o"] for k in range(NCORES)]
    z_full = [res1.results[k]["z_o"][:, HALO:] for k in range(NCORES)]

    maps = []
    for k in range(NCORES):
        y_st = np.ascontiguousarray(
            np.concatenate([y_sl[j][:, k * S:(k + 1) * S]
                            for j in range(NCORES)], axis=0))
        xr_k = np.ascontiguousarray(
            x[:, k * S:(k + 1) * S].reshape(NMT, 128, S).transpose(1, 0, 2))
        maps.append({
            "y_i": y_st, "z_i": np.ascontiguousarray(z_full[k]), "xr": xr_k,
            "outw": outw, "wenc": wenc, "wdec": wdec, "ws2c": ws2c,
            "wr_": Wr.astype(np.float32), "w1": w1, "w2": w2, "wc2s": wc2s,
            "benc": benc, "bs2c": bs2c, "br_": br, "b1": b1, "b2": b2,
            "bc2s": bc2s, "bdec": bdec, "lng": lng, "lnb": lnb,
            "sel8": sel8, "ident": ident, "ones128": ones128, "ones1": ones1,
            "cst": cstv,
        })
    return maps


def kernel(**inputs):
    l1_maps = _l1_host_inputs(inputs)
    nc1 = build_L1()
    res1 = run_bass_kernel_spmd(nc1, l1_maps, core_ids=list(range(NCORES)))
    l2_maps = _l2_host_inputs(inputs, res1)
    nc2 = build_L2()
    res2 = run_bass_kernel_spmd(nc2, l2_maps, core_ids=list(range(NCORES)))
    l3_maps = _l3_host_inputs(inputs, res1, res2)
    nc3 = build_L3()
    res3 = run_bass_kernel_spmd(nc3, l3_maps, core_ids=list(range(NCORES)))
    out = np.concatenate([res3.results[k]["out_o"] for k in range(NCORES)],
                         axis=1)
    return np.ascontiguousarray(out.T).reshape(1, T, D)


if __name__ == "__main__":
    pass


# version 04
# speedup vs baseline: 1.0960x; 1.0960x over previous
import numpy as np
from contextlib import ExitStack

import concourse.bass as bass
import concourse.mybir as mybir
from concourse.bass_utils import run_bass_kernel_spmd

F32 = mybir.dt.float32
F32R = mybir.dt.float32r
AF = mybir.ActivationFunctionType
ALU = mybir.AluOpType

NCORES = 8
T = 16384
S = T // NCORES
D = 1024
DI = 2048
DTR = 64
NST = 16
DSL = DI // NCORES
EPS = 1e-5
HALO = 3
NE = 8
HS = 2048
CM = 64
EH = 1024

SH = S + HALO
CH = [512, 512, 512, 512, 4]
CO = [0, 512, 1024, 1536, 2047]
NG = D // 128
NM = 2 * DI // 128
NDT = DI // 128


class Prog:

    def __init__(self, nc):
        self.nc = nc
        self.ops = {k: [] for k in
                    ("sync", "vector", "scalar", "tensor", "gpsimd")}
        self.counts = {}
        self.base_waits = []

    def op(self, engine, fn, waits=(), inc=None, inc_by=1):
        ev = None
        if inc is not None:
            key = inc.name
            self.counts[key] = self.counts.get(key, 0) + inc_by
            ev = (inc, self.counts[key])
        allw = list(self.base_waits) + [w for w in waits if w is not None]
        self.ops[engine].append((fn, allw,
                                 (inc, inc_by) if inc is not None else None))
        return ev

    def dma(self, engine, out, in_, sem, waits=()):
        return self.op(engine, lambda eng, o=out, i=in_: eng.dma_start(out=o, in_=i),
                       waits=waits, inc=sem, inc_by=16)

    def emit(self, block):
        def mk(name):
            def body(eng):
                hi = {}
                for fn, waits, inc in self.ops[name]:
                    for wt in waits:
                        sem, val = wt[0], wt[1]
                        force = len(wt) > 2 and wt[2]
                        if not force and hi.get(sem.name, -1) >= val:
                            continue
                        eng.wait_ge(sem, val)
                        hi[sem.name] = max(hi.get(sem.name, -1), val)
                    ins = fn(eng)
                    if inc is not None:
                        ins.then_inc(inc[0], inc[1])
            return body
        block.sync(mk("sync"))
        block.vector(mk("vector"))
        block.scalar(mk("scalar"))
        block.tensor(mk("tensor"))
        block.gpsimd(mk("gpsimd"))


def slot(c):
    return 515 * (c % 2)


def build_L1():
    nc = bass.Bass("TRN2", target_bir_lowering=False, debug=False,
                   detect_race_conditions=False)

    xh = nc.dram_tensor("xh", [128, NG, SH], F32, kind="ExternalInput").ap()
    inw = nc.dram_tensor("inw", [NM, 128, NG, 128], F32, kind="ExternalInput").ap()
    normw = nc.dram_tensor("normw", [128, NG], F32, kind="ExternalInput").ap()
    xpw = nc.dram_tensor("xpw", [128, NDT, 128], F32, kind="ExternalInput").ap()
    dtw = nc.dram_tensor("dtw", [128, DI], F32, kind="ExternalInput").ap()
    convw = nc.dram_tensor("convw", [128, NDT, 4], F32, kind="ExternalInput").ap()
    convb = nc.dram_tensor("convb", [128, NDT], F32, kind="ExternalInput").ap()
    dtb = nc.dram_tensor("dtb", [128, NDT], F32, kind="ExternalInput").ap()
    ones128 = nc.dram_tensor("ones128", [128, 2], F32, kind="ExternalInput").ap()
    ones1 = nc.dram_tensor("ones1", [1, 128], F32, kind="ExternalInput").ap()
    cst = nc.dram_tensor("cst", [128, 8], F32, kind="ExternalInput").ap()

    u_o = nc.dram_tensor("u_o", [DI, S], F32, kind="ExternalOutput").ap()
    z_o = nc.dram_tensor("z_o", [DI, SH], F32, kind="ExternalOutput").ap()
    dl_o = nc.dram_tensor("dl_o", [DI, S], F32, kind="ExternalOutput").ap()
    bc_o = nc.dram_tensor("bc_o", [64, S], F32, kind="ExternalOutput").ap()

    with ExitStack() as stk:
        e = stk.enter_context
        s_w = e(nc.sbuf_tensor("s_w", [128, 3, NG, 128], F32R))
        s_x = e(nc.sbuf_tensor("s_x", [128, 2, NG, 512], F32R))
        s_normw = e(nc.sbuf_tensor("s_normw", [128, NG], F32))
        s_xpw = e(nc.sbuf_tensor("s_xpw", [128, NDT, 128], F32R))
        s_dtw = e(nc.sbuf_tensor("s_dtw", [128, DI], F32R))
        s_convw = e(nc.sbuf_tensor("s_convw", [128, NDT, 4], F32))
        s_convb = e(nc.sbuf_tensor("s_convb", [128, NDT], F32))
        s_dtb = e(nc.sbuf_tensor("s_dtb", [128, NDT], F32))
        s_o128 = e(nc.sbuf_tensor("s_o128", [128, 2], F32R))
        s_o1 = e(nc.sbuf_tensor("s_o1", [1, 128], F32R))
        s_cst = e(nc.sbuf_tensor("s_cst", [128, 8], F32))
        s_sq = e(nc.sbuf_tensor("s_sq", [128, 2, 512], F32R))
        s_rstd = e(nc.sbuf_tensor("s_rstd", [1, 512], F32R))
        s_rstdb = e(nc.sbuf_tensor("s_rstdb", [128, 2, 512], F32))
        s_xp = e(nc.sbuf_tensor("s_xp", [128, NDT, 1030], F32))
        s_z = e(nc.sbuf_tensor("s_z", [128, 2, 512], F32))
        s_xc = e(nc.sbuf_tensor("s_xc", [128, 2, 512], F32))
        s_u = e(nc.sbuf_tensor("s_u", [128, 3, 512], F32R))
        s_xdbl = e(nc.sbuf_tensor("s_xdbl", [128, 512], F32R))
        s_sp = e(nc.sbuf_tensor("s_sp", [128, 2, 512], F32))
        s_dl = e(nc.sbuf_tensor("s_dl", [128, 2, 512], F32))
        p_mm = e(nc.psum_tensor("p_mm", [128, 2, 512], F32))
        p_ss = e(nc.psum_tensor("p_ss", [2, 512], F32))
        p_rb = e(nc.psum_tensor("p_rb", [128, 512], F32))
        p_xd = e(nc.psum_tensor("p_xd", [128, 512], F32))
        p_dt = e(nc.psum_tensor("p_dt", [128, 2, 512], F32))
        ld = e(nc.semaphore("ld"))
        lx = [e(nc.semaphore(f"lx{i}")) for i in range(2)]
        lw = [e(nc.semaphore(f"lw{i}")) for i in range(3)]
        oz = [e(nc.semaphore(f"oz{i}")) for i in range(2)]
        ou = [e(nc.semaphore(f"ou{i}")) for i in range(3)]
        od = [e(nc.semaphore(f"od{i}")) for i in range(2)]
        ob = e(nc.semaphore("ob"))
        tq = e(nc.semaphore("tq"))
        vq = e(nc.semaphore("vq"))
        aq = e(nc.semaphore("aq"))
        blk = e(nc.Block())

        P = Prog(nc)

        ev_c = None
        for dst, src in [(s_normw[:], normw), (s_xpw[:], xpw.bitcast(F32R)),
                         (s_dtw[:], dtw.bitcast(F32R)), (s_convw[:], convw),
                         (s_convb[:], convb), (s_dtb[:], dtb),
                         (s_o128[:], ones128.bitcast(F32R)),
                         (s_o1[:], ones1.bitcast(F32R)), (s_cst[:], cst)]:
            ev_c = P.dma("sync", dst, src, ld)
        P.base_waits = [ev_c]

        E = {}

        def prev_ring(kind, i, depth):
            return E.get((kind, i - depth))

        def chunk_section(c):
            w_ = CH[c]
            E[("xload", c)] = P.dma(
                "sync", s_x[:, c % 2, :, 0:w_],
                xh[:, :, CO[c]:CO[c] + w_].bitcast(F32R), lx[c % 2],
                waits=[E.get(("mm", (c - 2) * NM + NM - 1, NG - 1)),
                       E.get(("xscale", c - 2, NG - 1))])

            for g in range(NG):
                gg = c * NG + g
                E[("sq", c, g)] = P.op(
                    "scalar",
                    lambda eng, g=g, w_=w_, c=c: eng.activation(
                        s_sq[:, (c * NG + g) % 2, 0:w_],
                        s_x[:, c % 2, g, 0:w_].bitcast(F32), AF.Square),
                    waits=[E[("xload", c)], ev_c, prev_ring("stats", gg, 2)],
                    inc=aq)
                E[("stats", gg)] = P.op(
                    "tensor",
                    lambda eng, g=g, w_=w_, c=c: eng.matmul(
                        p_ss[:, 0:w_], s_o128[:],
                        s_sq[:, (c * NG + g) % 2, 0:w_],
                        start=(g == 0), stop=(g == NG - 1)),
                    waits=[E[("sq", c, g)],
                           E.get(("rstdL", c - 1)) if g == 0 else None],
                    inc=tq)
                E[("xscale", c, g)] = P.op(
                    "vector",
                    lambda eng, g=g, w_=w_, c=c: eng.tensor_scalar_mul(
                        s_x[:, c % 2, g, 0:w_],
                        s_x[:, c % 2, g, 0:w_].bitcast(F32),
                        s_normw[:, g:g + 1]),
                    waits=[E[("sq", c, g)]], inc=vq)

            E[("rstdL", c)] = P.op(
                "scalar",
                lambda eng, w_=w_: eng.activation(
                    s_rstd[0:1, 0:w_], p_ss[0:1, 0:w_],
                    AF.Ln, bias=s_cst[0:1, 0:1], scale=s_cst[0:1, 1:2]),
                waits=[E[("stats", c * NG + NG - 1)]], inc=aq)
            E[("rstd", c)] = P.op(
                "scalar",
                lambda eng, w_=w_: eng.activation(
                    s_rstd[0:1, 0:w_],
                    s_rstd[0:1, 0:w_].bitcast(F32), AF.Exp,
                    scale=s_cst[0:1, 4:5]),
                inc=aq)
            E[("rb", c)] = P.op(
                "tensor",
                lambda eng, w_=w_: eng.matmul(
                    p_rb[:, 0:w_], s_o1[:], s_rstd[0:1, 0:w_],
                    start=True, stop=True),
                waits=[E[("rstd", c)], E.get(("rstdb", c - 1))], inc=tq)
            E[("rstdb", c)] = P.op(
                "scalar",
                lambda eng, w_=w_, c=c: eng.activation(
                    s_rstdb[:, c % 2, 0:w_], p_rb[:, 0:w_], AF.Copy),
                waits=[E[("rb", c)], E.get(("evac", c - 2, NM - 1))], inc=aq)

            for m in range(NM):
                k = c * NM + m
                E[("wload", k)] = P.dma(
                    "gpsimd", s_w[:, k % 3, :, :], inw[m, :, :, :].bitcast(F32R),
                    lw[k % 3],
                    waits=[E.get(("mm", k - 3, NG - 1))])
                for g in range(NG):
                    bank_war = None
                    if g == 0:
                        bank_war = E.get(("evac", (k - 2) // NM, (k - 2) % NM)) \
                            if k - 2 >= 0 else None
                    E[("mm", k, g)] = P.op(
                        "tensor",
                        lambda eng, m=m, g=g, w_=w_, c=c, k=k: eng.matmul(
                            p_mm[:, m % 2, 0:w_],
                            s_w[:, k % 3, g, :],
                            s_x[:, c % 2, g, 0:w_],
                            start=(g == 0), stop=(g == NG - 1)),
                        waits=[E[("wload", k)], E[("xscale", c, g)], bank_war],
                        inc=tq)
                emm = E[("mm", k, NG - 1)]
                if m < NDT:
                    last = None
                    if c < 4:
                        last = P.op(
                            "vector",
                            lambda eng, m=m, w_=w_, c=c: eng.tensor_mul(
                                s_xp[:, m, slot(c):slot(c) + w_],
                                p_mm[:, m % 2, 0:w_], s_rstdb[:, c % 2, 0:w_]),
                            waits=[emm, E[("rstdb", c)],
                                   E.get(("conv", c - 2, m))],
                            inc=vq)
                    if c >= 1:
                        tl = 1 if c == 4 else 0
                        last = P.op(
                            "vector",
                            lambda eng, m=m, c=c, tl=tl: eng.tensor_mul(
                                s_xp[:, m, slot(c - 1) + 512:slot(c - 1) + 515],
                                p_mm[:, m % 2, tl:tl + 3],
                                s_rstdb[:, c % 2, tl:tl + 3]),
                            waits=[emm, E[("rstdb", c)],
                                   E.get(("conv", c - 3, m))],
                            inc=vq)
                        E[("tail", c, m)] = last
                    E[("evac", c, m)] = last
                else:
                    mz = m - NDT
                    E[("evac", c, m)] = P.op(
                        "vector",
                        lambda eng, m=m, mz=mz, w_=w_, c=c: eng.tensor_mul(
                            s_z[:, mz % 2, 0:w_],
                            p_mm[:, m % 2, 0:w_], s_rstdb[:, c % 2, 0:w_]),
                        waits=[emm, E[("rstdb", c)],
                               prev_ring("zdma", c * NDT + mz, 2)],
                        inc=vq)
                    zlo = 1 if c == 4 else 0
                    E[("zdma", c * NDT + mz)] = P.dma(
                        "sync", z_o[mz * 128:(mz + 1) * 128,
                                    CO[c] + zlo:CO[c] + w_],
                        s_z[:, mz % 2, zlo:w_], oz[mz % 2],
                        waits=[E[("evac", c, m)]])

        def out_section(j):
            for g in range(NDT):
                ug = j * NDT + g
                first = P.op(
                    "vector",
                    lambda eng, g=g, j=j: eng.tensor_scalar_mul(
                        s_xc[:, g % 2, :],
                        s_xp[:, g, slot(j) + 0:slot(j) + 512],
                        s_convw[:, g:g + 1, 0]),
                    waits=[E.get(("evac", j, g)), E.get(("tail", j + 1, g)),
                           prev_ring("silu", ug, 2)],
                    inc=vq)
                for jj in (1, 2, 3):
                    last = P.op(
                        "vector",
                        lambda eng, g=g, j=j, jj=jj: eng.scalar_tensor_tensor(
                            s_xc[:, g % 2, :],
                            s_xp[:, g, slot(j) + jj:slot(j) + jj + 512],
                            s_convw[:, g:g + 1, jj],
                            s_xc[:, g % 2, :],
                            ALU.mult, ALU.add),
                        inc=vq)
                E[("conv", j, g)] = last
                E[("silu", ug)] = P.op(
                    "scalar",
                    lambda eng, g=g, ug=ug: eng.activation(
                        s_u[:, ug % 3, :],
                        s_xc[:, g % 2, :], AF.Silu,
                        bias=s_convb[:, g:g + 1]),
                    waits=[E[("conv", j, g)],
                           prev_ring("xproj", ug, 3), prev_ring("udma", ug, 3)],
                    inc=aq)
                E[("udma", ug)] = P.dma(
                    "sync", u_o[g * 128:(g + 1) * 128, j * 512:(j + 1) * 512],
                    s_u[:, ug % 3, :].bitcast(F32), ou[ug % 3],
                    waits=[E[("silu", ug)]])
                E[("xproj", ug)] = P.op(
                    "tensor",
                    lambda eng, g=g, ug=ug: eng.matmul(
                        p_xd[:, :], s_xpw[:, g, :], s_u[:, ug % 3, :],
                        start=(g == 0), stop=(g == NDT - 1)),
                    waits=[E[("silu", ug)],
                           E.get(("xdev", j - 1)) if g == 0 else None],
                    inc=tq)
            E[("xdev", j)] = P.op(
                "scalar",
                lambda eng: eng.activation(s_xdbl[:, :],
                                           p_xd[:, :], AF.Copy),
                waits=[E[("xproj", j * NDT + NDT - 1)], E.get(("bcdma", j - 1)),
                       E.get(("dt", j - 1, NDT - 1))],
                inc=aq)
            E[("bcdma", j)] = P.dma(
                "sync", bc_o[0:48, j * 512:(j + 1) * 512],
                s_xdbl[0:48, :].bitcast(F32), ob, waits=[E[("xdev", j)]])
            for mt in range(NDT):
                mtg = j * NDT + mt
                E[("dt", j, mt)] = P.op(
                    "tensor",
                    lambda eng, mt=mt: eng.matmul(
                        p_dt[:, mt % 2, :],
                        s_dtw[64:128, mt * 128:(mt + 1) * 128],
                        s_xdbl[64:128, :], start=True, stop=True),
                    waits=[E[("xdev", j)], prev_ring("relu", mtg, 2)],
                    inc=tq)
                P.op("scalar",
                     lambda eng, mt=mt: eng.activation(
                         s_sp[:, mt % 2, :], p_dt[:, mt % 2, :], AF.Abs,
                         bias=s_dtb[:, mt:mt + 1]),
                     waits=[E[("dt", j, mt)], prev_ring("dladd", mtg, 2)],
                     inc=aq)
                P.op("scalar",
                     lambda eng, mt=mt: eng.activation(
                         s_sp[:, mt % 2, :], s_sp[:, mt % 2, :], AF.Exp,
                         scale=s_cst[:, 3:4]),
                     inc=aq)
                P.op("scalar",
                     lambda eng, mt=mt: eng.activation(
                         s_sp[:, mt % 2, :], s_sp[:, mt % 2, :], AF.Ln,
                         bias=s_cst[:, 2:3]),
                     inc=aq)
                E[("relu", mtg)] = P.op(
                    "scalar",
                    lambda eng, mt=mt: eng.activation(
                        s_dl[:, mt % 2, :], p_dt[:, mt % 2, :], AF.Relu,
                        bias=s_dtb[:, mt:mt + 1]),
                    waits=[prev_ring("dldma", mtg, 2)],
                    inc=aq)
                E[("dladd", mtg)] = P.op(
                    "vector",
                    lambda eng, mt=mt: eng.tensor_add(
                        s_dl[:, mt % 2, :], s_dl[:, mt % 2, :],
                        s_sp[:, mt % 2, :]),
                    waits=[E[("relu", mtg)]], inc=vq)
                E[("dldma", mtg)] = P.dma(
                    "sync", dl_o[mt * 128:(mt + 1) * 128, j * 512:(j + 1) * 512],
                    s_dl[:, mt % 2, :], od[mt % 2],
                    waits=[E[("dladd", mtg)]])

        for c in range(5):
            chunk_section(c)
            if c >= 1:
                out_section(c - 1)

        allsems = [ld] + lx + lw + oz + ou + od + [ob]
        P.op("gpsimd", lambda eng: eng.memset(s_o128[:].bitcast(F32), 1.0),
             waits=[(s, P.counts[s.name]) for s in allsems
                    if s.name in P.counts] +
                   [(vq, P.counts["vq"]), (aq, P.counts["aq"]),
                    (tq, P.counts["tq"])],
             inc=ld, inc_by=1)

        P.emit(blk)
    return nc


def _l1_host_inputs(inputs):
    x = np.ascontiguousarray(inputs["L0_attn_x"][0].T).astype(np.float32, copy=False)
    in_proj_w = inputs["L0_attn_in_proj_w"]
    x_proj_w = inputs["L0_attn_x_proj_w"]
    dt_proj_w = inputs["L0_attn_dt_proj_w"]
    conv_w = inputs["L0_attn_conv_w"]
    conv_b = inputs["L0_attn_conv_b"]
    dt_proj_b = inputs["L0_attn_dt_proj_b"]
    norm_w = inputs["L0_attn_norm_w"]

    inw = np.ascontiguousarray(
        np.asarray(in_proj_w).reshape(NM, 128, NG, 128).transpose(0, 3, 2, 1))
    normw = np.ascontiguousarray(np.asarray(norm_w).reshape(NG, 128).T)
    xpw = np.zeros((128, NDT, 128), dtype=np.float32)
    xpt = np.asarray(x_proj_w)
    for g in range(NDT):
        blkw = xpt[:, g * 128:(g + 1) * 128]
        xpw[:, g, 0:16] = blkw[64:80].T
        xpw[:, g, 32:48] = blkw[80:96].T
        xpw[:, g, 64:128] = blkw[0:64].T
    dtw = np.zeros((128, DI), dtype=np.float32)
    dtw[64:128, :] = np.asarray(dt_proj_w).T
    convw = np.ascontiguousarray(
        np.asarray(conv_w)[:, 0, :].reshape(NDT, 128, 4).transpose(1, 0, 2))
    convb = np.ascontiguousarray(np.asarray(conv_b).reshape(NDT, 128).T)
    dtb = np.ascontiguousarray(np.asarray(dt_proj_b).reshape(NDT, 128).T)
    ones128 = np.ones((128, 2), dtype=np.float32)
    ones1 = np.ones((1, 128), dtype=np.float32)
    cstv = np.zeros((128, 8), dtype=np.float32)
    cstv[:, 0] = EPS
    cstv[:, 1] = 1.0 / D
    cstv[:, 2] = 1.0
    cstv[:, 3] = -1.0
    cstv[:, 4] = -0.5

    maps = []
    for k in range(NCORES):
        lo = k * S - HALO
        if lo < 0:
            xh2 = np.concatenate(
                [np.zeros((D, HALO), np.float32), x[:, 0:(k + 1) * S]], axis=1)
        else:
            xh2 = x[:, lo:(k + 1) * S]
        xh2 = np.ascontiguousarray(xh2.reshape(NG, 128, SH).transpose(1, 0, 2))
        maps.append({
            "xh": xh2, "inw": inw, "normw": normw, "xpw": xpw, "dtw": dtw,
            "convw": convw, "convb": convb, "dtb": dtb,
            "ones128": ones128, "ones1": ones1, "cst": cstv,
        })
    return maps




TC2 = 1024
NC2 = T // TC2
NSUB = TC2 // 512


def build_L2():
    nc = bass.Bass("TRN2", target_bir_lowering=False, debug=False,
                   detect_race_conditions=False)

    u_i = nc.dram_tensor("u_i", [DSL, T], F32, kind="ExternalInput").ap()
    dl_i = nc.dram_tensor("dl_i", [DSL, T], F32, kind="ExternalInput").ap()
    B_i = nc.dram_tensor("B_i", [16, T], F32, kind="ExternalInput").ap()
    C_i = nc.dram_tensor("C_i", [16, T], F32, kind="ExternalInput").ap()
    alog = nc.dram_tensor("alog", [128, 2, NST], F32, kind="ExternalInput").ap()
    dvec = nc.dram_tensor("dvec", [128, 2], F32, kind="ExternalInput").ap()
    sel = nc.dram_tensor("sel", [16, 16 * 128], F32, kind="ExternalInput").ap()
    ident = nc.dram_tensor("ident", [128, 128], F32, kind="ExternalInput").ap()
    cst = nc.dram_tensor("cst", [128, 8], F32, kind="ExternalInput").ap()

    y_o = nc.dram_tensor("y_o", [DSL, T], F32, kind="ExternalOutput").ap()

    with ExitStack() as stk:
        e = stk.enter_context
        s_u = e(nc.sbuf_tensor("s_u", [128, 2, 3, TC2], F32))
        s_dl = e(nc.sbuf_tensor("s_dl", [128, 2, 3, TC2], F32))
        s_du = e(nc.sbuf_tensor("s_du", [128, 2, 3, TC2], F32))
        s_B = e(nc.sbuf_tensor("s_B", [16, 2, TC2], F32R))
        s_C = e(nc.sbuf_tensor("s_C", [16, 2, TC2], F32R))
        s_sel = e(nc.sbuf_tensor("s_sel", [16, 16 * 128], F32R))
        s_ident = e(nc.sbuf_tensor("s_ident", [128, 128], F32R))
        s_alog = e(nc.sbuf_tensor("s_alog", [128, 2, NST], F32))
        s_A = e(nc.sbuf_tensor("s_A", [128, 2, NST], F32))
        s_D = e(nc.sbuf_tensor("s_D", [128, 2], F32))
        s_cst = e(nc.sbuf_tensor("s_cst", [128, 8], F32))
        s_dA = e(nc.sbuf_tensor("s_dA", [128, 2, 2, TC2], F32))
        s_w = e(nc.sbuf_tensor("s_w", [128, 2, TC2], F32))
        s_h = e(nc.sbuf_tensor("s_h", [128, 2, TC2], F32))
        s_p = e(nc.sbuf_tensor("s_p", [128, 2, TC2], F32R))
        s_Ct = e(nc.sbuf_tensor("s_Ct", [128, 2, TC2], F32))
        s_carry = e(nc.sbuf_tensor("s_carry", [128, 2, NST], F32))
        s_y = e(nc.sbuf_tensor("s_y", [128, 2, TC2], F32))
        p_B = e(nc.psum_tensor("p_B", [128, 2, 512], F32))
        p_C = e(nc.psum_tensor("p_C", [128, 2, 512], F32))
        p_y = e(nc.psum_tensor("p_y", [128, 2, NSUB, 512], F32))
        ld = e(nc.semaphore("ld"))
        ldl = [e(nc.semaphore(f"ldl{i}")) for i in range(2)]
        lu = [e(nc.semaphore(f"lu{i}")) for i in range(2)]
        lbc = [e(nc.semaphore(f"lbc{i}")) for i in range(2)]
        oy = [e(nc.semaphore(f"oy{i}")) for i in range(2)]
        tq = e(nc.semaphore("tq"))
        vq = e(nc.semaphore("vq"))
        aq = e(nc.semaphore("aq"))
        pq = e(nc.semaphore("pq"))
        blk = e(nc.Block())

        P = Prog(nc)
        E = {}

        ev_c = None
        for dst, src_ in [(s_sel[:], sel.bitcast(F32R)),
                          (s_ident[:], ident.bitcast(F32R)),
                          (s_alog[:], alog), (s_D[:], dvec), (s_cst[:], cst)]:
            ev_c = P.dma("sync", dst, src_, ld)
        P.base_waits = [ev_c]

        eva0 = P.op("scalar",
                    lambda eng: eng.activation(s_A[:], s_alog[:], AF.Exp),
                    waits=[ev_c], inc=aq)
        evA = P.op("vector",
                   lambda eng: eng.tensor_scalar_mul(s_A[:], s_A[:],
                                                     s_cst[:, 3:4]),
                   waits=[eva0], inc=vq)

        NTOT = NC2 * NST

        def em_load(c):
            if c >= NC2:
                return
            r3 = c % 3
            for dt2 in range(2):
                E[("lu", c, dt2)] = P.dma(
                    "sync", s_u[:, dt2, r3, :],
                    u_i[dt2 * 128:(dt2 + 1) * 128, c * TC2:(c + 1) * TC2],
                    lu[c % 2], waits=[E.get(("yev", c - 3, dt2, NSUB - 1))])
                E[("ldl", c, dt2)] = P.dma(
                    "sync", s_dl[:, dt2, r3, :],
                    dl_i[dt2 * 128:(dt2 + 1) * 128, c * TC2:(c + 1) * TC2],
                    ldl[c % 2], waits=[E.get(("dA", c - 3, dt2, NST - 1))])

        def em_loadbc(c):
            if c >= NC2:
                return
            r = c % 2
            E[("lB", c)] = P.dma(
                "sync", s_B[:, r, :],
                B_i[:, c * TC2:(c + 1) * TC2].bitcast(F32R),
                lbc[r], waits=[E.get(("bb", c - 2, NST - 1, NSUB - 1))])
            E[("lC", c)] = P.dma(
                "sync", s_C[:, r, :],
                C_i[:, c * TC2:(c + 1) * TC2].bitcast(F32R),
                lbc[r], waits=[E.get(("cb", c - 2, NST - 1, NSUB - 1))])

        def em_du(c):
            r3 = c % 3
            for dt2 in range(2):
                E[("du", c, dt2)] = P.op(
                    "vector",
                    lambda eng, dt2=dt2, r3=r3: eng.tensor_mul(
                        s_du[:, dt2, r3, :], s_dl[:, dt2, r3, :],
                        s_u[:, dt2, r3, :]),
                    waits=[E[("lu", c, 1)], E[("ldl", c, 1)],
                           E.get(("w", c - 3, dt2, NST - 1))],
                    inc=vq)

        def em_dA(c, n):
            if c >= NC2:
                return
            for dt2 in range(2):
                E[("dA", c, dt2, n)] = P.op(
                    "scalar",
                    lambda eng, dt2=dt2, n=n, c=c: eng.activation(
                        s_dA[:, dt2, (n + NST * c) % 2, :],
                        s_dl[:, dt2, c % 3, :], AF.Exp,
                        scale=s_A[:, dt2, n:n + 1]),
                    waits=[E[("ldl", c, 1)], evA,
                           E.get(("scan", c, dt2, n - 2)) or
                           E.get(("scan", c - 1, dt2, NST - 2 + (n % 2)))],
                    inc=aq)

        def em_builds(g):
            if g >= NTOT:
                return
            c, n = divmod(g, NST)
            r = c % 2
            for s in range(NSUB):
                E[("bb", c, n, s)] = P.op(
                    "tensor",
                    lambda eng, n=n, s=s, r=r: eng.matmul(
                        p_B[:, s, :],
                        s_sel[:, n * 128:(n + 1) * 128],
                        s_B[:, r, s * 512:(s + 1) * 512],
                        start=True, stop=True),
                    waits=[E[("lC", c)], E.get(("w2l", g - 1, s))],
                    inc=tq)
            for s in range(NSUB):
                E[("cb", c, n, s)] = P.op(
                    "tensor",
                    lambda eng, n=n, s=s, r=r: eng.matmul(
                        p_C[:, s, :],
                        s_sel[:, n * 128:(n + 1) * 128],
                        s_C[:, r, s * 512:(s + 1) * 512],
                        start=True, stop=True),
                    waits=[E[("lC", c)], E.get(("ctev", g - 1, s))],
                    inc=tq)

        def em_w2scan(c, n, dt2):
            r3 = c % 3
            g = c * NST + n
            ev = None
            for s in range(NSUB):
                ev = P.op(
                    "vector",
                    lambda eng, dt2=dt2, s=s, r3=r3: eng.tensor_mul(
                        s_w[:, dt2, s * 512:(s + 1) * 512],
                        s_du[:, dt2, r3, s * 512:(s + 1) * 512],
                        p_B[:, s, :]),
                    waits=[E[("bb", c, n, s)], E[("du", c, dt2)]],
                    inc=vq)
                if dt2 == 1:
                    E[("w2l", g, s)] = ev
            E[("w", c, dt2, n)] = ev
            hwar = (E.get(("pp", c, dt2, n - 1)) if n >= 1
                    else E.get(("pp", c - 1, dt2, NST - 1)))
            E[("scan", c, dt2, n)] = P.op(
                "vector",
                lambda eng, dt2=dt2, n=n, c=c: eng.tensor_tensor_scan(
                    s_h[:, dt2, :],
                    s_dA[:, dt2, (n + NST * c) % 2, :],
                    s_w[:, dt2, :],
                    (0.0 if c == 0 else s_carry[:, dt2, n:n + 1]),
                    ALU.mult, ALU.add),
                waits=[E[("dA", c, dt2, n)],
                       (E[("carry", c - 1, dt2, n)][0],
                        E[("carry", c - 1, dt2, n)][1], True)
                       if c >= 1 else None, hwar],
                inc=vq)
            sc_ev = E[("scan", c, dt2, n)]
            E[("carry", c, dt2, n)] = P.op(
                "vector",
                lambda eng, dt2=dt2, n=n: eng.tensor_copy(
                    s_carry[:, dt2, n:n + 1],
                    s_h[:, dt2, TC2 - 1:TC2]),
                waits=[(sc_ev[0], sc_ev[1], True)],
                inc=vq)

        def em_ctev(c, n):
            g = c * NST + n
            for s in range(NSUB):
                E[("ctev", g, s)] = P.op(
                    "scalar",
                    lambda eng, n=n, s=s: eng.activation(
                        s_Ct[:, n % 2, s * 512:(s + 1) * 512],
                        p_C[:, s, :], AF.Copy),
                    waits=[E[("cb", c, n, s)],
                           E.get(("pp", c, 1, n - 2)) or
                           (E.get(("pp", c - 1, 1, NST - 2 + (n % 2)))
                            if c >= 1 else None)],
                    inc=aq)
            E[("ctevl", c, n)] = E[("ctev", g, NSUB - 1)]

        def em_pp(c, n):
            for dt2 in range(2):
                E[("pp", c, dt2, n)] = P.op(
                    "gpsimd",
                    lambda eng, dt2=dt2, n=n: eng.tensor_mul(
                        s_p[:, dt2, :], s_h[:, dt2, :], s_Ct[:, n % 2, :]),
                    waits=[E[("scan", c, dt2, n)], E[("ctevl", c, n)],
                           (E.get(("ya", c, dt2, n - 1, NSUB - 1)) if n >= 1
                            else E.get(("ya", c - 1, dt2, NST - 1,
                                        NSUB - 1)))],
                    inc=pq)

        def em_ya(c, n):
            for dt2 in range(2):
                for s in range(NSUB):
                    E[("ya", c, dt2, n, s)] = P.op(
                        "tensor",
                        lambda eng, dt2=dt2, n=n, s=s: eng.matmul(
                            p_y[:, dt2, s, :], s_ident[:],
                            s_p[:, dt2, s * 512:(s + 1) * 512],
                            start=(n == 0), stop=(n == NST - 1)),
                        waits=[E[("pp", c, dt2, n)],
                               (E.get(("yev", c - 1, dt2, s))
                                if n == 0 else None)],
                        inc=tq)

        em_load(0)
        em_load(1)
        em_loadbc(0)
        em_loadbc(1)
        em_du(0)
        em_dA(0, 0)
        em_builds(0)
        for c in range(NC2):
            if c > 0:
                em_du(c)
            em_load(c + 2)
            for n in range(NST):
                g = c * NST + n
                nc_, nn = divmod(g + 1, NST)
                em_dA(nc_, nn)
                em_w2scan(c, n, 0)
                em_ctev(c, n)
                em_w2scan(c, n, 1)
                em_builds(g + 1)
                if n >= 1:
                    em_ya(c, n - 1)
                em_pp(c, n)
            em_ya(c, NST - 1)
            em_loadbc(c + 2)
            r3 = c % 3
            for dt2 in range(2):
                for s in range(NSUB):
                    E[("yev", c, dt2, s)] = P.op(
                        "vector",
                        lambda eng, dt2=dt2, s=s, r3=r3:
                        eng.scalar_tensor_tensor(
                            s_y[:, dt2, s * 512:(s + 1) * 512],
                            s_u[:, dt2, r3, s * 512:(s + 1) * 512],
                            s_D[:, dt2:dt2 + 1],
                            p_y[:, dt2, s, :],
                            ALU.mult, ALU.add),
                        waits=[E[("ya", c, dt2, NST - 1, s)],
                               E.get(("ydma", c - 1, dt2))],
                        inc=vq)
                E[("ydma", c, dt2)] = P.dma(
                    "sync",
                    y_o[dt2 * 128:(dt2 + 1) * 128, c * TC2:(c + 1) * TC2],
                    s_y[:, dt2, :], oy[dt2],
                    waits=[E[("yev", c, dt2, NSUB - 1)]])

        allsems = [ld] + ldl + lu + lbc + oy
        P.op("gpsimd", lambda eng: eng.memset(s_cst[:, 7:8], 0.0),
             waits=[(sm, P.counts[sm.name]) for sm in allsems
                    if sm.name in P.counts] +
                   [(vq, P.counts["vq"]), (aq, P.counts["aq"]),
                    (tq, P.counts["tq"]), (pq, P.counts["pq"])],
             inc=pq, inc_by=1)

        P.emit(blk)
    return nc


def _l2_host_inputs(inputs, res1):
    A_log = np.asarray(inputs["L0_attn_A_log"])
    Dv = np.asarray(inputs["L0_attn_D"])
    u_full = [res1.results[k]["u_o"] for k in range(NCORES)]
    dl_full = [res1.results[k]["dl_o"] for k in range(NCORES)]
    bc_full = np.concatenate([res1.results[k]["bc_o"] for k in range(NCORES)],
                             axis=1)
    B_full = np.ascontiguousarray(bc_full[0:16])
    C_full = np.ascontiguousarray(bc_full[32:48])
    sel = np.zeros((16, 16 * 128), dtype=np.float32)
    for n in range(16):
        sel[n, n * 128:(n + 1) * 128] = 1.0
    ident = np.eye(128, dtype=np.float32)
    cstv = np.zeros((128, 8), dtype=np.float32)
    cstv[:, 0] = EPS
    cstv[:, 1] = 1.0 / D
    cstv[:, 2] = 1.0
    cstv[:, 3] = -1.0
    cstv[:, 4] = -0.5

    maps = []
    for k in range(NCORES):
        lo = k * DSL
        u_sl = np.ascontiguousarray(
            np.concatenate([u_full[j][lo:lo + DSL] for j in range(NCORES)],
                           axis=1))
        dl_sl = np.ascontiguousarray(
            np.concatenate([dl_full[j][lo:lo + DSL] for j in range(NCORES)],
                           axis=1))
        al = np.ascontiguousarray(
            A_log[lo:lo + DSL].reshape(2, 128, NST).transpose(1, 0, 2))
        dv = np.ascontiguousarray(Dv[lo:lo + DSL].reshape(2, 128).T)
        maps.append({
            "u_i": u_sl, "dl_i": dl_sl, "B_i": B_full, "C_i": C_full,
            "alog": al, "dvec": dv, "sel": sel, "ident": ident, "cst": cstv,
        })
    return maps


TCA = 512
NCA = S // TCA
TCB = 256
NCB = S // TCB
NMT = D // 128
NKT = DI // 128


def build_L3(debug=False):
    nc = bass.Bass("TRN2", target_bir_lowering=False, debug=False,
                   detect_race_conditions=False)
    BF16 = mybir.dt.bfloat16

    y_i = nc.dram_tensor("y_i", [DI, S], F32, kind="ExternalInput").ap()
    z_i = nc.dram_tensor("z_i", [DI, S], F32, kind="ExternalInput").ap()
    xr = nc.dram_tensor("xr", [128, NMT, S], F32, kind="ExternalInput").ap()
    outw = nc.dram_tensor("outw", [128, NKT, D], F32, kind="ExternalInput").ap()
    wenc = nc.dram_tensor("wenc", [128, NMT, HS], F32, kind="ExternalInput").ap()
    wdec = nc.dram_tensor("wdec", [128, NKT, D], F32, kind="ExternalInput").ap()
    ws2c = nc.dram_tensor("ws2c", [128, NKT, CM], F32, kind="ExternalInput").ap()
    wr_ = nc.dram_tensor("wr_", [64, 8], F32, kind="ExternalInput").ap()
    w1 = nc.dram_tensor("w1", [128, NE // 2, NE, 128], F32, kind="ExternalInput").ap()
    w2 = nc.dram_tensor("w2", [128, 8, NE, CM], F32, kind="ExternalInput").ap()
    wc2s = nc.dram_tensor("wc2s", [64, HS], F32, kind="ExternalInput").ap()
    benc = nc.dram_tensor("benc", [128, NKT], F32, kind="ExternalInput").ap()
    bs2c = nc.dram_tensor("bs2c", [64, 1], F32, kind="ExternalInput").ap()
    br_ = nc.dram_tensor("br_", [8, 1], F32, kind="ExternalInput").ap()
    b1 = nc.dram_tensor("b1", [128, NE, 8], F32, kind="ExternalInput").ap()
    b2 = nc.dram_tensor("b2", [64, NE], F32, kind="ExternalInput").ap()
    bc2s = nc.dram_tensor("bc2s", [128, NKT], F32, kind="ExternalInput").ap()
    bdec = nc.dram_tensor("bdec", [128, NMT], F32, kind="ExternalInput").ap()
    lng = nc.dram_tensor("lng", [128, NMT], F32, kind="ExternalInput").ap()
    lnb = nc.dram_tensor("lnb", [128, NMT], F32, kind="ExternalInput").ap()
    sel8 = nc.dram_tensor("sel8", [8, NE, CM], F32, kind="ExternalInput").ap()
    ident = nc.dram_tensor("ident", [128, 128], F32, kind="ExternalInput").ap()
    ones128 = nc.dram_tensor("ones128", [128, 2], F32, kind="ExternalInput").ap()
    ones1 = nc.dram_tensor("ones1", [1, 128], F32, kind="ExternalInput").ap()
    cst = nc.dram_tensor("cst", [128, 8], F32, kind="ExternalInput").ap()
    xn2_d = nc.dram_tensor("xn2_d", [D, S], F32,
                           kind=("ExternalOutput" if debug else "Internal"))
    if debug:
        dbg_gate = nc.dram_tensor("dbg_gate", [8, S], F32, kind="ExternalOutput").ap()
        dbg_eo = nc.dram_tensor("dbg_eo", [64, S], F32, kind="ExternalOutput").ap()
        dbg_ffn = nc.dram_tensor("dbg_ffn", [D, S], F32, kind="ExternalOutput").ap()
        dbg_x2 = nc.dram_tensor("dbg_x2", [D, S], F32, kind="ExternalOutput").ap()
        dbg_mub = nc.dram_tensor("dbg_mub", [128, S], F32, kind="ExternalOutput").ap()
        dbg_rsb = nc.dram_tensor("dbg_rsb", [128, S], F32, kind="ExternalOutput").ap()

    out_o = nc.dram_tensor("out_o", [D, S], F32, kind="ExternalOutput").ap()

    with ExitStack() as stk:
        e = stk.enter_context
        s_ws2c = e(nc.sbuf_tensor("s_ws2c", [128, NKT, CM], F32R))
        s_wr = e(nc.sbuf_tensor("s_wr", [64, 8], F32R))
        s_w1 = e(nc.sbuf_tensor("s_w1", [128, NE // 2, NE, 128], F32R))
        s_w2 = e(nc.sbuf_tensor("s_w2", [128, 8, NE, CM], F32R))
        s_wc2s = e(nc.sbuf_tensor("s_wc2s", [64, HS], F32R))
        s_benc = e(nc.sbuf_tensor("s_benc", [128, NKT], F32))
        s_bs2c = e(nc.sbuf_tensor("s_bs2c", [64, 1], F32))
        s_br = e(nc.sbuf_tensor("s_br", [8, 1], F32))
        s_b1 = e(nc.sbuf_tensor("s_b1", [128, NE, 8], F32))
        s_b2 = e(nc.sbuf_tensor("s_b2", [64, NE], F32))
        s_bc2s = e(nc.sbuf_tensor("s_bc2s", [128, NKT], F32))
        s_bdec = e(nc.sbuf_tensor("s_bdec", [128, NMT], F32))
        s_lng = e(nc.sbuf_tensor("s_lng", [128, NMT], F32))
        s_lnb = e(nc.sbuf_tensor("s_lnb", [128, NMT], F32))
        s_sel8 = e(nc.sbuf_tensor("s_sel8", [8, NE, CM], F32R))
        s_ident = e(nc.sbuf_tensor("s_ident", [128, 128], F32))
        s_o128 = e(nc.sbuf_tensor("s_o128", [128, 2], F32R))
        s_o1 = e(nc.sbuf_tensor("s_o1", [1, 128], F32R))
        s_cst = e(nc.sbuf_tensor("s_cst", [128, 8], F32))
        s_sq = e(nc.sbuf_tensor("s_sq3", [128, 2, TCA], F32R))
        s_mu = e(nc.sbuf_tensor("s_mu", [1, TCA], F32R))
        s_mu2 = e(nc.sbuf_tensor("s_mu2", [1, TCA], F32))
        s_var = e(nc.sbuf_tensor("s_var", [1, TCA], F32R))
        s_mub = e(nc.sbuf_tensor("s_mub", [128, TCA], F32))
        s_fnc = e(nc.sbuf_tensor("s_fnc", [1, 2], F32))
        s_rsb = e(nc.sbuf_tensor("s_rsb", [128, TCA], F32))
        p_mm = e(nc.psum_tensor("p_mm3", [128, 2, TCA], F32))
        p_cont = e(nc.psum_tensor("p_cont", [128, TCB], F32))
        p_ge = e(nc.psum_tensor("p_ge", [64, TCB], F32))
        p_ye = e(nc.psum_tensor("p_ye", [64, TCB], F32))
        p_s = e(nc.psum_tensor("p_s", [2, 2, TCA], F32))
        p_msc = e(nc.psum_tensor("p_msc", [128, 512], F32))
        ld = e(nc.semaphore("ld"))
        lwA = e(nc.semaphore("lwA"))
        lwB = e(nc.semaphore("lwB"))
        lyz = [e(nc.semaphore(f"lyz{i}")) for i in range(2)]
        lxt = [e(nc.semaphore(f"lxt{i}")) for i in range(2)]
        lxn = [e(nc.semaphore(f"lxn{i}")) for i in range(2)]
        oxn = [e(nc.semaphore(f"oxn{i}")) for i in range(2)]
        oo = [e(nc.semaphore(f"oo{i}")) for i in range(2)]
        tq = e(nc.semaphore("tq"))
        vq = e(nc.semaphore("vq"))
        aq = e(nc.semaphore("aq"))
        blk = e(nc.Block())

        P = Prog(nc)
        E = {}

        ev_c = None
        for dst, src_ in [(s_ws2c[:], ws2c.bitcast(F32R)), (s_wr[:], wr_.bitcast(F32R)),
                          (s_w1[:], w1.bitcast(F32R)), (s_w2[:], w2.bitcast(F32R)),
                          (s_wc2s[:], wc2s.bitcast(F32R)), (s_benc[:], benc),
                          (s_bs2c[:], bs2c), (s_br[:], br_), (s_b1[:], b1),
                          (s_b2[:], b2), (s_bc2s[:], bc2s), (s_bdec[:], bdec),
                          (s_lng[:], lng), (s_lnb[:], lnb),
                          (s_sel8[:], sel8.bitcast(F32R)), (s_ident[:], ident),
                          (s_o128[:], ones128.bitcast(F32R)),
                          (s_o1[:], ones1.bitcast(F32R)), (s_cst[:], cst)]:
            ev_c = P.dma("sync", dst, src_, ld)
        P.base_waits = [ev_c]

        lnstate = {"prev": None}

        def vfence():
            return P.op("vector",
                        lambda eng: eng.tensor_copy(s_fnc[0:1, 0:1],
                                                    s_fnc[0:1, 1:2]),
                        inc=vq)

        def layer_norm(tag, src_fn, src_ev, ntiles, w_, affine, dst_fn,
                       dst_war_fn, post_fn=None):
            prev = lnstate["prev"]
            for i in range(ntiles):
                E[(tag, "sq", i)] = P.op(
                    "scalar",
                    lambda eng, i=i: eng.activation(
                        s_sq[:, i % 2, 0:w_], src_fn(i).bitcast(F32),
                        AF.Square),
                    waits=[src_ev(i), E.get((tag, "st2", i - 2)),
                           (E.get((prev, "app2", i)) if prev else None)],
                    inc=aq)
                E[(tag, "st1", i)] = P.op(
                    "tensor",
                    lambda eng, i=i: eng.matmul(
                        p_s[:, 0, 0:w_], s_o128[:], src_fn(i),
                        start=(i == 0), stop=(i == ntiles - 1)),
                    waits=[src_ev(i),
                           (E.get((prev, "mu")) if prev and i == 0 else None)],
                    inc=tq)
                E[(tag, "st2", i)] = P.op(
                    "tensor",
                    lambda eng, i=i: eng.matmul(
                        p_s[:, 1, 0:w_], s_o128[:], s_sq[:, i % 2, 0:w_],
                        start=(i == 0), stop=(i == ntiles - 1)),
                    waits=[E[(tag, "sq", i)],
                           (E.get((prev, "var")) if prev and i == 0 else None)],
                    inc=tq)
            E[(tag, "mu")] = P.op(
                "vector",
                lambda eng: eng.tensor_scalar(
                    s_mu[0:1, 0:w_], p_s[0:1, 0, 0:w_], s_cst[0:1, 1:2],
                    None, ALU.mult),
                waits=[E[(tag, "st1", ntiles - 1)], ev_c,
                       (E.get((prev, "bc1")) if prev else None)],
                inc=vq)
            E[(tag, "mu")] = vfence()
            E[(tag, "mu2")] = P.op(
                "scalar",
                lambda eng: eng.activation(s_mu2[0:1, 0:w_],
                                           s_mu[0:1, 0:w_].bitcast(F32),
                                           AF.Square),
                waits=[E[(tag, "mu")],
                       (E.get((prev, "var")) if prev else None)],
                inc=aq)
            E[(tag, "var")] = P.op(
                "vector",
                lambda eng: eng.scalar_tensor_tensor(
                    s_var[0:1, 0:w_], p_s[0:1, 1, 0:w_],
                    s_cst[0:1, 1:2], s_mu2[0:1, 0:w_],
                    ALU.mult, ALU.subtract),
                waits=[E[(tag, "st2", ntiles - 1)], E[(tag, "mu2")],
                       (E.get((prev, "bc2")) if prev else None)],
                inc=vq)
            E[(tag, "var")] = vfence()
            E[(tag, "rstd")] = P.op(
                "scalar",
                lambda eng: eng.activation(s_var[0:1, 0:w_],
                                           s_var[0:1, 0:w_].bitcast(F32),
                                           AF.Ln, bias=s_cst[0:1, 0:1]),
                waits=[E[(tag, "var")]], inc=aq)
            E[(tag, "rstd2")] = P.op(
                "scalar",
                lambda eng: eng.activation(s_var[0:1, 0:w_],
                                           s_var[0:1, 0:w_].bitcast(F32),
                                           AF.Exp, scale=s_cst[0:1, 4:5]),
                inc=aq)
            E[(tag, "bc1")] = P.op(
                "tensor",
                lambda eng: eng.matmul(p_msc[:, 0:w_], s_o1[:],
                                       s_mu[0:1, 0:w_], start=True, stop=True),
                waits=[E[(tag, "mu")],
                       (E.get((prev, "bcev1")) if prev else None),
                       (E.get(("msc_free", tag[1])) if tag[0] == "lnF"
                        else None)],
                inc=tq)
            E[(tag, "bcev1")] = P.op(
                "scalar",
                lambda eng: eng.activation(s_mub[:, 0:w_], p_msc[:, 0:w_],
                                           AF.Copy),
                waits=[E[(tag, "bc1")],
                       (E.get((prev, "app1_last")) if prev else None)],
                inc=aq)
            E[(tag, "bc2")] = P.op(
                "tensor",
                lambda eng: eng.matmul(p_msc[:, 0:w_], s_o1[:],
                                       s_var[0:1, 0:w_], start=True, stop=True),
                waits=[E[(tag, "rstd2")], E[(tag, "bcev1")]], inc=tq)
            E[(tag, "bcev2")] = P.op(
                "scalar",
                lambda eng: eng.activation(s_rsb[:, 0:w_], p_msc[:, 0:w_],
                                           AF.Copy),
                waits=[E[(tag, "bc2")],
                       (E.get((prev, "app2_last")) if prev else None)],
                inc=aq)
            for i in range(ntiles):
                E[(tag, "app1", i)] = P.op(
                    "vector",
                    lambda eng, i=i: eng.tensor_sub(
                        s_sq[:, i % 2, 0:w_],
                        src_fn(i).bitcast(F32), s_mub[:, 0:w_]),
                    waits=[E[(tag, "bcev1")], E[(tag, "bcev2")],
                           E[(tag, "st2", i)]], inc=vq)
                ev = P.op(
                    "vector",
                    lambda eng, i=i: eng.tensor_mul(
                        dst_fn(i), s_sq[:, i % 2, 0:w_].bitcast(F32),
                        s_rsb[:, 0:w_]),
                    waits=[dst_war_fn(i)], inc=vq)
                if affine is not None:
                    gcol, bcol = affine
                    ev = P.op(
                        "vector",
                        lambda eng, i=i: eng.tensor_scalar(
                            dst_fn(i), dst_fn(i).bitcast(F32),
                            gcol(i), bcol(i), ALU.mult, ALU.add),
                        waits=[ev], inc=vq)
                E[(tag, "app2", i)] = ev
                if post_fn is not None:
                    post_fn(i, ev)
            E[(tag, "app1_last")] = E[(tag, "app1", ntiles - 1)]
            E[(tag, "app2_last")] = E[(tag, "app2", ntiles - 1)]
            lnstate["prev"] = tag
            return tag

        with ExitStack() as stkA:
            eA = stkA.enter_context
            s_outw = eA(nc.sbuf_tensor("s_outw", [128, NKT, D], BF16))
            s_yz = eA(nc.sbuf_tensor("s_yz", [128, 2, 2, TCA], F32))
            s_y2 = eA(nc.sbuf_tensor("s_y2", [128, NKT, TCA], BF16))
            s_xt = eA(nc.sbuf_tensor("s_xt", [128, 2, TCA], F32))
            s_x2 = eA(nc.sbuf_tensor("s_x2", [128, NMT, TCA], F32R))
            s_xn2c = eA(nc.sbuf_tensor("s_xn2c", [128, 2, TCA], F32))

            ev_wA = P.dma("gpsimd", s_outw[:], outw, lwA)

            for c in range(NCA):
                for kt in range(NKT):
                    gyk = c * NKT + kt
                    r = gyk % 2
                    war_yz = (E.get(("y2", (gyk - 2) // NKT, (gyk - 2) % NKT))
                              if gyk >= 2 else None)
                    E[("ly", c, kt)] = P.dma(
                        "sync", s_yz[:, 0, r, :],
                        y_i[kt * 128:(kt + 1) * 128, c * TCA:(c + 1) * TCA],
                        lyz[r], waits=[war_yz])
                    E[("lz", c, kt)] = P.dma(
                        "sync", s_yz[:, 1, r, :],
                        z_i[kt * 128:(kt + 1) * 128, c * TCA:(c + 1) * TCA],
                        lyz[r], waits=[])
                    E[("sz", c, kt)] = P.op(
                        "scalar",
                        lambda eng, r=r: eng.activation(
                            s_yz[:, 1, r, :], s_yz[:, 1, r, :], AF.Silu),
                        waits=[E[("lz", c, kt)]], inc=aq)
                    E[("y2", c, kt)] = P.op(
                        "vector",
                        lambda eng, r=r, kt=kt: eng.tensor_mul(
                            s_y2[:, kt, :], s_yz[:, 0, r, :],
                            s_yz[:, 1, r, :]),
                        waits=[E[("sz", c, kt)], E[("lz", c, kt)],
                               E.get(("mmA", c - 1, NMT - 1, kt))],
                        inc=vq)
                for mt in range(NMT):
                    gx = c * NMT + mt
                    E[("lxt", c, mt)] = P.dma(
                        "sync", s_xt[:, gx % 2, :],
                        xr[:, mt, c * TCA:(c + 1) * TCA], lxt[gx % 2],
                        waits=[E.get(("x2ev", (gx - 2) // NMT, (gx - 2) % NMT))
                               if gx >= 2 else None])
                    for kt in range(NKT):
                        E[("mmA", c, mt, kt)] = P.op(
                            "tensor",
                            lambda eng, mt=mt, kt=kt: eng.matmul(
                                p_mm[:, mt % 2, :],
                                s_outw[:, kt, mt * 128:(mt + 1) * 128],
                                s_y2[:, kt, :],
                                start=(kt == 0), stop=(kt == NKT - 1)),
                            waits=[E[("y2", c, kt)],
                                   ev_wA if c == 0 and mt == 0 else None,
                                   ((E.get(("x2ev", c, mt - 2)) if mt >= 2
                                     else E.get(("x2ev", c - 1,
                                                 NMT - 2 + mt)))
                                    if kt == 0 else None)],
                            inc=tq)
                    P.op("vector",
                         lambda eng, mt=mt, gx=gx: eng.tensor_add(
                             s_x2[:, mt, :], p_mm[:, mt % 2, :],
                             s_xt[:, gx % 2, :]),
                         waits=[E[("mmA", c, mt, NKT - 1)], E[("lxt", c, mt)],
                                E.get((("lnA", c - 1), "app2", mt))],
                         inc=vq)
                    E[("x2ev", c, mt)] = vfence()
                if debug == "full":
                    for mt in range(NMT):
                        P.dma("sync", dbg_x2[mt * 128:(mt + 1) * 128,
                                             c * TCA:(c + 1) * TCA],
                              s_x2[:, mt, :].bitcast(F32), oo[1],
                              waits=[E[("x2ev", c, mt)]])
                tg = ("lnA", c)

                def xn_post(i, ev, c=c):
                    gxi = c * NMT + i
                    E[("xndma", gxi)] = P.dma(
                        "sync", xn2_d.ap()[i * 128:(i + 1) * 128,
                                           c * TCA:(c + 1) * TCA],
                        s_xn2c[:, gxi % 2, :], oxn[gxi % 2],
                        waits=[ev])

                layer_norm(
                    tg, lambda i: s_x2[:, i, :],
                    lambda i, c=c: E[("x2ev", c, i)],
                    NMT, TCA, None,
                    lambda i, c=c: s_xn2c[:, (c * NMT + i) % 2, :],
                    lambda i, c=c: E.get(("xndma", c * NMT + i - 2)),
                    post_fn=xn_post)
                if debug == "full":
                    P.dma("sync", dbg_mub[:, c * TCA:(c + 1) * TCA],
                          s_mub[:, :], oo[1], waits=[E[(tg, "bcev1")],
                                                     E[(tg, "app1_last")]])
                    P.dma("sync", dbg_rsb[:, c * TCA:(c + 1) * TCA],
                          s_rsb[:, :], oo[1], waits=[E[(tg, "bcev2")],
                                                     E[(tg, "app2_last")]])
            ev_a_done = E[("xndma", NCA * NMT - 1)]

        with ExitStack() as stkB:
            eB = stkB.enter_context
            s_wenc = eB(nc.sbuf_tensor("s_wenc", [128, NMT, HS], BF16))
            s_wdec = eB(nc.sbuf_tensor("s_wdec", [128, NKT, D], BF16))
            s_xnb = eB(nc.sbuf_tensor("s_xnb", [128, NMT, TCB], BF16))
            s_xnf = eB(nc.sbuf_tensor("s_xnf", [128, NMT, TCB], F32))
            s_spk = eB(nc.sbuf_tensor("s_spk", [128, 2, TCB], F32R))
            s_cont = eB(nc.sbuf_tensor("s_cont", [128, TCB], F32R))
            s_lgt = eB(nc.sbuf_tensor("s_lgt", [8, TCB], F32))
            s_gt = eB(nc.sbuf_tensor("s_gt", [128, 2, 8], F32))
            s_gtmp = eB(nc.sbuf_tensor("s_gtmp", [128, 2, 8], F32))
            s_gm = eB(nc.sbuf_tensor("s_gm", [128, 2, 4], F32))
            s_grow = eB(nc.sbuf_tensor("s_grow", [8, TCB], F32R))
            s_he = eB(nc.sbuf_tensor("s_he", [128, 8, TCB], F32R))
            s_ye = eB(nc.sbuf_tensor("s_ye", [64, TCB], F32))
            s_eo = eB(nc.sbuf_tensor("s_eo", [64, 2, TCB], F32R))
            s_sp2 = eB(nc.sbuf_tensor("s_sp2", [128, NKT, TCB], BF16))
            s_dec = eB(nc.sbuf_tensor("s_dec", [128, NMT, TCB], F32R))
            s_out = eB(nc.sbuf_tensor("s_out", [128, NMT, TCB], F32))

            P.dma("gpsimd", s_wenc[:], wenc, lwB, waits=[ev_a_done, ev_wA])
            ev_wB = P.dma("gpsimd", s_wdec[:], wdec, lwB)

            for c in range(NCB):
                ca = (c * TCB) // TCA
                E[("lxnb", c)] = P.dma(
                    "gpsimd", s_xnb[:, :, :],
                    xn2_d.ap().rearrange("(m p) t -> p m t", p=128)[
                        :, :, c * TCB:(c + 1) * TCB],
                    lxn[0],
                    waits=[ev_a_done,
                           E.get(("mmB", c - 1, NKT - 1, NMT - 1))])
                E[("lxnf", c)] = P.dma(
                    "sync", s_xnf[:, :, :],
                    xn2_d.ap().rearrange("(m p) t -> p m t", p=128)[
                        :, :, c * TCB:(c + 1) * TCB],
                    lxn[1],
                    waits=[ev_a_done, E.get(("x4", c - 1, NMT - 1))])
                for kt in range(NKT):
                    for mt in range(NMT):
                        E[("mmB", c, kt, mt)] = P.op(
                            "tensor",
                            lambda eng, kt=kt, mt=mt: eng.matmul(
                                p_mm[:, kt % 2, 0:TCB],
                                s_wenc[:, mt, kt * 128:(kt + 1) * 128],
                                s_xnb[:, mt, :],
                                start=(mt == 0), stop=(mt == NMT - 1)),
                            waits=[E[("lxnb", c)],
                                   ev_wB if c == 0 and kt == 0 else None,
                                   ((E.get(("spk", c, kt - 2)) if kt >= 2
                                     else E.get(("decev", c - 1,
                                                 NMT - 2 + kt)))
                                    if mt == 0 else None)],
                            inc=tq)
                    E[("spk", c, kt)] = P.op(
                        "scalar",
                        lambda eng, kt=kt: eng.activation(
                            s_spk[:, kt % 2, :], p_mm[:, kt % 2, 0:TCB],
                            AF.Sigmoid, bias=s_benc[:, kt:kt + 1]),
                        waits=[E[("mmB", c, kt, NMT - 1)],
                               E.get(("s2c", c, kt - 2)),
                               E.get(("s2c", c - 1, NKT - 2 + (kt % 2)))
                               if kt < 2 else None],
                        inc=aq)
                    E[("s2c", c, kt)] = P.op(
                        "tensor",
                        lambda eng, kt=kt: eng.matmul(
                            p_cont[0:64, :], s_ws2c[:, kt, :],
                            s_spk[:, kt % 2, :],
                            start=(kt == 0), stop=(kt == NKT - 1)),
                        waits=[E[("spk", c, kt)],
                               E.get(("contev", c - 1)) if kt == 0 else None],
                        inc=tq)
                ev1 = P.op(
                    "vector",
                    lambda eng: eng.tensor_scalar(
                        s_cont[0:64, :], p_cont[0:64, :], s_bs2c[:, 0:1],
                        None, ALU.add),
                    waits=[E[("s2c", c, NKT - 1)],
                           E.get(("w1m", c - 1, NE - 1, NE - 1))],
                    inc=vq)
                P.op("vector",
                     lambda eng: eng.tensor_scalar(
                         s_cont[64:128, :], p_cont[0:64, :], s_bs2c[:, 0:1],
                         None, ALU.add),
                     inc=vq)
                E[("contev", c)] = vfence()
                E[("wr", c)] = P.op(
                    "tensor",
                    lambda eng: eng.matmul(p_msc[0:8, 0:TCB], s_wr[:],
                                           s_cont[0:64, :],
                                           start=True, stop=True),
                    waits=[E[("contev", c)], E.get(("lgtev", c - 1)),
                           E.get((("lnO", c - 1), "bcev2"))], inc=tq)
                E[("lgtev", c)] = P.op(
                    "vector",
                    lambda eng: eng.tensor_scalar(
                        s_lgt[:, :], p_msc[0:8, 0:TCB], s_br[:, 0:1],
                        None, ALU.add),
                    waits=[E[("wr", c)], E.get(("tp", c - 1, 1))], inc=vq)
                E[("lgtev", c)] = vfence()
                for i in range(2):
                    E[("tp", c, i)] = P.op(
                        "tensor",
                        lambda eng, i=i: eng.transpose(
                            p_msc[:, 256 + i * 8:256 + (i + 1) * 8],
                            s_lgt[:, i * 128:(i + 1) * 128],
                            s_ident[0:8, 0:8]),
                        waits=[E[("lgtev", c)], E.get(("gtev", c - 1))], inc=tq)
                E[("gtev", c)] = P.op(
                    "vector",
                    lambda eng: eng.tensor_copy(
                        s_gt[:, :, :].rearrange("p a b -> p (a b)"),
                        p_msc[:, 256:272]),
                    waits=[E[("tp", c, 0)], E[("tp", c, 1)],
                           E.get(("gdone", c - 1))], inc=vq)

                def gop(fn, waits=(), engname="vector"):
                    return P.op(engname, fn, waits=waits,
                                inc=(vq if engname == "vector" else aq))

                ev0 = E[("gtev", c)]
                evm = gop(lambda eng: eng.tensor_reduce(
                    s_gm[:, :, 0:1], s_gt[:, :, :], mybir.AxisListType.X,
                    ALU.max, negate=True), waits=[ev0])
                evm = vfence()
                eve1 = gop(lambda eng: eng.activation(
                    s_gt[:, 0, :], s_gt[:, 0, :], AF.Exp,
                    bias=s_gm[:, 0, 0:1]), engname="scalar",
                    waits=[(evm[0], evm[1], True)])
                eve2 = gop(lambda eng: eng.activation(
                    s_gt[:, 1, :], s_gt[:, 1, :], AF.Exp,
                    bias=s_gm[:, 1, 0:1]), engname="scalar")
                evm1 = gop(lambda eng: eng.tensor_reduce(
                    s_gm[:, :, 0:1], s_gt[:, :, :], mybir.AxisListType.X,
                    ALU.max), waits=[eve1, eve2])
                ev = (evm1[0], evm1[1], True)
                for i in range(2):
                    ev = gop(lambda eng, i=i: eng.tensor_scalar(
                        s_gtmp[:, i, :], s_gt[:, i, :], s_gm[:, i, 0:1], None,
                        ALU.is_equal), waits=[ev])
                    ev = gop(lambda eng, i=i: eng.scalar_tensor_tensor(
                        s_gtmp[:, i, :], s_gtmp[:, i, :], s_cst[:, 5:6],
                        s_gt[:, i, :], ALU.mult, ALU.add))
                evm2 = gop(lambda eng: eng.tensor_reduce(
                    s_gm[:, :, 3:4], s_gtmp[:, :, :], mybir.AxisListType.X,
                    ALU.max), waits=[ev])
                ev = (evm2[0], evm2[1], True)
                for i in range(2):
                    ev = gop(lambda eng, i=i: eng.tensor_scalar(
                        s_gtmp[:, i, :], s_gt[:, i, :], s_gm[:, i, 0:1], None,
                        ALU.is_equal), waits=[ev])
                    ev = gop(lambda eng, i=i: eng.tensor_scalar(
                        s_gtmp[:, i, :], s_gtmp[:, i, :], s_gm[:, i, 0:1],
                        None, ALU.mult))
                    ev = gop(lambda eng, i=i: eng.tensor_scalar(
                        s_gt[:, i, :], s_gt[:, i, :], s_gm[:, i, 3:4], None,
                        ALU.is_equal))
                    ev = gop(lambda eng, i=i: eng.scalar_tensor_tensor(
                        s_gt[:, i, :], s_gt[:, i, :], s_gm[:, i, 3:4],
                        s_gtmp[:, i, :], ALU.mult, ALU.add))
                evdn = gop(lambda eng: eng.tensor_add(
                    s_gm[:, :, 0:1], s_gm[:, :, 0:1], s_gm[:, :, 3:4]),
                    waits=[ev])
                evrg = gop(lambda eng: eng.reciprocal(
                    s_gm[:, :, 2:3], s_gm[:, :, 0:1]),
                    waits=[(evdn[0], evdn[1], True)])
                ev = (evrg[0], evrg[1], True)
                for i in range(2):
                    ev = gop(lambda eng, i=i: eng.tensor_scalar(
                        s_gt[:, i, :], s_gt[:, i, :], s_gm[:, i, 2:3], None,
                        ALU.mult), waits=[ev])
                ev_gate = vfence()
                E[("gdone", c)] = ev_gate
                for i in range(2):
                    E[("tpb", c, i)] = P.op(
                        "tensor",
                        lambda eng, i=i: eng.transpose(
                            p_msc[0:8, 256 + i * 128:256 + (i + 1) * 128],
                            s_gt[:, i, :], s_ident[:]),
                        waits=[ev_gate, E.get(("growev", c - 1))], inc=tq)
                E[("growev", c)] = P.op(
                    "scalar",
                    lambda eng: eng.activation(s_grow[:, :],
                                               p_msc[0:8, 256:512], AF.Copy),
                    waits=[E[("tpb", c, 0)], E[("tpb", c, 1)]], inc=aq)
                if debug == "full":
                    P.dma("sync", dbg_gate[:, c * TCB:(c + 1) * TCB],
                          s_grow[:, :].bitcast(F32), oo[0],
                          waits=[E[("growev", c)]])
                for ex in range(NE):
                    E[("ge", c, ex)] = P.op(
                        "tensor",
                        lambda eng, ex=ex: eng.matmul(
                            p_ge[:, :], s_sel8[:, ex, :], s_grow[:, :],
                            start=True, stop=True),
                        waits=[E[("growev", c)],
                               (E.get(("eotmp", c, ex - 1)) if ex >= 1
                                else E.get(("eotmp", c - 1, NE - 1)))],
                        inc=tq)
                    base = 64 * (ex % 2)
                    for mt in range(NE):
                        E[("w1m", c, ex, mt)] = P.op(
                            "tensor",
                            lambda eng, ex=ex, mt=mt, base=base: eng.matmul(
                                p_mm[:, mt % 2, 0:TCB],
                                s_w1[base:base + 64, ex // 2, mt, :],
                                s_cont[base:base + 64, :],
                                start=True, stop=True),
                            waits=[E[("contev", c)],
                                   (E.get(("he", c, ex, mt - 2)) if mt >= 2
                                    else (E.get(("he", c, ex - 1, NE - 2 + mt))
                                          if ex >= 1
                                          else E.get(("he", c - 1, NE - 1,
                                                      NE - 2 + mt))))],
                            inc=tq)
                        E[("he", c, ex, mt)] = P.op(
                            "scalar",
                            lambda eng, ex=ex, mt=mt: eng.activation(
                                s_he[:, mt, :], p_mm[:, mt % 2, 0:TCB],
                                AF.Relu, bias=s_b1[:, ex, mt:mt + 1]),
                            waits=[E[("w1m", c, ex, mt)],
                                   (E.get(("w2m", c, ex - 1, NE - 1))
                                    if mt < 2 and ex >= 1 else None),
                                   (E.get(("w2m", c - 1, NE - 1, NE - 1))
                                    if mt < 2 and ex == 0 else None)],
                            inc=aq)
                    for kt in range(NE):
                        E[("w2m", c, ex, kt)] = P.op(
                            "tensor",
                            lambda eng, ex=ex, kt=kt: eng.matmul(
                                p_ye[:, :], s_w2[:, kt, ex, :],
                                s_he[:, kt, :],
                                start=(kt == 0), stop=(kt == NE - 1)),
                            waits=[E[("he", c, ex, kt)],
                                   ((E.get(("yeev", c, ex - 1)) if ex >= 1
                                     else E.get(("yeev", c - 1, NE - 1)))
                                    if kt == 0 else None)],
                            inc=tq)
                    E[("yeev", c, ex)] = P.op(
                        "vector",
                        lambda eng, ex=ex: eng.tensor_scalar(
                            s_ye[:, :], p_ye[:, :], s_b2[:, ex:ex + 1],
                            None, ALU.add),
                        waits=[E[("w2m", c, ex, NE - 1)],
                               (E.get(("eotmp", c, ex - 1)) if ex >= 1
                                else E.get(("eotmp", c - 1, NE - 1)))],
                        inc=vq)
                    E[("eotmp", c, ex)] = P.op(
                        "vector",
                        lambda eng, ex=ex: eng.tensor_mul(
                            s_eo[:, 1, :], s_ye[:, :], p_ge[:, :]),
                        waits=[E[("yeev", c, ex)], E[("ge", c, ex)],
                               E.get(("eoev", c, ex - 1))], inc=vq)
                    if ex == 0:
                        E[("eoev", c, ex)] = P.op(
                            "vector",
                            lambda eng: eng.tensor_copy(
                                s_eo[:, 0, :], s_eo[:, 1, :].bitcast(F32)),
                            waits=[E.get(("c2s", c - 1, NKT - 1))], inc=vq)
                    else:
                        E[("eoev", c, ex)] = P.op(
                            "vector",
                            lambda eng: eng.tensor_add(
                                s_eo[:, 0, :], s_eo[:, 0, :].bitcast(F32),
                                s_eo[:, 1, :].bitcast(F32)),
                            inc=vq)
                if debug == "full":
                    P.dma("sync", dbg_eo[:, c * TCB:(c + 1) * TCB],
                          s_eo[:, 0, :].bitcast(F32), oo[0],
                          waits=[E[("eoev", c, NE - 1)]])
                E[("eoev", c, NE - 1)] = vfence()
                for mt in range(NKT):
                    E[("c2s", c, mt)] = P.op(
                        "tensor",
                        lambda eng, mt=mt: eng.matmul(
                            p_mm[:, mt % 2, 0:TCB],
                            s_wc2s[:, mt * 128:(mt + 1) * 128],
                            s_eo[:, 0, :],
                            start=True, stop=True),
                        waits=[E[("eoev", c, NE - 1)],
                               (E.get(("sp2", c, mt - 2)) if mt >= 2
                                else E.get(("decev", c - 1, NMT - 2 + mt)))],
                        inc=tq)
                    E[("sp2", c, mt)] = P.op(
                        "scalar",
                        lambda eng, mt=mt: eng.activation(
                            s_sp2[:, mt, :], p_mm[:, mt % 2, 0:TCB],
                            AF.Sigmoid, bias=s_bc2s[:, mt:mt + 1]),
                        waits=[E[("c2s", c, mt)]], inc=aq)
                for mt in range(NMT):
                    for kt in range(NKT):
                        E[("mmD", c, mt, kt)] = P.op(
                            "tensor",
                            lambda eng, mt=mt, kt=kt: eng.matmul(
                                p_mm[:, mt % 2, 0:TCB],
                                s_wdec[:, kt, mt * 128:(mt + 1) * 128],
                                s_sp2[:, kt, :],
                                start=(kt == 0), stop=(kt == NKT - 1)),
                            waits=[E[("sp2", c, kt)],
                                   ((E.get(("decev", c, mt - 2)) if mt >= 2
                                     else E.get(("sp2", c, NKT - 2 + mt)))
                                    if kt == 0 else None)],
                            inc=tq)
                    E[("decev", c, mt)] = P.op(
                        "vector",
                        lambda eng, mt=mt: eng.tensor_scalar(
                            s_dec[:, mt, :], p_mm[:, mt % 2, 0:TCB],
                            s_bdec[:, mt:mt + 1], None, ALU.add),
                        waits=[E[("mmD", c, mt, NKT - 1)],
                               E.get((("lnO", c - 1), "app1", mt))],
                        inc=vq)
                E[("msc_free", c)] = E[("growev", c)]
                tg2 = ("lnF", c)
                layer_norm(
                    tg2, lambda i: s_dec[:, i, :],
                    lambda i, c=c: E[("decev", c, i)],
                    NMT, TCB,
                    (lambda i: s_lng[:, i:i + 1], lambda i: s_lnb[:, i:i + 1]),
                    lambda i: s_dec[:, i, :],
                    lambda i: None)
                if debug == "full":
                    for mt in range(NMT):
                        P.dma("sync", dbg_ffn[mt * 128:(mt + 1) * 128,
                                              c * TCB:(c + 1) * TCB],
                              s_dec[:, mt, :].bitcast(F32), oo[0],
                              waits=[E[(tg2, "app2", mt)]])
                for mt in range(NMT):
                    E[("x4", c, mt)] = P.op(
                        "vector",
                        lambda eng, mt=mt: eng.tensor_add(
                            s_dec[:, mt, :], s_dec[:, mt, :].bitcast(F32),
                            s_xnf[:, mt, :]),
                        waits=[E[(tg2, "app2", mt)], E[("lxnf", c)],
                               ((oo[0], P.counts.get("oo0", 0))
                                if debug == "full" else None)],
                        inc=vq)
                E[("x4f", c)] = vfence()
                E[("B", c, "dece_war")] = E[("x4f", c)]
                tg3 = ("lnO", c)
                layer_norm(
                    tg3, lambda i: s_dec[:, i, :],
                    lambda i, c=c: E[("x4f", c)],
                    NMT, TCB, None,
                    lambda i: s_out[:, i, :],
                    lambda i, c=c: E.get(("odma", c - 1, NMT - 1)))
                for mt in range(NMT):
                    E[("odma", c, mt)] = P.dma(
                        "sync", out_o[mt * 128:(mt + 1) * 128,
                                      c * TCB:(c + 1) * TCB],
                        s_out[:, mt, :], oo[c % 2],
                        waits=[E[(tg3, "app2", mt)]])

            allsems = [ld, lwA, lwB] + lyz + lxt + lxn + oxn + oo
            P.op("gpsimd", lambda eng: eng.memset(s_cst[:, 7:8], 0.0),
                 waits=[(sm, P.counts[sm.name]) for sm in allsems
                        if sm.name in P.counts] +
                       [(vq, P.counts["vq"]), (aq, P.counts["aq"]),
                        (tq, P.counts["tq"])],
                 inc=ld, inc_by=1)

            P.emit(blk)
    return nc


def _l3_host_inputs(inputs, res1, res2):
    out_proj_w = np.asarray(inputs["L0_attn_out_proj_w"])
    W_enc = np.asarray(inputs["L0_ffn_W_enc"])
    W_s2c = np.asarray(inputs["L0_ffn_W_s2c"])
    Wr = np.asarray(inputs["L0_ffn_Wr"])
    W1 = np.asarray(inputs["L0_ffn_W1"])
    W2 = np.asarray(inputs["L0_ffn_W2"])
    W_c2s = np.asarray(inputs["L0_ffn_W_c2s"])
    W_dec = np.asarray(inputs["L0_ffn_W_dec"])

    outw = np.ascontiguousarray(
        out_proj_w.T.reshape(NKT, 128, D).transpose(1, 0, 2))
    wenc = np.ascontiguousarray(W_enc.reshape(NMT, 128, HS).transpose(1, 0, 2))
    wdec = np.ascontiguousarray(W_dec.reshape(NKT, 128, D).transpose(1, 0, 2))
    ws2c = np.ascontiguousarray(W_s2c.reshape(NKT, 128, CM).transpose(1, 0, 2))
    w1 = np.zeros((128, NE // 2, NE, 128), dtype=np.float32)
    for ex in range(NE):
        base = 64 * (ex % 2)
        w1[base:base + 64, ex // 2] = W1[ex].reshape(CM, NE, 128).transpose(
            0, 1, 2)[:, :, :]
    w2 = np.ascontiguousarray(
        W2.reshape(NE, 8, 128, CM).transpose(2, 1, 0, 3))
    wc2s = np.ascontiguousarray(W_c2s)
    benc = np.ascontiguousarray(
        np.asarray(inputs["L0_ffn_b_enc"]).reshape(NKT, 128).T)
    bs2c = np.asarray(inputs["L0_ffn_b_s2c"]).reshape(CM, 1).astype(np.float32)
    br = np.asarray(inputs["L0_ffn_br"]).reshape(8, 1).astype(np.float32)
    b1 = np.ascontiguousarray(
        np.asarray(inputs["L0_ffn_b1"]).reshape(NE, NE, 128).transpose(2, 0, 1))
    b2 = np.ascontiguousarray(np.asarray(inputs["L0_ffn_b2"]).T)
    bc2s = np.ascontiguousarray(
        np.asarray(inputs["L0_ffn_b_c2s"]).reshape(NKT, 128).T)
    bdec = np.ascontiguousarray(
        np.asarray(inputs["L0_ffn_b_dec"]).reshape(NMT, 128).T)
    lng = np.ascontiguousarray(
        np.asarray(inputs["L0_ffn_ln_g"]).reshape(NMT, 128).T)
    lnb = np.ascontiguousarray(
        np.asarray(inputs["L0_ffn_ln_b"]).reshape(NMT, 128).T)
    sel8 = np.zeros((8, NE, CM), dtype=np.float32)
    for ex in range(NE):
        sel8[ex, ex, :] = 1.0
    ident = np.eye(128, dtype=np.float32)
    ones128 = np.ones((128, 2), dtype=np.float32)
    ones1 = np.ones((1, 128), dtype=np.float32)
    cstv = np.zeros((128, 8), dtype=np.float32)
    cstv[:, 0] = EPS
    cstv[:, 1] = 1.0 / D
    cstv[:, 2] = 1.0
    cstv[:, 3] = -1.0
    cstv[:, 4] = -0.5
    cstv[:, 5] = -1e30

    x = np.ascontiguousarray(
        np.asarray(inputs["L0_attn_x"])[0].T).astype(np.float32, copy=False)
    y_sl = [res2.results[k]["y_o"] for k in range(NCORES)]
    z_full = [res1.results[k]["z_o"][:, HALO:] for k in range(NCORES)]

    maps = []
    for k in range(NCORES):
        y_st = np.ascontiguousarray(
            np.concatenate([y_sl[j][:, k * S:(k + 1) * S]
                            for j in range(NCORES)], axis=0))
        xr_k = np.ascontiguousarray(
            x[:, k * S:(k + 1) * S].reshape(NMT, 128, S).transpose(1, 0, 2))
        maps.append({
            "y_i": y_st, "z_i": np.ascontiguousarray(z_full[k]), "xr": xr_k,
            "outw": outw, "wenc": wenc, "wdec": wdec, "ws2c": ws2c,
            "wr_": Wr.astype(np.float32), "w1": w1, "w2": w2, "wc2s": wc2s,
            "benc": benc, "bs2c": bs2c, "br_": br, "b1": b1, "b2": b2,
            "bc2s": bc2s, "bdec": bdec, "lng": lng, "lnb": lnb,
            "sel8": sel8, "ident": ident, "ones128": ones128, "ones1": ones1,
            "cst": cstv,
        })
    return maps


def kernel(**inputs):
    l1_maps = _l1_host_inputs(inputs)
    nc1 = build_L1()
    res1 = run_bass_kernel_spmd(nc1, l1_maps, core_ids=list(range(NCORES)))
    l2_maps = _l2_host_inputs(inputs, res1)
    nc2 = build_L2()
    res2 = run_bass_kernel_spmd(nc2, l2_maps, core_ids=list(range(NCORES)))
    l3_maps = _l3_host_inputs(inputs, res1, res2)
    nc3 = build_L3()
    res3 = run_bass_kernel_spmd(nc3, l3_maps, core_ids=list(range(NCORES)))
    out = np.concatenate([res3.results[k]["out_o"] for k in range(NCORES)],
                         axis=1)
    return np.ascontiguousarray(out.T).reshape(1, T, D)


if __name__ == "__main__":
    pass


# version 05
# speedup vs baseline: 1.1742x; 1.0714x over previous
import numpy as np
from contextlib import ExitStack

import concourse.bass as bass
import concourse.mybir as mybir
from concourse.bass_utils import run_bass_kernel_spmd

F32 = mybir.dt.float32
F32R = mybir.dt.float32r
AF = mybir.ActivationFunctionType
ALU = mybir.AluOpType

NCORES = 8
T = 16384
S = T // NCORES
D = 1024
DI = 2048
DTR = 64
NST = 16
DSL = DI // NCORES
EPS = 1e-5
HALO = 3
NE = 8
HS = 2048
CM = 64
EH = 1024

SH = S + HALO
CH = [512, 512, 512, 512, 4]
CO = [0, 512, 1024, 1536, 2047]
NG = D // 128
NM = 2 * DI // 128
NDT = DI // 128


class Prog:

    def __init__(self, nc):
        self.nc = nc
        self.ops = {k: [] for k in
                    ("sync", "vector", "scalar", "tensor", "gpsimd")}
        self.counts = {}
        self.base_waits = []

    def op(self, engine, fn, waits=(), inc=None, inc_by=1):
        ev = None
        if inc is not None:
            key = inc.name
            self.counts[key] = self.counts.get(key, 0) + inc_by
            ev = (inc, self.counts[key])
        allw = list(self.base_waits) + [w for w in waits if w is not None]
        self.ops[engine].append((fn, allw,
                                 (inc, inc_by) if inc is not None else None))
        return ev

    def dma(self, engine, out, in_, sem, waits=()):
        return self.op(engine, lambda eng, o=out, i=in_: eng.dma_start(out=o, in_=i),
                       waits=waits, inc=sem, inc_by=16)

    def emit(self, block):
        def mk(name):
            def body(eng):
                hi = {}
                for fn, waits, inc in self.ops[name]:
                    for wt in waits:
                        sem, val = wt[0], wt[1]
                        force = len(wt) > 2 and wt[2]
                        if not force and hi.get(sem.name, -1) >= val:
                            continue
                        eng.wait_ge(sem, val)
                        hi[sem.name] = max(hi.get(sem.name, -1), val)
                    ins = fn(eng)
                    if inc is not None:
                        ins.then_inc(inc[0], inc[1])
            return body
        block.sync(mk("sync"))
        block.vector(mk("vector"))
        block.scalar(mk("scalar"))
        block.tensor(mk("tensor"))
        block.gpsimd(mk("gpsimd"))


def slot(c):
    return 515 * (c % 2)


def build_L1():
    nc = bass.Bass("TRN2", target_bir_lowering=False, debug=False,
                   detect_race_conditions=False)

    xh = nc.dram_tensor("xh", [128, NG, SH], F32, kind="ExternalInput").ap()
    inw = nc.dram_tensor("inw", [NM, 128, NG, 128], F32, kind="ExternalInput").ap()
    normw = nc.dram_tensor("normw", [128, NG], F32, kind="ExternalInput").ap()
    xpw = nc.dram_tensor("xpw", [128, NDT, 128], F32, kind="ExternalInput").ap()
    dtw = nc.dram_tensor("dtw", [128, DI], F32, kind="ExternalInput").ap()
    convw = nc.dram_tensor("convw", [128, NDT, 4], F32, kind="ExternalInput").ap()
    convb = nc.dram_tensor("convb", [128, NDT], F32, kind="ExternalInput").ap()
    dtb = nc.dram_tensor("dtb", [128, NDT], F32, kind="ExternalInput").ap()
    ones128 = nc.dram_tensor("ones128", [128, 2], F32, kind="ExternalInput").ap()
    ones1 = nc.dram_tensor("ones1", [1, 128], F32, kind="ExternalInput").ap()
    cst = nc.dram_tensor("cst", [128, 8], F32, kind="ExternalInput").ap()

    u_o = nc.dram_tensor("u_o", [DI, S], F32, kind="ExternalOutput").ap()
    z_o = nc.dram_tensor("z_o", [DI, SH], F32, kind="ExternalOutput").ap()
    dl_o = nc.dram_tensor("dl_o", [DI, S], F32, kind="ExternalOutput").ap()
    bc_o = nc.dram_tensor("bc_o", [64, S], F32, kind="ExternalOutput").ap()

    with ExitStack() as stk:
        e = stk.enter_context
        s_w = e(nc.sbuf_tensor("s_w", [128, 3, NG, 128], F32R))
        s_x = e(nc.sbuf_tensor("s_x", [128, 2, NG, 512], F32R))
        s_normw = e(nc.sbuf_tensor("s_normw", [128, NG], F32))
        s_xpw = e(nc.sbuf_tensor("s_xpw", [128, NDT, 128], F32R))
        s_dtw = e(nc.sbuf_tensor("s_dtw", [128, DI], F32R))
        s_convw = e(nc.sbuf_tensor("s_convw", [128, NDT, 4], F32))
        s_convb = e(nc.sbuf_tensor("s_convb", [128, NDT], F32))
        s_dtb = e(nc.sbuf_tensor("s_dtb", [128, NDT], F32))
        s_o128 = e(nc.sbuf_tensor("s_o128", [128, 2], F32R))
        s_o1 = e(nc.sbuf_tensor("s_o1", [1, 128], F32R))
        s_cst = e(nc.sbuf_tensor("s_cst", [128, 8], F32))
        s_sq = e(nc.sbuf_tensor("s_sq", [128, 2, 512], F32R))
        s_rstd = e(nc.sbuf_tensor("s_rstd", [1, 512], F32R))
        s_rstdb = e(nc.sbuf_tensor("s_rstdb", [128, 2, 512], F32))
        s_xp = e(nc.sbuf_tensor("s_xp", [128, NDT, 1030], F32))
        s_z = e(nc.sbuf_tensor("s_z", [128, 2, 512], F32))
        s_xc = e(nc.sbuf_tensor("s_xc", [128, 2, 512], F32))
        s_u = e(nc.sbuf_tensor("s_u", [128, 3, 512], F32R))
        s_xdbl = e(nc.sbuf_tensor("s_xdbl", [128, 512], F32R))
        s_sp = e(nc.sbuf_tensor("s_sp", [128, 2, 512], F32))
        s_dl = e(nc.sbuf_tensor("s_dl", [128, 2, 512], F32))
        p_mm = e(nc.psum_tensor("p_mm", [128, 2, 512], F32))
        p_ss = e(nc.psum_tensor("p_ss", [2, 512], F32))
        p_rb = e(nc.psum_tensor("p_rb", [128, 512], F32))
        p_xd = e(nc.psum_tensor("p_xd", [128, 512], F32))
        p_dt = e(nc.psum_tensor("p_dt", [128, 2, 512], F32))
        ld = e(nc.semaphore("ld"))
        lx = [e(nc.semaphore(f"lx{i}")) for i in range(2)]
        lw = [e(nc.semaphore(f"lw{i}")) for i in range(3)]
        oz = [e(nc.semaphore(f"oz{i}")) for i in range(2)]
        ou = [e(nc.semaphore(f"ou{i}")) for i in range(3)]
        od = [e(nc.semaphore(f"od{i}")) for i in range(2)]
        ob = e(nc.semaphore("ob"))
        tq = e(nc.semaphore("tq"))
        vq = e(nc.semaphore("vq"))
        aq = e(nc.semaphore("aq"))
        blk = e(nc.Block())

        P = Prog(nc)

        ev_c = None
        for dst, src in [(s_normw[:], normw), (s_xpw[:], xpw.bitcast(F32R)),
                         (s_dtw[:], dtw.bitcast(F32R)), (s_convw[:], convw),
                         (s_convb[:], convb), (s_dtb[:], dtb),
                         (s_o128[:], ones128.bitcast(F32R)),
                         (s_o1[:], ones1.bitcast(F32R)), (s_cst[:], cst)]:
            ev_c = P.dma("sync", dst, src, ld)
        P.base_waits = [ev_c]

        E = {}

        def prev_ring(kind, i, depth):
            return E.get((kind, i - depth))

        def chunk_section(c):
            w_ = CH[c]
            E[("xload", c)] = P.dma(
                "sync", s_x[:, c % 2, :, 0:w_],
                xh[:, :, CO[c]:CO[c] + w_].bitcast(F32R), lx[c % 2],
                waits=[E.get(("mm", (c - 2) * NM + NM - 1, NG - 1)),
                       E.get(("xscale", c - 2, NG - 1))])

            for g in range(NG):
                gg = c * NG + g
                E[("sq", c, g)] = P.op(
                    "scalar",
                    lambda eng, g=g, w_=w_, c=c: eng.activation(
                        s_sq[:, (c * NG + g) % 2, 0:w_],
                        s_x[:, c % 2, g, 0:w_].bitcast(F32), AF.Square),
                    waits=[E[("xload", c)], ev_c, prev_ring("stats", gg, 2)],
                    inc=aq)
                E[("stats", gg)] = P.op(
                    "tensor",
                    lambda eng, g=g, w_=w_, c=c: eng.matmul(
                        p_ss[:, 0:w_], s_o128[:],
                        s_sq[:, (c * NG + g) % 2, 0:w_],
                        start=(g == 0), stop=(g == NG - 1)),
                    waits=[E[("sq", c, g)],
                           E.get(("rstdL", c - 1)) if g == 0 else None],
                    inc=tq)
                E[("xscale", c, g)] = P.op(
                    "vector",
                    lambda eng, g=g, w_=w_, c=c: eng.tensor_scalar_mul(
                        s_x[:, c % 2, g, 0:w_],
                        s_x[:, c % 2, g, 0:w_].bitcast(F32),
                        s_normw[:, g:g + 1]),
                    waits=[E[("sq", c, g)]], inc=vq)

            E[("rstdL", c)] = P.op(
                "scalar",
                lambda eng, w_=w_: eng.activation(
                    s_rstd[0:1, 0:w_], p_ss[0:1, 0:w_],
                    AF.Ln, bias=s_cst[0:1, 0:1], scale=s_cst[0:1, 1:2]),
                waits=[E[("stats", c * NG + NG - 1)]], inc=aq)
            E[("rstd", c)] = P.op(
                "scalar",
                lambda eng, w_=w_: eng.activation(
                    s_rstd[0:1, 0:w_],
                    s_rstd[0:1, 0:w_].bitcast(F32), AF.Exp,
                    scale=s_cst[0:1, 4:5]),
                inc=aq)
            E[("rb", c)] = P.op(
                "tensor",
                lambda eng, w_=w_: eng.matmul(
                    p_rb[:, 0:w_], s_o1[:], s_rstd[0:1, 0:w_],
                    start=True, stop=True),
                waits=[E[("rstd", c)], E.get(("rstdb", c - 1))], inc=tq)
            E[("rstdb", c)] = P.op(
                "scalar",
                lambda eng, w_=w_, c=c: eng.activation(
                    s_rstdb[:, c % 2, 0:w_], p_rb[:, 0:w_], AF.Copy),
                waits=[E[("rb", c)], E.get(("evac", c - 2, NM - 1))], inc=aq)

            for m in range(NM):
                k = c * NM + m
                E[("wload", k)] = P.dma(
                    "gpsimd", s_w[:, k % 3, :, :], inw[m, :, :, :].bitcast(F32R),
                    lw[k % 3],
                    waits=[E.get(("mm", k - 3, NG - 1))])
                for g in range(NG):
                    bank_war = None
                    if g == 0:
                        bank_war = E.get(("evac", (k - 2) // NM, (k - 2) % NM)) \
                            if k - 2 >= 0 else None
                    E[("mm", k, g)] = P.op(
                        "tensor",
                        lambda eng, m=m, g=g, w_=w_, c=c, k=k: eng.matmul(
                            p_mm[:, m % 2, 0:w_],
                            s_w[:, k % 3, g, :],
                            s_x[:, c % 2, g, 0:w_],
                            start=(g == 0), stop=(g == NG - 1)),
                        waits=[E[("wload", k)], E[("xscale", c, g)], bank_war],
                        inc=tq)
                emm = E[("mm", k, NG - 1)]
                if m < NDT:
                    last = None
                    if c < 4:
                        last = P.op(
                            "vector",
                            lambda eng, m=m, w_=w_, c=c: eng.tensor_mul(
                                s_xp[:, m, slot(c):slot(c) + w_],
                                p_mm[:, m % 2, 0:w_], s_rstdb[:, c % 2, 0:w_]),
                            waits=[emm, E[("rstdb", c)],
                                   E.get(("conv", c - 2, m))],
                            inc=vq)
                    if c >= 1:
                        tl = 1 if c == 4 else 0
                        last = P.op(
                            "vector",
                            lambda eng, m=m, c=c, tl=tl: eng.tensor_mul(
                                s_xp[:, m, slot(c - 1) + 512:slot(c - 1) + 515],
                                p_mm[:, m % 2, tl:tl + 3],
                                s_rstdb[:, c % 2, tl:tl + 3]),
                            waits=[emm, E[("rstdb", c)],
                                   E.get(("conv", c - 3, m))],
                            inc=vq)
                        E[("tail", c, m)] = last
                    E[("evac", c, m)] = last
                else:
                    mz = m - NDT
                    E[("evac", c, m)] = P.op(
                        "vector",
                        lambda eng, m=m, mz=mz, w_=w_, c=c: eng.tensor_mul(
                            s_z[:, mz % 2, 0:w_],
                            p_mm[:, m % 2, 0:w_], s_rstdb[:, c % 2, 0:w_]),
                        waits=[emm, E[("rstdb", c)],
                               prev_ring("zdma", c * NDT + mz, 2)],
                        inc=vq)
                    zlo = 1 if c == 4 else 0
                    E[("zdma", c * NDT + mz)] = P.dma(
                        "sync", z_o[mz * 128:(mz + 1) * 128,
                                    CO[c] + zlo:CO[c] + w_],
                        s_z[:, mz % 2, zlo:w_], oz[mz % 2],
                        waits=[E[("evac", c, m)]])

        def out_section(j):
            for g in range(NDT):
                ug = j * NDT + g
                first = P.op(
                    "vector",
                    lambda eng, g=g, j=j: eng.tensor_scalar_mul(
                        s_xc[:, g % 2, :],
                        s_xp[:, g, slot(j) + 0:slot(j) + 512],
                        s_convw[:, g:g + 1, 0]),
                    waits=[E.get(("evac", j, g)), E.get(("tail", j + 1, g)),
                           prev_ring("silu", ug, 2)],
                    inc=vq)
                for jj in (1, 2, 3):
                    last = P.op(
                        "vector",
                        lambda eng, g=g, j=j, jj=jj: eng.scalar_tensor_tensor(
                            s_xc[:, g % 2, :],
                            s_xp[:, g, slot(j) + jj:slot(j) + jj + 512],
                            s_convw[:, g:g + 1, jj],
                            s_xc[:, g % 2, :],
                            ALU.mult, ALU.add),
                        inc=vq)
                E[("conv", j, g)] = last
                E[("silu", ug)] = P.op(
                    "scalar",
                    lambda eng, g=g, ug=ug: eng.activation(
                        s_u[:, ug % 3, :],
                        s_xc[:, g % 2, :], AF.Silu,
                        bias=s_convb[:, g:g + 1]),
                    waits=[E[("conv", j, g)],
                           prev_ring("xproj", ug, 3), prev_ring("udma", ug, 3)],
                    inc=aq)
                E[("udma", ug)] = P.dma(
                    "sync", u_o[g * 128:(g + 1) * 128, j * 512:(j + 1) * 512],
                    s_u[:, ug % 3, :].bitcast(F32), ou[ug % 3],
                    waits=[E[("silu", ug)]])
                E[("xproj", ug)] = P.op(
                    "tensor",
                    lambda eng, g=g, ug=ug: eng.matmul(
                        p_xd[:, :], s_xpw[:, g, :], s_u[:, ug % 3, :],
                        start=(g == 0), stop=(g == NDT - 1)),
                    waits=[E[("silu", ug)],
                           E.get(("xdev", j - 1)) if g == 0 else None],
                    inc=tq)
            E[("xdev", j)] = P.op(
                "scalar",
                lambda eng: eng.activation(s_xdbl[:, :],
                                           p_xd[:, :], AF.Copy),
                waits=[E[("xproj", j * NDT + NDT - 1)], E.get(("bcdma", j - 1)),
                       E.get(("dt", j - 1, NDT - 1))],
                inc=aq)
            E[("bcdma", j)] = P.dma(
                "sync", bc_o[0:48, j * 512:(j + 1) * 512],
                s_xdbl[0:48, :].bitcast(F32), ob, waits=[E[("xdev", j)]])
            for mt in range(NDT):
                mtg = j * NDT + mt
                E[("dt", j, mt)] = P.op(
                    "tensor",
                    lambda eng, mt=mt: eng.matmul(
                        p_dt[:, mt % 2, :],
                        s_dtw[64:128, mt * 128:(mt + 1) * 128],
                        s_xdbl[64:128, :], start=True, stop=True),
                    waits=[E[("xdev", j)], prev_ring("relu", mtg, 2)],
                    inc=tq)
                P.op("scalar",
                     lambda eng, mt=mt: eng.activation(
                         s_sp[:, mt % 2, :], p_dt[:, mt % 2, :], AF.Abs,
                         bias=s_dtb[:, mt:mt + 1]),
                     waits=[E[("dt", j, mt)], prev_ring("dladd", mtg, 2)],
                     inc=aq)
                P.op("scalar",
                     lambda eng, mt=mt: eng.activation(
                         s_sp[:, mt % 2, :], s_sp[:, mt % 2, :], AF.Exp,
                         scale=s_cst[:, 3:4]),
                     inc=aq)
                P.op("scalar",
                     lambda eng, mt=mt: eng.activation(
                         s_sp[:, mt % 2, :], s_sp[:, mt % 2, :], AF.Ln,
                         bias=s_cst[:, 2:3]),
                     inc=aq)
                E[("relu", mtg)] = P.op(
                    "scalar",
                    lambda eng, mt=mt: eng.activation(
                        s_dl[:, mt % 2, :], p_dt[:, mt % 2, :], AF.Relu,
                        bias=s_dtb[:, mt:mt + 1]),
                    waits=[prev_ring("dldma", mtg, 2)],
                    inc=aq)
                E[("dladd", mtg)] = P.op(
                    "vector",
                    lambda eng, mt=mt: eng.tensor_add(
                        s_dl[:, mt % 2, :], s_dl[:, mt % 2, :],
                        s_sp[:, mt % 2, :]),
                    waits=[E[("relu", mtg)]], inc=vq)
                E[("dldma", mtg)] = P.dma(
                    "sync", dl_o[mt * 128:(mt + 1) * 128, j * 512:(j + 1) * 512],
                    s_dl[:, mt % 2, :], od[mt % 2],
                    waits=[E[("dladd", mtg)]])

        for c in range(5):
            chunk_section(c)
            if c >= 1:
                out_section(c - 1)

        allsems = [ld] + lx + lw + oz + ou + od + [ob]
        P.op("gpsimd", lambda eng: eng.memset(s_o128[:].bitcast(F32), 1.0),
             waits=[(s, P.counts[s.name]) for s in allsems
                    if s.name in P.counts] +
                   [(vq, P.counts["vq"]), (aq, P.counts["aq"]),
                    (tq, P.counts["tq"])],
             inc=ld, inc_by=1)

        P.emit(blk)
    return nc


def _l1_host_inputs(inputs):
    x = np.ascontiguousarray(inputs["L0_attn_x"][0].T).astype(np.float32, copy=False)
    in_proj_w = inputs["L0_attn_in_proj_w"]
    x_proj_w = inputs["L0_attn_x_proj_w"]
    dt_proj_w = inputs["L0_attn_dt_proj_w"]
    conv_w = inputs["L0_attn_conv_w"]
    conv_b = inputs["L0_attn_conv_b"]
    dt_proj_b = inputs["L0_attn_dt_proj_b"]
    norm_w = inputs["L0_attn_norm_w"]

    inw = np.ascontiguousarray(
        np.asarray(in_proj_w).reshape(NM, 128, NG, 128).transpose(0, 3, 2, 1))
    normw = np.ascontiguousarray(np.asarray(norm_w).reshape(NG, 128).T)
    xpw = np.zeros((128, NDT, 128), dtype=np.float32)
    xpt = np.asarray(x_proj_w)
    for g in range(NDT):
        blkw = xpt[:, g * 128:(g + 1) * 128]
        xpw[:, g, 0:16] = blkw[64:80].T
        xpw[:, g, 32:48] = blkw[80:96].T
        xpw[:, g, 64:128] = blkw[0:64].T
    dtw = np.zeros((128, DI), dtype=np.float32)
    dtw[64:128, :] = np.asarray(dt_proj_w).T
    convw = np.ascontiguousarray(
        np.asarray(conv_w)[:, 0, :].reshape(NDT, 128, 4).transpose(1, 0, 2))
    convb = np.ascontiguousarray(np.asarray(conv_b).reshape(NDT, 128).T)
    dtb = np.ascontiguousarray(np.asarray(dt_proj_b).reshape(NDT, 128).T)
    ones128 = np.ones((128, 2), dtype=np.float32)
    ones1 = np.ones((1, 128), dtype=np.float32)
    cstv = np.zeros((128, 8), dtype=np.float32)
    cstv[:, 0] = EPS
    cstv[:, 1] = 1.0 / D
    cstv[:, 2] = 1.0
    cstv[:, 3] = -1.0
    cstv[:, 4] = -0.5

    maps = []
    for k in range(NCORES):
        lo = k * S - HALO
        if lo < 0:
            xh2 = np.concatenate(
                [np.zeros((D, HALO), np.float32), x[:, 0:(k + 1) * S]], axis=1)
        else:
            xh2 = x[:, lo:(k + 1) * S]
        xh2 = np.ascontiguousarray(xh2.reshape(NG, 128, SH).transpose(1, 0, 2))
        maps.append({
            "xh": xh2, "inw": inw, "normw": normw, "xpw": xpw, "dtw": dtw,
            "convw": convw, "convb": convb, "dtb": dtb,
            "ones128": ones128, "ones1": ones1, "cst": cstv,
        })
    return maps




TC2 = 1024
NC2 = T // TC2
NSUB = TC2 // 512


def build_L2():
    nc = bass.Bass("TRN2", target_bir_lowering=False, debug=False,
                   detect_race_conditions=False)

    u_i = nc.dram_tensor("u_i", [DSL, T], F32, kind="ExternalInput").ap()
    dl_i = nc.dram_tensor("dl_i", [DSL, T], F32, kind="ExternalInput").ap()
    B_i = nc.dram_tensor("B_i", [16, T], F32, kind="ExternalInput").ap()
    C_i = nc.dram_tensor("C_i", [16, T], F32, kind="ExternalInput").ap()
    alog = nc.dram_tensor("alog", [128, 2, NST], F32, kind="ExternalInput").ap()
    dvec = nc.dram_tensor("dvec", [128, 2], F32, kind="ExternalInput").ap()
    sel = nc.dram_tensor("sel", [16, 16 * 128], F32, kind="ExternalInput").ap()
    ident = nc.dram_tensor("ident", [128, 128], F32, kind="ExternalInput").ap()
    cst = nc.dram_tensor("cst", [128, 8], F32, kind="ExternalInput").ap()

    y_o = nc.dram_tensor("y_o", [DSL, T], F32, kind="ExternalOutput").ap()

    with ExitStack() as stk:
        e = stk.enter_context
        s_u = e(nc.sbuf_tensor("s_u", [128, 2, 3, TC2], F32))
        s_dl = e(nc.sbuf_tensor("s_dl", [128, 2, 3, TC2], F32))
        s_du = e(nc.sbuf_tensor("s_du", [128, 2, 3, TC2], F32))
        s_B = e(nc.sbuf_tensor("s_B", [16, 2, TC2], F32R))
        s_C = e(nc.sbuf_tensor("s_C", [16, 2, TC2], F32R))
        s_sel = e(nc.sbuf_tensor("s_sel", [16, 16 * 128], F32R))
        s_ident = e(nc.sbuf_tensor("s_ident", [128, 128], F32R))
        s_alog = e(nc.sbuf_tensor("s_alog", [128, 2, NST], F32))
        s_A = e(nc.sbuf_tensor("s_A", [128, 2, NST], F32))
        s_D = e(nc.sbuf_tensor("s_D", [128, 2], F32))
        s_cst = e(nc.sbuf_tensor("s_cst", [128, 8], F32))
        s_dA = e(nc.sbuf_tensor("s_dA", [128, 2, 2, TC2], F32))
        s_w = e(nc.sbuf_tensor("s_w", [128, 2, TC2], F32))
        s_h = e(nc.sbuf_tensor("s_h", [128, 2, 2, TC2], F32))
        s_p = e(nc.sbuf_tensor("s_p", [128, 2, 2, TC2], F32R))
        s_Ct = e(nc.sbuf_tensor("s_Ct", [128, 2, TC2], F32))
        s_carry = e(nc.sbuf_tensor("s_carry", [128, 2, NST], F32))
        s_y = e(nc.sbuf_tensor("s_y", [128, 2, TC2], F32))
        p_B = e(nc.psum_tensor("p_B", [128, 2, 512], F32))
        p_C = e(nc.psum_tensor("p_C", [128, 2, 512], F32))
        p_y = e(nc.psum_tensor("p_y", [128, 2, NSUB, 512], F32))
        ld = e(nc.semaphore("ld"))
        ldl = [e(nc.semaphore(f"ldl{i}")) for i in range(2)]
        lu = [e(nc.semaphore(f"lu{i}")) for i in range(2)]
        lbc = [e(nc.semaphore(f"lbc{i}")) for i in range(2)]
        oy = [e(nc.semaphore(f"oy{i}")) for i in range(2)]
        tq = e(nc.semaphore("tq"))
        vq = e(nc.semaphore("vq"))
        aq = e(nc.semaphore("aq"))
        pq = e(nc.semaphore("pq"))
        blk = e(nc.Block())

        P = Prog(nc)
        E = {}

        ev_c = None
        for dst, src_ in [(s_sel[:], sel.bitcast(F32R)),
                          (s_ident[:], ident.bitcast(F32R)),
                          (s_alog[:], alog), (s_D[:], dvec), (s_cst[:], cst)]:
            ev_c = P.dma("sync", dst, src_, ld)
        P.base_waits = [ev_c]

        eva0 = P.op("scalar",
                    lambda eng: eng.activation(s_A[:], s_alog[:], AF.Exp),
                    waits=[ev_c], inc=aq)
        evA = P.op("vector",
                   lambda eng: eng.tensor_scalar_mul(s_A[:], s_A[:],
                                                     s_cst[:, 3:4]),
                   waits=[eva0], inc=vq)

        NTOT = NC2 * NST

        def em_load(c):
            if c >= NC2:
                return
            r3 = c % 3
            for dt2 in range(2):
                E[("lu", c, dt2)] = P.dma(
                    "sync", s_u[:, dt2, r3, :],
                    u_i[dt2 * 128:(dt2 + 1) * 128, c * TC2:(c + 1) * TC2],
                    lu[c % 2], waits=[E.get(("yev", c - 3, dt2, NSUB - 1))])
                E[("ldl", c, dt2)] = P.dma(
                    "sync", s_dl[:, dt2, r3, :],
                    dl_i[dt2 * 128:(dt2 + 1) * 128, c * TC2:(c + 1) * TC2],
                    ldl[c % 2], waits=[E.get(("dA", c - 3, dt2, NST - 1))])

        def em_loadbc(c):
            if c >= NC2:
                return
            r = c % 2
            E[("lB", c)] = P.dma(
                "sync", s_B[:, r, :],
                B_i[:, c * TC2:(c + 1) * TC2].bitcast(F32R),
                lbc[r], waits=[E.get(("bb", c - 2, NST - 1, NSUB - 1))])
            E[("lC", c)] = P.dma(
                "sync", s_C[:, r, :],
                C_i[:, c * TC2:(c + 1) * TC2].bitcast(F32R),
                lbc[r], waits=[E.get(("cb", c - 2, NST - 1, NSUB - 1))])

        def em_du(c):
            r3 = c % 3
            for dt2 in range(2):
                E[("du", c, dt2)] = P.op(
                    "vector",
                    lambda eng, dt2=dt2, r3=r3: eng.tensor_mul(
                        s_du[:, dt2, r3, :], s_dl[:, dt2, r3, :],
                        s_u[:, dt2, r3, :]),
                    waits=[E[("lu", c, 1)], E[("ldl", c, 1)],
                           E.get(("w", c - 3, dt2, NST - 1))],
                    inc=vq)

        def em_dA(c, n):
            if c >= NC2:
                return
            for dt2 in range(2):
                E[("dA", c, dt2, n)] = P.op(
                    "scalar",
                    lambda eng, dt2=dt2, n=n, c=c: eng.activation(
                        s_dA[:, dt2, (n + NST * c) % 2, :],
                        s_dl[:, dt2, c % 3, :], AF.Exp,
                        scale=s_A[:, dt2, n:n + 1]),
                    waits=[E[("ldl", c, 1)], evA,
                           E.get(("scan", c, dt2, n - 2)) or
                           E.get(("scan", c - 1, dt2, NST - 2 + (n % 2)))],
                    inc=aq)

        def em_builds(g):
            if g >= NTOT:
                return
            c, n = divmod(g, NST)
            r = c % 2
            for s in range(NSUB):
                E[("bb", c, n, s)] = P.op(
                    "tensor",
                    lambda eng, n=n, s=s, r=r: eng.matmul(
                        p_B[:, s, :],
                        s_sel[:, n * 128:(n + 1) * 128],
                        s_B[:, r, s * 512:(s + 1) * 512],
                        start=True, stop=True),
                    waits=[E[("lC", c)], E.get(("w2l", g - 1, s))],
                    inc=tq)
            for s in range(NSUB):
                E[("cb", c, n, s)] = P.op(
                    "tensor",
                    lambda eng, n=n, s=s, r=r: eng.matmul(
                        p_C[:, s, :],
                        s_sel[:, n * 128:(n + 1) * 128],
                        s_C[:, r, s * 512:(s + 1) * 512],
                        start=True, stop=True),
                    waits=[E[("lC", c)], E.get(("ctev", g - 1, s))],
                    inc=tq)

        def em_w2scan(c, n, dt2):
            r3 = c % 3
            g = c * NST + n
            ev = None
            for s in range(NSUB):
                ev = P.op(
                    "vector",
                    lambda eng, dt2=dt2, s=s, r3=r3: eng.tensor_mul(
                        s_w[:, dt2, s * 512:(s + 1) * 512],
                        s_du[:, dt2, r3, s * 512:(s + 1) * 512],
                        p_B[:, s, :]),
                    waits=[E[("bb", c, n, s)], E[("du", c, dt2)]],
                    inc=vq)
                if dt2 == 1:
                    E[("w2l", g, s)] = ev
            E[("w", c, dt2, n)] = ev
            hwar = (E.get(("pp", c, dt2, n - 2)) if n >= 2
                    else E.get(("pp", c - 1, dt2, NST - 2 + (n % 2))))
            E[("scan", c, dt2, n)] = P.op(
                "vector",
                lambda eng, dt2=dt2, n=n, c=c: eng.tensor_tensor_scan(
                    s_h[:, dt2, n % 2, :],
                    s_dA[:, dt2, (n + NST * c) % 2, :],
                    s_w[:, dt2, :],
                    (0.0 if c == 0 else s_carry[:, dt2, n:n + 1]),
                    ALU.mult, ALU.add),
                waits=[E[("dA", c, dt2, n)],
                       (E[("carry", c - 1, dt2, n)][0],
                        E[("carry", c - 1, dt2, n)][1], True)
                       if c >= 1 else None, hwar],
                inc=vq)
            sc_ev = E[("scan", c, dt2, n)]
            E[("carry", c, dt2, n)] = P.op(
                "vector",
                lambda eng, dt2=dt2, n=n: eng.tensor_copy(
                    s_carry[:, dt2, n:n + 1],
                    s_h[:, dt2, n % 2, TC2 - 1:TC2]),
                waits=[(sc_ev[0], sc_ev[1], True)],
                inc=vq)

        def em_ctev(c, n):
            g = c * NST + n
            for s in range(NSUB):
                E[("ctev", g, s)] = P.op(
                    "scalar",
                    lambda eng, n=n, s=s: eng.activation(
                        s_Ct[:, n % 2, s * 512:(s + 1) * 512],
                        p_C[:, s, :], AF.Copy),
                    waits=[E[("cb", c, n, s)],
                           E.get(("pp", c, 1, n - 2)) or
                           (E.get(("pp", c - 1, 1, NST - 2 + (n % 2)))
                            if c >= 1 else None)],
                    inc=aq)
            E[("ctevl", c, n)] = E[("ctev", g, NSUB - 1)]

        def em_pp(c, n):
            for dt2 in range(2):
                E[("pp", c, dt2, n)] = P.op(
                    "gpsimd",
                    lambda eng, dt2=dt2, n=n: eng.tensor_mul(
                        s_p[:, dt2, n % 2, :], s_h[:, dt2, n % 2, :],
                        s_Ct[:, n % 2, :]),
                    waits=[E[("scan", c, dt2, n)], E[("ctevl", c, n)],
                           (E.get(("ya", c, dt2, n - 2, NSUB - 1)) if n >= 2
                            else E.get(("ya", c - 1, dt2, NST - 2 + (n % 2),
                                        NSUB - 1)))],
                    inc=pq)

        def em_ya(c, n):
            for dt2 in range(2):
                for s in range(NSUB):
                    E[("ya", c, dt2, n, s)] = P.op(
                        "tensor",
                        lambda eng, dt2=dt2, n=n, s=s: eng.matmul(
                            p_y[:, dt2, s, :], s_ident[:],
                            s_p[:, dt2, n % 2, s * 512:(s + 1) * 512],
                            start=(n == 0), stop=(n == NST - 1)),
                        waits=[E[("pp", c, dt2, n)],
                               (E.get(("yev", c - 1, dt2, s))
                                if n == 0 else None)],
                        inc=tq)

        em_load(0)
        em_load(1)
        em_loadbc(0)
        em_loadbc(1)
        em_du(0)
        em_dA(0, 0)
        em_builds(0)
        for c in range(NC2):
            if c > 0:
                em_du(c)
            em_load(c + 2)
            for n in range(NST):
                g = c * NST + n
                nc_, nn = divmod(g + 1, NST)
                em_dA(nc_, nn)
                em_w2scan(c, n, 0)
                em_ctev(c, n)
                em_w2scan(c, n, 1)
                em_builds(g + 1)
                if n >= 1:
                    em_ya(c, n - 1)
                em_pp(c, n)
            em_ya(c, NST - 1)
            em_loadbc(c + 2)
            r3 = c % 3
            for dt2 in range(2):
                for s in range(NSUB):
                    E[("yev", c, dt2, s)] = P.op(
                        "vector",
                        lambda eng, dt2=dt2, s=s, r3=r3:
                        eng.scalar_tensor_tensor(
                            s_y[:, dt2, s * 512:(s + 1) * 512],
                            s_u[:, dt2, r3, s * 512:(s + 1) * 512],
                            s_D[:, dt2:dt2 + 1],
                            p_y[:, dt2, s, :],
                            ALU.mult, ALU.add),
                        waits=[E[("ya", c, dt2, NST - 1, s)],
                               E.get(("ydma", c - 1, dt2))],
                        inc=vq)
                E[("ydma", c, dt2)] = P.dma(
                    "sync",
                    y_o[dt2 * 128:(dt2 + 1) * 128, c * TC2:(c + 1) * TC2],
                    s_y[:, dt2, :], oy[dt2],
                    waits=[E[("yev", c, dt2, NSUB - 1)]])

        allsems = [ld] + ldl + lu + lbc + oy
        P.op("gpsimd", lambda eng: eng.memset(s_cst[:, 7:8], 0.0),
             waits=[(sm, P.counts[sm.name]) for sm in allsems
                    if sm.name in P.counts] +
                   [(vq, P.counts["vq"]), (aq, P.counts["aq"]),
                    (tq, P.counts["tq"]), (pq, P.counts["pq"])],
             inc=pq, inc_by=1)

        P.emit(blk)
    return nc


def _l2_host_inputs(inputs, res1):
    A_log = np.asarray(inputs["L0_attn_A_log"])
    Dv = np.asarray(inputs["L0_attn_D"])
    u_full = [res1.results[k]["u_o"] for k in range(NCORES)]
    dl_full = [res1.results[k]["dl_o"] for k in range(NCORES)]
    bc_full = np.concatenate([res1.results[k]["bc_o"] for k in range(NCORES)],
                             axis=1)
    B_full = np.ascontiguousarray(bc_full[0:16])
    C_full = np.ascontiguousarray(bc_full[32:48])
    sel = np.zeros((16, 16 * 128), dtype=np.float32)
    for n in range(16):
        sel[n, n * 128:(n + 1) * 128] = 1.0
    ident = np.eye(128, dtype=np.float32)
    cstv = np.zeros((128, 8), dtype=np.float32)
    cstv[:, 0] = EPS
    cstv[:, 1] = 1.0 / D
    cstv[:, 2] = 1.0
    cstv[:, 3] = -1.0
    cstv[:, 4] = -0.5

    maps = []
    for k in range(NCORES):
        lo = k * DSL
        u_sl = np.ascontiguousarray(
            np.concatenate([u_full[j][lo:lo + DSL] for j in range(NCORES)],
                           axis=1))
        dl_sl = np.ascontiguousarray(
            np.concatenate([dl_full[j][lo:lo + DSL] for j in range(NCORES)],
                           axis=1))
        al = np.ascontiguousarray(
            A_log[lo:lo + DSL].reshape(2, 128, NST).transpose(1, 0, 2))
        dv = np.ascontiguousarray(Dv[lo:lo + DSL].reshape(2, 128).T)
        maps.append({
            "u_i": u_sl, "dl_i": dl_sl, "B_i": B_full, "C_i": C_full,
            "alog": al, "dvec": dv, "sel": sel, "ident": ident, "cst": cstv,
        })
    return maps


TCA = 512
NCA = S // TCA
TCB = 256
NCB = S // TCB
NMT = D // 128
NKT = DI // 128


def build_L3(debug=False):
    nc = bass.Bass("TRN2", target_bir_lowering=False, debug=False,
                   detect_race_conditions=False)
    BF16 = mybir.dt.bfloat16

    y_i = nc.dram_tensor("y_i", [DI, S], F32, kind="ExternalInput").ap()
    z_i = nc.dram_tensor("z_i", [DI, S], F32, kind="ExternalInput").ap()
    xr = nc.dram_tensor("xr", [128, NMT, S], F32, kind="ExternalInput").ap()
    outw = nc.dram_tensor("outw", [128, NKT, D], F32, kind="ExternalInput").ap()
    wenc = nc.dram_tensor("wenc", [128, NMT, HS], F32, kind="ExternalInput").ap()
    wdec = nc.dram_tensor("wdec", [128, NKT, D], F32, kind="ExternalInput").ap()
    ws2c = nc.dram_tensor("ws2c", [128, NKT, CM], F32, kind="ExternalInput").ap()
    wr_ = nc.dram_tensor("wr_", [64, 8], F32, kind="ExternalInput").ap()
    w1 = nc.dram_tensor("w1", [128, NE // 2, NE, 128], F32, kind="ExternalInput").ap()
    w2 = nc.dram_tensor("w2", [128, 8, NE, CM], F32, kind="ExternalInput").ap()
    wc2s = nc.dram_tensor("wc2s", [64, HS], F32, kind="ExternalInput").ap()
    benc = nc.dram_tensor("benc", [128, NKT], F32, kind="ExternalInput").ap()
    bs2c = nc.dram_tensor("bs2c", [64, 1], F32, kind="ExternalInput").ap()
    br_ = nc.dram_tensor("br_", [8, 1], F32, kind="ExternalInput").ap()
    b1 = nc.dram_tensor("b1", [128, NE, 8], F32, kind="ExternalInput").ap()
    b2 = nc.dram_tensor("b2", [64, NE], F32, kind="ExternalInput").ap()
    bc2s = nc.dram_tensor("bc2s", [128, NKT], F32, kind="ExternalInput").ap()
    bdec = nc.dram_tensor("bdec", [128, NMT], F32, kind="ExternalInput").ap()
    lng = nc.dram_tensor("lng", [128, NMT], F32, kind="ExternalInput").ap()
    lnb = nc.dram_tensor("lnb", [128, NMT], F32, kind="ExternalInput").ap()
    sel8 = nc.dram_tensor("sel8", [8, NE, CM], F32, kind="ExternalInput").ap()
    ident = nc.dram_tensor("ident", [128, 128], F32, kind="ExternalInput").ap()
    ones128 = nc.dram_tensor("ones128", [128, 2], F32, kind="ExternalInput").ap()
    ones1 = nc.dram_tensor("ones1", [1, 128], F32, kind="ExternalInput").ap()
    cst = nc.dram_tensor("cst", [128, 8], F32, kind="ExternalInput").ap()
    xn2_d = nc.dram_tensor("xn2_d", [D, S], F32,
                           kind=("ExternalOutput" if debug else "Internal"))
    if debug:
        dbg_gate = nc.dram_tensor("dbg_gate", [8, S], F32, kind="ExternalOutput").ap()
        dbg_eo = nc.dram_tensor("dbg_eo", [64, S], F32, kind="ExternalOutput").ap()
        dbg_ffn = nc.dram_tensor("dbg_ffn", [D, S], F32, kind="ExternalOutput").ap()
        dbg_x2 = nc.dram_tensor("dbg_x2", [D, S], F32, kind="ExternalOutput").ap()
        dbg_mub = nc.dram_tensor("dbg_mub", [128, S], F32, kind="ExternalOutput").ap()
        dbg_rsb = nc.dram_tensor("dbg_rsb", [128, S], F32, kind="ExternalOutput").ap()

    out_o = nc.dram_tensor("out_o", [D, S], F32, kind="ExternalOutput").ap()

    with ExitStack() as stk:
        e = stk.enter_context
        s_ws2c = e(nc.sbuf_tensor("s_ws2c", [128, NKT, CM], F32R))
        s_wr = e(nc.sbuf_tensor("s_wr", [64, 8], F32R))
        s_w1 = e(nc.sbuf_tensor("s_w1", [128, NE // 2, NE, 128], F32R))
        s_w2 = e(nc.sbuf_tensor("s_w2", [128, 8, NE, CM], F32R))
        s_wc2s = e(nc.sbuf_tensor("s_wc2s", [64, HS], F32R))
        s_benc = e(nc.sbuf_tensor("s_benc", [128, NKT], F32))
        s_bs2c = e(nc.sbuf_tensor("s_bs2c", [64, 1], F32))
        s_br = e(nc.sbuf_tensor("s_br", [8, 1], F32))
        s_b1 = e(nc.sbuf_tensor("s_b1", [128, NE, 8], F32))
        s_b2 = e(nc.sbuf_tensor("s_b2", [64, NE], F32))
        s_bc2s = e(nc.sbuf_tensor("s_bc2s", [128, NKT], F32))
        s_bdec = e(nc.sbuf_tensor("s_bdec", [128, NMT], F32))
        s_lng = e(nc.sbuf_tensor("s_lng", [128, NMT], F32))
        s_lnb = e(nc.sbuf_tensor("s_lnb", [128, NMT], F32))
        s_sel8 = e(nc.sbuf_tensor("s_sel8", [8, NE, CM], F32R))
        s_ident = e(nc.sbuf_tensor("s_ident", [128, 128], F32))
        s_o128 = e(nc.sbuf_tensor("s_o128", [128, 2], F32R))
        s_o1 = e(nc.sbuf_tensor("s_o1", [1, 128], F32R))
        s_cst = e(nc.sbuf_tensor("s_cst", [128, 8], F32))
        s_sq = e(nc.sbuf_tensor("s_sq3", [128, 2, TCA], F32R))
        s_mu = e(nc.sbuf_tensor("s_mu", [1, TCA], F32R))
        s_mu2 = e(nc.sbuf_tensor("s_mu2", [1, TCA], F32))
        s_var = e(nc.sbuf_tensor("s_var", [1, TCA], F32R))
        s_mub = e(nc.sbuf_tensor("s_mub", [128, TCA], F32))
        s_fnc = e(nc.sbuf_tensor("s_fnc", [1, 2], F32))
        s_rsb = e(nc.sbuf_tensor("s_rsb", [128, TCA], F32))
        p_mm = e(nc.psum_tensor("p_mm3", [128, 2, TCA], F32))
        p_cont = e(nc.psum_tensor("p_cont", [128, TCB], F32))
        p_ge = e(nc.psum_tensor("p_ge", [64, TCB], F32))
        p_ye = e(nc.psum_tensor("p_ye", [64, TCB], F32))
        p_s = e(nc.psum_tensor("p_s", [2, 2, TCA], F32))
        p_msc = e(nc.psum_tensor("p_msc", [128, 512], F32))
        ld = e(nc.semaphore("ld"))
        lwA = e(nc.semaphore("lwA"))
        lwB = e(nc.semaphore("lwB"))
        lyz = [e(nc.semaphore(f"lyz{i}")) for i in range(2)]
        lxt = [e(nc.semaphore(f"lxt{i}")) for i in range(2)]
        lxn = [e(nc.semaphore(f"lxn{i}")) for i in range(2)]
        oxn = [e(nc.semaphore(f"oxn{i}")) for i in range(2)]
        oo = [e(nc.semaphore(f"oo{i}")) for i in range(2)]
        tq = e(nc.semaphore("tq"))
        vq = e(nc.semaphore("vq"))
        aq = e(nc.semaphore("aq"))
        blk = e(nc.Block())

        P = Prog(nc)
        E = {}

        ev_c = None
        for dst, src_ in [(s_ws2c[:], ws2c.bitcast(F32R)), (s_wr[:], wr_.bitcast(F32R)),
                          (s_w1[:], w1.bitcast(F32R)), (s_w2[:], w2.bitcast(F32R)),
                          (s_wc2s[:], wc2s.bitcast(F32R)), (s_benc[:], benc),
                          (s_bs2c[:], bs2c), (s_br[:], br_), (s_b1[:], b1),
                          (s_b2[:], b2), (s_bc2s[:], bc2s), (s_bdec[:], bdec),
                          (s_lng[:], lng), (s_lnb[:], lnb),
                          (s_sel8[:], sel8.bitcast(F32R)), (s_ident[:], ident),
                          (s_o128[:], ones128.bitcast(F32R)),
                          (s_o1[:], ones1.bitcast(F32R)), (s_cst[:], cst)]:
            ev_c = P.dma("sync", dst, src_, ld)
        P.base_waits = [ev_c]

        lnstate = {"prev": None}

        def vfence():
            return P.op("vector",
                        lambda eng: eng.tensor_copy(s_fnc[0:1, 0:1],
                                                    s_fnc[0:1, 1:2]),
                        inc=vq)

        def layer_norm(tag, src_fn, src_ev, ntiles, w_, affine, dst_fn,
                       dst_war_fn, post_fn=None):
            prev = lnstate["prev"]
            for i in range(ntiles):
                E[(tag, "sq", i)] = P.op(
                    "scalar",
                    lambda eng, i=i: eng.activation(
                        s_sq[:, i % 2, 0:w_], src_fn(i).bitcast(F32),
                        AF.Square),
                    waits=[src_ev(i), E.get((tag, "st2", i - 2)),
                           (E.get((prev, "app2", i)) if prev else None)],
                    inc=aq)
                E[(tag, "st1", i)] = P.op(
                    "tensor",
                    lambda eng, i=i: eng.matmul(
                        p_s[:, 0, 0:w_], s_o128[:], src_fn(i),
                        start=(i == 0), stop=(i == ntiles - 1)),
                    waits=[src_ev(i),
                           (E.get((prev, "mu")) if prev and i == 0 else None)],
                    inc=tq)
                E[(tag, "st2", i)] = P.op(
                    "tensor",
                    lambda eng, i=i: eng.matmul(
                        p_s[:, 1, 0:w_], s_o128[:], s_sq[:, i % 2, 0:w_],
                        start=(i == 0), stop=(i == ntiles - 1)),
                    waits=[E[(tag, "sq", i)],
                           (E.get((prev, "var")) if prev and i == 0 else None)],
                    inc=tq)
            E[(tag, "mu")] = P.op(
                "vector",
                lambda eng: eng.tensor_scalar(
                    s_mu[0:1, 0:w_], p_s[0:1, 0, 0:w_], s_cst[0:1, 1:2],
                    None, ALU.mult),
                waits=[E[(tag, "st1", ntiles - 1)], ev_c,
                       (E.get((prev, "bc1")) if prev else None)],
                inc=vq)
            E[(tag, "mu")] = vfence()
            E[(tag, "mu2")] = P.op(
                "scalar",
                lambda eng: eng.activation(s_mu2[0:1, 0:w_],
                                           s_mu[0:1, 0:w_].bitcast(F32),
                                           AF.Square),
                waits=[E[(tag, "mu")],
                       (E.get((prev, "var")) if prev else None)],
                inc=aq)
            E[(tag, "var")] = P.op(
                "vector",
                lambda eng: eng.scalar_tensor_tensor(
                    s_var[0:1, 0:w_], p_s[0:1, 1, 0:w_],
                    s_cst[0:1, 1:2], s_mu2[0:1, 0:w_],
                    ALU.mult, ALU.subtract),
                waits=[E[(tag, "st2", ntiles - 1)], E[(tag, "mu2")],
                       (E.get((prev, "bc2")) if prev else None)],
                inc=vq)
            E[(tag, "var")] = vfence()
            E[(tag, "rstd")] = P.op(
                "scalar",
                lambda eng: eng.activation(s_var[0:1, 0:w_],
                                           s_var[0:1, 0:w_].bitcast(F32),
                                           AF.Ln, bias=s_cst[0:1, 0:1]),
                waits=[E[(tag, "var")]], inc=aq)
            E[(tag, "rstd2")] = P.op(
                "scalar",
                lambda eng: eng.activation(s_var[0:1, 0:w_],
                                           s_var[0:1, 0:w_].bitcast(F32),
                                           AF.Exp, scale=s_cst[0:1, 4:5]),
                inc=aq)
            E[(tag, "bc1")] = P.op(
                "tensor",
                lambda eng: eng.matmul(p_msc[:, 0:w_], s_o1[:],
                                       s_mu[0:1, 0:w_], start=True, stop=True),
                waits=[E[(tag, "mu")],
                       (E.get((prev, "bcev1")) if prev else None),
                       (E.get(("msc_free", tag[1])) if tag[0] == "lnF"
                        else None)],
                inc=tq)
            E[(tag, "bcev1")] = P.op(
                "scalar",
                lambda eng: eng.activation(s_mub[:, 0:w_], p_msc[:, 0:w_],
                                           AF.Copy),
                waits=[E[(tag, "bc1")],
                       (E.get((prev, "app1_last")) if prev else None)],
                inc=aq)
            E[(tag, "bc2")] = P.op(
                "tensor",
                lambda eng: eng.matmul(p_msc[:, 0:w_], s_o1[:],
                                       s_var[0:1, 0:w_], start=True, stop=True),
                waits=[E[(tag, "rstd2")], E[(tag, "bcev1")]], inc=tq)
            E[(tag, "bcev2")] = P.op(
                "scalar",
                lambda eng: eng.activation(s_rsb[:, 0:w_], p_msc[:, 0:w_],
                                           AF.Copy),
                waits=[E[(tag, "bc2")],
                       (E.get((prev, "app2_last")) if prev else None)],
                inc=aq)
            for i in range(ntiles):
                E[(tag, "app1", i)] = P.op(
                    "vector",
                    lambda eng, i=i: eng.tensor_sub(
                        s_sq[:, i % 2, 0:w_],
                        src_fn(i).bitcast(F32), s_mub[:, 0:w_]),
                    waits=[E[(tag, "bcev1")], E[(tag, "bcev2")],
                           E[(tag, "st2", i)]], inc=vq)
                ev = P.op(
                    "vector",
                    lambda eng, i=i: eng.tensor_mul(
                        dst_fn(i), s_sq[:, i % 2, 0:w_].bitcast(F32),
                        s_rsb[:, 0:w_]),
                    waits=[dst_war_fn(i)], inc=vq)
                if affine is not None:
                    gcol, bcol = affine
                    ev = P.op(
                        "vector",
                        lambda eng, i=i: eng.tensor_scalar(
                            dst_fn(i), dst_fn(i).bitcast(F32),
                            gcol(i), bcol(i), ALU.mult, ALU.add),
                        waits=[ev], inc=vq)
                E[(tag, "app2", i)] = ev
                if post_fn is not None:
                    post_fn(i, ev)
            E[(tag, "app1_last")] = E[(tag, "app1", ntiles - 1)]
            E[(tag, "app2_last")] = E[(tag, "app2", ntiles - 1)]
            lnstate["prev"] = tag
            return tag

        with ExitStack() as stkA:
            eA = stkA.enter_context
            s_outw = eA(nc.sbuf_tensor("s_outw", [128, NKT, D], BF16))
            s_yz = eA(nc.sbuf_tensor("s_yz", [128, 2, 2, TCA], F32))
            s_y2 = eA(nc.sbuf_tensor("s_y2", [128, NKT, TCA], BF16))
            s_xt = eA(nc.sbuf_tensor("s_xt", [128, 2, TCA], F32))
            s_x2 = eA(nc.sbuf_tensor("s_x2", [128, NMT, TCA], F32R))
            s_xn2c = eA(nc.sbuf_tensor("s_xn2c", [128, 2, TCA], F32))

            ev_wA = P.dma("gpsimd", s_outw[:], outw, lwA)

            for c in range(NCA):
                for kt in range(NKT):
                    gyk = c * NKT + kt
                    r = gyk % 2
                    war_yz = (E.get(("y2", (gyk - 2) // NKT, (gyk - 2) % NKT))
                              if gyk >= 2 else None)
                    E[("ly", c, kt)] = P.dma(
                        "sync", s_yz[:, 0, r, :],
                        y_i[kt * 128:(kt + 1) * 128, c * TCA:(c + 1) * TCA],
                        lyz[r], waits=[war_yz])
                    E[("lz", c, kt)] = P.dma(
                        "sync", s_yz[:, 1, r, :],
                        z_i[kt * 128:(kt + 1) * 128, c * TCA:(c + 1) * TCA],
                        lyz[r], waits=[])
                    E[("sz", c, kt)] = P.op(
                        "scalar",
                        lambda eng, r=r: eng.activation(
                            s_yz[:, 1, r, :], s_yz[:, 1, r, :], AF.Silu),
                        waits=[E[("lz", c, kt)]], inc=aq)
                    E[("y2", c, kt)] = P.op(
                        "vector",
                        lambda eng, r=r, kt=kt: eng.tensor_mul(
                            s_y2[:, kt, :], s_yz[:, 0, r, :],
                            s_yz[:, 1, r, :]),
                        waits=[E[("sz", c, kt)], E[("lz", c, kt)],
                               E.get(("mmA", c - 1, NMT - 1, kt))],
                        inc=vq)
                for mt in range(NMT):
                    gx = c * NMT + mt
                    E[("lxt", c, mt)] = P.dma(
                        "sync", s_xt[:, gx % 2, :],
                        xr[:, mt, c * TCA:(c + 1) * TCA], lxt[gx % 2],
                        waits=[E.get(("x2ev", (gx - 2) // NMT, (gx - 2) % NMT))
                               if gx >= 2 else None])
                    for kt in range(NKT):
                        E[("mmA", c, mt, kt)] = P.op(
                            "tensor",
                            lambda eng, mt=mt, kt=kt: eng.matmul(
                                p_mm[:, mt % 2, :],
                                s_outw[:, kt, mt * 128:(mt + 1) * 128],
                                s_y2[:, kt, :],
                                start=(kt == 0), stop=(kt == NKT - 1)),
                            waits=[E[("y2", c, kt)],
                                   ev_wA if c == 0 and mt == 0 else None,
                                   ((E.get(("x2ev", c, mt - 2)) if mt >= 2
                                     else E.get(("x2ev", c - 1,
                                                 NMT - 2 + mt)))
                                    if kt == 0 else None)],
                            inc=tq)
                    P.op("vector",
                         lambda eng, mt=mt, gx=gx: eng.tensor_add(
                             s_x2[:, mt, :], p_mm[:, mt % 2, :],
                             s_xt[:, gx % 2, :]),
                         waits=[E[("mmA", c, mt, NKT - 1)], E[("lxt", c, mt)],
                                E.get((("lnA", c - 1), "app2", mt))],
                         inc=vq)
                    E[("x2ev", c, mt)] = vfence()
                if debug == "full":
                    for mt in range(NMT):
                        P.dma("sync", dbg_x2[mt * 128:(mt + 1) * 128,
                                             c * TCA:(c + 1) * TCA],
                              s_x2[:, mt, :].bitcast(F32), oo[1],
                              waits=[E[("x2ev", c, mt)]])
                tg = ("lnA", c)

                def xn_post(i, ev, c=c):
                    gxi = c * NMT + i
                    E[("xndma", gxi)] = P.dma(
                        "sync", xn2_d.ap()[i * 128:(i + 1) * 128,
                                           c * TCA:(c + 1) * TCA],
                        s_xn2c[:, gxi % 2, :], oxn[gxi % 2],
                        waits=[ev])

                layer_norm(
                    tg, lambda i: s_x2[:, i, :],
                    lambda i, c=c: E[("x2ev", c, i)],
                    NMT, TCA, None,
                    lambda i, c=c: s_xn2c[:, (c * NMT + i) % 2, :],
                    lambda i, c=c: E.get(("xndma", c * NMT + i - 2)),
                    post_fn=xn_post)
                if debug == "full":
                    P.dma("sync", dbg_mub[:, c * TCA:(c + 1) * TCA],
                          s_mub[:, :], oo[1], waits=[E[(tg, "bcev1")],
                                                     E[(tg, "app1_last")]])
                    P.dma("sync", dbg_rsb[:, c * TCA:(c + 1) * TCA],
                          s_rsb[:, :], oo[1], waits=[E[(tg, "bcev2")],
                                                     E[(tg, "app2_last")]])
            ev_a_done = E[("xndma", NCA * NMT - 1)]

        with ExitStack() as stkB:
            eB = stkB.enter_context
            s_wenc = eB(nc.sbuf_tensor("s_wenc", [128, NMT, HS], BF16))
            s_wdec = eB(nc.sbuf_tensor("s_wdec", [128, NKT, D], BF16))
            s_xnb = eB(nc.sbuf_tensor("s_xnb", [128, NMT, TCB], BF16))
            s_xnf = eB(nc.sbuf_tensor("s_xnf", [128, NMT, TCB], F32))
            s_spk = eB(nc.sbuf_tensor("s_spk", [128, 2, TCB], F32R))
            s_cont = eB(nc.sbuf_tensor("s_cont", [128, TCB], F32R))
            s_lgt = eB(nc.sbuf_tensor("s_lgt", [8, TCB], F32))
            s_gt = eB(nc.sbuf_tensor("s_gt", [128, 2, 8], F32))
            s_gtmp = eB(nc.sbuf_tensor("s_gtmp", [128, 2, 8], F32))
            s_gm = eB(nc.sbuf_tensor("s_gm", [128, 2, 4], F32))
            s_grow = eB(nc.sbuf_tensor("s_grow", [8, TCB], F32R))
            s_he = eB(nc.sbuf_tensor("s_he", [128, 8, TCB], F32R))
            s_ye = eB(nc.sbuf_tensor("s_ye", [64, TCB], F32))
            s_eo = eB(nc.sbuf_tensor("s_eo", [64, 2, TCB], F32R))
            s_sp2 = eB(nc.sbuf_tensor("s_sp2", [128, NKT, TCB], BF16))
            s_dec = eB(nc.sbuf_tensor("s_dec", [128, NMT, TCB], F32R))
            s_out = eB(nc.sbuf_tensor("s_out", [128, NMT, TCB], F32))

            P.dma("gpsimd", s_wenc[:], wenc, lwB, waits=[ev_a_done, ev_wA])
            ev_wB = P.dma("gpsimd", s_wdec[:], wdec, lwB)

            for c in range(NCB):
                ca = (c * TCB) // TCA
                E[("lxnb", c)] = P.dma(
                    "gpsimd", s_xnb[:, :, :],
                    xn2_d.ap().rearrange("(m p) t -> p m t", p=128)[
                        :, :, c * TCB:(c + 1) * TCB],
                    lxn[0],
                    waits=[ev_a_done,
                           E.get(("mmB", c - 1, NKT - 1, NMT - 1))])
                E[("lxnf", c)] = P.dma(
                    "sync", s_xnf[:, :, :],
                    xn2_d.ap().rearrange("(m p) t -> p m t", p=128)[
                        :, :, c * TCB:(c + 1) * TCB],
                    lxn[1],
                    waits=[ev_a_done, E.get(("x4", c - 1, NMT - 1))])
                for kt in range(NKT):
                    for mt in range(NMT):
                        E[("mmB", c, kt, mt)] = P.op(
                            "tensor",
                            lambda eng, kt=kt, mt=mt: eng.matmul(
                                p_mm[:, kt % 2, 0:TCB],
                                s_wenc[:, mt, kt * 128:(kt + 1) * 128],
                                s_xnb[:, mt, :],
                                start=(mt == 0), stop=(mt == NMT - 1)),
                            waits=[E[("lxnb", c)],
                                   ev_wB if c == 0 and kt == 0 else None,
                                   ((E.get(("spk", c, kt - 2)) if kt >= 2
                                     else E.get(("decev", c - 1,
                                                 NMT - 2 + kt)))
                                    if mt == 0 else None)],
                            inc=tq)
                    E[("spk", c, kt)] = P.op(
                        "scalar",
                        lambda eng, kt=kt: eng.activation(
                            s_spk[:, kt % 2, :], p_mm[:, kt % 2, 0:TCB],
                            AF.Sigmoid, bias=s_benc[:, kt:kt + 1]),
                        waits=[E[("mmB", c, kt, NMT - 1)],
                               E.get(("s2c", c, kt - 2)),
                               E.get(("s2c", c - 1, NKT - 2 + (kt % 2)))
                               if kt < 2 else None],
                        inc=aq)
                    E[("s2c", c, kt)] = P.op(
                        "tensor",
                        lambda eng, kt=kt: eng.matmul(
                            p_cont[0:64, :], s_ws2c[:, kt, :],
                            s_spk[:, kt % 2, :],
                            start=(kt == 0), stop=(kt == NKT - 1)),
                        waits=[E[("spk", c, kt)],
                               E.get(("contev", c - 1)) if kt == 0 else None],
                        inc=tq)
                ev1 = P.op(
                    "vector",
                    lambda eng: eng.tensor_scalar(
                        s_cont[0:64, :], p_cont[0:64, :], s_bs2c[:, 0:1],
                        None, ALU.add),
                    waits=[E[("s2c", c, NKT - 1)],
                           E.get(("w1m", c - 1, NE - 1, NE - 1))],
                    inc=vq)
                P.op("vector",
                     lambda eng: eng.tensor_scalar(
                         s_cont[64:128, :], p_cont[0:64, :], s_bs2c[:, 0:1],
                         None, ALU.add),
                     inc=vq)
                E[("contev", c)] = vfence()
                E[("wr", c)] = P.op(
                    "tensor",
                    lambda eng: eng.matmul(p_msc[0:8, 0:TCB], s_wr[:],
                                           s_cont[0:64, :],
                                           start=True, stop=True),
                    waits=[E[("contev", c)], E.get(("lgtev", c - 1)),
                           E.get((("lnO", c - 1), "bcev2"))], inc=tq)
                E[("lgtev", c)] = P.op(
                    "vector",
                    lambda eng: eng.tensor_scalar(
                        s_lgt[:, :], p_msc[0:8, 0:TCB], s_br[:, 0:1],
                        None, ALU.add),
                    waits=[E[("wr", c)], E.get(("tp", c - 1, 1))], inc=vq)
                E[("lgtev", c)] = vfence()
                for i in range(2):
                    E[("tp", c, i)] = P.op(
                        "tensor",
                        lambda eng, i=i: eng.transpose(
                            p_msc[:, 256 + i * 8:256 + (i + 1) * 8],
                            s_lgt[:, i * 128:(i + 1) * 128],
                            s_ident[0:8, 0:8]),
                        waits=[E[("lgtev", c)], E.get(("gtev", c - 1))], inc=tq)
                E[("gtev", c)] = P.op(
                    "vector",
                    lambda eng: eng.tensor_copy(
                        s_gt[:, :, :].rearrange("p a b -> p (a b)"),
                        p_msc[:, 256:272]),
                    waits=[E[("tp", c, 0)], E[("tp", c, 1)],
                           E.get(("gdone", c - 1))], inc=vq)

                def gop(fn, waits=(), engname="vector"):
                    return P.op(engname, fn, waits=waits,
                                inc=(vq if engname == "vector" else aq))

                ev0 = E[("gtev", c)]
                evm = gop(lambda eng: eng.tensor_reduce(
                    s_gm[:, :, 0:1], s_gt[:, :, :], mybir.AxisListType.X,
                    ALU.max, negate=True), waits=[ev0])
                evm = vfence()
                eve1 = gop(lambda eng: eng.activation(
                    s_gt[:, 0, :], s_gt[:, 0, :], AF.Exp,
                    bias=s_gm[:, 0, 0:1]), engname="scalar",
                    waits=[(evm[0], evm[1], True)])
                eve2 = gop(lambda eng: eng.activation(
                    s_gt[:, 1, :], s_gt[:, 1, :], AF.Exp,
                    bias=s_gm[:, 1, 0:1]), engname="scalar")
                evm1 = gop(lambda eng: eng.tensor_reduce(
                    s_gm[:, :, 0:1], s_gt[:, :, :], mybir.AxisListType.X,
                    ALU.max), waits=[eve1, eve2])
                ev = (evm1[0], evm1[1], True)
                for i in range(2):
                    ev = gop(lambda eng, i=i: eng.tensor_scalar(
                        s_gtmp[:, i, :], s_gt[:, i, :], s_gm[:, i, 0:1], None,
                        ALU.is_equal), waits=[ev])
                    ev = gop(lambda eng, i=i: eng.scalar_tensor_tensor(
                        s_gtmp[:, i, :], s_gtmp[:, i, :], s_cst[:, 5:6],
                        s_gt[:, i, :], ALU.mult, ALU.add))
                evm2 = gop(lambda eng: eng.tensor_reduce(
                    s_gm[:, :, 3:4], s_gtmp[:, :, :], mybir.AxisListType.X,
                    ALU.max), waits=[ev])
                ev = (evm2[0], evm2[1], True)
                for i in range(2):
                    ev = gop(lambda eng, i=i: eng.tensor_scalar(
                        s_gtmp[:, i, :], s_gt[:, i, :], s_gm[:, i, 0:1], None,
                        ALU.is_equal), waits=[ev])
                    ev = gop(lambda eng, i=i: eng.tensor_scalar(
                        s_gtmp[:, i, :], s_gtmp[:, i, :], s_gm[:, i, 0:1],
                        None, ALU.mult))
                    ev = gop(lambda eng, i=i: eng.tensor_scalar(
                        s_gt[:, i, :], s_gt[:, i, :], s_gm[:, i, 3:4], None,
                        ALU.is_equal))
                    ev = gop(lambda eng, i=i: eng.scalar_tensor_tensor(
                        s_gt[:, i, :], s_gt[:, i, :], s_gm[:, i, 3:4],
                        s_gtmp[:, i, :], ALU.mult, ALU.add))
                evdn = gop(lambda eng: eng.tensor_add(
                    s_gm[:, :, 0:1], s_gm[:, :, 0:1], s_gm[:, :, 3:4]),
                    waits=[ev])
                evrg = gop(lambda eng: eng.reciprocal(
                    s_gm[:, :, 2:3], s_gm[:, :, 0:1]),
                    waits=[(evdn[0], evdn[1], True)])
                ev = (evrg[0], evrg[1], True)
                for i in range(2):
                    ev = gop(lambda eng, i=i: eng.tensor_scalar(
                        s_gt[:, i, :], s_gt[:, i, :], s_gm[:, i, 2:3], None,
                        ALU.mult), waits=[ev])
                ev_gate = vfence()
                E[("gdone", c)] = ev_gate
                for i in range(2):
                    E[("tpb", c, i)] = P.op(
                        "tensor",
                        lambda eng, i=i: eng.transpose(
                            p_msc[0:8, 256 + i * 128:256 + (i + 1) * 128],
                            s_gt[:, i, :], s_ident[:]),
                        waits=[ev_gate, E.get(("growev", c - 1))], inc=tq)
                E[("growev", c)] = P.op(
                    "scalar",
                    lambda eng: eng.activation(s_grow[:, :],
                                               p_msc[0:8, 256:512], AF.Copy),
                    waits=[E[("tpb", c, 0)], E[("tpb", c, 1)]], inc=aq)
                if debug == "full":
                    P.dma("sync", dbg_gate[:, c * TCB:(c + 1) * TCB],
                          s_grow[:, :].bitcast(F32), oo[0],
                          waits=[E[("growev", c)]])
                for ex in range(NE):
                    E[("ge", c, ex)] = P.op(
                        "tensor",
                        lambda eng, ex=ex: eng.matmul(
                            p_ge[:, :], s_sel8[:, ex, :], s_grow[:, :],
                            start=True, stop=True),
                        waits=[E[("growev", c)],
                               (E.get(("eotmp", c, ex - 1)) if ex >= 1
                                else E.get(("eotmp", c - 1, NE - 1)))],
                        inc=tq)
                    base = 64 * (ex % 2)
                    for mt in range(NE):
                        E[("w1m", c, ex, mt)] = P.op(
                            "tensor",
                            lambda eng, ex=ex, mt=mt, base=base: eng.matmul(
                                p_mm[:, mt % 2, 0:TCB],
                                s_w1[base:base + 64, ex // 2, mt, :],
                                s_cont[base:base + 64, :],
                                start=True, stop=True),
                            waits=[E[("contev", c)],
                                   (E.get(("he", c, ex, mt - 2)) if mt >= 2
                                    else (E.get(("he", c, ex - 1, NE - 2 + mt))
                                          if ex >= 1
                                          else E.get(("he", c - 1, NE - 1,
                                                      NE - 2 + mt))))],
                            inc=tq)
                        E[("he", c, ex, mt)] = P.op(
                            "scalar",
                            lambda eng, ex=ex, mt=mt: eng.activation(
                                s_he[:, mt, :], p_mm[:, mt % 2, 0:TCB],
                                AF.Relu, bias=s_b1[:, ex, mt:mt + 1]),
                            waits=[E[("w1m", c, ex, mt)],
                                   (E.get(("w2m", c, ex - 1, NE - 1))
                                    if mt < 2 and ex >= 1 else None),
                                   (E.get(("w2m", c - 1, NE - 1, NE - 1))
                                    if mt < 2 and ex == 0 else None)],
                            inc=aq)
                    for kt in range(NE):
                        E[("w2m", c, ex, kt)] = P.op(
                            "tensor",
                            lambda eng, ex=ex, kt=kt: eng.matmul(
                                p_ye[:, :], s_w2[:, kt, ex, :],
                                s_he[:, kt, :],
                                start=(kt == 0), stop=(kt == NE - 1)),
                            waits=[E[("he", c, ex, kt)],
                                   ((E.get(("yeev", c, ex - 1)) if ex >= 1
                                     else E.get(("yeev", c - 1, NE - 1)))
                                    if kt == 0 else None)],
                            inc=tq)
                    E[("yeev", c, ex)] = P.op(
                        "vector",
                        lambda eng, ex=ex: eng.tensor_scalar(
                            s_ye[:, :], p_ye[:, :], s_b2[:, ex:ex + 1],
                            None, ALU.add),
                        waits=[E[("w2m", c, ex, NE - 1)],
                               (E.get(("eotmp", c, ex - 1)) if ex >= 1
                                else E.get(("eotmp", c - 1, NE - 1)))],
                        inc=vq)
                    E[("eotmp", c, ex)] = P.op(
                        "vector",
                        lambda eng, ex=ex: eng.tensor_mul(
                            s_eo[:, 1, :], s_ye[:, :], p_ge[:, :]),
                        waits=[E[("yeev", c, ex)], E[("ge", c, ex)],
                               E.get(("eoev", c, ex - 1))], inc=vq)
                    if ex == 0:
                        E[("eoev", c, ex)] = P.op(
                            "vector",
                            lambda eng: eng.tensor_copy(
                                s_eo[:, 0, :], s_eo[:, 1, :].bitcast(F32)),
                            waits=[E.get(("c2s", c - 1, NKT - 1))], inc=vq)
                    else:
                        E[("eoev", c, ex)] = P.op(
                            "vector",
                            lambda eng: eng.tensor_add(
                                s_eo[:, 0, :], s_eo[:, 0, :].bitcast(F32),
                                s_eo[:, 1, :].bitcast(F32)),
                            inc=vq)
                if debug == "full":
                    P.dma("sync", dbg_eo[:, c * TCB:(c + 1) * TCB],
                          s_eo[:, 0, :].bitcast(F32), oo[0],
                          waits=[E[("eoev", c, NE - 1)]])
                E[("eoev", c, NE - 1)] = vfence()
                for mt in range(NKT):
                    E[("c2s", c, mt)] = P.op(
                        "tensor",
                        lambda eng, mt=mt: eng.matmul(
                            p_mm[:, mt % 2, 0:TCB],
                            s_wc2s[:, mt * 128:(mt + 1) * 128],
                            s_eo[:, 0, :],
                            start=True, stop=True),
                        waits=[E[("eoev", c, NE - 1)],
                               (E.get(("sp2", c, mt - 2)) if mt >= 2
                                else E.get(("decev", c - 1, NMT - 2 + mt)))],
                        inc=tq)
                    E[("sp2", c, mt)] = P.op(
                        "scalar",
                        lambda eng, mt=mt: eng.activation(
                            s_sp2[:, mt, :], p_mm[:, mt % 2, 0:TCB],
                            AF.Sigmoid, bias=s_bc2s[:, mt:mt + 1]),
                        waits=[E[("c2s", c, mt)]], inc=aq)
                for mt in range(NMT):
                    for kt in range(NKT):
                        E[("mmD", c, mt, kt)] = P.op(
                            "tensor",
                            lambda eng, mt=mt, kt=kt: eng.matmul(
                                p_mm[:, mt % 2, 0:TCB],
                                s_wdec[:, kt, mt * 128:(mt + 1) * 128],
                                s_sp2[:, kt, :],
                                start=(kt == 0), stop=(kt == NKT - 1)),
                            waits=[E[("sp2", c, kt)],
                                   ((E.get(("decev", c, mt - 2)) if mt >= 2
                                     else E.get(("sp2", c, NKT - 2 + mt)))
                                    if kt == 0 else None)],
                            inc=tq)
                    E[("decev", c, mt)] = P.op(
                        "vector",
                        lambda eng, mt=mt: eng.tensor_scalar(
                            s_dec[:, mt, :], p_mm[:, mt % 2, 0:TCB],
                            s_bdec[:, mt:mt + 1], None, ALU.add),
                        waits=[E[("mmD", c, mt, NKT - 1)],
                               E.get((("lnO", c - 1), "app1", mt))],
                        inc=vq)
                E[("msc_free", c)] = E[("growev", c)]
                tg2 = ("lnF", c)
                layer_norm(
                    tg2, lambda i: s_dec[:, i, :],
                    lambda i, c=c: E[("decev", c, i)],
                    NMT, TCB,
                    (lambda i: s_lng[:, i:i + 1], lambda i: s_lnb[:, i:i + 1]),
                    lambda i: s_dec[:, i, :],
                    lambda i: None)
                if debug == "full":
                    for mt in range(NMT):
                        P.dma("sync", dbg_ffn[mt * 128:(mt + 1) * 128,
                                              c * TCB:(c + 1) * TCB],
                              s_dec[:, mt, :].bitcast(F32), oo[0],
                              waits=[E[(tg2, "app2", mt)]])
                for mt in range(NMT):
                    E[("x4", c, mt)] = P.op(
                        "vector",
                        lambda eng, mt=mt: eng.tensor_add(
                            s_dec[:, mt, :], s_dec[:, mt, :].bitcast(F32),
                            s_xnf[:, mt, :]),
                        waits=[E[(tg2, "app2", mt)], E[("lxnf", c)],
                               ((oo[0], P.counts.get("oo0", 0))
                                if debug == "full" else None)],
                        inc=vq)
                E[("x4f", c)] = vfence()
                E[("B", c, "dece_war")] = E[("x4f", c)]
                tg3 = ("lnO", c)
                layer_norm(
                    tg3, lambda i: s_dec[:, i, :],
                    lambda i, c=c: E[("x4f", c)],
                    NMT, TCB, None,
                    lambda i: s_out[:, i, :],
                    lambda i, c=c: E.get(("odma", c - 1, NMT - 1)))
                for mt in range(NMT):
                    E[("odma", c, mt)] = P.dma(
                        "sync", out_o[mt * 128:(mt + 1) * 128,
                                      c * TCB:(c + 1) * TCB],
                        s_out[:, mt, :], oo[c % 2],
                        waits=[E[(tg3, "app2", mt)]])

            allsems = [ld, lwA, lwB] + lyz + lxt + lxn + oxn + oo
            P.op("gpsimd", lambda eng: eng.memset(s_cst[:, 7:8], 0.0),
                 waits=[(sm, P.counts[sm.name]) for sm in allsems
                        if sm.name in P.counts] +
                       [(vq, P.counts["vq"]), (aq, P.counts["aq"]),
                        (tq, P.counts["tq"])],
                 inc=ld, inc_by=1)

            P.emit(blk)
    return nc


def _l3_host_inputs(inputs, res1, res2):
    out_proj_w = np.asarray(inputs["L0_attn_out_proj_w"])
    W_enc = np.asarray(inputs["L0_ffn_W_enc"])
    W_s2c = np.asarray(inputs["L0_ffn_W_s2c"])
    Wr = np.asarray(inputs["L0_ffn_Wr"])
    W1 = np.asarray(inputs["L0_ffn_W1"])
    W2 = np.asarray(inputs["L0_ffn_W2"])
    W_c2s = np.asarray(inputs["L0_ffn_W_c2s"])
    W_dec = np.asarray(inputs["L0_ffn_W_dec"])

    outw = np.ascontiguousarray(
        out_proj_w.T.reshape(NKT, 128, D).transpose(1, 0, 2))
    wenc = np.ascontiguousarray(W_enc.reshape(NMT, 128, HS).transpose(1, 0, 2))
    wdec = np.ascontiguousarray(W_dec.reshape(NKT, 128, D).transpose(1, 0, 2))
    ws2c = np.ascontiguousarray(W_s2c.reshape(NKT, 128, CM).transpose(1, 0, 2))
    w1 = np.zeros((128, NE // 2, NE, 128), dtype=np.float32)
    for ex in range(NE):
        base = 64 * (ex % 2)
        w1[base:base + 64, ex // 2] = W1[ex].reshape(CM, NE, 128).transpose(
            0, 1, 2)[:, :, :]
    w2 = np.ascontiguousarray(
        W2.reshape(NE, 8, 128, CM).transpose(2, 1, 0, 3))
    wc2s = np.ascontiguousarray(W_c2s)
    benc = np.ascontiguousarray(
        np.asarray(inputs["L0_ffn_b_enc"]).reshape(NKT, 128).T)
    bs2c = np.asarray(inputs["L0_ffn_b_s2c"]).reshape(CM, 1).astype(np.float32)
    br = np.asarray(inputs["L0_ffn_br"]).reshape(8, 1).astype(np.float32)
    b1 = np.ascontiguousarray(
        np.asarray(inputs["L0_ffn_b1"]).reshape(NE, NE, 128).transpose(2, 0, 1))
    b2 = np.ascontiguousarray(np.asarray(inputs["L0_ffn_b2"]).T)
    bc2s = np.ascontiguousarray(
        np.asarray(inputs["L0_ffn_b_c2s"]).reshape(NKT, 128).T)
    bdec = np.ascontiguousarray(
        np.asarray(inputs["L0_ffn_b_dec"]).reshape(NMT, 128).T)
    lng = np.ascontiguousarray(
        np.asarray(inputs["L0_ffn_ln_g"]).reshape(NMT, 128).T)
    lnb = np.ascontiguousarray(
        np.asarray(inputs["L0_ffn_ln_b"]).reshape(NMT, 128).T)
    sel8 = np.zeros((8, NE, CM), dtype=np.float32)
    for ex in range(NE):
        sel8[ex, ex, :] = 1.0
    ident = np.eye(128, dtype=np.float32)
    ones128 = np.ones((128, 2), dtype=np.float32)
    ones1 = np.ones((1, 128), dtype=np.float32)
    cstv = np.zeros((128, 8), dtype=np.float32)
    cstv[:, 0] = EPS
    cstv[:, 1] = 1.0 / D
    cstv[:, 2] = 1.0
    cstv[:, 3] = -1.0
    cstv[:, 4] = -0.5
    cstv[:, 5] = -1e30

    x = np.ascontiguousarray(
        np.asarray(inputs["L0_attn_x"])[0].T).astype(np.float32, copy=False)
    y_sl = [res2.results[k]["y_o"] for k in range(NCORES)]
    z_full = [res1.results[k]["z_o"][:, HALO:] for k in range(NCORES)]

    maps = []
    for k in range(NCORES):
        y_st = np.ascontiguousarray(
            np.concatenate([y_sl[j][:, k * S:(k + 1) * S]
                            for j in range(NCORES)], axis=0))
        xr_k = np.ascontiguousarray(
            x[:, k * S:(k + 1) * S].reshape(NMT, 128, S).transpose(1, 0, 2))
        maps.append({
            "y_i": y_st, "z_i": np.ascontiguousarray(z_full[k]), "xr": xr_k,
            "outw": outw, "wenc": wenc, "wdec": wdec, "ws2c": ws2c,
            "wr_": Wr.astype(np.float32), "w1": w1, "w2": w2, "wc2s": wc2s,
            "benc": benc, "bs2c": bs2c, "br_": br, "b1": b1, "b2": b2,
            "bc2s": bc2s, "bdec": bdec, "lng": lng, "lnb": lnb,
            "sel8": sel8, "ident": ident, "ones128": ones128, "ones1": ones1,
            "cst": cstv,
        })
    return maps


def kernel(**inputs):
    l1_maps = _l1_host_inputs(inputs)
    nc1 = build_L1()
    res1 = run_bass_kernel_spmd(nc1, l1_maps, core_ids=list(range(NCORES)))
    l2_maps = _l2_host_inputs(inputs, res1)
    nc2 = build_L2()
    res2 = run_bass_kernel_spmd(nc2, l2_maps, core_ids=list(range(NCORES)))
    l3_maps = _l3_host_inputs(inputs, res1, res2)
    nc3 = build_L3()
    res3 = run_bass_kernel_spmd(nc3, l3_maps, core_ids=list(range(NCORES)))
    out = np.concatenate([res3.results[k]["out_o"] for k in range(NCORES)],
                         axis=1)
    return np.ascontiguousarray(out.T).reshape(1, T, D)


if __name__ == "__main__":
    pass


# version 06
# speedup vs baseline: 1.1768x; 1.0023x over previous
import numpy as np
from contextlib import ExitStack

import concourse.bass as bass
import concourse.mybir as mybir
from concourse.bass_utils import run_bass_kernel_spmd

F32 = mybir.dt.float32
F32R = mybir.dt.float32r
AF = mybir.ActivationFunctionType
ALU = mybir.AluOpType

NCORES = 8
T = 16384
S = T // NCORES
D = 1024
DI = 2048
DTR = 64
NST = 16
DSL = DI // NCORES
EPS = 1e-5
HALO = 3
NE = 8
HS = 2048
CM = 64
EH = 1024

SH = S + HALO
CH = [512, 512, 512, 512, 4]
CO = [0, 512, 1024, 1536, 2047]
NG = D // 128
NM = 2 * DI // 128
NDT = DI // 128


class Prog:

    def __init__(self, nc):
        self.nc = nc
        self.ops = {k: [] for k in
                    ("sync", "vector", "scalar", "tensor", "gpsimd")}
        self.counts = {}
        self.base_waits = []

    def op(self, engine, fn, waits=(), inc=None, inc_by=1):
        ev = None
        if inc is not None:
            key = inc.name
            self.counts[key] = self.counts.get(key, 0) + inc_by
            ev = (inc, self.counts[key])
        allw = list(self.base_waits) + [w for w in waits if w is not None]
        self.ops[engine].append((fn, allw,
                                 (inc, inc_by) if inc is not None else None))
        return ev

    def dma(self, engine, out, in_, sem, waits=()):
        return self.op(engine, lambda eng, o=out, i=in_: eng.dma_start(out=o, in_=i),
                       waits=waits, inc=sem, inc_by=16)

    def emit(self, block):
        def mk(name):
            def body(eng):
                hi = {}
                for fn, waits, inc in self.ops[name]:
                    for wt in waits:
                        sem, val = wt[0], wt[1]
                        force = len(wt) > 2 and wt[2]
                        if not force and hi.get(sem.name, -1) >= val:
                            continue
                        eng.wait_ge(sem, val)
                        hi[sem.name] = max(hi.get(sem.name, -1), val)
                    ins = fn(eng)
                    if inc is not None:
                        ins.then_inc(inc[0], inc[1])
            return body
        block.sync(mk("sync"))
        block.vector(mk("vector"))
        block.scalar(mk("scalar"))
        block.tensor(mk("tensor"))
        block.gpsimd(mk("gpsimd"))


def slot(c):
    return 515 * (c % 2)


def build_L1():
    nc = bass.Bass("TRN2", target_bir_lowering=False, debug=False,
                   detect_race_conditions=False)

    xh = nc.dram_tensor("xh", [128, NG, SH], F32, kind="ExternalInput").ap()
    inw = nc.dram_tensor("inw", [NM, 128, NG, 128], F32, kind="ExternalInput").ap()
    normw = nc.dram_tensor("normw", [128, NG], F32, kind="ExternalInput").ap()
    xpw = nc.dram_tensor("xpw", [128, NDT, 128], F32, kind="ExternalInput").ap()
    dtw = nc.dram_tensor("dtw", [128, DI], F32, kind="ExternalInput").ap()
    convw = nc.dram_tensor("convw", [128, NDT, 4], F32, kind="ExternalInput").ap()
    convb = nc.dram_tensor("convb", [128, NDT], F32, kind="ExternalInput").ap()
    dtb = nc.dram_tensor("dtb", [128, NDT], F32, kind="ExternalInput").ap()
    ones128 = nc.dram_tensor("ones128", [128, 2], F32, kind="ExternalInput").ap()
    ones1 = nc.dram_tensor("ones1", [1, 128], F32, kind="ExternalInput").ap()
    cst = nc.dram_tensor("cst", [128, 8], F32, kind="ExternalInput").ap()

    u_o = nc.dram_tensor("u_o", [DI, S], F32, kind="ExternalOutput").ap()
    z_o = nc.dram_tensor("z_o", [DI, SH], F32, kind="ExternalOutput").ap()
    dl_o = nc.dram_tensor("dl_o", [DI, S], F32, kind="ExternalOutput").ap()
    bc_o = nc.dram_tensor("bc_o", [64, S], F32, kind="ExternalOutput").ap()

    with ExitStack() as stk:
        e = stk.enter_context
        s_w = e(nc.sbuf_tensor("s_w", [128, 3, NG, 128], F32R))
        s_x = e(nc.sbuf_tensor("s_x", [128, 2, NG, 512], F32R))
        s_normw = e(nc.sbuf_tensor("s_normw", [128, NG], F32))
        s_xpw = e(nc.sbuf_tensor("s_xpw", [128, NDT, 128], F32R))
        s_dtw = e(nc.sbuf_tensor("s_dtw", [128, DI], F32R))
        s_convw = e(nc.sbuf_tensor("s_convw", [128, NDT, 4], F32))
        s_convb = e(nc.sbuf_tensor("s_convb", [128, NDT], F32))
        s_dtb = e(nc.sbuf_tensor("s_dtb", [128, NDT], F32))
        s_o128 = e(nc.sbuf_tensor("s_o128", [128, 2], F32R))
        s_o1 = e(nc.sbuf_tensor("s_o1", [1, 128], F32R))
        s_cst = e(nc.sbuf_tensor("s_cst", [128, 8], F32))
        s_sq = e(nc.sbuf_tensor("s_sq", [128, 2, 512], F32R))
        s_rstd = e(nc.sbuf_tensor("s_rstd", [1, 512], F32R))
        s_rstdb = e(nc.sbuf_tensor("s_rstdb", [128, 2, 512], F32))
        s_xp = e(nc.sbuf_tensor("s_xp", [128, NDT, 1030], F32))
        s_z = e(nc.sbuf_tensor("s_z", [128, 2, 512], F32))
        s_xc = e(nc.sbuf_tensor("s_xc", [128, 2, 512], F32))
        s_u = e(nc.sbuf_tensor("s_u", [128, 3, 512], F32R))
        s_xdbl = e(nc.sbuf_tensor("s_xdbl", [128, 512], F32R))
        s_sp = e(nc.sbuf_tensor("s_sp", [128, 2, 512], F32))
        s_dl = e(nc.sbuf_tensor("s_dl", [128, 2, 512], F32))
        p_mm = e(nc.psum_tensor("p_mm", [128, 2, 512], F32))
        p_ss = e(nc.psum_tensor("p_ss", [2, 512], F32))
        p_rb = e(nc.psum_tensor("p_rb", [128, 512], F32))
        p_xd = e(nc.psum_tensor("p_xd", [128, 512], F32))
        p_dt = e(nc.psum_tensor("p_dt", [128, 2, 512], F32))
        ld = e(nc.semaphore("ld"))
        lx = [e(nc.semaphore(f"lx{i}")) for i in range(2)]
        lw = [e(nc.semaphore(f"lw{i}")) for i in range(3)]
        oz = [e(nc.semaphore(f"oz{i}")) for i in range(2)]
        ou = [e(nc.semaphore(f"ou{i}")) for i in range(3)]
        od = [e(nc.semaphore(f"od{i}")) for i in range(2)]
        ob = e(nc.semaphore("ob"))
        tq = e(nc.semaphore("tq"))
        vq = e(nc.semaphore("vq"))
        aq = e(nc.semaphore("aq"))
        blk = e(nc.Block())

        P = Prog(nc)

        ev_c = None
        for dst, src in [(s_normw[:], normw), (s_xpw[:], xpw.bitcast(F32R)),
                         (s_dtw[:], dtw.bitcast(F32R)), (s_convw[:], convw),
                         (s_convb[:], convb), (s_dtb[:], dtb),
                         (s_o128[:], ones128.bitcast(F32R)),
                         (s_o1[:], ones1.bitcast(F32R)), (s_cst[:], cst)]:
            ev_c = P.dma("sync", dst, src, ld)
        P.base_waits = [ev_c]

        E = {}

        def prev_ring(kind, i, depth):
            return E.get((kind, i - depth))

        def chunk_section(c):
            w_ = CH[c]
            E[("xload", c)] = P.dma(
                "sync", s_x[:, c % 2, :, 0:w_],
                xh[:, :, CO[c]:CO[c] + w_].bitcast(F32R), lx[c % 2],
                waits=[E.get(("mm", (c - 2) * NM + NM - 1, NG - 1)),
                       E.get(("xscale", c - 2, NG - 1))])

            for g in range(NG):
                gg = c * NG + g
                E[("sq", c, g)] = P.op(
                    "scalar",
                    lambda eng, g=g, w_=w_, c=c: eng.activation(
                        s_sq[:, (c * NG + g) % 2, 0:w_],
                        s_x[:, c % 2, g, 0:w_].bitcast(F32), AF.Square),
                    waits=[E[("xload", c)], ev_c, prev_ring("stats", gg, 2)],
                    inc=aq)
                E[("stats", gg)] = P.op(
                    "tensor",
                    lambda eng, g=g, w_=w_, c=c: eng.matmul(
                        p_ss[:, 0:w_], s_o128[:],
                        s_sq[:, (c * NG + g) % 2, 0:w_],
                        start=(g == 0), stop=(g == NG - 1)),
                    waits=[E[("sq", c, g)],
                           E.get(("rstdL", c - 1)) if g == 0 else None],
                    inc=tq)
                E[("xscale", c, g)] = P.op(
                    "vector",
                    lambda eng, g=g, w_=w_, c=c: eng.tensor_scalar_mul(
                        s_x[:, c % 2, g, 0:w_],
                        s_x[:, c % 2, g, 0:w_].bitcast(F32),
                        s_normw[:, g:g + 1]),
                    waits=[E[("sq", c, g)]], inc=vq)

            E[("rstdL", c)] = P.op(
                "scalar",
                lambda eng, w_=w_: eng.activation(
                    s_rstd[0:1, 0:w_], p_ss[0:1, 0:w_],
                    AF.Ln, bias=s_cst[0:1, 0:1], scale=s_cst[0:1, 1:2]),
                waits=[E[("stats", c * NG + NG - 1)]], inc=aq)
            E[("rstd", c)] = P.op(
                "scalar",
                lambda eng, w_=w_: eng.activation(
                    s_rstd[0:1, 0:w_],
                    s_rstd[0:1, 0:w_].bitcast(F32), AF.Exp,
                    scale=s_cst[0:1, 4:5]),
                inc=aq)
            E[("rb", c)] = P.op(
                "tensor",
                lambda eng, w_=w_: eng.matmul(
                    p_rb[:, 0:w_], s_o1[:], s_rstd[0:1, 0:w_],
                    start=True, stop=True),
                waits=[E[("rstd", c)], E.get(("rstdb", c - 1))], inc=tq)
            E[("rstdb", c)] = P.op(
                "scalar",
                lambda eng, w_=w_, c=c: eng.activation(
                    s_rstdb[:, c % 2, 0:w_], p_rb[:, 0:w_], AF.Copy),
                waits=[E[("rb", c)], E.get(("evac", c - 2, NM - 1))], inc=aq)

            for m in range(NM):
                k = c * NM + m
                E[("wload", k)] = P.dma(
                    "gpsimd", s_w[:, k % 3, :, :], inw[m, :, :, :].bitcast(F32R),
                    lw[k % 3],
                    waits=[E.get(("mm", k - 3, NG - 1))])
                for g in range(NG):
                    bank_war = None
                    if g == 0:
                        bank_war = E.get(("evac", (k - 2) // NM, (k - 2) % NM)) \
                            if k - 2 >= 0 else None
                    E[("mm", k, g)] = P.op(
                        "tensor",
                        lambda eng, m=m, g=g, w_=w_, c=c, k=k: eng.matmul(
                            p_mm[:, m % 2, 0:w_],
                            s_w[:, k % 3, g, :],
                            s_x[:, c % 2, g, 0:w_],
                            start=(g == 0), stop=(g == NG - 1)),
                        waits=[E[("wload", k)], E[("xscale", c, g)], bank_war],
                        inc=tq)
                emm = E[("mm", k, NG - 1)]
                if m < NDT:
                    last = None
                    if c < 4:
                        last = P.op(
                            "vector",
                            lambda eng, m=m, w_=w_, c=c: eng.tensor_mul(
                                s_xp[:, m, slot(c):slot(c) + w_],
                                p_mm[:, m % 2, 0:w_], s_rstdb[:, c % 2, 0:w_]),
                            waits=[emm, E[("rstdb", c)],
                                   E.get(("conv", c - 2, m))],
                            inc=vq)
                    if c >= 1:
                        tl = 1 if c == 4 else 0
                        last = P.op(
                            "vector",
                            lambda eng, m=m, c=c, tl=tl: eng.tensor_mul(
                                s_xp[:, m, slot(c - 1) + 512:slot(c - 1) + 515],
                                p_mm[:, m % 2, tl:tl + 3],
                                s_rstdb[:, c % 2, tl:tl + 3]),
                            waits=[emm, E[("rstdb", c)],
                                   E.get(("conv", c - 3, m))],
                            inc=vq)
                        E[("tail", c, m)] = last
                    E[("evac", c, m)] = last
                else:
                    mz = m - NDT
                    E[("evac", c, m)] = P.op(
                        "vector",
                        lambda eng, m=m, mz=mz, w_=w_, c=c: eng.tensor_mul(
                            s_z[:, mz % 2, 0:w_],
                            p_mm[:, m % 2, 0:w_], s_rstdb[:, c % 2, 0:w_]),
                        waits=[emm, E[("rstdb", c)],
                               prev_ring("zdma", c * NDT + mz, 2)],
                        inc=vq)
                    zlo = 1 if c == 4 else 0
                    E[("zdma", c * NDT + mz)] = P.dma(
                        "sync", z_o[mz * 128:(mz + 1) * 128,
                                    CO[c] + zlo:CO[c] + w_],
                        s_z[:, mz % 2, zlo:w_], oz[mz % 2],
                        waits=[E[("evac", c, m)]])

        def out_section(j):
            for g in range(NDT):
                ug = j * NDT + g
                first = P.op(
                    "vector",
                    lambda eng, g=g, j=j: eng.tensor_scalar_mul(
                        s_xc[:, g % 2, :],
                        s_xp[:, g, slot(j) + 0:slot(j) + 512],
                        s_convw[:, g:g + 1, 0]),
                    waits=[E.get(("evac", j, g)), E.get(("tail", j + 1, g)),
                           prev_ring("silu", ug, 2)],
                    inc=vq)
                for jj in (1, 2, 3):
                    last = P.op(
                        "vector",
                        lambda eng, g=g, j=j, jj=jj: eng.scalar_tensor_tensor(
                            s_xc[:, g % 2, :],
                            s_xp[:, g, slot(j) + jj:slot(j) + jj + 512],
                            s_convw[:, g:g + 1, jj],
                            s_xc[:, g % 2, :],
                            ALU.mult, ALU.add),
                        inc=vq)
                E[("conv", j, g)] = last
                E[("silu", ug)] = P.op(
                    "scalar",
                    lambda eng, g=g, ug=ug: eng.activation(
                        s_u[:, ug % 3, :],
                        s_xc[:, g % 2, :], AF.Silu,
                        bias=s_convb[:, g:g + 1]),
                    waits=[E[("conv", j, g)],
                           prev_ring("xproj", ug, 3), prev_ring("udma", ug, 3)],
                    inc=aq)
                E[("udma", ug)] = P.dma(
                    "sync", u_o[g * 128:(g + 1) * 128, j * 512:(j + 1) * 512],
                    s_u[:, ug % 3, :].bitcast(F32), ou[ug % 3],
                    waits=[E[("silu", ug)]])
                E[("xproj", ug)] = P.op(
                    "tensor",
                    lambda eng, g=g, ug=ug: eng.matmul(
                        p_xd[:, :], s_xpw[:, g, :], s_u[:, ug % 3, :],
                        start=(g == 0), stop=(g == NDT - 1)),
                    waits=[E[("silu", ug)],
                           E.get(("xdev", j - 1)) if g == 0 else None],
                    inc=tq)
            E[("xdev", j)] = P.op(
                "scalar",
                lambda eng: eng.activation(s_xdbl[:, :],
                                           p_xd[:, :], AF.Copy),
                waits=[E[("xproj", j * NDT + NDT - 1)], E.get(("bcdma", j - 1)),
                       E.get(("dt", j - 1, NDT - 1))],
                inc=aq)
            E[("bcdma", j)] = P.dma(
                "sync", bc_o[0:48, j * 512:(j + 1) * 512],
                s_xdbl[0:48, :].bitcast(F32), ob, waits=[E[("xdev", j)]])
            for mt in range(NDT):
                mtg = j * NDT + mt
                E[("dt", j, mt)] = P.op(
                    "tensor",
                    lambda eng, mt=mt: eng.matmul(
                        p_dt[:, mt % 2, :],
                        s_dtw[64:128, mt * 128:(mt + 1) * 128],
                        s_xdbl[64:128, :], start=True, stop=True),
                    waits=[E[("xdev", j)], prev_ring("relu", mtg, 2)],
                    inc=tq)
                P.op("scalar",
                     lambda eng, mt=mt: eng.activation(
                         s_sp[:, mt % 2, :], p_dt[:, mt % 2, :], AF.Abs,
                         bias=s_dtb[:, mt:mt + 1]),
                     waits=[E[("dt", j, mt)], prev_ring("dladd", mtg, 2)],
                     inc=aq)
                P.op("scalar",
                     lambda eng, mt=mt: eng.activation(
                         s_sp[:, mt % 2, :], s_sp[:, mt % 2, :], AF.Exp,
                         scale=s_cst[:, 3:4]),
                     inc=aq)
                P.op("scalar",
                     lambda eng, mt=mt: eng.activation(
                         s_sp[:, mt % 2, :], s_sp[:, mt % 2, :], AF.Ln,
                         bias=s_cst[:, 2:3]),
                     inc=aq)
                E[("relu", mtg)] = P.op(
                    "scalar",
                    lambda eng, mt=mt: eng.activation(
                        s_dl[:, mt % 2, :], p_dt[:, mt % 2, :], AF.Relu,
                        bias=s_dtb[:, mt:mt + 1]),
                    waits=[prev_ring("dldma", mtg, 2)],
                    inc=aq)
                E[("dladd", mtg)] = P.op(
                    "vector",
                    lambda eng, mt=mt: eng.tensor_add(
                        s_dl[:, mt % 2, :], s_dl[:, mt % 2, :],
                        s_sp[:, mt % 2, :]),
                    waits=[E[("relu", mtg)]], inc=vq)
                E[("dldma", mtg)] = P.dma(
                    "sync", dl_o[mt * 128:(mt + 1) * 128, j * 512:(j + 1) * 512],
                    s_dl[:, mt % 2, :], od[mt % 2],
                    waits=[E[("dladd", mtg)]])

        for c in range(5):
            chunk_section(c)
            if c >= 1:
                out_section(c - 1)

        allsems = [ld] + lx + lw + oz + ou + od + [ob]
        P.op("gpsimd", lambda eng: eng.memset(s_o128[:].bitcast(F32), 1.0),
             waits=[(s, P.counts[s.name]) for s in allsems
                    if s.name in P.counts] +
                   [(vq, P.counts["vq"]), (aq, P.counts["aq"]),
                    (tq, P.counts["tq"])],
             inc=ld, inc_by=1)

        P.emit(blk)
    return nc


def _l1_host_inputs(inputs):
    x = np.ascontiguousarray(inputs["L0_attn_x"][0].T).astype(np.float32, copy=False)
    in_proj_w = inputs["L0_attn_in_proj_w"]
    x_proj_w = inputs["L0_attn_x_proj_w"]
    dt_proj_w = inputs["L0_attn_dt_proj_w"]
    conv_w = inputs["L0_attn_conv_w"]
    conv_b = inputs["L0_attn_conv_b"]
    dt_proj_b = inputs["L0_attn_dt_proj_b"]
    norm_w = inputs["L0_attn_norm_w"]

    inw = np.ascontiguousarray(
        np.asarray(in_proj_w).reshape(NM, 128, NG, 128).transpose(0, 3, 2, 1))
    normw = np.ascontiguousarray(np.asarray(norm_w).reshape(NG, 128).T)
    xpw = np.zeros((128, NDT, 128), dtype=np.float32)
    xpt = np.asarray(x_proj_w)
    for g in range(NDT):
        blkw = xpt[:, g * 128:(g + 1) * 128]
        xpw[:, g, 0:16] = blkw[64:80].T
        xpw[:, g, 32:48] = blkw[80:96].T
        xpw[:, g, 64:128] = blkw[0:64].T
    dtw = np.zeros((128, DI), dtype=np.float32)
    dtw[64:128, :] = np.asarray(dt_proj_w).T
    convw = np.ascontiguousarray(
        np.asarray(conv_w)[:, 0, :].reshape(NDT, 128, 4).transpose(1, 0, 2))
    convb = np.ascontiguousarray(np.asarray(conv_b).reshape(NDT, 128).T)
    dtb = np.ascontiguousarray(np.asarray(dt_proj_b).reshape(NDT, 128).T)
    ones128 = np.ones((128, 2), dtype=np.float32)
    ones1 = np.ones((1, 128), dtype=np.float32)
    cstv = np.zeros((128, 8), dtype=np.float32)
    cstv[:, 0] = EPS
    cstv[:, 1] = 1.0 / D
    cstv[:, 2] = 1.0
    cstv[:, 3] = -1.0
    cstv[:, 4] = -0.5

    maps = []
    for k in range(NCORES):
        lo = k * S - HALO
        if lo < 0:
            xh2 = np.concatenate(
                [np.zeros((D, HALO), np.float32), x[:, 0:(k + 1) * S]], axis=1)
        else:
            xh2 = x[:, lo:(k + 1) * S]
        xh2 = np.ascontiguousarray(xh2.reshape(NG, 128, SH).transpose(1, 0, 2))
        maps.append({
            "xh": xh2, "inw": inw, "normw": normw, "xpw": xpw, "dtw": dtw,
            "convw": convw, "convb": convb, "dtb": dtb,
            "ones128": ones128, "ones1": ones1, "cst": cstv,
        })
    return maps




TC2 = 1024
NC2 = T // TC2
NSUB = TC2 // 512


def build_L2():
    nc = bass.Bass("TRN2", target_bir_lowering=False, debug=False,
                   detect_race_conditions=False)

    u_i = nc.dram_tensor("u_i", [DSL, T], F32, kind="ExternalInput").ap()
    dl_i = nc.dram_tensor("dl_i", [DSL, T], F32, kind="ExternalInput").ap()
    B_i = nc.dram_tensor("B_i", [16, T], F32, kind="ExternalInput").ap()
    C_i = nc.dram_tensor("C_i", [16, T], F32, kind="ExternalInput").ap()
    alog = nc.dram_tensor("alog", [128, 2, NST], F32, kind="ExternalInput").ap()
    dvec = nc.dram_tensor("dvec", [128, 2], F32, kind="ExternalInput").ap()
    sel = nc.dram_tensor("sel", [16, 16 * 128], F32, kind="ExternalInput").ap()
    ident = nc.dram_tensor("ident", [128, 128], F32, kind="ExternalInput").ap()
    cst = nc.dram_tensor("cst", [128, 8], F32, kind="ExternalInput").ap()

    y_o = nc.dram_tensor("y_o", [DSL, T], F32, kind="ExternalOutput").ap()

    with ExitStack() as stk:
        e = stk.enter_context
        s_u = e(nc.sbuf_tensor("s_u", [128, 2, 3, TC2], F32))
        s_dl = e(nc.sbuf_tensor("s_dl", [128, 2, 3, TC2], F32))
        s_du = e(nc.sbuf_tensor("s_du", [128, 2, 3, TC2], F32))
        s_B = e(nc.sbuf_tensor("s_B", [16, 2, TC2], F32R))
        s_C = e(nc.sbuf_tensor("s_C", [16, 2, TC2], F32R))
        s_sel = e(nc.sbuf_tensor("s_sel", [16, 16 * 128], F32R))
        s_ident = e(nc.sbuf_tensor("s_ident", [128, 128], F32R))
        s_alog = e(nc.sbuf_tensor("s_alog", [128, 2, NST], F32))
        s_A = e(nc.sbuf_tensor("s_A", [128, 2, NST], F32))
        s_D = e(nc.sbuf_tensor("s_D", [128, 2], F32))
        s_cst = e(nc.sbuf_tensor("s_cst", [128, 8], F32))
        s_dA = e(nc.sbuf_tensor("s_dA", [128, 2, 2, TC2], F32))
        s_w = e(nc.sbuf_tensor("s_w", [128, 2, TC2], F32))
        s_h = e(nc.sbuf_tensor("s_h", [128, 2, 2, TC2], F32))
        s_p = e(nc.sbuf_tensor("s_p", [128, 2, 2, TC2], F32R))
        s_Ct = e(nc.sbuf_tensor("s_Ct", [128, 2, TC2], F32))
        s_carry = e(nc.sbuf_tensor("s_carry", [128, 2, NST], F32))
        s_y = e(nc.sbuf_tensor("s_y", [128, 2, TC2], F32))
        p_B = e(nc.psum_tensor("p_B", [128, 2, 512], F32))
        p_C = e(nc.psum_tensor("p_C", [128, 2, 512], F32))
        p_y = e(nc.psum_tensor("p_y", [128, 2, NSUB, 512], F32))
        ld = e(nc.semaphore("ld"))
        ldl = [e(nc.semaphore(f"ldl{i}")) for i in range(2)]
        lu = [e(nc.semaphore(f"lu{i}")) for i in range(2)]
        lbc = [e(nc.semaphore(f"lbc{i}")) for i in range(2)]
        oy = [e(nc.semaphore(f"oy{i}")) for i in range(2)]
        tq = e(nc.semaphore("tq"))
        vq = e(nc.semaphore("vq"))
        aq = e(nc.semaphore("aq"))
        pq = e(nc.semaphore("pq"))
        blk = e(nc.Block())

        P = Prog(nc)
        E = {}

        ev_c = None
        for dst, src_ in [(s_sel[:], sel.bitcast(F32R)),
                          (s_ident[:], ident.bitcast(F32R)),
                          (s_alog[:], alog), (s_D[:], dvec), (s_cst[:], cst)]:
            ev_c = P.dma("sync", dst, src_, ld)
        P.base_waits = [ev_c]

        eva0 = P.op("scalar",
                    lambda eng: eng.activation(s_A[:], s_alog[:], AF.Exp),
                    waits=[ev_c], inc=aq)
        evA = P.op("vector",
                   lambda eng: eng.tensor_scalar_mul(s_A[:], s_A[:],
                                                     s_cst[:, 3:4]),
                   waits=[eva0], inc=vq)

        NTOT = NC2 * NST

        def em_load(c):
            if c >= NC2:
                return
            r3 = c % 3
            for dt2 in range(2):
                E[("lu", c, dt2)] = P.dma(
                    "sync", s_u[:, dt2, r3, :],
                    u_i[dt2 * 128:(dt2 + 1) * 128, c * TC2:(c + 1) * TC2],
                    lu[c % 2], waits=[E.get(("yev", c - 3, dt2, NSUB - 1))])
                E[("ldl", c, dt2)] = P.dma(
                    "sync", s_dl[:, dt2, r3, :],
                    dl_i[dt2 * 128:(dt2 + 1) * 128, c * TC2:(c + 1) * TC2],
                    ldl[c % 2], waits=[E.get(("dA", c - 3, dt2, NST - 1))])

        def em_loadbc(c):
            if c >= NC2:
                return
            r = c % 2
            E[("lB", c)] = P.dma(
                "sync", s_B[:, r, :],
                B_i[:, c * TC2:(c + 1) * TC2].bitcast(F32R),
                lbc[r], waits=[E.get(("bb", c - 2, NST - 1, NSUB - 1))])
            E[("lC", c)] = P.dma(
                "sync", s_C[:, r, :],
                C_i[:, c * TC2:(c + 1) * TC2].bitcast(F32R),
                lbc[r], waits=[E.get(("cb", c - 2, NST - 1, NSUB - 1))])

        def em_du(c):
            r3 = c % 3
            for dt2 in range(2):
                E[("du", c, dt2)] = P.op(
                    "vector",
                    lambda eng, dt2=dt2, r3=r3: eng.tensor_mul(
                        s_du[:, dt2, r3, :], s_dl[:, dt2, r3, :],
                        s_u[:, dt2, r3, :]),
                    waits=[E[("lu", c, 1)], E[("ldl", c, 1)],
                           E.get(("w", c - 3, dt2, NST - 1))],
                    inc=vq)

        def em_dA(c, n):
            if c >= NC2:
                return
            for dt2 in range(2):
                E[("dA", c, dt2, n)] = P.op(
                    "scalar",
                    lambda eng, dt2=dt2, n=n, c=c: eng.activation(
                        s_dA[:, dt2, (n + NST * c) % 2, :],
                        s_dl[:, dt2, c % 3, :], AF.Exp,
                        scale=s_A[:, dt2, n:n + 1]),
                    waits=[E[("ldl", c, 1)], evA,
                           E.get(("scan", c, dt2, n - 2)) or
                           E.get(("scan", c - 1, dt2, NST - 2 + (n % 2)))],
                    inc=aq)

        def em_builds(g):
            if g >= NTOT:
                return
            c, n = divmod(g, NST)
            r = c % 2
            for s in range(NSUB):
                E[("bb", c, n, s)] = P.op(
                    "tensor",
                    lambda eng, n=n, s=s, r=r: eng.matmul(
                        p_B[:, s, :],
                        s_sel[:, n * 128:(n + 1) * 128],
                        s_B[:, r, s * 512:(s + 1) * 512],
                        start=True, stop=True),
                    waits=[E[("lC", c)], E.get(("w2l", g - 1, s))],
                    inc=tq)
            for s in range(NSUB):
                E[("cb", c, n, s)] = P.op(
                    "tensor",
                    lambda eng, n=n, s=s, r=r: eng.matmul(
                        p_C[:, s, :],
                        s_sel[:, n * 128:(n + 1) * 128],
                        s_C[:, r, s * 512:(s + 1) * 512],
                        start=True, stop=True),
                    waits=[E[("lC", c)], E.get(("ctev", g - 1, s))],
                    inc=tq)

        def em_w2scan(c, n, dt2):
            r3 = c % 3
            g = c * NST + n
            ev = None
            for s in range(NSUB):
                ev = P.op(
                    "vector",
                    lambda eng, dt2=dt2, s=s, r3=r3: eng.tensor_mul(
                        s_w[:, dt2, s * 512:(s + 1) * 512],
                        s_du[:, dt2, r3, s * 512:(s + 1) * 512],
                        p_B[:, s, :]),
                    waits=[E[("bb", c, n, s)], E[("du", c, dt2)]],
                    inc=vq)
                if dt2 == 1:
                    E[("w2l", g, s)] = ev
            E[("w", c, dt2, n)] = ev
            hwar = (E.get(("pp", c, dt2, n - 2)) if n >= 2
                    else E.get(("pp", c - 1, dt2, NST - 2 + (n % 2))))
            E[("scan", c, dt2, n)] = P.op(
                "vector",
                lambda eng, dt2=dt2, n=n, c=c: eng.tensor_tensor_scan(
                    s_h[:, dt2, n % 2, :],
                    s_dA[:, dt2, (n + NST * c) % 2, :],
                    s_w[:, dt2, :],
                    (0.0 if c == 0 else s_carry[:, dt2, n:n + 1]),
                    ALU.mult, ALU.add),
                waits=[E[("dA", c, dt2, n)],
                       (E[("carry", c - 1, dt2, n)][0],
                        E[("carry", c - 1, dt2, n)][1], True)
                       if c >= 1 else None, hwar],
                inc=vq)
            sc_ev = E[("scan", c, dt2, n)]
            E[("carry", c, dt2, n)] = P.op(
                "vector",
                lambda eng, dt2=dt2, n=n: eng.tensor_copy(
                    s_carry[:, dt2, n:n + 1],
                    s_h[:, dt2, n % 2, TC2 - 1:TC2]),
                waits=[(sc_ev[0], sc_ev[1], True)],
                inc=vq)

        def em_ctev(c, n):
            g = c * NST + n
            for s in range(NSUB):
                E[("ctev", g, s)] = P.op(
                    "scalar",
                    lambda eng, n=n, s=s: eng.activation(
                        s_Ct[:, n % 2, s * 512:(s + 1) * 512],
                        p_C[:, s, :], AF.Copy),
                    waits=[E[("cb", c, n, s)],
                           E.get(("pp", c, 1, n - 2)) or
                           (E.get(("pp", c - 1, 1, NST - 2 + (n % 2)))
                            if c >= 1 else None)],
                    inc=aq)
            E[("ctevl", c, n)] = E[("ctev", g, NSUB - 1)]

        def em_pp(c, n):
            for dt2 in range(2):
                E[("pp", c, dt2, n)] = P.op(
                    "gpsimd",
                    lambda eng, dt2=dt2, n=n: eng.tensor_mul(
                        s_p[:, dt2, n % 2, :], s_h[:, dt2, n % 2, :],
                        s_Ct[:, n % 2, :]),
                    waits=[E[("scan", c, dt2, n)], E[("ctevl", c, n)],
                           (E.get(("ya", c, dt2, n - 2, NSUB - 1)) if n >= 2
                            else E.get(("ya", c - 1, dt2, NST - 2 + (n % 2),
                                        NSUB - 1)))],
                    inc=pq)

        def em_ya(c, n):
            for dt2 in range(2):
                for s in range(NSUB):
                    E[("ya", c, dt2, n, s)] = P.op(
                        "tensor",
                        lambda eng, dt2=dt2, n=n, s=s: eng.matmul(
                            p_y[:, dt2, s, :], s_ident[:],
                            s_p[:, dt2, n % 2, s * 512:(s + 1) * 512],
                            start=(n == 0), stop=(n == NST - 1)),
                        waits=[E[("pp", c, dt2, n)],
                               (E.get(("yev", c - 1, dt2, s))
                                if n == 0 else None)],
                        inc=tq)

        em_load(0)
        em_load(1)
        em_loadbc(0)
        em_loadbc(1)
        em_du(0)
        em_dA(0, 0)
        em_builds(0)
        for c in range(NC2):
            if c > 0:
                em_du(c)
            em_load(c + 2)
            for n in range(NST):
                g = c * NST + n
                nc_, nn = divmod(g + 1, NST)
                em_dA(nc_, nn)
                em_w2scan(c, n, 0)
                em_ctev(c, n)
                em_w2scan(c, n, 1)
                em_builds(g + 1)
                if n >= 1:
                    em_ya(c, n - 1)
                em_pp(c, n)
            em_ya(c, NST - 1)
            em_loadbc(c + 2)
            r3 = c % 3
            for dt2 in range(2):
                for s in range(NSUB):
                    E[("yev", c, dt2, s)] = P.op(
                        "vector",
                        lambda eng, dt2=dt2, s=s, r3=r3:
                        eng.scalar_tensor_tensor(
                            s_y[:, dt2, s * 512:(s + 1) * 512],
                            s_u[:, dt2, r3, s * 512:(s + 1) * 512],
                            s_D[:, dt2:dt2 + 1],
                            p_y[:, dt2, s, :],
                            ALU.mult, ALU.add),
                        waits=[E[("ya", c, dt2, NST - 1, s)],
                               E.get(("ydma", c - 1, dt2))],
                        inc=vq)
                E[("ydma", c, dt2)] = P.dma(
                    "sync",
                    y_o[dt2 * 128:(dt2 + 1) * 128, c * TC2:(c + 1) * TC2],
                    s_y[:, dt2, :], oy[dt2],
                    waits=[E[("yev", c, dt2, NSUB - 1)]])

        allsems = [ld] + ldl + lu + lbc + oy
        P.op("gpsimd", lambda eng: eng.memset(s_cst[:, 7:8], 0.0),
             waits=[(sm, P.counts[sm.name]) for sm in allsems
                    if sm.name in P.counts] +
                   [(vq, P.counts["vq"]), (aq, P.counts["aq"]),
                    (tq, P.counts["tq"]), (pq, P.counts["pq"])],
             inc=pq, inc_by=1)

        P.emit(blk)
    return nc


def _l2_host_inputs(inputs, res1):
    A_log = np.asarray(inputs["L0_attn_A_log"])
    Dv = np.asarray(inputs["L0_attn_D"])
    u_full = [res1.results[k]["u_o"] for k in range(NCORES)]
    dl_full = [res1.results[k]["dl_o"] for k in range(NCORES)]
    bc_full = np.concatenate([res1.results[k]["bc_o"] for k in range(NCORES)],
                             axis=1)
    B_full = np.ascontiguousarray(bc_full[0:16])
    C_full = np.ascontiguousarray(bc_full[32:48])
    sel = np.zeros((16, 16 * 128), dtype=np.float32)
    for n in range(16):
        sel[n, n * 128:(n + 1) * 128] = 1.0
    ident = np.eye(128, dtype=np.float32)
    cstv = np.zeros((128, 8), dtype=np.float32)
    cstv[:, 0] = EPS
    cstv[:, 1] = 1.0 / D
    cstv[:, 2] = 1.0
    cstv[:, 3] = -1.0
    cstv[:, 4] = -0.5

    maps = []
    for k in range(NCORES):
        lo = k * DSL
        u_sl = np.ascontiguousarray(
            np.concatenate([u_full[j][lo:lo + DSL] for j in range(NCORES)],
                           axis=1))
        dl_sl = np.ascontiguousarray(
            np.concatenate([dl_full[j][lo:lo + DSL] for j in range(NCORES)],
                           axis=1))
        al = np.ascontiguousarray(
            A_log[lo:lo + DSL].reshape(2, 128, NST).transpose(1, 0, 2))
        dv = np.ascontiguousarray(Dv[lo:lo + DSL].reshape(2, 128).T)
        maps.append({
            "u_i": u_sl, "dl_i": dl_sl, "B_i": B_full, "C_i": C_full,
            "alog": al, "dvec": dv, "sel": sel, "ident": ident, "cst": cstv,
        })
    return maps


TCA = 512
NCA = S // TCA
TCB = 256
NCB = S // TCB
NMT = D // 128
NKT = DI // 128


def build_L3(debug=False):
    nc = bass.Bass("TRN2", target_bir_lowering=False, debug=False,
                   detect_race_conditions=False)
    BF16 = mybir.dt.bfloat16

    y_i = nc.dram_tensor("y_i", [DI, S], F32, kind="ExternalInput").ap()
    z_i = nc.dram_tensor("z_i", [DI, S], F32, kind="ExternalInput").ap()
    xr = nc.dram_tensor("xr", [128, NMT, S], F32, kind="ExternalInput").ap()
    outw = nc.dram_tensor("outw", [128, NKT, D], F32, kind="ExternalInput").ap()
    wenc = nc.dram_tensor("wenc", [128, NMT, HS], F32, kind="ExternalInput").ap()
    wdec = nc.dram_tensor("wdec", [128, NKT, D], F32, kind="ExternalInput").ap()
    ws2c = nc.dram_tensor("ws2c", [128, NKT, CM], F32, kind="ExternalInput").ap()
    wr_ = nc.dram_tensor("wr_", [64, 8], F32, kind="ExternalInput").ap()
    w1 = nc.dram_tensor("w1", [128, NE // 2, NE, 128], F32, kind="ExternalInput").ap()
    w2 = nc.dram_tensor("w2", [128, 8, NE, CM], F32, kind="ExternalInput").ap()
    wc2s = nc.dram_tensor("wc2s", [64, HS], F32, kind="ExternalInput").ap()
    benc = nc.dram_tensor("benc", [128, NKT], F32, kind="ExternalInput").ap()
    bs2c = nc.dram_tensor("bs2c", [64, 1], F32, kind="ExternalInput").ap()
    br_ = nc.dram_tensor("br_", [8, 1], F32, kind="ExternalInput").ap()
    b1 = nc.dram_tensor("b1", [128, NE, 8], F32, kind="ExternalInput").ap()
    b2 = nc.dram_tensor("b2", [64, NE], F32, kind="ExternalInput").ap()
    bc2s = nc.dram_tensor("bc2s", [128, NKT], F32, kind="ExternalInput").ap()
    bdec = nc.dram_tensor("bdec", [128, NMT], F32, kind="ExternalInput").ap()
    lng = nc.dram_tensor("lng", [128, NMT], F32, kind="ExternalInput").ap()
    lnb = nc.dram_tensor("lnb", [128, NMT], F32, kind="ExternalInput").ap()
    sel8 = nc.dram_tensor("sel8", [8, NE, CM], F32, kind="ExternalInput").ap()
    ident = nc.dram_tensor("ident", [128, 128], F32, kind="ExternalInput").ap()
    ones128 = nc.dram_tensor("ones128", [128, 2], F32, kind="ExternalInput").ap()
    ones1 = nc.dram_tensor("ones1", [1, 128], F32, kind="ExternalInput").ap()
    cst = nc.dram_tensor("cst", [128, 8], F32, kind="ExternalInput").ap()
    xn2_d = nc.dram_tensor("xn2_d", [D, S], F32,
                           kind=("ExternalOutput" if debug else "Internal"))
    if debug:
        dbg_gate = nc.dram_tensor("dbg_gate", [8, S], F32, kind="ExternalOutput").ap()
        dbg_eo = nc.dram_tensor("dbg_eo", [64, S], F32, kind="ExternalOutput").ap()
        dbg_ffn = nc.dram_tensor("dbg_ffn", [D, S], F32, kind="ExternalOutput").ap()
        dbg_x2 = nc.dram_tensor("dbg_x2", [D, S], F32, kind="ExternalOutput").ap()
        dbg_mub = nc.dram_tensor("dbg_mub", [128, S], F32, kind="ExternalOutput").ap()
        dbg_rsb = nc.dram_tensor("dbg_rsb", [128, S], F32, kind="ExternalOutput").ap()

    out_o = nc.dram_tensor("out_o", [D, S], F32, kind="ExternalOutput").ap()

    with ExitStack() as stk:
        e = stk.enter_context
        s_ws2c = e(nc.sbuf_tensor("s_ws2c", [128, NKT, CM], F32R))
        s_wr = e(nc.sbuf_tensor("s_wr", [64, 8], F32R))
        s_w1 = e(nc.sbuf_tensor("s_w1", [128, NE // 2, NE, 128], F32R))
        s_w2 = e(nc.sbuf_tensor("s_w2", [128, 8, NE, CM], F32R))
        s_wc2s = e(nc.sbuf_tensor("s_wc2s", [64, HS], F32R))
        s_benc = e(nc.sbuf_tensor("s_benc", [128, NKT], F32))
        s_bs2c = e(nc.sbuf_tensor("s_bs2c", [64, 1], F32))
        s_br = e(nc.sbuf_tensor("s_br", [8, 1], F32))
        s_b1 = e(nc.sbuf_tensor("s_b1", [128, NE, 8], F32))
        s_b2 = e(nc.sbuf_tensor("s_b2", [64, NE], F32))
        s_bc2s = e(nc.sbuf_tensor("s_bc2s", [128, NKT], F32))
        s_bdec = e(nc.sbuf_tensor("s_bdec", [128, NMT], F32))
        s_lng = e(nc.sbuf_tensor("s_lng", [128, NMT], F32))
        s_lnb = e(nc.sbuf_tensor("s_lnb", [128, NMT], F32))
        s_sel8 = e(nc.sbuf_tensor("s_sel8", [8, NE, CM], F32R))
        s_ident = e(nc.sbuf_tensor("s_ident", [128, 128], F32))
        s_o128 = e(nc.sbuf_tensor("s_o128", [128, 2], F32R))
        s_o1 = e(nc.sbuf_tensor("s_o1", [1, 128], F32R))
        s_cst = e(nc.sbuf_tensor("s_cst", [128, 8], F32))
        s_sq = e(nc.sbuf_tensor("s_sq3", [128, 2, TCA], F32R))
        s_mu = e(nc.sbuf_tensor("s_mu", [1, TCA], F32R))
        s_mu2 = e(nc.sbuf_tensor("s_mu2", [1, TCA], F32))
        s_var = e(nc.sbuf_tensor("s_var", [1, TCA], F32R))
        s_mub = e(nc.sbuf_tensor("s_mub", [128, TCA], F32))
        s_fnc = e(nc.sbuf_tensor("s_fnc", [1, 2], F32))
        s_rsb = e(nc.sbuf_tensor("s_rsb", [128, TCA], F32))
        p_mm = e(nc.psum_tensor("p_mm3", [128, 2, TCA], F32))
        p_cont = e(nc.psum_tensor("p_cont", [128, TCB], F32))
        p_ge = e(nc.psum_tensor("p_ge", [64, TCB], F32))
        p_ye = e(nc.psum_tensor("p_ye", [64, TCB], F32))
        p_s = e(nc.psum_tensor("p_s", [2, 2, TCA], F32))
        p_msc = e(nc.psum_tensor("p_msc", [128, 512], F32))
        ld = e(nc.semaphore("ld"))
        lwA = e(nc.semaphore("lwA"))
        lwB = e(nc.semaphore("lwB"))
        lyz = [e(nc.semaphore(f"lyz{i}")) for i in range(2)]
        lxt = [e(nc.semaphore(f"lxt{i}")) for i in range(2)]
        lxn = [e(nc.semaphore(f"lxn{i}")) for i in range(2)]
        oxn = [e(nc.semaphore(f"oxn{i}")) for i in range(2)]
        oo = [e(nc.semaphore(f"oo{i}")) for i in range(2)]
        tq = e(nc.semaphore("tq"))
        vq = e(nc.semaphore("vq"))
        aq = e(nc.semaphore("aq"))
        blk = e(nc.Block())

        P = Prog(nc)
        E = {}

        ev_c = None
        for dst, src_ in [(s_wr[:], wr_.bitcast(F32R)), (s_benc[:], benc),
                          (s_bs2c[:], bs2c), (s_br[:], br_), (s_b1[:], b1),
                          (s_b2[:], b2), (s_bc2s[:], bc2s), (s_bdec[:], bdec),
                          (s_lng[:], lng), (s_lnb[:], lnb),
                          (s_sel8[:], sel8.bitcast(F32R)), (s_ident[:], ident),
                          (s_o128[:], ones128.bitcast(F32R)),
                          (s_o1[:], ones1.bitcast(F32R)), (s_cst[:], cst)]:
            ev_c = P.dma("sync", dst, src_, ld)
        P.base_waits = [ev_c]

        lnstate = {"prev": None}

        def vfence():
            return P.op("vector",
                        lambda eng: eng.tensor_copy(s_fnc[0:1, 0:1],
                                                    s_fnc[0:1, 1:2]),
                        inc=vq)

        def layer_norm(tag, src_fn, src_ev, ntiles, w_, affine, dst_fn,
                       dst_war_fn, post_fn=None):
            prev = lnstate["prev"]
            for i in range(ntiles):
                E[(tag, "sq", i)] = P.op(
                    "scalar",
                    lambda eng, i=i: eng.activation(
                        s_sq[:, i % 2, 0:w_], src_fn(i).bitcast(F32),
                        AF.Square),
                    waits=[src_ev(i), E.get((tag, "st2", i - 2)),
                           (E.get((prev, "app2", i)) if prev else None)],
                    inc=aq)
                E[(tag, "st1", i)] = P.op(
                    "tensor",
                    lambda eng, i=i: eng.matmul(
                        p_s[:, 0, 0:w_], s_o128[:], src_fn(i),
                        start=(i == 0), stop=(i == ntiles - 1)),
                    waits=[src_ev(i),
                           (E.get((prev, "mu")) if prev and i == 0 else None)],
                    inc=tq)
                E[(tag, "st2", i)] = P.op(
                    "tensor",
                    lambda eng, i=i: eng.matmul(
                        p_s[:, 1, 0:w_], s_o128[:], s_sq[:, i % 2, 0:w_],
                        start=(i == 0), stop=(i == ntiles - 1)),
                    waits=[E[(tag, "sq", i)],
                           (E.get((prev, "var")) if prev and i == 0 else None)],
                    inc=tq)
            E[(tag, "mu")] = P.op(
                "vector",
                lambda eng: eng.tensor_scalar(
                    s_mu[0:1, 0:w_], p_s[0:1, 0, 0:w_], s_cst[0:1, 1:2],
                    None, ALU.mult),
                waits=[E[(tag, "st1", ntiles - 1)], ev_c,
                       (E.get((prev, "bc1")) if prev else None)],
                inc=vq)
            E[(tag, "mu")] = vfence()
            E[(tag, "mu2")] = P.op(
                "scalar",
                lambda eng: eng.activation(s_mu2[0:1, 0:w_],
                                           s_mu[0:1, 0:w_].bitcast(F32),
                                           AF.Square),
                waits=[E[(tag, "mu")],
                       (E.get((prev, "var")) if prev else None)],
                inc=aq)
            E[(tag, "var")] = P.op(
                "vector",
                lambda eng: eng.scalar_tensor_tensor(
                    s_var[0:1, 0:w_], p_s[0:1, 1, 0:w_],
                    s_cst[0:1, 1:2], s_mu2[0:1, 0:w_],
                    ALU.mult, ALU.subtract),
                waits=[E[(tag, "st2", ntiles - 1)], E[(tag, "mu2")],
                       (E.get((prev, "bc2")) if prev else None)],
                inc=vq)
            E[(tag, "var")] = vfence()
            E[(tag, "rstd")] = P.op(
                "scalar",
                lambda eng: eng.activation(s_var[0:1, 0:w_],
                                           s_var[0:1, 0:w_].bitcast(F32),
                                           AF.Ln, bias=s_cst[0:1, 0:1]),
                waits=[E[(tag, "var")]], inc=aq)
            E[(tag, "rstd2")] = P.op(
                "scalar",
                lambda eng: eng.activation(s_var[0:1, 0:w_],
                                           s_var[0:1, 0:w_].bitcast(F32),
                                           AF.Exp, scale=s_cst[0:1, 4:5]),
                inc=aq)
            E[(tag, "bc1")] = P.op(
                "tensor",
                lambda eng: eng.matmul(p_msc[:, 0:w_], s_o1[:],
                                       s_mu[0:1, 0:w_], start=True, stop=True),
                waits=[E[(tag, "mu")],
                       (E.get((prev, "bcev1")) if prev else None),
                       (E.get(("msc_free", tag[1])) if tag[0] == "lnF"
                        else None)],
                inc=tq)
            E[(tag, "bcev1")] = P.op(
                "scalar",
                lambda eng: eng.activation(s_mub[:, 0:w_], p_msc[:, 0:w_],
                                           AF.Copy),
                waits=[E[(tag, "bc1")],
                       (E.get((prev, "app1_last")) if prev else None)],
                inc=aq)
            E[(tag, "bc2")] = P.op(
                "tensor",
                lambda eng: eng.matmul(p_msc[:, 0:w_], s_o1[:],
                                       s_var[0:1, 0:w_], start=True, stop=True),
                waits=[E[(tag, "rstd2")], E[(tag, "bcev1")]], inc=tq)
            E[(tag, "bcev2")] = P.op(
                "scalar",
                lambda eng: eng.activation(s_rsb[:, 0:w_], p_msc[:, 0:w_],
                                           AF.Copy),
                waits=[E[(tag, "bc2")],
                       (E.get((prev, "app2_last")) if prev else None)],
                inc=aq)
            for i in range(ntiles):
                E[(tag, "app1", i)] = P.op(
                    "vector",
                    lambda eng, i=i: eng.tensor_sub(
                        s_sq[:, i % 2, 0:w_],
                        src_fn(i).bitcast(F32), s_mub[:, 0:w_]),
                    waits=[E[(tag, "bcev1")], E[(tag, "bcev2")],
                           E[(tag, "st2", i)]], inc=vq)
                ev = P.op(
                    "vector",
                    lambda eng, i=i: eng.tensor_mul(
                        dst_fn(i), s_sq[:, i % 2, 0:w_].bitcast(F32),
                        s_rsb[:, 0:w_]),
                    waits=[dst_war_fn(i)], inc=vq)
                if affine is not None:
                    gcol, bcol = affine
                    ev = P.op(
                        "vector",
                        lambda eng, i=i: eng.tensor_scalar(
                            dst_fn(i), dst_fn(i).bitcast(F32),
                            gcol(i), bcol(i), ALU.mult, ALU.add),
                        waits=[ev], inc=vq)
                E[(tag, "app2", i)] = ev
                if post_fn is not None:
                    post_fn(i, ev)
            E[(tag, "app1_last")] = E[(tag, "app1", ntiles - 1)]
            E[(tag, "app2_last")] = E[(tag, "app2", ntiles - 1)]
            lnstate["prev"] = tag
            return tag

        with ExitStack() as stkA:
            eA = stkA.enter_context
            s_outw = eA(nc.sbuf_tensor("s_outw", [128, NKT, D], BF16))
            s_yz = eA(nc.sbuf_tensor("s_yz", [128, 2, 2, TCA], F32))
            s_y2 = eA(nc.sbuf_tensor("s_y2", [128, NKT, TCA], BF16))
            s_xt = eA(nc.sbuf_tensor("s_xt", [128, 2, TCA], F32))
            s_x2 = eA(nc.sbuf_tensor("s_x2", [128, NMT, TCA], F32R))
            s_xn2c = eA(nc.sbuf_tensor("s_xn2c", [128, 2, TCA], F32))

            ev_wA = P.dma("gpsimd", s_outw[:], outw, lwA)
            ev_wffn = None
            for dst, src_ in [(s_ws2c[:], ws2c.bitcast(F32R)),
                              (s_w1[:], w1.bitcast(F32R)),
                              (s_w2[:], w2.bitcast(F32R)),
                              (s_wc2s[:], wc2s.bitcast(F32R))]:
                ev_wffn = P.dma("gpsimd", dst, src_, lwA)

            def emA_y2(c):
                for kt in range(NKT):
                    gyk = c * NKT + kt
                    r = gyk % 2
                    war_yz = (E.get(("y2", (gyk - 2) // NKT, (gyk - 2) % NKT))
                              if gyk >= 2 else None)
                    E[("ly", c, kt)] = P.dma(
                        "sync", s_yz[:, 0, r, :],
                        y_i[kt * 128:(kt + 1) * 128, c * TCA:(c + 1) * TCA],
                        lyz[r], waits=[war_yz])
                    E[("lz", c, kt)] = P.dma(
                        "sync", s_yz[:, 1, r, :],
                        z_i[kt * 128:(kt + 1) * 128, c * TCA:(c + 1) * TCA],
                        lyz[r], waits=[])
                    E[("sz", c, kt)] = P.op(
                        "scalar",
                        lambda eng, r=r: eng.activation(
                            s_yz[:, 1, r, :], s_yz[:, 1, r, :], AF.Silu),
                        waits=[E[("lz", c, kt)]], inc=aq)
                    E[("y2", c, kt)] = P.op(
                        "vector",
                        lambda eng, r=r, kt=kt: eng.tensor_mul(
                            s_y2[:, kt, :], s_yz[:, 0, r, :],
                            s_yz[:, 1, r, :]),
                        waits=[E[("sz", c, kt)], E[("lz", c, kt)],
                               E.get(("mmA", c - 1, NMT - 1, kt))],
                        inc=vq)

            def emA_mm(c):
                for mt in range(NMT):
                    gx = c * NMT + mt
                    E[("lxt", c, mt)] = P.dma(
                        "sync", s_xt[:, gx % 2, :],
                        xr[:, mt, c * TCA:(c + 1) * TCA], lxt[gx % 2],
                        waits=[E.get(("x2ev", (gx - 2) // NMT, (gx - 2) % NMT))
                               if gx >= 2 else None])
                    for kt in range(NKT):
                        E[("mmA", c, mt, kt)] = P.op(
                            "tensor",
                            lambda eng, mt=mt, kt=kt: eng.matmul(
                                p_mm[:, mt % 2, :],
                                s_outw[:, kt, mt * 128:(mt + 1) * 128],
                                s_y2[:, kt, :],
                                start=(kt == 0), stop=(kt == NKT - 1)),
                            waits=[E[("y2", c, kt)],
                                   ev_wA if c == 0 and mt == 0 else None,
                                   ((E.get(("x2ev", c, mt - 2)) if mt >= 2
                                     else E.get(("x2ev", c - 1,
                                                 NMT - 2 + mt)))
                                    if kt == 0 else None)],
                            inc=tq)
                    P.op("vector",
                         lambda eng, mt=mt, gx=gx: eng.tensor_add(
                             s_x2[:, mt, :], p_mm[:, mt % 2, :],
                             s_xt[:, gx % 2, :]),
                         waits=[E[("mmA", c, mt, NKT - 1)], E[("lxt", c, mt)],
                                E.get((("lnA", c - 1), "app2", mt))],
                         inc=vq)
                    E[("x2ev", c, mt)] = vfence()
                if debug == "full":
                    for mt in range(NMT):
                        P.dma("sync", dbg_x2[mt * 128:(mt + 1) * 128,
                                             c * TCA:(c + 1) * TCA],
                              s_x2[:, mt, :].bitcast(F32), oo[1],
                              waits=[E[("x2ev", c, mt)]])

            def emA_ln(c):
                tg = ("lnA", c)

                def xn_post(i, ev, c=c):
                    gxi = c * NMT + i
                    E[("xndma", gxi)] = P.dma(
                        "sync", xn2_d.ap()[i * 128:(i + 1) * 128,
                                           c * TCA:(c + 1) * TCA],
                        s_xn2c[:, gxi % 2, :], oxn[gxi % 2],
                        waits=[ev])

                layer_norm(
                    tg, lambda i: s_x2[:, i, :],
                    lambda i, c=c: E[("x2ev", c, i)],
                    NMT, TCA, None,
                    lambda i, c=c: s_xn2c[:, (c * NMT + i) % 2, :],
                    lambda i, c=c: E.get(("xndma", c * NMT + i - 2)),
                    post_fn=xn_post)
                if debug == "full":
                    P.dma("sync", dbg_mub[:, c * TCA:(c + 1) * TCA],
                          s_mub[:, :], oo[1], waits=[E[(tg, "bcev1")],
                                                     E[(tg, "app1_last")]])
                    P.dma("sync", dbg_rsb[:, c * TCA:(c + 1) * TCA],
                          s_rsb[:, :], oo[1], waits=[E[(tg, "bcev2")],
                                                     E[(tg, "app2_last")]])

            for c in range(NCA):
                emA_y2(c)
                if c >= 1:
                    emA_ln(c - 1)
                emA_mm(c)
            emA_ln(NCA - 1)
            ev_a_done = E[("xndma", NCA * NMT - 1)]

        with ExitStack() as stkB:
            eB = stkB.enter_context
            s_wenc = eB(nc.sbuf_tensor("s_wenc", [128, NMT, HS], BF16))
            s_wdec = eB(nc.sbuf_tensor("s_wdec", [128, NKT, D], BF16))
            s_xnb = eB(nc.sbuf_tensor("s_xnb", [128, NMT, TCB], BF16))
            s_xnf = eB(nc.sbuf_tensor("s_xnf", [128, NMT, TCB], F32))
            s_spk = eB(nc.sbuf_tensor("s_spk", [128, 2, TCB], F32R))
            s_cont = eB(nc.sbuf_tensor("s_cont", [128, TCB], F32R))
            s_lgt = eB(nc.sbuf_tensor("s_lgt", [8, TCB], F32))
            s_gt = eB(nc.sbuf_tensor("s_gt", [128, 2, 8], F32))
            s_gtmp = eB(nc.sbuf_tensor("s_gtmp", [128, 2, 8], F32))
            s_gm = eB(nc.sbuf_tensor("s_gm", [128, 2, 4], F32))
            s_grow = eB(nc.sbuf_tensor("s_grow", [8, TCB], F32R))
            s_he = eB(nc.sbuf_tensor("s_he", [128, 8, TCB], F32R))
            s_ye = eB(nc.sbuf_tensor("s_ye", [64, TCB], F32))
            s_eo = eB(nc.sbuf_tensor("s_eo", [64, 2, TCB], F32R))
            s_sp2 = eB(nc.sbuf_tensor("s_sp2", [128, NKT, TCB], BF16))
            s_dec = eB(nc.sbuf_tensor("s_dec", [128, NMT, TCB], F32R))
            s_out = eB(nc.sbuf_tensor("s_out", [128, NMT, TCB], F32))

            P.dma("gpsimd", s_wenc[:], wenc, lwB, waits=[ev_a_done, ev_wA])
            ev_wB = P.dma("gpsimd", s_wdec[:], wdec, lwB)

            for c in range(NCB):
                ca = (c * TCB) // TCA
                E[("lxnb", c)] = P.dma(
                    "gpsimd", s_xnb[:, :, :],
                    xn2_d.ap().rearrange("(m p) t -> p m t", p=128)[
                        :, :, c * TCB:(c + 1) * TCB],
                    lxn[0],
                    waits=[ev_a_done,
                           E.get(("mmB", c - 1, NKT - 1, NMT - 1))])
                E[("lxnf", c)] = P.dma(
                    "sync", s_xnf[:, :, :],
                    xn2_d.ap().rearrange("(m p) t -> p m t", p=128)[
                        :, :, c * TCB:(c + 1) * TCB],
                    lxn[1],
                    waits=[ev_a_done, E.get(("x4", c - 1, NMT - 1))])
                for kt in range(NKT):
                    for mt in range(NMT):
                        E[("mmB", c, kt, mt)] = P.op(
                            "tensor",
                            lambda eng, kt=kt, mt=mt: eng.matmul(
                                p_mm[:, kt % 2, 0:TCB],
                                s_wenc[:, mt, kt * 128:(kt + 1) * 128],
                                s_xnb[:, mt, :],
                                start=(mt == 0), stop=(mt == NMT - 1)),
                            waits=[E[("lxnb", c)],
                                   ev_wB if c == 0 and kt == 0 else None,
                                   ((E.get(("spk", c, kt - 2)) if kt >= 2
                                     else E.get(("decev", c - 1,
                                                 NMT - 2 + kt)))
                                    if mt == 0 else None)],
                            inc=tq)
                    E[("spk", c, kt)] = P.op(
                        "scalar",
                        lambda eng, kt=kt: eng.activation(
                            s_spk[:, kt % 2, :], p_mm[:, kt % 2, 0:TCB],
                            AF.Sigmoid, bias=s_benc[:, kt:kt + 1]),
                        waits=[E[("mmB", c, kt, NMT - 1)],
                               E.get(("s2c", c, kt - 2)),
                               E.get(("s2c", c - 1, NKT - 2 + (kt % 2)))
                               if kt < 2 else None],
                        inc=aq)
                    E[("s2c", c, kt)] = P.op(
                        "tensor",
                        lambda eng, kt=kt: eng.matmul(
                            p_cont[0:64, :], s_ws2c[:, kt, :],
                            s_spk[:, kt % 2, :],
                            start=(kt == 0), stop=(kt == NKT - 1)),
                        waits=[E[("spk", c, kt)],
                               (ev_wffn if c == 0 and kt == 0 else None),
                               E.get(("contev", c - 1)) if kt == 0 else None],
                        inc=tq)
                ev1 = P.op(
                    "vector",
                    lambda eng: eng.tensor_scalar(
                        s_cont[0:64, :], p_cont[0:64, :], s_bs2c[:, 0:1],
                        None, ALU.add),
                    waits=[E[("s2c", c, NKT - 1)],
                           E.get(("w1m", c - 1, NE - 1, NE - 1))],
                    inc=vq)
                P.op("vector",
                     lambda eng: eng.tensor_scalar(
                         s_cont[64:128, :], p_cont[0:64, :], s_bs2c[:, 0:1],
                         None, ALU.add),
                     inc=vq)
                E[("contev", c)] = vfence()
                E[("wr", c)] = P.op(
                    "tensor",
                    lambda eng: eng.matmul(p_msc[0:8, 0:TCB], s_wr[:],
                                           s_cont[0:64, :],
                                           start=True, stop=True),
                    waits=[E[("contev", c)], E.get(("lgtev", c - 1)),
                           E.get((("lnO", c - 1), "bcev2"))], inc=tq)
                E[("lgtev", c)] = P.op(
                    "vector",
                    lambda eng: eng.tensor_scalar(
                        s_lgt[:, :], p_msc[0:8, 0:TCB], s_br[:, 0:1],
                        None, ALU.add),
                    waits=[E[("wr", c)], E.get(("tp", c - 1, 1))], inc=vq)
                E[("lgtev", c)] = vfence()
                for i in range(2):
                    E[("tp", c, i)] = P.op(
                        "tensor",
                        lambda eng, i=i: eng.transpose(
                            p_msc[:, 256 + i * 8:256 + (i + 1) * 8],
                            s_lgt[:, i * 128:(i + 1) * 128],
                            s_ident[0:8, 0:8]),
                        waits=[E[("lgtev", c)], E.get(("gtev", c - 1))], inc=tq)
                E[("gtev", c)] = P.op(
                    "vector",
                    lambda eng: eng.tensor_copy(
                        s_gt[:, :, :].rearrange("p a b -> p (a b)"),
                        p_msc[:, 256:272]),
                    waits=[E[("tp", c, 0)], E[("tp", c, 1)],
                           E.get(("gdone", c - 1))], inc=vq)

                def gop(fn, waits=(), engname="vector"):
                    return P.op(engname, fn, waits=waits,
                                inc=(vq if engname == "vector" else aq))

                ev0 = E[("gtev", c)]
                evm = gop(lambda eng: eng.tensor_reduce(
                    s_gm[:, :, 0:1], s_gt[:, :, :], mybir.AxisListType.X,
                    ALU.max, negate=True), waits=[ev0])
                evm = vfence()
                eve1 = gop(lambda eng: eng.activation(
                    s_gt[:, 0, :], s_gt[:, 0, :], AF.Exp,
                    bias=s_gm[:, 0, 0:1]), engname="scalar",
                    waits=[(evm[0], evm[1], True)])
                eve2 = gop(lambda eng: eng.activation(
                    s_gt[:, 1, :], s_gt[:, 1, :], AF.Exp,
                    bias=s_gm[:, 1, 0:1]), engname="scalar")
                evm1 = gop(lambda eng: eng.tensor_reduce(
                    s_gm[:, :, 0:1], s_gt[:, :, :], mybir.AxisListType.X,
                    ALU.max), waits=[eve1, eve2])
                ev = (evm1[0], evm1[1], True)
                for i in range(2):
                    ev = gop(lambda eng, i=i: eng.tensor_scalar(
                        s_gtmp[:, i, :], s_gt[:, i, :], s_gm[:, i, 0:1], None,
                        ALU.is_equal), waits=[ev])
                    ev = gop(lambda eng, i=i: eng.scalar_tensor_tensor(
                        s_gtmp[:, i, :], s_gtmp[:, i, :], s_cst[:, 5:6],
                        s_gt[:, i, :], ALU.mult, ALU.add))
                evm2 = gop(lambda eng: eng.tensor_reduce(
                    s_gm[:, :, 3:4], s_gtmp[:, :, :], mybir.AxisListType.X,
                    ALU.max), waits=[ev])
                ev = (evm2[0], evm2[1], True)
                for i in range(2):
                    ev = gop(lambda eng, i=i: eng.tensor_scalar(
                        s_gtmp[:, i, :], s_gt[:, i, :], s_gm[:, i, 0:1], None,
                        ALU.is_equal), waits=[ev])
                    ev = gop(lambda eng, i=i: eng.tensor_scalar(
                        s_gtmp[:, i, :], s_gtmp[:, i, :], s_gm[:, i, 0:1],
                        None, ALU.mult))
                    ev = gop(lambda eng, i=i: eng.tensor_scalar(
                        s_gt[:, i, :], s_gt[:, i, :], s_gm[:, i, 3:4], None,
                        ALU.is_equal))
                    ev = gop(lambda eng, i=i: eng.scalar_tensor_tensor(
                        s_gt[:, i, :], s_gt[:, i, :], s_gm[:, i, 3:4],
                        s_gtmp[:, i, :], ALU.mult, ALU.add))
                evdn = gop(lambda eng: eng.tensor_add(
                    s_gm[:, :, 0:1], s_gm[:, :, 0:1], s_gm[:, :, 3:4]),
                    waits=[ev])
                evrg = gop(lambda eng: eng.reciprocal(
                    s_gm[:, :, 2:3], s_gm[:, :, 0:1]),
                    waits=[(evdn[0], evdn[1], True)])
                ev = (evrg[0], evrg[1], True)
                for i in range(2):
                    ev = gop(lambda eng, i=i: eng.tensor_scalar(
                        s_gt[:, i, :], s_gt[:, i, :], s_gm[:, i, 2:3], None,
                        ALU.mult), waits=[ev])
                ev_gate = vfence()
                E[("gdone", c)] = ev_gate
                for i in range(2):
                    E[("tpb", c, i)] = P.op(
                        "tensor",
                        lambda eng, i=i: eng.transpose(
                            p_msc[0:8, 256 + i * 128:256 + (i + 1) * 128],
                            s_gt[:, i, :], s_ident[:]),
                        waits=[ev_gate, E.get(("growev", c - 1))], inc=tq)
                E[("growev", c)] = P.op(
                    "scalar",
                    lambda eng: eng.activation(s_grow[:, :],
                                               p_msc[0:8, 256:512], AF.Copy),
                    waits=[E[("tpb", c, 0)], E[("tpb", c, 1)]], inc=aq)
                if debug == "full":
                    P.dma("sync", dbg_gate[:, c * TCB:(c + 1) * TCB],
                          s_grow[:, :].bitcast(F32), oo[0],
                          waits=[E[("growev", c)]])
                for ex in range(NE):
                    E[("ge", c, ex)] = P.op(
                        "tensor",
                        lambda eng, ex=ex: eng.matmul(
                            p_ge[:, :], s_sel8[:, ex, :], s_grow[:, :],
                            start=True, stop=True),
                        waits=[E[("growev", c)],
                               (E.get(("eotmp", c, ex - 1)) if ex >= 1
                                else E.get(("eotmp", c - 1, NE - 1)))],
                        inc=tq)
                    base = 64 * (ex % 2)
                    for mt in range(NE):
                        E[("w1m", c, ex, mt)] = P.op(
                            "tensor",
                            lambda eng, ex=ex, mt=mt, base=base: eng.matmul(
                                p_mm[:, mt % 2, 0:TCB],
                                s_w1[base:base + 64, ex // 2, mt, :],
                                s_cont[base:base + 64, :],
                                start=True, stop=True),
                            waits=[E[("contev", c)],
                                   (ev_wffn if c == 0 and ex == 0 and mt == 0
                                    else None),
                                   (E.get(("he", c, ex, mt - 2)) if mt >= 2
                                    else (E.get(("he", c, ex - 1, NE - 2 + mt))
                                          if ex >= 1
                                          else E.get(("he", c - 1, NE - 1,
                                                      NE - 2 + mt))))],
                            inc=tq)
                        E[("he", c, ex, mt)] = P.op(
                            "scalar",
                            lambda eng, ex=ex, mt=mt: eng.activation(
                                s_he[:, mt, :], p_mm[:, mt % 2, 0:TCB],
                                AF.Relu, bias=s_b1[:, ex, mt:mt + 1]),
                            waits=[E[("w1m", c, ex, mt)],
                                   (E.get(("w2m", c, ex - 1, NE - 1))
                                    if mt < 2 and ex >= 1 else None),
                                   (E.get(("w2m", c - 1, NE - 1, NE - 1))
                                    if mt < 2 and ex == 0 else None)],
                            inc=aq)
                    for kt in range(NE):
                        E[("w2m", c, ex, kt)] = P.op(
                            "tensor",
                            lambda eng, ex=ex, kt=kt: eng.matmul(
                                p_ye[:, :], s_w2[:, kt, ex, :],
                                s_he[:, kt, :],
                                start=(kt == 0), stop=(kt == NE - 1)),
                            waits=[E[("he", c, ex, kt)],
                                   (ev_wffn if c == 0 and ex == 0 and kt == 0
                                    else None),
                                   ((E.get(("yeev", c, ex - 1)) if ex >= 1
                                     else E.get(("yeev", c - 1, NE - 1)))
                                    if kt == 0 else None)],
                            inc=tq)
                    E[("yeev", c, ex)] = P.op(
                        "vector",
                        lambda eng, ex=ex: eng.tensor_scalar(
                            s_ye[:, :], p_ye[:, :], s_b2[:, ex:ex + 1],
                            None, ALU.add),
                        waits=[E[("w2m", c, ex, NE - 1)],
                               (E.get(("eotmp", c, ex - 1)) if ex >= 1
                                else E.get(("eotmp", c - 1, NE - 1)))],
                        inc=vq)
                    E[("eotmp", c, ex)] = P.op(
                        "vector",
                        lambda eng, ex=ex: eng.tensor_mul(
                            s_eo[:, 1, :], s_ye[:, :], p_ge[:, :]),
                        waits=[E[("yeev", c, ex)], E[("ge", c, ex)],
                               E.get(("eoev", c, ex - 1))], inc=vq)
                    if ex == 0:
                        E[("eoev", c, ex)] = P.op(
                            "vector",
                            lambda eng: eng.tensor_copy(
                                s_eo[:, 0, :], s_eo[:, 1, :].bitcast(F32)),
                            waits=[E.get(("c2s", c - 1, NKT - 1))], inc=vq)
                    else:
                        E[("eoev", c, ex)] = P.op(
                            "vector",
                            lambda eng: eng.tensor_add(
                                s_eo[:, 0, :], s_eo[:, 0, :].bitcast(F32),
                                s_eo[:, 1, :].bitcast(F32)),
                            inc=vq)
                if debug == "full":
                    P.dma("sync", dbg_eo[:, c * TCB:(c + 1) * TCB],
                          s_eo[:, 0, :].bitcast(F32), oo[0],
                          waits=[E[("eoev", c, NE - 1)]])
                E[("eoev", c, NE - 1)] = vfence()
                for mt in range(NKT):
                    E[("c2s", c, mt)] = P.op(
                        "tensor",
                        lambda eng, mt=mt: eng.matmul(
                            p_mm[:, mt % 2, 0:TCB],
                            s_wc2s[:, mt * 128:(mt + 1) * 128],
                            s_eo[:, 0, :],
                            start=True, stop=True),
                        waits=[E[("eoev", c, NE - 1)],
                               (ev_wffn if c == 0 and mt == 0 else None),
                               (E.get(("sp2", c, mt - 2)) if mt >= 2
                                else E.get(("decev", c - 1, NMT - 2 + mt)))],
                        inc=tq)
                    E[("sp2", c, mt)] = P.op(
                        "scalar",
                        lambda eng, mt=mt: eng.activation(
                            s_sp2[:, mt, :], p_mm[:, mt % 2, 0:TCB],
                            AF.Sigmoid, bias=s_bc2s[:, mt:mt + 1]),
                        waits=[E[("c2s", c, mt)]], inc=aq)
                for mt in range(NMT):
                    for kt in range(NKT):
                        E[("mmD", c, mt, kt)] = P.op(
                            "tensor",
                            lambda eng, mt=mt, kt=kt: eng.matmul(
                                p_mm[:, mt % 2, 0:TCB],
                                s_wdec[:, kt, mt * 128:(mt + 1) * 128],
                                s_sp2[:, kt, :],
                                start=(kt == 0), stop=(kt == NKT - 1)),
                            waits=[E[("sp2", c, kt)],
                                   ((E.get(("decev", c, mt - 2)) if mt >= 2
                                     else E.get(("sp2", c, NKT - 2 + mt)))
                                    if kt == 0 else None)],
                            inc=tq)
                    E[("decev", c, mt)] = P.op(
                        "vector",
                        lambda eng, mt=mt: eng.tensor_scalar(
                            s_dec[:, mt, :], p_mm[:, mt % 2, 0:TCB],
                            s_bdec[:, mt:mt + 1], None, ALU.add),
                        waits=[E[("mmD", c, mt, NKT - 1)],
                               E.get((("lnO", c - 1), "app1", mt))],
                        inc=vq)
                E[("msc_free", c)] = E[("growev", c)]
                tg2 = ("lnF", c)
                layer_norm(
                    tg2, lambda i: s_dec[:, i, :],
                    lambda i, c=c: E[("decev", c, i)],
                    NMT, TCB,
                    (lambda i: s_lng[:, i:i + 1], lambda i: s_lnb[:, i:i + 1]),
                    lambda i: s_dec[:, i, :],
                    lambda i: None)
                if debug == "full":
                    for mt in range(NMT):
                        P.dma("sync", dbg_ffn[mt * 128:(mt + 1) * 128,
                                              c * TCB:(c + 1) * TCB],
                              s_dec[:, mt, :].bitcast(F32), oo[0],
                              waits=[E[(tg2, "app2", mt)]])
                for mt in range(NMT):
                    E[("x4", c, mt)] = P.op(
                        "vector",
                        lambda eng, mt=mt: eng.tensor_add(
                            s_dec[:, mt, :], s_dec[:, mt, :].bitcast(F32),
                            s_xnf[:, mt, :]),
                        waits=[E[(tg2, "app2", mt)], E[("lxnf", c)],
                               ((oo[0], P.counts.get("oo0", 0))
                                if debug == "full" else None)],
                        inc=vq)
                E[("x4f", c)] = vfence()
                E[("B", c, "dece_war")] = E[("x4f", c)]
                tg3 = ("lnO", c)
                layer_norm(
                    tg3, lambda i: s_dec[:, i, :],
                    lambda i, c=c: E[("x4f", c)],
                    NMT, TCB, None,
                    lambda i: s_out[:, i, :],
                    lambda i, c=c: E.get(("odma", c - 1, NMT - 1)))
                for mt in range(NMT):
                    E[("odma", c, mt)] = P.dma(
                        "sync", out_o[mt * 128:(mt + 1) * 128,
                                      c * TCB:(c + 1) * TCB],
                        s_out[:, mt, :], oo[c % 2],
                        waits=[E[(tg3, "app2", mt)]])

            allsems = [ld, lwA, lwB] + lyz + lxt + lxn + oxn + oo
            P.op("gpsimd", lambda eng: eng.memset(s_cst[:, 7:8], 0.0),
                 waits=[(sm, P.counts[sm.name]) for sm in allsems
                        if sm.name in P.counts] +
                       [(vq, P.counts["vq"]), (aq, P.counts["aq"]),
                        (tq, P.counts["tq"])],
                 inc=ld, inc_by=1)

            P.emit(blk)
    return nc


def _l3_host_inputs(inputs, res1, res2):
    out_proj_w = np.asarray(inputs["L0_attn_out_proj_w"])
    W_enc = np.asarray(inputs["L0_ffn_W_enc"])
    W_s2c = np.asarray(inputs["L0_ffn_W_s2c"])
    Wr = np.asarray(inputs["L0_ffn_Wr"])
    W1 = np.asarray(inputs["L0_ffn_W1"])
    W2 = np.asarray(inputs["L0_ffn_W2"])
    W_c2s = np.asarray(inputs["L0_ffn_W_c2s"])
    W_dec = np.asarray(inputs["L0_ffn_W_dec"])

    outw = np.ascontiguousarray(
        out_proj_w.T.reshape(NKT, 128, D).transpose(1, 0, 2))
    wenc = np.ascontiguousarray(W_enc.reshape(NMT, 128, HS).transpose(1, 0, 2))
    wdec = np.ascontiguousarray(W_dec.reshape(NKT, 128, D).transpose(1, 0, 2))
    ws2c = np.ascontiguousarray(W_s2c.reshape(NKT, 128, CM).transpose(1, 0, 2))
    w1 = np.zeros((128, NE // 2, NE, 128), dtype=np.float32)
    for ex in range(NE):
        base = 64 * (ex % 2)
        w1[base:base + 64, ex // 2] = W1[ex].reshape(CM, NE, 128).transpose(
            0, 1, 2)[:, :, :]
    w2 = np.ascontiguousarray(
        W2.reshape(NE, 8, 128, CM).transpose(2, 1, 0, 3))
    wc2s = np.ascontiguousarray(W_c2s)
    benc = np.ascontiguousarray(
        np.asarray(inputs["L0_ffn_b_enc"]).reshape(NKT, 128).T)
    bs2c = np.asarray(inputs["L0_ffn_b_s2c"]).reshape(CM, 1).astype(np.float32)
    br = np.asarray(inputs["L0_ffn_br"]).reshape(8, 1).astype(np.float32)
    b1 = np.ascontiguousarray(
        np.asarray(inputs["L0_ffn_b1"]).reshape(NE, NE, 128).transpose(2, 0, 1))
    b2 = np.ascontiguousarray(np.asarray(inputs["L0_ffn_b2"]).T)
    bc2s = np.ascontiguousarray(
        np.asarray(inputs["L0_ffn_b_c2s"]).reshape(NKT, 128).T)
    bdec = np.ascontiguousarray(
        np.asarray(inputs["L0_ffn_b_dec"]).reshape(NMT, 128).T)
    lng = np.ascontiguousarray(
        np.asarray(inputs["L0_ffn_ln_g"]).reshape(NMT, 128).T)
    lnb = np.ascontiguousarray(
        np.asarray(inputs["L0_ffn_ln_b"]).reshape(NMT, 128).T)
    sel8 = np.zeros((8, NE, CM), dtype=np.float32)
    for ex in range(NE):
        sel8[ex, ex, :] = 1.0
    ident = np.eye(128, dtype=np.float32)
    ones128 = np.ones((128, 2), dtype=np.float32)
    ones1 = np.ones((1, 128), dtype=np.float32)
    cstv = np.zeros((128, 8), dtype=np.float32)
    cstv[:, 0] = EPS
    cstv[:, 1] = 1.0 / D
    cstv[:, 2] = 1.0
    cstv[:, 3] = -1.0
    cstv[:, 4] = -0.5
    cstv[:, 5] = -1e30

    x = np.ascontiguousarray(
        np.asarray(inputs["L0_attn_x"])[0].T).astype(np.float32, copy=False)
    y_sl = [res2.results[k]["y_o"] for k in range(NCORES)]
    z_full = [res1.results[k]["z_o"][:, HALO:] for k in range(NCORES)]

    maps = []
    for k in range(NCORES):
        y_st = np.ascontiguousarray(
            np.concatenate([y_sl[j][:, k * S:(k + 1) * S]
                            for j in range(NCORES)], axis=0))
        xr_k = np.ascontiguousarray(
            x[:, k * S:(k + 1) * S].reshape(NMT, 128, S).transpose(1, 0, 2))
        maps.append({
            "y_i": y_st, "z_i": np.ascontiguousarray(z_full[k]), "xr": xr_k,
            "outw": outw, "wenc": wenc, "wdec": wdec, "ws2c": ws2c,
            "wr_": Wr.astype(np.float32), "w1": w1, "w2": w2, "wc2s": wc2s,
            "benc": benc, "bs2c": bs2c, "br_": br, "b1": b1, "b2": b2,
            "bc2s": bc2s, "bdec": bdec, "lng": lng, "lnb": lnb,
            "sel8": sel8, "ident": ident, "ones128": ones128, "ones1": ones1,
            "cst": cstv,
        })
    return maps


def kernel(**inputs):
    l1_maps = _l1_host_inputs(inputs)
    nc1 = build_L1()
    res1 = run_bass_kernel_spmd(nc1, l1_maps, core_ids=list(range(NCORES)))
    l2_maps = _l2_host_inputs(inputs, res1)
    nc2 = build_L2()
    res2 = run_bass_kernel_spmd(nc2, l2_maps, core_ids=list(range(NCORES)))
    l3_maps = _l3_host_inputs(inputs, res1, res2)
    nc3 = build_L3()
    res3 = run_bass_kernel_spmd(nc3, l3_maps, core_ids=list(range(NCORES)))
    out = np.concatenate([res3.results[k]["out_o"] for k in range(NCORES)],
                         axis=1)
    return np.ascontiguousarray(out.T).reshape(1, T, D)


if __name__ == "__main__":
    pass


# version 07
# speedup vs baseline: 1.1915x; 1.0124x over previous
import numpy as np
from contextlib import ExitStack

import concourse.bass as bass
import concourse.mybir as mybir
from concourse.bass_utils import run_bass_kernel_spmd

F32 = mybir.dt.float32
F32R = mybir.dt.float32r
AF = mybir.ActivationFunctionType
ALU = mybir.AluOpType

NCORES = 8
T = 16384
S = T // NCORES
D = 1024
DI = 2048
DTR = 64
NST = 16
DSL = DI // NCORES
EPS = 1e-5
HALO = 3
NE = 8
HS = 2048
CM = 64
EH = 1024

SH = S + HALO
CH = [512, 512, 512, 512, 4]
CO = [0, 512, 1024, 1536, 2047]
NG = D // 128
NM = 2 * DI // 128
NDT = DI // 128


class Prog:

    def __init__(self, nc):
        self.nc = nc
        self.ops = {k: [] for k in
                    ("sync", "vector", "scalar", "tensor", "gpsimd")}
        self.counts = {}
        self.base_waits = []

    def op(self, engine, fn, waits=(), inc=None, inc_by=1):
        ev = None
        if inc is not None:
            key = inc.name
            self.counts[key] = self.counts.get(key, 0) + inc_by
            ev = (inc, self.counts[key])
        allw = list(self.base_waits) + [w for w in waits if w is not None]
        self.ops[engine].append((fn, allw,
                                 (inc, inc_by) if inc is not None else None))
        return ev

    def dma(self, engine, out, in_, sem, waits=()):
        return self.op(engine, lambda eng, o=out, i=in_: eng.dma_start(out=o, in_=i),
                       waits=waits, inc=sem, inc_by=16)

    def emit(self, block):
        def mk(name):
            def body(eng):
                hi = {}
                for fn, waits, inc in self.ops[name]:
                    for wt in waits:
                        sem, val = wt[0], wt[1]
                        force = len(wt) > 2 and wt[2]
                        if not force and hi.get(sem.name, -1) >= val:
                            continue
                        eng.wait_ge(sem, val)
                        hi[sem.name] = max(hi.get(sem.name, -1), val)
                    ins = fn(eng)
                    if inc is not None:
                        ins.then_inc(inc[0], inc[1])
            return body
        block.sync(mk("sync"))
        block.vector(mk("vector"))
        block.scalar(mk("scalar"))
        block.tensor(mk("tensor"))
        block.gpsimd(mk("gpsimd"))


def slot(c):
    return 515 * (c % 2)


def build_L1():
    nc = bass.Bass("TRN2", target_bir_lowering=False, debug=False,
                   detect_race_conditions=False)

    xh = nc.dram_tensor("xh", [128, NG, SH], F32, kind="ExternalInput").ap()
    inw = nc.dram_tensor("inw", [NM, 128, NG, 128], F32, kind="ExternalInput").ap()
    normw = nc.dram_tensor("normw", [128, NG], F32, kind="ExternalInput").ap()
    xpw = nc.dram_tensor("xpw", [128, NDT, 128], F32, kind="ExternalInput").ap()
    dtw = nc.dram_tensor("dtw", [128, DI], F32, kind="ExternalInput").ap()
    convw = nc.dram_tensor("convw", [128, NDT, 4], F32, kind="ExternalInput").ap()
    convb = nc.dram_tensor("convb", [128, NDT], F32, kind="ExternalInput").ap()
    dtb = nc.dram_tensor("dtb", [128, NDT], F32, kind="ExternalInput").ap()
    ones128 = nc.dram_tensor("ones128", [128, 2], F32, kind="ExternalInput").ap()
    ones1 = nc.dram_tensor("ones1", [1, 128], F32, kind="ExternalInput").ap()
    cst = nc.dram_tensor("cst", [128, 8], F32, kind="ExternalInput").ap()

    u_o = nc.dram_tensor("u_o", [DI, S], F32, kind="ExternalOutput").ap()
    z_o = nc.dram_tensor("z_o", [DI, SH], F32, kind="ExternalOutput").ap()
    dl_o = nc.dram_tensor("dl_o", [DI, S], F32, kind="ExternalOutput").ap()
    bc_o = nc.dram_tensor("bc_o", [64, S], F32, kind="ExternalOutput").ap()

    with ExitStack() as stk:
        e = stk.enter_context
        s_w = e(nc.sbuf_tensor("s_w", [128, 3, NG, 128], F32R))
        s_x = e(nc.sbuf_tensor("s_x", [128, 2, NG, 512], F32R))
        s_normw = e(nc.sbuf_tensor("s_normw", [128, NG], F32))
        s_xpw = e(nc.sbuf_tensor("s_xpw", [128, NDT, 128], F32R))
        s_dtw = e(nc.sbuf_tensor("s_dtw", [128, DI], F32R))
        s_convw = e(nc.sbuf_tensor("s_convw", [128, NDT, 4], F32))
        s_convb = e(nc.sbuf_tensor("s_convb", [128, NDT], F32))
        s_dtb = e(nc.sbuf_tensor("s_dtb", [128, NDT], F32))
        s_o128 = e(nc.sbuf_tensor("s_o128", [128, 2], F32R))
        s_o1 = e(nc.sbuf_tensor("s_o1", [1, 128], F32R))
        s_cst = e(nc.sbuf_tensor("s_cst", [128, 8], F32))
        s_sq = e(nc.sbuf_tensor("s_sq", [128, 2, 512], F32R))
        s_rstd = e(nc.sbuf_tensor("s_rstd", [1, 512], F32R))
        s_rstdb = e(nc.sbuf_tensor("s_rstdb", [128, 2, 512], F32))
        s_xp = e(nc.sbuf_tensor("s_xp", [128, NDT, 1030], F32))
        s_z = e(nc.sbuf_tensor("s_z", [128, 2, 512], F32))
        s_xc = e(nc.sbuf_tensor("s_xc", [128, 2, 512], F32))
        s_u = e(nc.sbuf_tensor("s_u", [128, 3, 512], F32R))
        s_xdbl = e(nc.sbuf_tensor("s_xdbl", [128, 512], F32R))
        s_sp = e(nc.sbuf_tensor("s_sp", [128, 2, 512], F32))
        s_dl = e(nc.sbuf_tensor("s_dl", [128, 2, 512], F32))
        p_mm = e(nc.psum_tensor("p_mm", [128, 2, 512], F32))
        p_ss = e(nc.psum_tensor("p_ss", [2, 512], F32))
        p_rb = e(nc.psum_tensor("p_rb", [128, 512], F32))
        p_xd = e(nc.psum_tensor("p_xd", [128, 512], F32))
        p_dt = e(nc.psum_tensor("p_dt", [128, 2, 512], F32))
        ld = e(nc.semaphore("ld"))
        lx = [e(nc.semaphore(f"lx{i}")) for i in range(2)]
        lw = [e(nc.semaphore(f"lw{i}")) for i in range(3)]
        oz = [e(nc.semaphore(f"oz{i}")) for i in range(2)]
        ou = [e(nc.semaphore(f"ou{i}")) for i in range(3)]
        od = [e(nc.semaphore(f"od{i}")) for i in range(2)]
        ob = e(nc.semaphore("ob"))
        tq = e(nc.semaphore("tq"))
        vq = e(nc.semaphore("vq"))
        aq = e(nc.semaphore("aq"))
        blk = e(nc.Block())

        P = Prog(nc)

        ev_c = None
        for dst, src in [(s_normw[:], normw), (s_xpw[:], xpw.bitcast(F32R)),
                         (s_dtw[:], dtw.bitcast(F32R)), (s_convw[:], convw),
                         (s_convb[:], convb), (s_dtb[:], dtb),
                         (s_o128[:], ones128.bitcast(F32R)),
                         (s_o1[:], ones1.bitcast(F32R)), (s_cst[:], cst)]:
            ev_c = P.dma("sync", dst, src, ld)
        P.base_waits = [ev_c]

        E = {}

        def prev_ring(kind, i, depth):
            return E.get((kind, i - depth))

        def chunk_section(c):
            w_ = CH[c]
            E[("xload", c)] = P.dma(
                "sync", s_x[:, c % 2, :, 0:w_],
                xh[:, :, CO[c]:CO[c] + w_].bitcast(F32R), lx[c % 2],
                waits=[E.get(("mm", (c - 2) * NM + NM - 1, NG - 1)),
                       E.get(("xscale", c - 2, NG - 1))])

            for g in range(NG):
                gg = c * NG + g
                E[("sq", c, g)] = P.op(
                    "scalar",
                    lambda eng, g=g, w_=w_, c=c: eng.activation(
                        s_sq[:, (c * NG + g) % 2, 0:w_],
                        s_x[:, c % 2, g, 0:w_].bitcast(F32), AF.Square),
                    waits=[E[("xload", c)], ev_c, prev_ring("stats", gg, 2)],
                    inc=aq)
                E[("stats", gg)] = P.op(
                    "tensor",
                    lambda eng, g=g, w_=w_, c=c: eng.matmul(
                        p_ss[:, 0:w_], s_o128[:],
                        s_sq[:, (c * NG + g) % 2, 0:w_],
                        start=(g == 0), stop=(g == NG - 1)),
                    waits=[E[("sq", c, g)],
                           E.get(("rstdL", c - 1)) if g == 0 else None],
                    inc=tq)
                E[("xscale", c, g)] = P.op(
                    "vector",
                    lambda eng, g=g, w_=w_, c=c: eng.tensor_scalar_mul(
                        s_x[:, c % 2, g, 0:w_],
                        s_x[:, c % 2, g, 0:w_].bitcast(F32),
                        s_normw[:, g:g + 1]),
                    waits=[E[("sq", c, g)]], inc=vq)

            E[("rstdL", c)] = P.op(
                "scalar",
                lambda eng, w_=w_: eng.activation(
                    s_rstd[0:1, 0:w_], p_ss[0:1, 0:w_],
                    AF.Ln, bias=s_cst[0:1, 0:1], scale=s_cst[0:1, 1:2]),
                waits=[E[("stats", c * NG + NG - 1)]], inc=aq)
            E[("rstd", c)] = P.op(
                "scalar",
                lambda eng, w_=w_: eng.activation(
                    s_rstd[0:1, 0:w_],
                    s_rstd[0:1, 0:w_].bitcast(F32), AF.Exp,
                    scale=s_cst[0:1, 4:5]),
                inc=aq)
            E[("rb", c)] = P.op(
                "tensor",
                lambda eng, w_=w_: eng.matmul(
                    p_rb[:, 0:w_], s_o1[:], s_rstd[0:1, 0:w_],
                    start=True, stop=True),
                waits=[E[("rstd", c)], E.get(("rstdb", c - 1))], inc=tq)
            E[("rstdb", c)] = P.op(
                "scalar",
                lambda eng, w_=w_, c=c: eng.activation(
                    s_rstdb[:, c % 2, 0:w_], p_rb[:, 0:w_], AF.Copy),
                waits=[E[("rb", c)], E.get(("evac", c - 2, NM - 1))], inc=aq)

            for m in range(NM):
                k = c * NM + m
                E[("wload", k)] = P.dma(
                    "gpsimd", s_w[:, k % 3, :, :], inw[m, :, :, :].bitcast(F32R),
                    lw[k % 3],
                    waits=[E.get(("mm", k - 3, NG - 1))])
                for g in range(NG):
                    bank_war = None
                    if g == 0:
                        bank_war = E.get(("evac", (k - 2) // NM, (k - 2) % NM)) \
                            if k - 2 >= 0 else None
                    E[("mm", k, g)] = P.op(
                        "tensor",
                        lambda eng, m=m, g=g, w_=w_, c=c, k=k: eng.matmul(
                            p_mm[:, m % 2, 0:w_],
                            s_w[:, k % 3, g, :],
                            s_x[:, c % 2, g, 0:w_],
                            start=(g == 0), stop=(g == NG - 1)),
                        waits=[E[("wload", k)], E[("xscale", c, g)], bank_war],
                        inc=tq)
                emm = E[("mm", k, NG - 1)]
                if m < NDT:
                    last = None
                    if c < 4:
                        last = P.op(
                            "vector",
                            lambda eng, m=m, w_=w_, c=c: eng.tensor_mul(
                                s_xp[:, m, slot(c):slot(c) + w_],
                                p_mm[:, m % 2, 0:w_], s_rstdb[:, c % 2, 0:w_]),
                            waits=[emm, E[("rstdb", c)],
                                   E.get(("conv", c - 2, m))],
                            inc=vq)
                    if c >= 1:
                        tl = 1 if c == 4 else 0
                        last = P.op(
                            "vector",
                            lambda eng, m=m, c=c, tl=tl: eng.tensor_mul(
                                s_xp[:, m, slot(c - 1) + 512:slot(c - 1) + 515],
                                p_mm[:, m % 2, tl:tl + 3],
                                s_rstdb[:, c % 2, tl:tl + 3]),
                            waits=[emm, E[("rstdb", c)],
                                   E.get(("conv", c - 3, m))],
                            inc=vq)
                        E[("tail", c, m)] = last
                    E[("evac", c, m)] = last
                else:
                    mz = m - NDT
                    E[("evac", c, m)] = P.op(
                        "vector",
                        lambda eng, m=m, mz=mz, w_=w_, c=c: eng.tensor_mul(
                            s_z[:, mz % 2, 0:w_],
                            p_mm[:, m % 2, 0:w_], s_rstdb[:, c % 2, 0:w_]),
                        waits=[emm, E[("rstdb", c)],
                               prev_ring("zdma", c * NDT + mz, 2)],
                        inc=vq)
                    zlo = 1 if c == 4 else 0
                    E[("zdma", c * NDT + mz)] = P.dma(
                        "sync", z_o[mz * 128:(mz + 1) * 128,
                                    CO[c] + zlo:CO[c] + w_],
                        s_z[:, mz % 2, zlo:w_], oz[mz % 2],
                        waits=[E[("evac", c, m)]])

        def out_section(j):
            for g in range(NDT):
                ug = j * NDT + g
                first = P.op(
                    "vector",
                    lambda eng, g=g, j=j: eng.tensor_scalar_mul(
                        s_xc[:, g % 2, :],
                        s_xp[:, g, slot(j) + 0:slot(j) + 512],
                        s_convw[:, g:g + 1, 0]),
                    waits=[E.get(("evac", j, g)), E.get(("tail", j + 1, g)),
                           prev_ring("silu", ug, 2)],
                    inc=vq)
                for jj in (1, 2, 3):
                    last = P.op(
                        "vector",
                        lambda eng, g=g, j=j, jj=jj: eng.scalar_tensor_tensor(
                            s_xc[:, g % 2, :],
                            s_xp[:, g, slot(j) + jj:slot(j) + jj + 512],
                            s_convw[:, g:g + 1, jj],
                            s_xc[:, g % 2, :],
                            ALU.mult, ALU.add),
                        inc=vq)
                E[("conv", j, g)] = last
                E[("silu", ug)] = P.op(
                    "scalar",
                    lambda eng, g=g, ug=ug: eng.activation(
                        s_u[:, ug % 3, :],
                        s_xc[:, g % 2, :], AF.Silu,
                        bias=s_convb[:, g:g + 1]),
                    waits=[E[("conv", j, g)],
                           prev_ring("xproj", ug, 3), prev_ring("udma", ug, 3)],
                    inc=aq)
                E[("udma", ug)] = P.dma(
                    "sync", u_o[g * 128:(g + 1) * 128, j * 512:(j + 1) * 512],
                    s_u[:, ug % 3, :].bitcast(F32), ou[ug % 3],
                    waits=[E[("silu", ug)]])
                E[("xproj", ug)] = P.op(
                    "tensor",
                    lambda eng, g=g, ug=ug: eng.matmul(
                        p_xd[:, :], s_xpw[:, g, :], s_u[:, ug % 3, :],
                        start=(g == 0), stop=(g == NDT - 1)),
                    waits=[E[("silu", ug)],
                           E.get(("xdev", j - 1)) if g == 0 else None],
                    inc=tq)
            E[("xdev", j)] = P.op(
                "scalar",
                lambda eng: eng.activation(s_xdbl[:, :],
                                           p_xd[:, :], AF.Copy),
                waits=[E[("xproj", j * NDT + NDT - 1)], E.get(("bcdma", j - 1)),
                       E.get(("dt", j - 1, NDT - 1))],
                inc=aq)
            E[("bcdma", j)] = P.dma(
                "sync", bc_o[0:48, j * 512:(j + 1) * 512],
                s_xdbl[0:48, :].bitcast(F32), ob, waits=[E[("xdev", j)]])
            for mt in range(NDT):
                mtg = j * NDT + mt
                E[("dt", j, mt)] = P.op(
                    "tensor",
                    lambda eng, mt=mt: eng.matmul(
                        p_dt[:, mt % 2, :],
                        s_dtw[64:128, mt * 128:(mt + 1) * 128],
                        s_xdbl[64:128, :], start=True, stop=True),
                    waits=[E[("xdev", j)], prev_ring("relu", mtg, 2)],
                    inc=tq)
                P.op("scalar",
                     lambda eng, mt=mt: eng.activation(
                         s_sp[:, mt % 2, :], p_dt[:, mt % 2, :], AF.Abs,
                         bias=s_dtb[:, mt:mt + 1]),
                     waits=[E[("dt", j, mt)], prev_ring("dladd", mtg, 2)],
                     inc=aq)
                P.op("scalar",
                     lambda eng, mt=mt: eng.activation(
                         s_sp[:, mt % 2, :], s_sp[:, mt % 2, :], AF.Exp,
                         scale=s_cst[:, 3:4]),
                     inc=aq)
                P.op("scalar",
                     lambda eng, mt=mt: eng.activation(
                         s_sp[:, mt % 2, :], s_sp[:, mt % 2, :], AF.Ln,
                         bias=s_cst[:, 2:3]),
                     inc=aq)
                E[("relu", mtg)] = P.op(
                    "scalar",
                    lambda eng, mt=mt: eng.activation(
                        s_dl[:, mt % 2, :], p_dt[:, mt % 2, :], AF.Relu,
                        bias=s_dtb[:, mt:mt + 1]),
                    waits=[prev_ring("dldma", mtg, 2)],
                    inc=aq)
                E[("dladd", mtg)] = P.op(
                    "vector",
                    lambda eng, mt=mt: eng.tensor_add(
                        s_dl[:, mt % 2, :], s_dl[:, mt % 2, :],
                        s_sp[:, mt % 2, :]),
                    waits=[E[("relu", mtg)]], inc=vq)
                E[("dldma", mtg)] = P.dma(
                    "sync", dl_o[mt * 128:(mt + 1) * 128, j * 512:(j + 1) * 512],
                    s_dl[:, mt % 2, :], od[mt % 2],
                    waits=[E[("dladd", mtg)]])

        for c in range(5):
            chunk_section(c)
            if c >= 1:
                out_section(c - 1)

        allsems = [ld] + lx + lw + oz + ou + od + [ob]
        P.op("gpsimd", lambda eng: eng.memset(s_o128[:].bitcast(F32), 1.0),
             waits=[(s, P.counts[s.name]) for s in allsems
                    if s.name in P.counts] +
                   [(vq, P.counts["vq"]), (aq, P.counts["aq"]),
                    (tq, P.counts["tq"])],
             inc=ld, inc_by=1)

        P.emit(blk)
    return nc


def _l1_host_inputs(inputs):
    x = np.ascontiguousarray(inputs["L0_attn_x"][0].T).astype(np.float32, copy=False)
    in_proj_w = inputs["L0_attn_in_proj_w"]
    x_proj_w = inputs["L0_attn_x_proj_w"]
    dt_proj_w = inputs["L0_attn_dt_proj_w"]
    conv_w = inputs["L0_attn_conv_w"]
    conv_b = inputs["L0_attn_conv_b"]
    dt_proj_b = inputs["L0_attn_dt_proj_b"]
    norm_w = inputs["L0_attn_norm_w"]

    inw = np.ascontiguousarray(
        np.asarray(in_proj_w).reshape(NM, 128, NG, 128).transpose(0, 3, 2, 1))
    normw = np.ascontiguousarray(np.asarray(norm_w).reshape(NG, 128).T)
    xpw = np.zeros((128, NDT, 128), dtype=np.float32)
    xpt = np.asarray(x_proj_w)
    for g in range(NDT):
        blkw = xpt[:, g * 128:(g + 1) * 128]
        xpw[:, g, 0:16] = blkw[64:80].T
        xpw[:, g, 32:48] = blkw[80:96].T
        xpw[:, g, 64:128] = blkw[0:64].T
    dtw = np.zeros((128, DI), dtype=np.float32)
    dtw[64:128, :] = np.asarray(dt_proj_w).T
    convw = np.ascontiguousarray(
        np.asarray(conv_w)[:, 0, :].reshape(NDT, 128, 4).transpose(1, 0, 2))
    convb = np.ascontiguousarray(np.asarray(conv_b).reshape(NDT, 128).T)
    dtb = np.ascontiguousarray(np.asarray(dt_proj_b).reshape(NDT, 128).T)
    ones128 = np.ones((128, 2), dtype=np.float32)
    ones1 = np.ones((1, 128), dtype=np.float32)
    cstv = np.zeros((128, 8), dtype=np.float32)
    cstv[:, 0] = EPS
    cstv[:, 1] = 1.0 / D
    cstv[:, 2] = 1.0
    cstv[:, 3] = -1.0
    cstv[:, 4] = -0.5

    maps = []
    for k in range(NCORES):
        lo = k * S - HALO
        if lo < 0:
            xh2 = np.concatenate(
                [np.zeros((D, HALO), np.float32), x[:, 0:(k + 1) * S]], axis=1)
        else:
            xh2 = x[:, lo:(k + 1) * S]
        xh2 = np.ascontiguousarray(xh2.reshape(NG, 128, SH).transpose(1, 0, 2))
        maps.append({
            "xh": xh2, "inw": inw, "normw": normw, "xpw": xpw, "dtw": dtw,
            "convw": convw, "convb": convb, "dtb": dtb,
            "ones128": ones128, "ones1": ones1, "cst": cstv,
        })
    return maps




TC2 = 1024
NC2 = T // TC2
NSUB = TC2 // 512


def build_L2():
    nc = bass.Bass("TRN2", target_bir_lowering=False, debug=False,
                   detect_race_conditions=False)

    u_i = nc.dram_tensor("u_i", [DSL, T], F32, kind="ExternalInput").ap()
    dl_i = nc.dram_tensor("dl_i", [DSL, T], F32, kind="ExternalInput").ap()
    B_i = nc.dram_tensor("B_i", [16, T], F32, kind="ExternalInput").ap()
    C_i = nc.dram_tensor("C_i", [16, T], F32, kind="ExternalInput").ap()
    alog = nc.dram_tensor("alog", [128, 2, NST], F32, kind="ExternalInput").ap()
    dvec = nc.dram_tensor("dvec", [128, 2], F32, kind="ExternalInput").ap()
    sel = nc.dram_tensor("sel", [16, 16 * 128], F32, kind="ExternalInput").ap()
    ident = nc.dram_tensor("ident", [128, 128], F32, kind="ExternalInput").ap()
    cst = nc.dram_tensor("cst", [128, 8], F32, kind="ExternalInput").ap()

    y_o = nc.dram_tensor("y_o", [DSL, T], F32, kind="ExternalOutput").ap()

    with ExitStack() as stk:
        e = stk.enter_context
        s_u = e(nc.sbuf_tensor("s_u", [128, 2, 3, TC2], F32))
        s_dl = e(nc.sbuf_tensor("s_dl", [128, 2, 3, TC2], F32))
        s_du = e(nc.sbuf_tensor("s_du", [128, 2, 3, TC2], F32))
        s_B = e(nc.sbuf_tensor("s_B", [16, 2, TC2], F32R))
        s_C = e(nc.sbuf_tensor("s_C", [16, 2, TC2], F32R))
        s_sel = e(nc.sbuf_tensor("s_sel", [16, 16 * 128], F32R))
        s_ident = e(nc.sbuf_tensor("s_ident", [128, 128], F32R))
        s_alog = e(nc.sbuf_tensor("s_alog", [128, 2, NST], F32))
        s_A = e(nc.sbuf_tensor("s_A", [128, 2, NST], F32))
        s_D = e(nc.sbuf_tensor("s_D", [128, 2], F32))
        s_cst = e(nc.sbuf_tensor("s_cst", [128, 8], F32))
        s_dA = e(nc.sbuf_tensor("s_dA", [128, 2, 2, TC2], F32))
        s_w = e(nc.sbuf_tensor("s_w", [128, 2, TC2], F32))
        s_h = e(nc.sbuf_tensor("s_h", [128, 2, 2, TC2], F32))
        s_p = e(nc.sbuf_tensor("s_p", [128, 2, 2, TC2], F32R))
        s_Ct = e(nc.sbuf_tensor("s_Ct", [128, 2, TC2], F32))
        s_carry = e(nc.sbuf_tensor("s_carry", [128, 2, NST], F32))
        s_y = e(nc.sbuf_tensor("s_y", [128, 2, TC2], F32))
        p_B = e(nc.psum_tensor("p_B", [128, 2, 512], F32))
        p_C = e(nc.psum_tensor("p_C", [128, 2, 512], F32))
        p_y = e(nc.psum_tensor("p_y", [128, 2, NSUB, 512], F32))
        ld = e(nc.semaphore("ld"))
        ldl = [e(nc.semaphore(f"ldl{i}")) for i in range(3)]
        lu = [e(nc.semaphore(f"lu{i}")) for i in range(3)]
        lbc = [e(nc.semaphore(f"lbc{i}")) for i in range(2)]
        oy = [e(nc.semaphore(f"oy{i}")) for i in range(2)]
        tq = e(nc.semaphore("tq"))
        vq = e(nc.semaphore("vq"))
        aq = e(nc.semaphore("aq"))
        pq = e(nc.semaphore("pq"))
        blk = e(nc.Block())

        P = Prog(nc)
        E = {}

        ev_c = None
        for dst, src_ in [(s_sel[:], sel.bitcast(F32R)),
                          (s_ident[:], ident.bitcast(F32R)),
                          (s_alog[:], alog), (s_D[:], dvec), (s_cst[:], cst)]:
            ev_c = P.dma("sync", dst, src_, ld)
        P.base_waits = [ev_c]

        eva0 = P.op("scalar",
                    lambda eng: eng.activation(s_A[:], s_alog[:], AF.Exp),
                    waits=[ev_c], inc=aq)
        evA = P.op("vector",
                   lambda eng: eng.tensor_scalar_mul(s_A[:], s_A[:],
                                                     s_cst[:, 3:4]),
                   waits=[eva0], inc=vq)

        NTOT = NC2 * NST

        def em_load(c):
            if c >= NC2:
                return
            r3 = c % 3
            for dt2 in range(2):
                E[("lu", c, dt2)] = P.dma(
                    "sync", s_u[:, dt2, r3, :],
                    u_i[dt2 * 128:(dt2 + 1) * 128, c * TC2:(c + 1) * TC2],
                    lu[c % 3], waits=[E.get(("yev", c - 3, dt2, NSUB - 1))])
                E[("ldl", c, dt2)] = P.dma(
                    "sync", s_dl[:, dt2, r3, :],
                    dl_i[dt2 * 128:(dt2 + 1) * 128, c * TC2:(c + 1) * TC2],
                    ldl[c % 3], waits=[E.get(("dA", c - 3, dt2, NST - 1))])

        def em_loadbc(c):
            if c >= NC2:
                return
            r = c % 2
            E[("lB", c)] = P.dma(
                "sync", s_B[:, r, :],
                B_i[:, c * TC2:(c + 1) * TC2].bitcast(F32R),
                lbc[r], waits=[E.get(("bb", c - 2, NST - 1, NSUB - 1))])
            E[("lC", c)] = P.dma(
                "sync", s_C[:, r, :],
                C_i[:, c * TC2:(c + 1) * TC2].bitcast(F32R),
                lbc[r], waits=[E.get(("cb", c - 2, NST - 1, NSUB - 1))])

        def em_du(c):
            r3 = c % 3
            for dt2 in range(2):
                E[("du", c, dt2)] = P.op(
                    "vector",
                    lambda eng, dt2=dt2, r3=r3: eng.tensor_mul(
                        s_du[:, dt2, r3, :], s_dl[:, dt2, r3, :],
                        s_u[:, dt2, r3, :]),
                    waits=[E[("lu", c, 1)], E[("ldl", c, 1)],
                           E.get(("w", c - 3, dt2, NST - 1))],
                    inc=vq)

        def em_dA(c, n):
            if c >= NC2:
                return
            for dt2 in range(2):
                E[("dA", c, dt2, n)] = P.op(
                    "scalar",
                    lambda eng, dt2=dt2, n=n, c=c: eng.activation(
                        s_dA[:, dt2, (n + NST * c) % 2, :],
                        s_dl[:, dt2, c % 3, :], AF.Exp,
                        scale=s_A[:, dt2, n:n + 1]),
                    waits=[E[("ldl", c, 1)], evA,
                           E.get(("scan", c, dt2, n - 2)) or
                           E.get(("scan", c - 1, dt2, NST - 2 + (n % 2)))],
                    inc=aq)

        def em_builds(g):
            if g >= NTOT:
                return
            c, n = divmod(g, NST)
            r = c % 2
            for s in range(NSUB):
                E[("bb", c, n, s)] = P.op(
                    "tensor",
                    lambda eng, n=n, s=s, r=r: eng.matmul(
                        p_B[:, s, :],
                        s_sel[:, n * 128:(n + 1) * 128],
                        s_B[:, r, s * 512:(s + 1) * 512],
                        start=True, stop=True),
                    waits=[E[("lC", c)], E.get(("w2l", g - 1, s))],
                    inc=tq)
            for s in range(NSUB):
                E[("cb", c, n, s)] = P.op(
                    "tensor",
                    lambda eng, n=n, s=s, r=r: eng.matmul(
                        p_C[:, s, :],
                        s_sel[:, n * 128:(n + 1) * 128],
                        s_C[:, r, s * 512:(s + 1) * 512],
                        start=True, stop=True),
                    waits=[E[("lC", c)], E.get(("ctev", g - 1, s))],
                    inc=tq)

        def em_w2scan(c, n, dt2):
            r3 = c % 3
            g = c * NST + n
            ev = None
            for s in range(NSUB):
                ev = P.op(
                    "vector",
                    lambda eng, dt2=dt2, s=s, r3=r3: eng.tensor_mul(
                        s_w[:, dt2, s * 512:(s + 1) * 512],
                        s_du[:, dt2, r3, s * 512:(s + 1) * 512],
                        p_B[:, s, :]),
                    waits=[E[("bb", c, n, s)], E[("du", c, dt2)]],
                    inc=vq)
                if dt2 == 1:
                    E[("w2l", g, s)] = ev
            E[("w", c, dt2, n)] = ev
            hwar = (E.get(("pp", c, dt2, n - 2)) if n >= 2
                    else E.get(("pp", c - 1, dt2, NST - 2 + (n % 2))))
            E[("scan", c, dt2, n)] = P.op(
                "vector",
                lambda eng, dt2=dt2, n=n, c=c: eng.tensor_tensor_scan(
                    s_h[:, dt2, n % 2, :],
                    s_dA[:, dt2, (n + NST * c) % 2, :],
                    s_w[:, dt2, :],
                    (0.0 if c == 0 else s_carry[:, dt2, n:n + 1]),
                    ALU.mult, ALU.add),
                waits=[E[("dA", c, dt2, n)],
                       (E[("carry", c - 1, dt2, n)][0],
                        E[("carry", c - 1, dt2, n)][1], True)
                       if c >= 1 else None, hwar],
                inc=vq)
            sc_ev = E[("scan", c, dt2, n)]
            E[("carry", c, dt2, n)] = P.op(
                "vector",
                lambda eng, dt2=dt2, n=n: eng.tensor_copy(
                    s_carry[:, dt2, n:n + 1],
                    s_h[:, dt2, n % 2, TC2 - 1:TC2]),
                waits=[(sc_ev[0], sc_ev[1], True)],
                inc=vq)

        def em_ctev(c, n):
            g = c * NST + n
            for s in range(NSUB):
                E[("ctev", g, s)] = P.op(
                    "scalar",
                    lambda eng, n=n, s=s: eng.activation(
                        s_Ct[:, n % 2, s * 512:(s + 1) * 512],
                        p_C[:, s, :], AF.Copy),
                    waits=[E[("cb", c, n, s)],
                           E.get(("pp", c, 1, n - 2)) or
                           (E.get(("pp", c - 1, 1, NST - 2 + (n % 2)))
                            if c >= 1 else None)],
                    inc=aq)
            E[("ctevl", c, n)] = E[("ctev", g, NSUB - 1)]

        def em_pp(c, n):
            for dt2 in range(2):
                E[("pp", c, dt2, n)] = P.op(
                    "gpsimd",
                    lambda eng, dt2=dt2, n=n: eng.tensor_mul(
                        s_p[:, dt2, n % 2, :], s_h[:, dt2, n % 2, :],
                        s_Ct[:, n % 2, :]),
                    waits=[E[("scan", c, dt2, n)], E[("ctevl", c, n)],
                           (E.get(("ya", c, dt2, n - 2, NSUB - 1)) if n >= 2
                            else E.get(("ya", c - 1, dt2, NST - 2 + (n % 2),
                                        NSUB - 1)))],
                    inc=pq)

        def em_ya(c, n):
            for dt2 in range(2):
                for s in range(NSUB):
                    E[("ya", c, dt2, n, s)] = P.op(
                        "tensor",
                        lambda eng, dt2=dt2, n=n, s=s: eng.matmul(
                            p_y[:, dt2, s, :], s_ident[:],
                            s_p[:, dt2, n % 2, s * 512:(s + 1) * 512],
                            start=(n == 0), stop=(n == NST - 1)),
                        waits=[E[("pp", c, dt2, n)],
                               (E.get(("yev", c - 1, dt2, s))
                                if n == 0 else None)],
                        inc=tq)

        em_load(0)
        em_load(1)
        em_loadbc(0)
        em_loadbc(1)
        em_du(0)
        em_dA(0, 0)
        em_builds(0)
        for c in range(NC2):
            if c > 0:
                em_du(c)
            em_load(c + 2)
            for n in range(NST):
                g = c * NST + n
                nc_, nn = divmod(g + 1, NST)
                em_dA(nc_, nn)
                em_w2scan(c, n, 0)
                em_ctev(c, n)
                em_w2scan(c, n, 1)
                em_builds(g + 1)
                if n >= 1:
                    em_ya(c, n - 1)
                em_pp(c, n)
            em_ya(c, NST - 1)
            em_loadbc(c + 2)
            r3 = c % 3
            for dt2 in range(2):
                for s in range(NSUB):
                    E[("yev", c, dt2, s)] = P.op(
                        "vector",
                        lambda eng, dt2=dt2, s=s, r3=r3:
                        eng.scalar_tensor_tensor(
                            s_y[:, dt2, s * 512:(s + 1) * 512],
                            s_u[:, dt2, r3, s * 512:(s + 1) * 512],
                            s_D[:, dt2:dt2 + 1],
                            p_y[:, dt2, s, :],
                            ALU.mult, ALU.add),
                        waits=[E[("ya", c, dt2, NST - 1, s)],
                               E.get(("ydma", c - 1, dt2))],
                        inc=vq)
                E[("ydma", c, dt2)] = P.dma(
                    "sync",
                    y_o[dt2 * 128:(dt2 + 1) * 128, c * TC2:(c + 1) * TC2],
                    s_y[:, dt2, :], oy[dt2],
                    waits=[E[("yev", c, dt2, NSUB - 1)]])

        allsems = [ld] + ldl + lu + lbc + oy
        P.op("gpsimd", lambda eng: eng.memset(s_cst[:, 7:8], 0.0),
             waits=[(sm, P.counts[sm.name]) for sm in allsems
                    if sm.name in P.counts] +
                   [(vq, P.counts["vq"]), (aq, P.counts["aq"]),
                    (tq, P.counts["tq"]), (pq, P.counts["pq"])],
             inc=pq, inc_by=1)

        P.emit(blk)
    return nc


def _l2_host_inputs(inputs, res1):
    A_log = np.asarray(inputs["L0_attn_A_log"])
    Dv = np.asarray(inputs["L0_attn_D"])
    u_full = [res1.results[k]["u_o"] for k in range(NCORES)]
    dl_full = [res1.results[k]["dl_o"] for k in range(NCORES)]
    bc_full = np.concatenate([res1.results[k]["bc_o"] for k in range(NCORES)],
                             axis=1)
    B_full = np.ascontiguousarray(bc_full[0:16])
    C_full = np.ascontiguousarray(bc_full[32:48])
    sel = np.zeros((16, 16 * 128), dtype=np.float32)
    for n in range(16):
        sel[n, n * 128:(n + 1) * 128] = 1.0
    ident = np.eye(128, dtype=np.float32)
    cstv = np.zeros((128, 8), dtype=np.float32)
    cstv[:, 0] = EPS
    cstv[:, 1] = 1.0 / D
    cstv[:, 2] = 1.0
    cstv[:, 3] = -1.0
    cstv[:, 4] = -0.5

    maps = []
    for k in range(NCORES):
        lo = k * DSL
        u_sl = np.ascontiguousarray(
            np.concatenate([u_full[j][lo:lo + DSL] for j in range(NCORES)],
                           axis=1))
        dl_sl = np.ascontiguousarray(
            np.concatenate([dl_full[j][lo:lo + DSL] for j in range(NCORES)],
                           axis=1))
        al = np.ascontiguousarray(
            A_log[lo:lo + DSL].reshape(2, 128, NST).transpose(1, 0, 2))
        dv = np.ascontiguousarray(Dv[lo:lo + DSL].reshape(2, 128).T)
        maps.append({
            "u_i": u_sl, "dl_i": dl_sl, "B_i": B_full, "C_i": C_full,
            "alog": al, "dvec": dv, "sel": sel, "ident": ident, "cst": cstv,
        })
    return maps


TCA = 512
NCA = S // TCA
TCB = 256
NCB = S // TCB
NMT = D // 128
NKT = DI // 128


def build_L3(debug=False):
    nc = bass.Bass("TRN2", target_bir_lowering=False, debug=False,
                   detect_race_conditions=False)
    BF16 = mybir.dt.bfloat16

    y_i = nc.dram_tensor("y_i", [DI, S], F32, kind="ExternalInput").ap()
    z_i = nc.dram_tensor("z_i", [DI, S], F32, kind="ExternalInput").ap()
    xr = nc.dram_tensor("xr", [128, NMT, S], F32, kind="ExternalInput").ap()
    outw = nc.dram_tensor("outw", [128, NKT, D], F32, kind="ExternalInput").ap()
    wenc = nc.dram_tensor("wenc", [128, NMT, HS], F32, kind="ExternalInput").ap()
    wdec = nc.dram_tensor("wdec", [128, NKT, D], F32, kind="ExternalInput").ap()
    ws2c = nc.dram_tensor("ws2c", [128, NKT, CM], F32, kind="ExternalInput").ap()
    wr_ = nc.dram_tensor("wr_", [64, 8], F32, kind="ExternalInput").ap()
    w1 = nc.dram_tensor("w1", [128, NE // 2, NE, 128], F32, kind="ExternalInput").ap()
    w2 = nc.dram_tensor("w2", [128, 8, NE, CM], F32, kind="ExternalInput").ap()
    wc2s = nc.dram_tensor("wc2s", [64, HS], F32, kind="ExternalInput").ap()
    benc = nc.dram_tensor("benc", [128, NKT], F32, kind="ExternalInput").ap()
    bs2c = nc.dram_tensor("bs2c", [64, 1], F32, kind="ExternalInput").ap()
    br_ = nc.dram_tensor("br_", [8, 1], F32, kind="ExternalInput").ap()
    b1 = nc.dram_tensor("b1", [128, NE, 8], F32, kind="ExternalInput").ap()
    b2 = nc.dram_tensor("b2", [64, NE], F32, kind="ExternalInput").ap()
    bc2s = nc.dram_tensor("bc2s", [128, NKT], F32, kind="ExternalInput").ap()
    bdec = nc.dram_tensor("bdec", [128, NMT], F32, kind="ExternalInput").ap()
    lng = nc.dram_tensor("lng", [128, NMT], F32, kind="ExternalInput").ap()
    lnb = nc.dram_tensor("lnb", [128, NMT], F32, kind="ExternalInput").ap()
    sel8 = nc.dram_tensor("sel8", [8, NE, CM], F32, kind="ExternalInput").ap()
    ident = nc.dram_tensor("ident", [128, 128], F32, kind="ExternalInput").ap()
    ones128 = nc.dram_tensor("ones128", [128, 2], F32, kind="ExternalInput").ap()
    ones1 = nc.dram_tensor("ones1", [1, 128], F32, kind="ExternalInput").ap()
    cst = nc.dram_tensor("cst", [128, 8], F32, kind="ExternalInput").ap()
    xn2_d = nc.dram_tensor("xn2_d", [D, S], F32,
                           kind=("ExternalOutput" if debug else "Internal"))
    if debug:
        dbg_gate = nc.dram_tensor("dbg_gate", [8, S], F32, kind="ExternalOutput").ap()
        dbg_eo = nc.dram_tensor("dbg_eo", [64, S], F32, kind="ExternalOutput").ap()
        dbg_ffn = nc.dram_tensor("dbg_ffn", [D, S], F32, kind="ExternalOutput").ap()
        dbg_x2 = nc.dram_tensor("dbg_x2", [D, S], F32, kind="ExternalOutput").ap()
        dbg_mub = nc.dram_tensor("dbg_mub", [128, S], F32, kind="ExternalOutput").ap()
        dbg_rsb = nc.dram_tensor("dbg_rsb", [128, S], F32, kind="ExternalOutput").ap()

    out_o = nc.dram_tensor("out_o", [D, S], F32, kind="ExternalOutput").ap()

    with ExitStack() as stk:
        e = stk.enter_context
        s_ws2c = e(nc.sbuf_tensor("s_ws2c", [128, NKT, CM], F32R))
        s_wr = e(nc.sbuf_tensor("s_wr", [64, 8], F32R))
        s_w1 = e(nc.sbuf_tensor("s_w1", [128, NE // 2, NE, 128], F32R))
        s_w2 = e(nc.sbuf_tensor("s_w2", [128, 8, NE, CM], F32R))
        s_wc2s = e(nc.sbuf_tensor("s_wc2s", [64, HS], F32R))
        s_benc = e(nc.sbuf_tensor("s_benc", [128, NKT], F32))
        s_bs2c = e(nc.sbuf_tensor("s_bs2c", [64, 1], F32))
        s_br = e(nc.sbuf_tensor("s_br", [8, 1], F32))
        s_b1 = e(nc.sbuf_tensor("s_b1", [128, NE, 8], F32))
        s_b2 = e(nc.sbuf_tensor("s_b2", [64, NE], F32))
        s_bc2s = e(nc.sbuf_tensor("s_bc2s", [128, NKT], F32))
        s_bdec = e(nc.sbuf_tensor("s_bdec", [128, NMT], F32))
        s_lng = e(nc.sbuf_tensor("s_lng", [128, NMT], F32))
        s_lnb = e(nc.sbuf_tensor("s_lnb", [128, NMT], F32))
        s_sel8 = e(nc.sbuf_tensor("s_sel8", [8, NE, CM], F32R))
        s_ident = e(nc.sbuf_tensor("s_ident", [128, 128], F32))
        s_o128 = e(nc.sbuf_tensor("s_o128", [128, 2], F32R))
        s_o1 = e(nc.sbuf_tensor("s_o1", [1, 128], F32R))
        s_cst = e(nc.sbuf_tensor("s_cst", [128, 8], F32))
        s_sq = e(nc.sbuf_tensor("s_sq3", [128, 2, TCA], F32R))
        s_mu = e(nc.sbuf_tensor("s_mu", [1, TCA], F32R))
        s_mu2 = e(nc.sbuf_tensor("s_mu2", [1, TCA], F32))
        s_var = e(nc.sbuf_tensor("s_var", [1, TCA], F32R))
        s_mub = e(nc.sbuf_tensor("s_mub", [128, TCA], F32))
        s_fnc = e(nc.sbuf_tensor("s_fnc", [1, 256], F32))
        s_rsb = e(nc.sbuf_tensor("s_rsb", [128, TCA], F32))
        p_mm = e(nc.psum_tensor("p_mm3", [128, 2, TCA], F32))
        p_cont = e(nc.psum_tensor("p_cont", [128, TCB], F32))
        p_ge = e(nc.psum_tensor("p_ge", [64, TCB], F32))
        p_ye = e(nc.psum_tensor("p_ye", [64, TCB], F32))
        p_s = e(nc.psum_tensor("p_s", [2, 2, TCA], F32))
        p_msc = e(nc.psum_tensor("p_msc", [128, 512], F32))
        ld = e(nc.semaphore("ld"))
        lwA = e(nc.semaphore("lwA"))
        lwB = e(nc.semaphore("lwB"))
        lwF = e(nc.semaphore("lwF"))
        lws = [e(nc.semaphore(f"lws{i}")) for i in range(4)]
        lyz = [e(nc.semaphore(f"lyz{i}")) for i in range(2)]
        lxt = [e(nc.semaphore(f"lxt{i}")) for i in range(2)]
        lxn = [e(nc.semaphore(f"lxn{i}")) for i in range(2)]
        oxn = [e(nc.semaphore(f"oxn{i}")) for i in range(2)]
        oo = [e(nc.semaphore(f"oo{i}")) for i in range(2)]
        tq = e(nc.semaphore("tq"))
        vq = e(nc.semaphore("vq"))
        aq = e(nc.semaphore("aq"))
        blk = e(nc.Block())

        P = Prog(nc)
        E = {}

        ev_c = None
        for dst, src_ in [(s_wr[:], wr_.bitcast(F32R)), (s_benc[:], benc),
                          (s_bs2c[:], bs2c), (s_br[:], br_), (s_b1[:], b1),
                          (s_b2[:], b2), (s_bc2s[:], bc2s), (s_bdec[:], bdec),
                          (s_lng[:], lng), (s_lnb[:], lnb),
                          (s_sel8[:], sel8.bitcast(F32R)), (s_ident[:], ident),
                          (s_o128[:], ones128.bitcast(F32R)),
                          (s_o1[:], ones1.bitcast(F32R)), (s_cst[:], cst)]:
            ev_c = P.dma("sync", dst, src_, ld)
        P.base_waits = [ev_c]

        lnstate = {"prev": None}

        fnc_i = [0]

        def vfence():
            fnc_i[0] = (fnc_i[0] + 1) % 256
            k = fnc_i[0]
            return P.op("vector",
                        lambda eng, k=k: eng.tensor_copy(s_fnc[0:1, k:k + 1],
                                                         s_cst[0:1, 2:3]),
                        inc=vq)

        def layer_norm_stats(tag, src_fn, src_ev, ntiles, w_):
            prev = lnstate["prev"]
            lnstate[("prev_of", tag)] = prev
            lnstate["prev"] = tag
            pnt = lnstate.get(("nt", prev), 0)
            lnstate[("nt", tag)] = ntiles
            for i in range(ntiles):
                E[(tag, "sq", i)] = P.op(
                    "scalar",
                    lambda eng, i=i: eng.activation(
                        s_sq[:, i % 2, 0:w_], src_fn(i).bitcast(F32),
                        AF.Square),
                    waits=[src_ev(i), E.get((tag, "st2", i - 2)),
                           (E.get((prev, "app2", pnt - 2 + (i % 2)))
                            if prev else None)],
                    inc=aq)
                E[(tag, "st1", i)] = P.op(
                    "tensor",
                    lambda eng, i=i: eng.matmul(
                        p_s[:, 0, 0:w_], s_o128[:], src_fn(i),
                        start=(i == 0), stop=(i == ntiles - 1)),
                    waits=[src_ev(i),
                           (E.get((prev, "mu")) if prev and i == 0 else None)],
                    inc=tq)
                E[(tag, "st2", i)] = P.op(
                    "tensor",
                    lambda eng, i=i: eng.matmul(
                        p_s[:, 1, 0:w_], s_o128[:], s_sq[:, i % 2, 0:w_],
                        start=(i == 0), stop=(i == ntiles - 1)),
                    waits=[E[(tag, "sq", i)],
                           (E.get((prev, "var")) if prev and i == 0 else None)],
                    inc=tq)

        def layer_norm_tail(tag, src_fn, ntiles, w_, affine, dst_fn,
                            dst_war_fn, post_fn=None):
            prev = lnstate[("prev_of", tag)]
            E[(tag, "mu")] = P.op(
                "vector",
                lambda eng: eng.tensor_scalar(
                    s_mu[0:1, 0:w_], p_s[0:1, 0, 0:w_], s_cst[0:1, 1:2],
                    None, ALU.mult),
                waits=[E[(tag, "st1", ntiles - 1)], ev_c,
                       (E.get((prev, "bc1")) if prev else None)],
                inc=vq)
            E[(tag, "mu")] = vfence()
            E[(tag, "mu2")] = P.op(
                "scalar",
                lambda eng: eng.activation(s_mu2[0:1, 0:w_],
                                           s_mu[0:1, 0:w_].bitcast(F32),
                                           AF.Square),
                waits=[E[(tag, "mu")],
                       (E.get((prev, "var")) if prev else None)],
                inc=aq)
            E[(tag, "var")] = P.op(
                "vector",
                lambda eng: eng.scalar_tensor_tensor(
                    s_var[0:1, 0:w_], p_s[0:1, 1, 0:w_],
                    s_cst[0:1, 1:2], s_mu2[0:1, 0:w_],
                    ALU.mult, ALU.subtract),
                waits=[E[(tag, "st2", ntiles - 1)], E[(tag, "mu2")],
                       (E.get((prev, "bc2")) if prev else None)],
                inc=vq)
            E[(tag, "var")] = vfence()
            E[(tag, "rstd")] = P.op(
                "scalar",
                lambda eng: eng.activation(s_var[0:1, 0:w_],
                                           s_var[0:1, 0:w_].bitcast(F32),
                                           AF.Ln, bias=s_cst[0:1, 0:1]),
                waits=[E[(tag, "var")]], inc=aq)
            _r1 = E[(tag, "rstd")]
            E[(tag, "rstd2")] = P.op(
                "scalar",
                lambda eng: eng.activation(s_var[0:1, 0:w_],
                                           s_var[0:1, 0:w_].bitcast(F32),
                                           AF.Exp, scale=s_cst[0:1, 4:5]),
                waits=[(_r1[0], _r1[1], True)], inc=aq)
            E[(tag, "bc1")] = P.op(
                "tensor",
                lambda eng: eng.matmul(p_msc[:, 0:w_], s_o1[:],
                                       s_mu[0:1, 0:w_], start=True, stop=True),
                waits=[E[(tag, "mu")],
                       (E.get((prev, "bcev1")) if prev else None),
                       (E.get(("msc_free", tag[1])) if tag[0] == "lnF"
                        else None)],
                inc=tq)
            E[(tag, "bcev1")] = P.op(
                "scalar",
                lambda eng: eng.activation(s_mub[:, 0:w_], p_msc[:, 0:w_],
                                           AF.Copy),
                waits=[E[(tag, "bc1")],
                       (E.get((prev, "app1_last")) if prev else None)],
                inc=aq)
            E[(tag, "bc2")] = P.op(
                "tensor",
                lambda eng: eng.matmul(p_msc[:, 0:w_], s_o1[:],
                                       s_var[0:1, 0:w_], start=True, stop=True),
                waits=[E[(tag, "rstd2")], E[(tag, "bcev1")]], inc=tq)
            E[(tag, "bcev2")] = P.op(
                "scalar",
                lambda eng: eng.activation(s_rsb[:, 0:w_], p_msc[:, 0:w_],
                                           AF.Copy),
                waits=[E[(tag, "bc2")],
                       (E.get((prev, "app2_last")) if prev else None)],
                inc=aq)
            for i in range(ntiles):
                E[(tag, "app1", i)] = P.op(
                    "vector",
                    lambda eng, i=i: eng.tensor_sub(
                        s_sq[:, i % 2, 0:w_],
                        src_fn(i).bitcast(F32), s_mub[:, 0:w_]),
                    waits=[E[(tag, "bcev1")], E[(tag, "bcev2")],
                           E[(tag, "st2", i)]], inc=vq)
                ev = P.op(
                    "vector",
                    lambda eng, i=i: eng.tensor_mul(
                        dst_fn(i), s_sq[:, i % 2, 0:w_].bitcast(F32),
                        s_rsb[:, 0:w_]),
                    waits=[dst_war_fn(i)], inc=vq)
                if affine is not None:
                    gcol, bcol = affine
                    ev = P.op(
                        "vector",
                        lambda eng, i=i: eng.tensor_scalar(
                            dst_fn(i), dst_fn(i).bitcast(F32),
                            gcol(i), bcol(i), ALU.mult, ALU.add),
                        waits=[ev], inc=vq)
                E[(tag, "app2", i)] = ev
                if post_fn is not None:
                    post_fn(i, ev)
            E[(tag, "app1_last")] = E[(tag, "app1", ntiles - 1)]
            E[(tag, "app2_last")] = E[(tag, "app2", ntiles - 1)]
            return tag

        def layer_norm(tag, src_fn, src_ev, ntiles, w_, affine, dst_fn,
                       dst_war_fn, post_fn=None):
            layer_norm_stats(tag, src_fn, src_ev, ntiles, w_)
            return layer_norm_tail(tag, src_fn, ntiles, w_, affine, dst_fn,
                                   dst_war_fn, post_fn=post_fn)

        s_xnb = e(nc.sbuf_tensor("s_xnb", [128, NMT, TCB], BF16))
        s_xnf = e(nc.sbuf_tensor("s_xnf", [128, NMT, TCB], F32))

        with ExitStack() as stkA:
            eA = stkA.enter_context
            s_outw = eA(nc.sbuf_tensor("s_outw", [128, NKT, D], BF16))
            s_yz = eA(nc.sbuf_tensor("s_yz", [128, 2, 2, TCA], F32))
            s_y2 = eA(nc.sbuf_tensor("s_y2", [128, NKT, TCA], BF16))
            s_xt = eA(nc.sbuf_tensor("s_xt", [128, 2, TCA], F32))
            s_x2 = eA(nc.sbuf_tensor("s_x2", [128, 2, NMT, TCA], F32R))
            s_xn2c = eA(nc.sbuf_tensor("s_xn2c", [128, 2, TCA], F32))

            ev_wA = P.dma("gpsimd", s_outw[:], outw, lwA)
            ev_wffn = None
            for dst, src_ in [(s_ws2c[:], ws2c.bitcast(F32R)),
                              (s_w1[:], w1.bitcast(F32R)),
                              (s_w2[:], w2.bitcast(F32R)),
                              (s_wc2s[:], wc2s.bitcast(F32R))]:
                ev_wffn = P.dma("gpsimd", dst, src_, lwF)

            def emA_y2(c):
                for kt in range(NKT):
                    gyk = c * NKT + kt
                    r = gyk % 2
                    war_yz = (E.get(("y2", (gyk - 2) // NKT, (gyk - 2) % NKT))
                              if gyk >= 2 else None)
                    E[("ly", c, kt)] = P.dma(
                        "sync", s_yz[:, 0, r, :],
                        y_i[kt * 128:(kt + 1) * 128, c * TCA:(c + 1) * TCA],
                        lyz[r], waits=[war_yz])
                    E[("lz", c, kt)] = P.dma(
                        "sync", s_yz[:, 1, r, :],
                        z_i[kt * 128:(kt + 1) * 128, c * TCA:(c + 1) * TCA],
                        lyz[r], waits=[])
                    E[("sz", c, kt)] = P.op(
                        "scalar",
                        lambda eng, r=r: eng.activation(
                            s_yz[:, 1, r, :], s_yz[:, 1, r, :], AF.Silu),
                        waits=[E[("lz", c, kt)]], inc=aq)
                    E[("y2", c, kt)] = P.op(
                        "vector",
                        lambda eng, r=r, kt=kt: eng.tensor_mul(
                            s_y2[:, kt, :], s_yz[:, 0, r, :],
                            s_yz[:, 1, r, :]),
                        waits=[E[("sz", c, kt)], E[("lz", c, kt)],
                               E.get(("mmA", c - 1, NMT - 1, kt))],
                        inc=vq)

            def emA_mm(c):
                for mt in range(NMT):
                    gx = c * NMT + mt
                    E[("lxt", c, mt)] = P.dma(
                        "sync", s_xt[:, gx % 2, :],
                        xr[:, mt, c * TCA:(c + 1) * TCA], lxt[gx % 2],
                        waits=[E.get(("x2ev", (gx - 2) // NMT, (gx - 2) % NMT))
                               if gx >= 2 else None])
                    for kt in range(NKT):
                        E[("mmA", c, mt, kt)] = P.op(
                            "tensor",
                            lambda eng, mt=mt, kt=kt: eng.matmul(
                                p_mm[:, mt % 2, :],
                                s_outw[:, kt, mt * 128:(mt + 1) * 128],
                                s_y2[:, kt, :],
                                start=(kt == 0), stop=(kt == NKT - 1)),
                            waits=[E[("y2", c, kt)],
                                   ev_wA if c == 0 and mt == 0 else None,
                                   ((E.get(("x2ev", c, mt - 2)) if mt >= 2
                                     else E.get(("x2ev", c - 1,
                                                 NMT - 2 + mt)))
                                    if kt == 0 else None)],
                            inc=tq)
                    P.op("vector",
                         lambda eng, mt=mt, gx=gx, c=c: eng.tensor_add(
                             s_x2[:, c % 2, mt, :], p_mm[:, mt % 2, :],
                             s_xt[:, gx % 2, :]),
                         waits=[E[("mmA", c, mt, NKT - 1)], E[("lxt", c, mt)],
                                E.get((("lnA", c - 2), "sq", mt))],
                         inc=vq)
                    E[("x2ev", c, mt)] = vfence()
                if debug == "full":
                    for mt in range(NMT):
                        P.dma("sync", dbg_x2[mt * 128:(mt + 1) * 128,
                                             c * TCA:(c + 1) * TCA],
                              s_x2[:, c % 2, mt, :].bitcast(F32), oo[1],
                              waits=[E[("x2ev", c, mt)]])

            def emA_lnStats(c):
                layer_norm_stats(
                    ("lnA", c), lambda i, c=c: s_x2[:, c % 2, i, :],
                    lambda i, c=c: E[("x2ev", c, i)], NMT, TCA)

            def emA_ln(c):
                tg = ("lnA", c)

                def xn_post(i, ev, c=c):
                    gxi = c * NMT + i
                    E[("xndma", gxi)] = P.dma(
                        "sync", xn2_d.ap()[i * 128:(i + 1) * 128,
                                           c * TCA:(c + 1) * TCA],
                        s_xn2c[:, gxi % 2, :], oxn[gxi % 2],
                        waits=[ev])

                layer_norm_tail(
                    tg, lambda i, c=c: s_x2[:, c % 2, i, :],
                    NMT, TCA, None,
                    lambda i, c=c: s_xn2c[:, (c * NMT + i) % 2, :],
                    lambda i, c=c: E.get(("xndma", c * NMT + i - 2)),
                    post_fn=xn_post)
                if debug == "full":
                    P.dma("sync", dbg_mub[:, c * TCA:(c + 1) * TCA],
                          s_mub[:, :], oo[1], waits=[E[(tg, "bcev1")],
                                                     E[(tg, "app1_last")]])
                    P.dma("sync", dbg_rsb[:, c * TCA:(c + 1) * TCA],
                          s_rsb[:, :], oo[1], waits=[E[(tg, "bcev2")],
                                                     E[(tg, "app2_last")]])

            for c in range(NCA):
                if c >= 1:
                    emA_lnStats(c - 1)
                emA_y2(c)
                emA_mm(c)
                if c >= 1:
                    emA_ln(c - 1)
            emA_lnStats(NCA - 1)
            emA_ln(NCA - 1)
            ev_a_done = E[("xndma", NCA * NMT - 1)]
            ev_a_done0 = E[("xndma", NCA * NMT - 2)]

        with ExitStack() as stkB:
            eB = stkB.enter_context
            s_wenc = eB(nc.sbuf_tensor("s_wenc", [128, NMT, HS], BF16))
            s_wdec = eB(nc.sbuf_tensor("s_wdec", [128, NKT, D], BF16))
            s_spk = eB(nc.sbuf_tensor("s_spk", [128, 2, TCB], F32R))
            s_cont = eB(nc.sbuf_tensor("s_cont", [128, TCB], F32R))
            s_lgt = eB(nc.sbuf_tensor("s_lgt", [8, TCB], F32))
            s_gt = eB(nc.sbuf_tensor("s_gt", [128, 2, 8], F32))
            s_gtmp = eB(nc.sbuf_tensor("s_gtmp", [128, 2, 8], F32))
            s_gm = eB(nc.sbuf_tensor("s_gm", [128, 2, 4], F32))
            s_grow = eB(nc.sbuf_tensor("s_grow", [8, TCB], F32R))
            s_he = eB(nc.sbuf_tensor("s_he", [128, 8, TCB], F32R))
            s_ye = eB(nc.sbuf_tensor("s_ye", [64, TCB], F32))
            s_eo = eB(nc.sbuf_tensor("s_eo", [64, 2, TCB], F32R))
            s_sp2 = eB(nc.sbuf_tensor("s_sp2", [128, NKT, TCB], BF16))
            s_dec = eB(nc.sbuf_tensor("s_dec", [128, NMT, TCB], F32R))
            s_out = eB(nc.sbuf_tensor("s_out", [128, NMT, TCB], F32))

            ev_wg = []
            for g in range(4):
                ev_wg.append(P.dma(
                    "gpsimd", s_wenc[:, :, g * 512:(g + 1) * 512],
                    wenc[:, :, g * 512:(g + 1) * 512], lws[g],
                    waits=([ev_a_done, ev_a_done0, ev_wA] if g == 0 else [])))
            ev_wenc = [ev_wg[kt // 4] for kt in range(NKT)]
            ev_wdec = P.dma("gpsimd", s_wdec[:], wdec, lwB)

            for c in range(NCB):
                ca = (c * TCB) // TCA
                xnrdy = ([E[("xndma", NMT - 2)], E[("xndma", NMT - 1)]]
                         if c == 0 else [ev_a_done, ev_a_done0])
                E[("lxnb", c)] = P.dma(
                    "gpsimd", s_xnb[:, :, :],
                    xn2_d.ap().rearrange("(m p) t -> p m t", p=128)[
                        :, :, c * TCB:(c + 1) * TCB],
                    lxn[0],
                    waits=[ev_a_done, ev_a_done0,
                           E.get(("mmB", c - 1, NKT - 1, NMT - 1))])
                E[("lxnf", c)] = P.dma(
                    "sync", s_xnf[:, :, :],
                    xn2_d.ap().rearrange("(m p) t -> p m t", p=128)[
                        :, :, c * TCB:(c + 1) * TCB],
                    lxn[1],
                    waits=xnrdy + [E.get(("x4", c - 1, NMT - 1))])
                for kt in range(NKT):
                    for mt in range(NMT):
                        E[("mmB", c, kt, mt)] = P.op(
                            "tensor",
                            lambda eng, kt=kt, mt=mt: eng.matmul(
                                p_mm[:, kt % 2, 0:TCB],
                                s_wenc[:, mt, kt * 128:(kt + 1) * 128],
                                s_xnb[:, mt, :],
                                start=(mt == 0), stop=(mt == NMT - 1)),
                            waits=[E[("lxnb", c)],
                                   ev_wenc[kt] if c == 0 else None,
                                   ((E.get(("spk", c, kt - 2)) if kt >= 2
                                     else E.get(("decev", c - 1,
                                                 NMT - 2 + kt)))
                                    if mt == 0 else None)],
                            inc=tq)
                    E[("spk", c, kt)] = P.op(
                        "scalar",
                        lambda eng, kt=kt: eng.activation(
                            s_spk[:, kt % 2, :], p_mm[:, kt % 2, 0:TCB],
                            AF.Sigmoid, bias=s_benc[:, kt:kt + 1]),
                        waits=[E[("mmB", c, kt, NMT - 1)],
                               E.get(("s2c", c, kt - 2)),
                               E.get(("s2c", c - 1, NKT - 2 + (kt % 2)))
                               if kt < 2 else None],
                        inc=aq)
                    E[("s2c", c, kt)] = P.op(
                        "tensor",
                        lambda eng, kt=kt: eng.matmul(
                            p_cont[0:64, :], s_ws2c[:, kt, :],
                            s_spk[:, kt % 2, :],
                            start=(kt == 0), stop=(kt == NKT - 1)),
                        waits=[E[("spk", c, kt)],
                               (ev_wffn if c == 0 and kt == 0 else None),
                               E.get(("contev", c - 1)) if kt == 0 else None],
                        inc=tq)
                ev1 = P.op(
                    "vector",
                    lambda eng: eng.tensor_scalar(
                        s_cont[0:64, :], p_cont[0:64, :], s_bs2c[:, 0:1],
                        None, ALU.add),
                    waits=[E[("s2c", c, NKT - 1)],
                           E.get(("w1m", c - 1, NE - 1, NE - 1))],
                    inc=vq)
                P.op("vector",
                     lambda eng: eng.tensor_scalar(
                         s_cont[64:128, :], p_cont[0:64, :], s_bs2c[:, 0:1],
                         None, ALU.add),
                     inc=vq)
                E[("contev", c)] = vfence()
                E[("wr", c)] = P.op(
                    "tensor",
                    lambda eng: eng.matmul(p_msc[0:8, 0:TCB], s_wr[:],
                                           s_cont[0:64, :],
                                           start=True, stop=True),
                    waits=[E[("contev", c)], E.get(("lgtev", c - 1)),
                           E.get((("lnO", c - 1), "bcev2"))], inc=tq)
                E[("lgtev", c)] = P.op(
                    "vector",
                    lambda eng: eng.tensor_scalar(
                        s_lgt[:, :], p_msc[0:8, 0:TCB], s_br[:, 0:1],
                        None, ALU.add),
                    waits=[E[("wr", c)], E.get(("tp", c - 1, 1))], inc=vq)
                E[("lgtev", c)] = vfence()
                for i in range(2):
                    E[("tp", c, i)] = P.op(
                        "tensor",
                        lambda eng, i=i: eng.transpose(
                            p_msc[:, 256 + i * 8:256 + (i + 1) * 8],
                            s_lgt[:, i * 128:(i + 1) * 128],
                            s_ident[0:8, 0:8]),
                        waits=[E[("lgtev", c)], E.get(("gtev", c - 1))], inc=tq)
                E[("gtev", c)] = P.op(
                    "vector",
                    lambda eng: eng.tensor_copy(
                        s_gt[:, :, :].rearrange("p a b -> p (a b)"),
                        p_msc[:, 256:272]),
                    waits=[E[("tp", c, 0)], E[("tp", c, 1)],
                           E.get(("gdone", c - 1))], inc=vq)

                def gop(fn, waits=(), engname="vector"):
                    return P.op(engname, fn, waits=waits,
                                inc=(vq if engname == "vector" else aq))

                ev0 = E[("gtev", c)]
                evm = gop(lambda eng: eng.tensor_reduce(
                    s_gm[:, :, 0:1], s_gt[:, :, :], mybir.AxisListType.X,
                    ALU.max, negate=True), waits=[ev0])
                evm = vfence()
                eve1 = gop(lambda eng: eng.activation(
                    s_gt[:, 0, :], s_gt[:, 0, :], AF.Exp,
                    bias=s_gm[:, 0, 0:1]), engname="scalar",
                    waits=[(evm[0], evm[1], True)])
                eve2 = gop(lambda eng: eng.activation(
                    s_gt[:, 1, :], s_gt[:, 1, :], AF.Exp,
                    bias=s_gm[:, 1, 0:1]), engname="scalar")
                evm1 = gop(lambda eng: eng.tensor_reduce(
                    s_gm[:, :, 0:1], s_gt[:, :, :], mybir.AxisListType.X,
                    ALU.max), waits=[eve1, eve2])
                ev = (evm1[0], evm1[1], True)
                for i in range(2):
                    ev = gop(lambda eng, i=i: eng.tensor_scalar(
                        s_gtmp[:, i, :], s_gt[:, i, :], s_gm[:, i, 0:1], None,
                        ALU.is_equal), waits=[ev])
                    ev = gop(lambda eng, i=i: eng.scalar_tensor_tensor(
                        s_gtmp[:, i, :], s_gtmp[:, i, :], s_cst[:, 5:6],
                        s_gt[:, i, :], ALU.mult, ALU.add))
                evm2 = gop(lambda eng: eng.tensor_reduce(
                    s_gm[:, :, 3:4], s_gtmp[:, :, :], mybir.AxisListType.X,
                    ALU.max), waits=[ev])
                ev = (evm2[0], evm2[1], True)
                for i in range(2):
                    ev = gop(lambda eng, i=i: eng.tensor_scalar(
                        s_gtmp[:, i, :], s_gt[:, i, :], s_gm[:, i, 0:1], None,
                        ALU.is_equal), waits=[ev])
                    ev = gop(lambda eng, i=i: eng.tensor_scalar(
                        s_gtmp[:, i, :], s_gtmp[:, i, :], s_gm[:, i, 0:1],
                        None, ALU.mult))
                    ev = gop(lambda eng, i=i: eng.tensor_scalar(
                        s_gt[:, i, :], s_gt[:, i, :], s_gm[:, i, 3:4], None,
                        ALU.is_equal))
                    ev = gop(lambda eng, i=i: eng.scalar_tensor_tensor(
                        s_gt[:, i, :], s_gt[:, i, :], s_gm[:, i, 3:4],
                        s_gtmp[:, i, :], ALU.mult, ALU.add))
                evdn = gop(lambda eng: eng.tensor_add(
                    s_gm[:, :, 0:1], s_gm[:, :, 0:1], s_gm[:, :, 3:4]),
                    waits=[ev])
                evrg = gop(lambda eng: eng.reciprocal(
                    s_gm[:, :, 2:3], s_gm[:, :, 0:1]),
                    waits=[(evdn[0], evdn[1], True)])
                ev = (evrg[0], evrg[1], True)
                for i in range(2):
                    ev = gop(lambda eng, i=i: eng.tensor_scalar(
                        s_gt[:, i, :], s_gt[:, i, :], s_gm[:, i, 2:3], None,
                        ALU.mult), waits=[ev])
                ev_gate = vfence()
                E[("gdone", c)] = ev_gate
                for i in range(2):
                    E[("tpb", c, i)] = P.op(
                        "tensor",
                        lambda eng, i=i: eng.transpose(
                            p_msc[0:8, 256 + i * 128:256 + (i + 1) * 128],
                            s_gt[:, i, :], s_ident[:]),
                        waits=[ev_gate, E.get(("growev", c - 1))], inc=tq)
                E[("growev", c)] = P.op(
                    "scalar",
                    lambda eng: eng.activation(s_grow[:, :],
                                               p_msc[0:8, 256:512], AF.Copy),
                    waits=[E[("tpb", c, 0)], E[("tpb", c, 1)]], inc=aq)
                if debug == "full":
                    P.dma("sync", dbg_gate[:, c * TCB:(c + 1) * TCB],
                          s_grow[:, :].bitcast(F32), oo[0],
                          waits=[E[("growev", c)]])
                for ex in range(NE):
                    E[("ge", c, ex)] = P.op(
                        "tensor",
                        lambda eng, ex=ex: eng.matmul(
                            p_ge[:, :], s_sel8[:, ex, :], s_grow[:, :],
                            start=True, stop=True),
                        waits=[E[("growev", c)],
                               (E.get(("eotmp", c, ex - 1)) if ex >= 1
                                else E.get(("eotmp", c - 1, NE - 1)))],
                        inc=tq)
                    base = 64 * (ex % 2)
                    for mt in range(NE):
                        E[("w1m", c, ex, mt)] = P.op(
                            "tensor",
                            lambda eng, ex=ex, mt=mt, base=base: eng.matmul(
                                p_mm[:, mt % 2, 0:TCB],
                                s_w1[base:base + 64, ex // 2, mt, :],
                                s_cont[base:base + 64, :],
                                start=True, stop=True),
                            waits=[E[("contev", c)],
                                   (ev_wffn if c == 0 and ex == 0 and mt == 0
                                    else None),
                                   (E.get(("he", c, ex, mt - 2)) if mt >= 2
                                    else (E.get(("he", c, ex - 1, NE - 2 + mt))
                                          if ex >= 1
                                          else E.get(("he", c - 1, NE - 1,
                                                      NE - 2 + mt))))],
                            inc=tq)
                        E[("he", c, ex, mt)] = P.op(
                            "scalar",
                            lambda eng, ex=ex, mt=mt: eng.activation(
                                s_he[:, mt, :], p_mm[:, mt % 2, 0:TCB],
                                AF.Relu, bias=s_b1[:, ex, mt:mt + 1]),
                            waits=[E[("w1m", c, ex, mt)],
                                   (E.get(("w2m", c, ex - 1, NE - 1))
                                    if mt < 2 and ex >= 1 else None),
                                   (E.get(("w2m", c - 1, NE - 1, NE - 1))
                                    if mt < 2 and ex == 0 else None)],
                            inc=aq)
                    for kt in range(NE):
                        E[("w2m", c, ex, kt)] = P.op(
                            "tensor",
                            lambda eng, ex=ex, kt=kt: eng.matmul(
                                p_ye[:, :], s_w2[:, kt, ex, :],
                                s_he[:, kt, :],
                                start=(kt == 0), stop=(kt == NE - 1)),
                            waits=[E[("he", c, ex, kt)],
                                   (ev_wffn if c == 0 and ex == 0 and kt == 0
                                    else None),
                                   ((E.get(("yeev", c, ex - 1)) if ex >= 1
                                     else E.get(("yeev", c - 1, NE - 1)))
                                    if kt == 0 else None)],
                            inc=tq)
                    E[("yeev", c, ex)] = P.op(
                        "vector",
                        lambda eng, ex=ex: eng.tensor_scalar(
                            s_ye[:, :], p_ye[:, :], s_b2[:, ex:ex + 1],
                            None, ALU.add),
                        waits=[E[("w2m", c, ex, NE - 1)],
                               (E.get(("eotmp", c, ex - 1)) if ex >= 1
                                else E.get(("eotmp", c - 1, NE - 1)))],
                        inc=vq)
                    E[("eotmp", c, ex)] = P.op(
                        "vector",
                        lambda eng, ex=ex: eng.tensor_mul(
                            s_eo[:, 1, :], s_ye[:, :], p_ge[:, :]),
                        waits=[E[("yeev", c, ex)], E[("ge", c, ex)],
                               E.get(("eoev", c, ex - 1))], inc=vq)
                    if ex == 0:
                        E[("eoev", c, ex)] = P.op(
                            "vector",
                            lambda eng: eng.tensor_copy(
                                s_eo[:, 0, :], s_eo[:, 1, :].bitcast(F32)),
                            waits=[E.get(("c2s", c - 1, NKT - 1))], inc=vq)
                    else:
                        E[("eoev", c, ex)] = P.op(
                            "vector",
                            lambda eng: eng.tensor_add(
                                s_eo[:, 0, :], s_eo[:, 0, :].bitcast(F32),
                                s_eo[:, 1, :].bitcast(F32)),
                            inc=vq)
                if debug == "full":
                    P.dma("sync", dbg_eo[:, c * TCB:(c + 1) * TCB],
                          s_eo[:, 0, :].bitcast(F32), oo[0],
                          waits=[E[("eoev", c, NE - 1)]])
                E[("eoev", c, NE - 1)] = vfence()
                for mt in range(NKT):
                    E[("c2s", c, mt)] = P.op(
                        "tensor",
                        lambda eng, mt=mt: eng.matmul(
                            p_mm[:, mt % 2, 0:TCB],
                            s_wc2s[:, mt * 128:(mt + 1) * 128],
                            s_eo[:, 0, :],
                            start=True, stop=True),
                        waits=[E[("eoev", c, NE - 1)],
                               (ev_wffn if c == 0 and mt == 0 else None),
                               (E.get(("sp2", c, mt - 2)) if mt >= 2
                                else E.get(("decev", c - 1, NMT - 2 + mt)))],
                        inc=tq)
                    E[("sp2", c, mt)] = P.op(
                        "scalar",
                        lambda eng, mt=mt: eng.activation(
                            s_sp2[:, mt, :], p_mm[:, mt % 2, 0:TCB],
                            AF.Sigmoid, bias=s_bc2s[:, mt:mt + 1]),
                        waits=[E[("c2s", c, mt)]], inc=aq)
                for mt in range(NMT):
                    for kt in range(NKT):
                        E[("mmD", c, mt, kt)] = P.op(
                            "tensor",
                            lambda eng, mt=mt, kt=kt: eng.matmul(
                                p_mm[:, mt % 2, 0:TCB],
                                s_wdec[:, kt, mt * 128:(mt + 1) * 128],
                                s_sp2[:, kt, :],
                                start=(kt == 0), stop=(kt == NKT - 1)),
                            waits=[E[("sp2", c, kt)],
                                   (ev_wdec if c == 0 and mt == 0 and kt == 0
                                    else None),
                                   ((E.get(("decev", c, mt - 2)) if mt >= 2
                                     else E.get(("sp2", c, NKT - 2 + mt)))
                                    if kt == 0 else None)],
                            inc=tq)
                    E[("decev", c, mt)] = P.op(
                        "vector",
                        lambda eng, mt=mt: eng.tensor_scalar(
                            s_dec[:, mt, :], p_mm[:, mt % 2, 0:TCB],
                            s_bdec[:, mt:mt + 1], None, ALU.add),
                        waits=[E[("mmD", c, mt, NKT - 1)],
                               E.get((("lnO", c - 1), "app1", mt))],
                        inc=vq)
                E[("msc_free", c)] = E[("growev", c)]
                tg2 = ("lnF", c)
                layer_norm(
                    tg2, lambda i: s_dec[:, i, :],
                    lambda i, c=c: E[("decev", c, i)],
                    NMT, TCB,
                    (lambda i: s_lng[:, i:i + 1], lambda i: s_lnb[:, i:i + 1]),
                    lambda i: s_dec[:, i, :],
                    lambda i: None)
                if debug == "full":
                    for mt in range(NMT):
                        P.dma("sync", dbg_ffn[mt * 128:(mt + 1) * 128,
                                              c * TCB:(c + 1) * TCB],
                              s_dec[:, mt, :].bitcast(F32), oo[0],
                              waits=[E[(tg2, "app2", mt)]])
                for mt in range(NMT):
                    E[("x4", c, mt)] = P.op(
                        "vector",
                        lambda eng, mt=mt: eng.tensor_add(
                            s_dec[:, mt, :], s_dec[:, mt, :].bitcast(F32),
                            s_xnf[:, mt, :]),
                        waits=[E[(tg2, "app2", mt)], E[("lxnf", c)],
                               ((oo[0], P.counts.get("oo0", 0))
                                if debug == "full" else None)],
                        inc=vq)
                E[("x4f", c)] = vfence()
                E[("B", c, "dece_war")] = E[("x4f", c)]
                tg3 = ("lnO", c)
                layer_norm(
                    tg3, lambda i: s_dec[:, i, :],
                    lambda i, c=c: E[("x4f", c)],
                    NMT, TCB, None,
                    lambda i: s_out[:, i, :],
                    lambda i, c=c: E.get(("odma", c - 1, NMT - 1)))
                for mt in range(NMT):
                    E[("odma", c, mt)] = P.dma(
                        "sync", out_o[mt * 128:(mt + 1) * 128,
                                      c * TCB:(c + 1) * TCB],
                        s_out[:, mt, :], oo[c % 2],
                        waits=[E[(tg3, "app2", mt)]])

            allsems = [ld, lwA, lwB, lwF] + lws + lyz + lxt + lxn + oxn + oo
            P.op("gpsimd", lambda eng: eng.memset(s_cst[:, 7:8], 0.0),
                 waits=[(sm, P.counts[sm.name]) for sm in allsems
                        if sm.name in P.counts] +
                       [(vq, P.counts["vq"]), (aq, P.counts["aq"]),
                        (tq, P.counts["tq"])],
                 inc=ld, inc_by=1)

            P.emit(blk)
    return nc


def _l3_host_inputs(inputs, res1, res2):
    out_proj_w = np.asarray(inputs["L0_attn_out_proj_w"])
    W_enc = np.asarray(inputs["L0_ffn_W_enc"])
    W_s2c = np.asarray(inputs["L0_ffn_W_s2c"])
    Wr = np.asarray(inputs["L0_ffn_Wr"])
    W1 = np.asarray(inputs["L0_ffn_W1"])
    W2 = np.asarray(inputs["L0_ffn_W2"])
    W_c2s = np.asarray(inputs["L0_ffn_W_c2s"])
    W_dec = np.asarray(inputs["L0_ffn_W_dec"])

    outw = np.ascontiguousarray(
        out_proj_w.T.reshape(NKT, 128, D).transpose(1, 0, 2))
    wenc = np.ascontiguousarray(W_enc.reshape(NMT, 128, HS).transpose(1, 0, 2))
    wdec = np.ascontiguousarray(W_dec.reshape(NKT, 128, D).transpose(1, 0, 2))
    ws2c = np.ascontiguousarray(W_s2c.reshape(NKT, 128, CM).transpose(1, 0, 2))
    w1 = np.zeros((128, NE // 2, NE, 128), dtype=np.float32)
    for ex in range(NE):
        base = 64 * (ex % 2)
        w1[base:base + 64, ex // 2] = W1[ex].reshape(CM, NE, 128).transpose(
            0, 1, 2)[:, :, :]
    w2 = np.ascontiguousarray(
        W2.reshape(NE, 8, 128, CM).transpose(2, 1, 0, 3))
    wc2s = np.ascontiguousarray(W_c2s)
    benc = np.ascontiguousarray(
        np.asarray(inputs["L0_ffn_b_enc"]).reshape(NKT, 128).T)
    bs2c = np.asarray(inputs["L0_ffn_b_s2c"]).reshape(CM, 1).astype(np.float32)
    br = np.asarray(inputs["L0_ffn_br"]).reshape(8, 1).astype(np.float32)
    b1 = np.ascontiguousarray(
        np.asarray(inputs["L0_ffn_b1"]).reshape(NE, NE, 128).transpose(2, 0, 1))
    b2 = np.ascontiguousarray(np.asarray(inputs["L0_ffn_b2"]).T)
    bc2s = np.ascontiguousarray(
        np.asarray(inputs["L0_ffn_b_c2s"]).reshape(NKT, 128).T)
    bdec = np.ascontiguousarray(
        np.asarray(inputs["L0_ffn_b_dec"]).reshape(NMT, 128).T)
    lng = np.ascontiguousarray(
        np.asarray(inputs["L0_ffn_ln_g"]).reshape(NMT, 128).T)
    lnb = np.ascontiguousarray(
        np.asarray(inputs["L0_ffn_ln_b"]).reshape(NMT, 128).T)
    sel8 = np.zeros((8, NE, CM), dtype=np.float32)
    for ex in range(NE):
        sel8[ex, ex, :] = 1.0
    ident = np.eye(128, dtype=np.float32)
    ones128 = np.ones((128, 2), dtype=np.float32)
    ones1 = np.ones((1, 128), dtype=np.float32)
    cstv = np.zeros((128, 8), dtype=np.float32)
    cstv[:, 0] = EPS
    cstv[:, 1] = 1.0 / D
    cstv[:, 2] = 1.0
    cstv[:, 3] = -1.0
    cstv[:, 4] = -0.5
    cstv[:, 5] = -1e30

    x = np.ascontiguousarray(
        np.asarray(inputs["L0_attn_x"])[0].T).astype(np.float32, copy=False)
    y_sl = [res2.results[k]["y_o"] for k in range(NCORES)]
    z_full = [res1.results[k]["z_o"][:, HALO:] for k in range(NCORES)]

    maps = []
    for k in range(NCORES):
        y_st = np.ascontiguousarray(
            np.concatenate([y_sl[j][:, k * S:(k + 1) * S]
                            for j in range(NCORES)], axis=0))
        xr_k = np.ascontiguousarray(
            x[:, k * S:(k + 1) * S].reshape(NMT, 128, S).transpose(1, 0, 2))
        maps.append({
            "y_i": y_st, "z_i": np.ascontiguousarray(z_full[k]), "xr": xr_k,
            "outw": outw, "wenc": wenc, "wdec": wdec, "ws2c": ws2c,
            "wr_": Wr.astype(np.float32), "w1": w1, "w2": w2, "wc2s": wc2s,
            "benc": benc, "bs2c": bs2c, "br_": br, "b1": b1, "b2": b2,
            "bc2s": bc2s, "bdec": bdec, "lng": lng, "lnb": lnb,
            "sel8": sel8, "ident": ident, "ones128": ones128, "ones1": ones1,
            "cst": cstv,
        })
    return maps


def kernel(**inputs):
    l1_maps = _l1_host_inputs(inputs)
    nc1 = build_L1()
    res1 = run_bass_kernel_spmd(nc1, l1_maps, core_ids=list(range(NCORES)))
    l2_maps = _l2_host_inputs(inputs, res1)
    nc2 = build_L2()
    res2 = run_bass_kernel_spmd(nc2, l2_maps, core_ids=list(range(NCORES)))
    l3_maps = _l3_host_inputs(inputs, res1, res2)
    nc3 = build_L3()
    res3 = run_bass_kernel_spmd(nc3, l3_maps, core_ids=list(range(NCORES)))
    out = np.concatenate([res3.results[k]["out_o"] for k in range(NCORES)],
                         axis=1)
    return np.ascontiguousarray(out.T).reshape(1, T, D)


if __name__ == "__main__":
    pass
